# version 10
# baseline (speedup 1.0000x reference)
import os
import numpy as np
import ml_dtypes
from contextlib import ExitStack
import concourse.bass as bass
import concourse.mybir as mybir
from concourse.bass_utils import run_bass_kernel_spmd

F32 = mybir.dt.float32
BF16 = mybir.dt.bfloat16
AF = mybir.ActivationFunctionType
ALU = mybir.AluOpType
AX = mybir.AxisListType

D = 2048
NH = 8
DV = 256
NKC = 16
E = 16
NEG = -1e30
QSCALE = float(128) ** -0.5
NU = 128 * 2056


def build_nc(TPC):
    KPH = int(os.environ.get("KPH", "99"))
    NCH = TPC // 128
    NTB = TPC // 512
    nc = bass.Bass("TRN2", target_bir_lowering=False)

    def di(name, shape, dt=F32):
        return nc.dram_tensor(name, shape, dt, kind="ExternalInput").ap()

    xT = di("xT", [D, TPC])
    cTb = di("cTb", [128, NKC])
    wmod_sl = di("wmod_sl", [D, 3072])
    wqT = di("wqT", [128, NH, NKC * 128])
    wkT = di("wkT", [128, NH, NKC * 128])
    wvT = di("wvT", [128, NH, NKC * 256])
    wogT = di("wogT", [128, NKC, NKC * 128])
    wifT = di("wifT", [128, 1, NKC * 16])
    bifi = di("bifi", [8, 1])
    biff = di("biff", [8, 1])
    gamma = di("gamma", [1, NH * DV])
    woutT = di("woutT", [128, NKC, NKC * 128])
    gateT = di("gateT", [128, 1, NKC * 16])
    upT = di("upT", [E, 128, 22, NKC * 128])
    downT = di("downT", [E, 128, 16, 11 * 128])
    supT = di("supT", [128, 44, NKC * 128])
    sdownT = di("sdownT", [128, 16, 22 * 128])
    mask01 = di("mask01", [128, 128])
    identb = di("identb", [128, 128], BF16)
    identf = di("identf", [128, 128])
    tril16 = di("tril16", [128, 16])
    tril256 = di("tril256", [128, 256])
    ccflag = di("ccflag", [8, 3])
    ccneg = di("ccneg", [8, 3])
    yT = nc.dram_tensor("yT", [D, TPC], F32, kind="ExternalOutput").ap()

    ag1_in = nc.dram_tensor("ag1_in", [1, 3072], F32, kind="Internal").ap()
    ag1_out = nc.dram_tensor("ag1_out", [4, 3072], F32, kind="Internal").ap()
    ag2_in = nc.dram_tensor("ag2_in", [1, NU + 32], F32, kind="Internal").ap()
    ag2_out = nc.dram_tensor("ag2_out", [4, NU + 32], F32, kind="Internal").ap()

    ctx = ExitStack()
    ctx.__enter__()

    def sb(name, shape, dt=F32):
        return ctx.enter_context(nc.sbuf_tensor(name, shape, dt))

    h16 = sb("h16", [128, NCH * 2056], BF16)
    ogb = sb("ogb", [128, NKC * TPC], BF16)
    v1 = sb("v1", [128, NCH * 2056], BF16)
    bqk = sb("bqk", [128, NKC * TPC], BF16)
    wbuf = sb("wbuf", [128, 9728], BF16)
    wsm = sb("wsm", [128, 512], BF16)
    cst = sb("cst", [128, 2056])
    c16 = sb("c16", [128, 2056], BF16)
    lanA = sb("lanA", [128, TPC])
    lanB = sb("lanB", [128, TPC])
    e32t = sb("e32t", [128, NCH * 32])
    e24t = sb("e24t", [128, NCH * 24])
    e24b = sb("e24b", [128, NCH * 24], BF16)
    modv = sb("modv", [128, 6 * 16])
    gab = sb("gab", [128, NH * DV], BF16)
    mkb = sb("mkb", [128, 128])
    idb = sb("idb", [128, 128], BF16)
    idf = sb("idf", [128, 128])
    t16c = sb("t16c", [128, 16])
    t256c = sb("t256c", [128, 256])
    onesf = sb("onesf", [1, 128])
    onesbf = sb("onesbf", [1, 128], BF16)
    onescb = sb("onescb", [128, 1], BF16)
    scT = sb("scT", [128, NKC], BF16)
    bifib = sb("bifib", [8, 1])
    biffb = sb("biffb", [8, 1])
    bifis = sb("bifis", [8, 1])
    biffs = sb("biffs", [8, 1])
    gmr = sb("gmr", [1, NH * DV])
    ccf = sb("ccf", [8, 3])
    ccn = sb("ccn", [8, 3])
    wrowp = sb("wrowp", [1, 8])
    wrow1 = sb("wrow1", [1, 8])
    wbcs = sb("wbcs", [128, 16])
    scrA = sb("scrA", [128, 1024])
    a3s = sb("a3s", [128, 2 * 128], BF16)
    k1s = sb("k1s", [128, 2 * 128], BF16)
    t1s = sb("t1s", [128, 2056], BF16)
    xsq = t1s
    sq8 = sb("sq8", [128, 48])
    sgil = sb("sgil", [128, 2 * 512], BF16)
    tmul = sb("tmul", [128, 2 * 512], BF16)
    x2f = sb("x2f", [128, 2 * 512])
    xre = sb("xre", [128, 2 * 512])
    scrC = xre
    lanC = xre
    cwl16 = gab[0:16, 0:TPC]
    cwrow = scrA[0:1, :].bitcast(BF16)
    cwbc = gab[:, TPC:2 * TPC] if TPC <= NH * DV // 2 else gab[:, 0:TPC]
    g16a = sb("g16a", [128, 16])
    g16b = sb("g16b", [128, 16])
    g16d = sb("g16d", [128, 16])
    g4a = sb("g4a", [128, 4])
    g256 = e32t[:, 0:256]
    g256b = t1s[:, 0:512].bitcast(F32)
    gc1 = sb("gc1", [128, 1])
    gc2 = sb("gc2", [128, 1])

    hob = scrA[:].bitcast(BF16)
    uvw = v1[:].bitcast(F32)

    def psm(name):
        return ctx.enter_context(nc.psum_tensor(name, [128, 512], F32))

    pA, pB, pC, pD, pE, pF, pG = (psm(n) for n in "pA pB pC pD pE pF pG".split())
    PSF = [pA, pB, pC, pD, pE, pF, pG]
    pT = ctx.enter_context(nc.psum_tensor("pT", [128, 1024], BF16))

    SEM, C = {}, {}
    for s in ["qsc", "qxb", "qw0", "qw1", "qw2", "qw3", "qc", "qio",
              "qxr0", "qxr1", "qst0", "qst1", "qac0", "qac1",
              "ste", "sve", "sse", "scc"]:
        SEM[s] = ctx.enter_context(nc.semaphore(s))
        C[s] = 0
    GSEM = ctx.enter_context(nc.semaphore("gsem"))
    G = [0]
    EV = []
    EVG = []
    PHM = {}

    def PH(n):
        if n not in PHM:
            PHM[n] = len(EV)

    def on(eng, fn):
        EV.append((eng, fn))
        EVG.append(G[0])

    def chain(eng):
        v = G[0]
        if v:
            on(eng, lambda e, v=v: e.wait_ge(GSEM, v))

    def dma(eng, out, in_, sem, accum=False, slow=False):
        chain(eng)
        G[0] += 16
        C[sem] += 16
        v = C[sem]
        kw = {}
        if accum:
            kw["accum_op"] = ALU.add
        if slow:
            kw["allow_slow_non_contiguous"] = True
        on(eng, lambda e, o=out, i=in_, kw=kw: e.dma_start(out=o, in_=i, **kw).then_inc(GSEM, 16))
        return v

    def wait(eng, sem, val):
        pass

    def op(eng, fn, sem=None):
        chain(eng)
        G[0] += 1
        if eng == "vector":
            on(eng, fn)
            on(eng, lambda e: e.drain())
            on(eng, lambda e: e.sem_inc(GSEM, 1))
        else:
            on(eng, lambda e, f=fn: f(e).then_inc(GSEM, 1))
        if sem is None:
            return None
        C[sem] += 1
        return C[sem]

    def drain(eng):
        on(eng, lambda e: e.drain())

    def opd(eng, fn, sem):
        chain(eng)
        G[0] += 1
        on(eng, fn)
        on(eng, lambda e: e.drain())
        on(eng, lambda e: e.sem_inc(GSEM, 1))
        C[sem] += 1
        return C[sem]


    ld0 = 0
    for out, src in [(mkb[:], mask01), (idb[:], identb), (idf[:], identf),
                     (gmr[:], gamma), (bifib[:], bifi), (biffb[:], biff), (ccf[:], ccflag),
                     (ccn[:], ccneg), (t16c[:], tril16), (t256c[:], tril256)]:
        ld0 = dma("sync", out, src, "qc")
    csem = dma("gpsimd", scT[:], cTb, "qsc")
    xbsem = dma("gpsimd", ogb[:, 0:NKC * TPC].rearrange("p (g t) -> p g t", g=NKC),
                xT.rearrange("(g p) t -> g p t", p=128).transpose([1, 0, 2]), "qxb")
    op("vector", lambda e: e.memset(onesf[:], 1.0))
    op("vector", lambda e: e.memset(onesbf[:], 1.0))
    op("vector", lambda e: e.memset(onescb[:], 1.0))
    op("vector", lambda e: e.memset(lanB[96:104, :], 0.0))
    op("vector", lambda e: e.memset(cst[:], 0.0))
    op("vector", lambda e: e.memset(c16[:], 0.0))
    vini = op("vector", lambda e: e.memset(v1[:], 1.0), "sve")
    wait("vector", "qc", ld0)
    op("vector", lambda e: e.tensor_scalar(bifis[:], bifib[:], 1.0 / 15.0, None, ALU.mult))
    vbif = op("vector", lambda e: e.tensor_scalar(biffs[:], biffb[:], 1.0 / 15.0, None, ALU.mult), "sve")
    gv = 0
    wait("tensor", "qc", ld0)
    wait("tensor", "sve", vini)
    for h in range(NH):
        p = PSF[h % 2]
        if h >= 2:
            wait("tensor", "sve", gv - 1)
        tv = op("tensor", lambda e, h=h, p=p: e.matmul(
            p[:, 0:256], onesf[0:1, :], gmr[0:1, h * 256:h * 256 + 256],
            start=True, stop=True), "ste")
        wait("vector", "ste", tv)
        gv = op("vector", lambda e, h=h, p=p: e.tensor_copy(
            gab[:, h * 256:h * 256 + 256], p[:, 0:256]), "sve")

    PH(1)
    wait("scalar", "qsc", csem)
    scs = op("scalar", lambda e: e.activation(scT[:], scT[:], AF.Silu), "sse")
    mod_last = [0] * NKC
    for kk in range(NKC):
        so = (kk % 2) * 3072
        if kk >= 2:
            wait("gpsimd", "ste", mod_last[kk - 2])
        wl = dma("gpsimd", wbuf[:, so:so + 3072],
                 wmod_sl[kk * 128:kk * 128 + 128, :], "qw0" if kk % 2 == 0 else "qw1")
        wait("tensor", "qw0" if kk % 2 == 0 else "qw1", wl)
        if kk == 0:
            wait("tensor", "sse", scs)
            wait("tensor", "sve", gv)
        mv = 0
        for nb in range(6):
            mv = op("tensor", lambda e, kk=kk, nb=nb, so=so: e.matmul(
                PSF[nb][0:1, :], scT[:, kk:kk + 1],
                wbuf[:, so + nb * 512: so + nb * 512 + 512],
                start=(kk == 0), stop=(kk == NKC - 1)), "ste")
        mod_last[kk] = mv
    wait("vector", "ste", mod_last[NKC - 1])
    mev = 0
    for nb in range(6):
        r = (nb // 2) * 32
        co = (nb % 2) * 512
        mev = op("vector", lambda e, nb=nb, r=r, co=co: e.tensor_copy(
            scrA[r:r + 1, co:co + 512], PSF[nb][0:1, :]), "sve")
    wait("sync", "sve", mev)
    agi = dma("sync", ag1_in[0, :].rearrange("(a b) -> a b", a=3),
              scrA[0:65:32, 0:1024], "qio")
    wait("gpsimd", "qio", agi)
    cc1 = op("gpsimd", lambda e: e.collective_compute(
        "AllGather", ALU.bypass, ins=[ag1_in[:]], outs=[ag1_out[:]],
        replica_groups=[[0, 1, 2, 3], [4, 5, 6, 7]]), "scc")
    wait("sync", "scc", cc1)
    ag1f = ag1_out.rearrange("a b -> (a b)")
    mvl = 0
    for v6 in range(6):
        mvl = dma("sync", modv[:, v6 * 16:v6 * 16 + 16],
                  ag1f[2048 * v6:2048 * v6 + 2048].rearrange("(c p) -> p c", p=128), "qio",
                  slow=True)
    wait("vector", "qio", mvl)
    op("vector", lambda e: e.tensor_scalar(modv[:, 16:32], modv[:, 16:32], 1.0, None, ALU.add))
    op("vector", lambda e: e.tensor_scalar(modv[:, 64:80], modv[:, 64:80], 1.0, None, ALU.add))
    mgv = 0
    for blk in (2, 5):
        sl = modv[:, blk * 16:blk * 16 + 16]
        op("vector", lambda e, sl=sl: e.tensor_scalar(g16a[:], sl, -1.0, None, ALU.mult))
        op("vector", lambda e, sl=sl: e.tensor_scalar(g16b[:], sl, 0.0, None, ALU.is_ge))
        drain("vector")
        op("vector", lambda e, sl=sl: e.tensor_max(g16d[:], sl, g16a[:]))
        drain("vector")
        op("vector", lambda e: e.tensor_scalar(g16a[:], g16d[:], 0.25, None, ALU.max))
        drain("vector")
        op("vector", lambda e: e.tensor_scalar(g16b[:], g16b[:], 2.0, -1.0, ALU.mult, ALU.add))
        drain("vector")
        mgv = op("vector", lambda e, sl=sl: e.tensor_mul(sl, g16a[:], g16b[:]), "sve")

    def layer_norm(src16, dst16, cacol, sacol, rsem, rval):
        wait("scalar", rsem, rval)
        wait("tensor", rsem, rval)
        wait("tensor", "sve", C["sve"])
        mm_last = [0] * NKC
        for kk in range(NKC):
            so = (kk % 2) * TPC
            if kk >= 2:
                wait("scalar", "ste", mm_last[kk - 2])
            sqd = op("scalar", lambda e, kk=kk, so=so: e.activation(
                xsq[:, so:so + TPC], src16[:, kk * TPC:kk * TPC + TPC], AF.Square), "sse")
            for tb in range(NTB):
                op("tensor", lambda e, kk=kk, tb=tb: e.matmul(
                    [pA, pB][tb][0:1, :], onescb[:, 0:1],
                    src16[:, kk * TPC + tb * 512: kk * TPC + tb * 512 + 512],
                    start=(kk == 0), stop=(kk == NKC - 1)), "ste")
            wait("tensor", "sse", sqd)
            mm2 = 0
            for tb in range(NTB):
                mm2 = op("tensor", lambda e, kk=kk, tb=tb, so=so: e.matmul(
                    [pC, pD][tb][0:1, :], onescb[:, 0:1],
                    xsq[:, so + tb * 512: so + tb * 512 + 512],
                    start=(kk == 0), stop=(kk == NKC - 1)), "ste")
            mm_last[kk] = mm2
        PH(21)
        wait("vector", "ste", mm_last[NKC - 1])
        for tb in range(NTB):
            op("vector", lambda e, tb=tb: e.tensor_scalar(
                lanB[0:1, tb * 512:tb * 512 + 512], [pA, pB][tb][0:1, :],
                1.0 / D, None, ALU.mult))
            op("vector", lambda e, tb=tb: e.tensor_scalar(
                lanA[0:1, tb * 512:tb * 512 + 512], [pC, pD][tb][0:1, :],
                1.0 / D, None, ALU.mult))
        drain("vector")
        op("vector", lambda e: e.tensor_mul(lanC[0:1, 0:TPC],
                                            lanB[0:1, 0:TPC], lanB[0:1, 0:TPC]))
        drain("vector")
        op("vector", lambda e: e.tensor_sub(lanB[32:33, 0:TPC],
                                            lanA[0:1, 0:TPC], lanC[0:1, 0:TPC]))
        drain("vector")
        op("vector", lambda e: e.tensor_scalar(lanB[32:33, 0:TPC],
                                               lanB[32:33, 0:TPC], 1e-5, None, ALU.add))
        drain("vector")
        rvv = opd("vector", lambda e: e.reciprocal(lanA[32:33, 0:TPC],
                                                   lanB[32:33, 0:TPC]), "sve")
        wait("scalar", "sve", rvv)
        rsv = op("scalar", lambda e: e.activation(lanA[0:1, 0:TPC],
                                                  lanA[32:33, 0:TPC], AF.Sqrt), "sse")
        op("vector", lambda e: e.tensor_scalar(lanC[0:1, 0:TPC],
                                               lanB[0:1, 0:TPC], -1.0, None, ALU.mult))
        wait("vector", "sse", rsv)
        drain("vector")
        nmv = op("vector", lambda e: e.tensor_mul(lanB[0:1, 0:TPC],
                                                  lanC[0:1, 0:TPC],
                                                  lanA[0:1, 0:TPC]), "sve")
        PH(22)
        wait("tensor", "sve", nmv)
        bc1 = 0
        for tb in range(NTB):
            bc1 = op("tensor", lambda e, tb=tb: e.matmul(
                [pA, pB][tb][:, :], onesf[0:1, :],
                lanA[0:1, tb * 512:tb * 512 + 512], start=True, stop=True), "ste")
            bc1 = op("tensor", lambda e, tb=tb: e.matmul(
                [pC, pD][tb][:, :], onesf[0:1, :],
                lanB[0:1, tb * 512:tb * 512 + 512], start=True, stop=True), "ste")
        wait("vector", "ste", bc1)
        PH(23)
        outv = 0
        for kk in range(NKC):
            for tb in range(NTB):
                so = tb * 512
                op("vector", lambda e, kk=kk, tb=tb, so=so: e.tensor_tensor(
                    scrC[:, so:so + 512],
                    src16[:, kk * TPC + so: kk * TPC + so + 512], [pA, pB][tb][:, :], ALU.mult))
            for tb in range(NTB):
                so = tb * 512
                op("vector", lambda e, kk=kk, tb=tb, so=so: e.tensor_tensor(
                    scrC[:, so:so + 512], scrC[:, so:so + 512], [pC, pD][tb][:, :], ALU.add))
            drain("vector")
            outv = op("vector", lambda e, kk=kk: e.tensor_scalar(
                dst16[:, kk * TPC:kk * TPC + TPC], scrC[:, 0:TPC],
                cacol[:, kk:kk + 1], sacol[:, kk:kk + 1], ALU.mult, ALU.add), "sve")
            drain("vector")
        return outv

    PH(2)
    wait("vector", "sve", mgv)
    h1v = layer_norm(ogb, h16, modv[:, 16:32], modv[:, 0:16], "qxb", xbsem)

    def h1sl(kk, tb, width=512):
        return h16[:, kk * TPC + tb * 512: kk * TPC + tb * 512 + width]

    PH(3)
    blocks = [("q", h, wqT[:, h, :]) for h in range(NH)]
    blocks += [("k", h, wkT[:, h, :]) for h in range(NH)]
    blocks += [("og", ob, wogT[:, ob, :]) for ob in range(NKC)]
    ring = [None] * 4
    evpar = [0, 0]
    for bi, (kind, idx, wsl) in enumerate(blocks):
        so = (bi % 4) * 2048
        if ring[bi % 4] is not None:
            wait("gpsimd", "ste", ring[bi % 4])
        lv = dma("gpsimd", wbuf[:, so:so + 2048], wsl, "qw%d" % (bi % 4))
        wait("tensor", "qw%d" % (bi % 4), lv)
        if bi == 0:
            wait("tensor", "sve", h1v)
        if bi >= 1:
            wait("tensor", "sve" if kind != "og" or blocks[bi - 1][0] != "og" else "sse",
                 evpar[0] if False else None)
        mmv = 0
        for tb in range(NTB):
            p = [pE, pF][tb]
            if bi >= 1:
                wait("tensor", evgrd[tb][0], evgrd[tb][1])
            for kk in range(NKC):
                mmv = op("tensor", lambda e, so=so, kk=kk, tb=tb, p=p: e.matmul(
                    p[:, :], wbuf[:, so + kk * 128: so + kk * 128 + 128],
                    h1sl(kk, tb), start=(kk == 0), stop=(kk == NKC - 1)), "ste")
        ring[bi % 4] = mmv
        wait("vector" if kind != "og" else "scalar", "ste", mmv)
        evgrd = {}
        for tb in range(NTB):
            p = [pE, pF][tb]
            if kind == "q":
                ev = op("vector", lambda e, idx=idx, tb=tb, p=p: e.tensor_scalar(
                    bqk[:, idx * TPC + tb * 512: idx * TPC + tb * 512 + 512],
                    p[:, :], QSCALE, None, ALU.mult), "sve")
                evgrd[tb] = ("sve", ev)
            elif kind == "k":
                ev = op("vector", lambda e, idx=idx, tb=tb, p=p: e.tensor_copy(
                    bqk[:, (NH + idx) * TPC + tb * 512: (NH + idx) * TPC + tb * 512 + 512],
                    p[:, :]), "sve")
                evgrd[tb] = ("sve", ev)
            else:
                ev = op("scalar", lambda e, idx=idx, tb=tb, p=p: e.activation(
                    ogb[:, idx * TPC + tb * 512: idx * TPC + tb * 512 + 512],
                    p[:, :], AF.Sigmoid), "sse")
                evgrd[tb] = ("sse", ev)
    og_done = evgrd[NTB - 1][1]

    PH(35)
    wifl = dma("gpsimd", wsm[:, 0:256], wifT[:, 0, :], "qw3")
    wait("tensor", "qw3", wifl)
    ifm = 0
    for tb in range(NTB):
        for half in range(2):
            p = ([pA, pB] if half == 0 else [pC, pD])[tb]
            for kk in range(NKC):
                ifm = op("tensor", lambda e, kk=kk, tb=tb, p=p, half=half: e.matmul(
                    p[0:8, :], wsm[:, kk * 16 + half * 8: kk * 16 + half * 8 + 8],
                    h1sl(kk, tb), start=(kk == 0), stop=(kk == NKC - 1)), "ste")
    wait("scalar", "ste", ifm)
    wait("scalar", "sve", vbif)
    for tb in range(NTB):
        op("scalar", lambda e, tb=tb: e.activation(
            lanC[64:72, tb * 512:tb * 512 + 512], [pA, pB][tb][0:8, :],
            AF.Tanh, bias=bifis[:, 0:1], scale=1.0 / 15.0))
        op("scalar", lambda e, tb=tb: e.activation(
            lanC[32:40, tb * 512:tb * 512 + 512], [pC, pD][tb][0:8, :],
            AF.Tanh, bias=biffs[:, 0:1], scale=1.0 / 15.0))
    tg = op("scalar", lambda e: e.activation(gc1[0:8, 0:1], bifis[:, 0:1], AF.Square), "sse")
    wait("vector", "sse", tg)
    op("vector", lambda e: e.tensor_scalar(
        lanC[64:72, 0:TPC], lanC[64:72, 0:TPC], 15.0, None, ALU.mult))
    ifv = op("vector", lambda e: e.tensor_scalar(
        lanC[32:40, 0:TPC], lanC[32:40, 0:TPC], 15.0, None, ALU.mult), "sve")
    drain("vector")

    PH(36)
    wait("vector", "sve", vini)
    vld = [0] * NH
    vld[0] = dma("gpsimd", wbuf[:, 0:4096], wvT[:, 0, :], "qw0")
    vld[1] = dma("gpsimd", wbuf[:, 4096:8192], wvT[:, 1, :], "qw1")
    vev = {0: 0, 1: 0}
    for h in range(NH):
        so = (h % 2) * 4096
        wait("tensor", "qw%d" % (h % 2), vld[h])
        lastmm = 0
        for tt in range(NCH):
            p = [pC, pD][tt % 2]
            if tt >= 2 or h > 0:
                wait("tensor", "sve", vev[tt % 2])
            mmv = 0
            for kk in range(NKC):
                mmv = op("tensor", lambda e, h=h, tt=tt, kk=kk, so=so, p=p: e.matmul(
                    p[:, 0:256], h16[:, kk * TPC + tt * 128: kk * TPC + tt * 128 + 128],
                    wbuf[:, so + kk * 256: so + kk * 256 + 256],
                    start=(kk == 0), stop=(kk == NKC - 1)), "ste")
            wait("vector", "ste", mmv)
            vev[tt % 2] = op("vector", lambda e, h=h, tt=tt, p=p: e.tensor_copy(
                v1[:, tt * 2056 + 257 * h: tt * 2056 + 257 * h + 256], p[:, 0:256]), "sve")
            lastmm = mmv
        if h + 2 < NH:
            wait("gpsimd", "ste", lastmm)
            vld[h + 2] = dma("gpsimd", wbuf[:, so:so + 4096], wvT[:, h + 2, :], "qw%d" % (h % 2))
    v_done = vev[(NCH - 1) % 2]

    PH(4)
    wait("scalar", "sve", ifv)
    op("scalar", lambda e: e.activation(
        lanC[96:104, 0:TPC], lanC[32:40, 0:TPC], AF.Sigmoid))
    drain("scalar")
    flv = op("scalar", lambda e: e.activation(
        lanA[64:72, 0:TPC], lanC[96:104, 0:TPC], AF.Ln), "sse")
    wait("vector", "sse", flv)
    op("vector", lambda e: e.memset(lanB[64:72, 0:TPC], 0.0))
    drain("vector")
    op("vector", lambda e: e.tensor_tensor_scan(
        lanB[96:104, 0:TPC], lanA[64:72, 0:TPC],
        lanB[64:72, 0:TPC], 0.0, ALU.add, ALU.add))
    drain("vector")
    op("vector", lambda e: e.tensor_tensor_scan(
        lanB[64:72, 0:TPC], lanA[64:72, 0:TPC],
        lanC[64:72, 0:TPC], NEG, ALU.add, ALU.max))
    drain("vector")
    op("vector", lambda e: e.tensor_copy(lanA[64:72, 0:TPC], lanB[96:104, 0:TPC]))
    drain("vector")
    op("vector", lambda e: e.tensor_sub(lanB[32:40, 0:TPC],
                                        lanC[64:72, 0:TPC], lanA[64:72, 0:TPC]))
    op("vector", lambda e: e.tensor_sub(lanA[32:40, 0:TPC],
                                        lanA[64:72, 0:TPC], lanB[64:72, 0:TPC]))
    drain("vector")
    e32p = 0
    for cc_ in range(NCH):
        sl = slice(cc_ * 128, cc_ * 128 + 128)
        bc0 = lanA[64:72, cc_ * 128 - 1:cc_ * 128] if cc_ > 0 else None
        bce = lanA[64:72, cc_ * 128 + 127:cc_ * 128 + 128]
        if bc0 is not None:
            op("vector", lambda e, sl=sl, bc0=bc0: e.tensor_scalar(
                lanA[0:8, sl], lanB[32:40, sl], bc0, None, ALU.add))
            op("vector", lambda e, sl=sl, bc0=bc0: e.tensor_scalar(
                lanC[0:8, sl], lanA[32:40, sl], bc0, None, ALU.subtract))
            op("vector", lambda e, cc_=cc_, bce=bce, bc0=bc0: e.tensor_scalar(
                gc2[0:8, 0:1], bce, bc0, None, ALU.subtract))
        else:
            op("vector", lambda e, sl=sl: e.tensor_copy(lanA[0:8, sl], lanB[32:40, sl]))
            op("vector", lambda e, sl=sl: e.tensor_copy(lanC[0:8, sl], lanA[32:40, sl]))
            op("vector", lambda e, bce=bce: e.tensor_copy(gc2[0:8, 0:1], bce))
        op("vector", lambda e, sl=sl, bce=bce: e.tensor_scalar(
            lanB[0:8, sl], lanB[32:40, sl], bce, None, ALU.add))
        drain("vector")
        e32p = op("vector", lambda e, sl=sl: e.tensor_scalar(
            lanC[32:40, sl], lanB[32:40, sl], 0.0, gc2[0:8, 0:1], ALU.mult, ALU.add), "sve")
        drain("vector")
    wait("scalar", "sve", e32p)
    for seg, base in [(lanA, 0), (lanB, 0), (lanC, 0)]:
        op("scalar", lambda e, seg=seg, base=base: e.activation(
            seg[base:base + 8, 0:TPC], seg[base:base + 8, 0:TPC], AF.Exp))
        drain("scalar")
    e32e = op("scalar", lambda e: e.activation(
        lanC[32:40, 0:TPC], lanC[32:40, 0:TPC], AF.Exp), "sse")
    wait("tensor", "sse", e32e)
    etv = 0
    esrc = [(lanA, 0), (lanB, 0), (lanC, 0), (lanC, 32)]
    for cc_ in range(NCH):
        for qi, (seg, base) in enumerate(esrc):
            if cc_ + qi >= 1:
                wait("tensor", "sve", etv)
            tv = op("tensor", lambda e, cc_=cc_, seg=seg, base=base: e.transpose(
                pG[0:128, 0:8], seg[base:base + 8, cc_ * 128:cc_ * 128 + 128],
                idf[base:base + 8, base:base + 8]), "ste")
            wait("vector", "ste", tv)
            etv = op("vector", lambda e, cc_=cc_, qi=qi: e.tensor_copy(
                e32t[:, cc_ * 32 + qi * 8:cc_ * 32 + qi * 8 + 8], pG[:, 0:8]), "sve")

    PH(5)
    wait("tensor", "sve", C["sve"])
    lastC16 = [C["sve"]] * NH
    gA3 = {0: 0, 1: 0}
    gK1 = {0: 0, 1: 0}
    gHq = {0: 0, 1: 0}
    gCb = {0: 0, 1: 0}
    for cc_ in range(NCH):
        for h in range(NH):
            par = (cc_ * NH + h) % 2
            pS = [pA, pB][par]
            pH = [pC, pD][par]
            pCa = [pE, pF][par]
            qsl = bqk[:, h * TPC + cc_ * 128: h * TPC + cc_ * 128 + 128]
            ksl = bqk[:, (NH + h) * TPC + cc_ * 128: (NH + h) * TPC + cc_ * 128 + 128]
            vsl = v1[:, cc_ * 2056 + 257 * h: cc_ * 2056 + 257 * h + 257]
            colfac = e32t[:, cc_ * 32 + h: cc_ * 32 + h + 1]
            kfac = e32t[:, cc_ * 32 + 8 + h: cc_ * 32 + 8 + h + 1]
            rowsc = e32t[:, cc_ * 32 + 16 + h: cc_ * 32 + 16 + h + 1]
            dcy = e32t[:, cc_ * 32 + 24 + h: cc_ * 32 + 24 + h + 1]
            wait("tensor", "sve", gA3[par])
            smm = op("tensor", lambda e, ksl=ksl, qsl=qsl, pS=pS: e.matmul(
                pS[:, 0:128], ksl, qsl, start=True, stop=True), "ste")
            wait("tensor", "sve", gK1[par])
            tmm = op("tensor", lambda e, ksl=ksl, par=par: e.transpose(
                pT[:, par * 512:par * 512 + 128], ksl, idb[:, :]), "ste")
            wait("vector", "ste", tmm)
            gA3[par] = op("vector", lambda e, pS=pS, colfac=colfac, par=par: e.scalar_tensor_tensor(
                a3s[:, par * 128:par * 128 + 128], pS[:, 0:128], colfac, mkb[:, :],
                ALU.mult, ALU.mult), "sve")
            gK1[par] = op("vector", lambda e, par=par, kfac=kfac: e.tensor_scalar(
                k1s[:, par * 128:par * 128 + 128], pT[:, par * 512:par * 512 + 128],
                kfac, None, ALU.mult), "sve")
            wait("tensor", "sve", gA3[par])
            wait("tensor", "sve", gHq[par])
            op("tensor", lambda e, par=par, vsl=vsl, pH=pH: e.matmul(
                pH[:, 0:257], a3s[:, par * 128:par * 128 + 128], vsl,
                start=True, stop=False), "ste")
            wait("tensor", "sve", lastC16[h])
            hmm = op("tensor", lambda e, qsl=qsl, h=h, pH=pH: e.matmul(
                pH[:, 0:257], qsl, c16[:, 257 * h:257 * h + 257],
                start=False, stop=True), "ste")
            wait("tensor", "sve", gCb[par])
            cmm = op("tensor", lambda e, par=par, vsl=vsl, pCa=pCa: e.matmul(
                pCa[:, 0:257], k1s[:, par * 128:par * 128 + 128], vsl,
                start=True, stop=True), "ste")
            wait("vector", "ste", hmm)
            gHq[par] = op("vector", lambda e, cc_=cc_, h=h, pH=pH, rowsc=rowsc: e.tensor_scalar(
                h16[:, cc_ * 2056 + 257 * h: cc_ * 2056 + 257 * h + 257],
                pH[:, 0:257], rowsc, None, ALU.mult), "sve")
            wait("vector", "ste", cmm)
            gCb[par] = op("vector", lambda e, h=h, dcy=dcy, pCa=pCa: e.scalar_tensor_tensor(
                cst[:, 257 * h:257 * h + 257], cst[:, 257 * h:257 * h + 257],
                dcy, pCa[:, 0:257], ALU.mult, ALU.add), "sve")
            drain("vector")
            lastC16[h] = op("vector", lambda e, h=h: e.tensor_copy(
                c16[:, 257 * h:257 * h + 257], cst[:, 257 * h:257 * h + 257]), "sve")
    passa_v = C["sve"]

    PH(6)
    wait("sync", "sve", passa_v)
    dma("sync", ag2_in[0, 0:NU].rearrange("(p f) -> p f", p=128), cst[:], "qio")
    dma("sync", ag2_in[0, NU:NU + 8].rearrange("(p f) -> p f", p=8),
        lanA[64:72, TPC - 1:TPC], "qio")
    ag2v = dma("sync", ag2_in[0, NU + 8:NU + 16].rearrange("(p f) -> p f", p=8),
               lanB[64:72, TPC - 1:TPC], "qio")
    wait("gpsimd", "qio", ag2v)
    cc2 = op("gpsimd", lambda e: e.collective_compute(
        "AllGather", ALU.bypass, ins=[ag2_in[:]], outs=[ag2_out[:]],
        replica_groups=[[0, 1, 2, 3], [4, 5, 6, 7]]), "scc")
    wait("sync", "scc", cc2)
    uld = 0
    for j in range(3):
        uld = dma("sync", uvw[:, j * 2056:(j + 1) * 2056],
                  ag2_out[j, 0:NU].rearrange("(p f) -> p f", p=128), "qio")
        uld = dma("sync", lanC[0:8, 2 * j:2 * j + 1],
                  ag2_out[j, NU:NU + 8].rearrange("(p f) -> p f", p=8), "qio")
        uld = dma("sync", lanC[0:8, 2 * j + 1:2 * j + 2],
                  ag2_out[j, NU + 8:NU + 16].rearrange("(p f) -> p f", p=8), "qio")

    PH(7)
    op("vector", lambda e: e.memset(cst[:], 0.0))
    op("vector", lambda e: e.memset(lanC[0:8, 6:7], 0.0))
    wait("vector", "qio", uld)
    for j in range(3):
        op("vector", lambda e, j=j: e.tensor_mul(lanC[0:8, 9:10], lanC[0:8, 2 * j + 1:2 * j + 2], ccf[:, j:j + 1]))
        op("vector", lambda e, j=j: e.tensor_mul(lanC[0:8, 10:11], lanC[0:8, 2 * j:2 * j + 1], ccf[:, j:j + 1]))
        drain("vector")
        op("vector", lambda e, j=j: e.tensor_add(lanC[0:8, 9:10], lanC[0:8, 9:10], ccn[:, j:j + 1]))
        op("vector", lambda e: e.tensor_add(lanC[0:8, 11:12], lanC[0:8, 10:11], lanC[0:8, 6:7]))
        drain("vector")
        op("vector", lambda e: e.tensor_max(lanC[0:8, 12:13], lanC[0:8, 11:12], lanC[0:8, 9:10]))
        drain("vector")
        op("vector", lambda e: e.tensor_sub(lanC[0:8, 13:14], lanC[0:8, 11:12], lanC[0:8, 12:13]))
        op("vector", lambda e: e.tensor_scalar(lanC[0:8, 14:15], lanC[0:8, 12:13], -1.0, None, ALU.mult))
        drain("vector")
        wex = op("vector", lambda e: e.tensor_copy(lanC[0:8, 6:7], lanC[0:8, 12:13]), "sve")
        wait("scalar", "sve", wex)
        op("scalar", lambda e: e.activation(lanC[0:8, 13:14], lanC[0:8, 13:14], AF.Exp))
        drain("scalar")
        sxv = op("scalar", lambda e: e.activation(lanC[0:8, 14:15], lanC[0:8, 14:15], AF.Exp), "sse")
        wait("vector", "sse", sxv)
        wnv = op("vector", lambda e, j=j: e.tensor_mul(lanC[0:8, 15:16], lanC[0:8, 14:15], ccf[:, j:j + 1]), "sve")
        wait("tensor", "sve", wnv)
        op("tensor", lambda e: e.transpose(
            pG[0:1, 0:8], lanC[0:8, 13:14], idf[0:8, 0:8]), "ste")
        t1v = op("tensor", lambda e: e.transpose(
            pG[0:1, 8:16], lanC[0:8, 15:16], idf[0:8, 0:8]), "ste")
        wait("vector", "ste", t1v)
        op("vector", lambda e: e.tensor_copy(wrowp[:, :], pG[0:1, 0:8]))
        t2v = op("vector", lambda e: e.tensor_copy(wrow1[:, :], pG[0:1, 8:16]), "sve")
        wait("tensor", "sve", t2v)
        b2 = op("tensor", lambda e: e.matmul(pG[:, 0:8], onesf[0:1, :], wrowp[0:1, :],
                                             start=True, stop=True), "ste")
        b2 = op("tensor", lambda e: e.matmul(pG[:, 8:16], onesf[0:1, :], wrow1[0:1, :],
                                             start=True, stop=True), "ste")
        wait("vector", "ste", b2)
        op("vector", lambda e: e.tensor_copy(wbcs[:, :], pG[:, 0:16]))
        drain("vector")
        for h in range(NH):
            op("vector", lambda e, j=j, h=h: e.tensor_scalar(
                scrA[:, 0:257],
                uvw[:, j * 2056 + 257 * h: j * 2056 + 257 * h + 257],
                wbcs[:, 8 + h:8 + h + 1], None, ALU.mult))
            drain("vector")
            op("vector", lambda e, h=h: e.scalar_tensor_tensor(
                cst[:, 257 * h:257 * h + 257], cst[:, 257 * h:257 * h + 257],
                wbcs[:, h:h + 1], scrA[:, 0:257], ALU.mult, ALU.add))
            drain("vector")
    p16v = op("vector", lambda e: e.tensor_copy(c16[:], cst[:]), "sve")

    PH(8)
    op("vector", lambda e: e.tensor_scalar(
        lanC[64:72, 0:TPC], lanA[64:72, 0:TPC], lanC[0:8, 6:7], None, ALU.add))
    drain("vector")
    op("vector", lambda e: e.tensor_max(
        lanA[64:72, 0:TPC], lanB[64:72, 0:TPC], lanC[64:72, 0:TPC]))
    drain("vector")
    op("vector", lambda e: e.tensor_sub(
        lanA[0:8, 0:TPC], lanB[64:72, 0:TPC], lanA[64:72, 0:TPC]))
    op("vector", lambda e: e.tensor_sub(
        lanB[0:8, 0:TPC], lanC[64:72, 0:TPC], lanA[64:72, 0:TPC]))
    e24v = op("vector", lambda e: e.tensor_scalar(
        lanC[0:8, 0:TPC], lanA[64:72, 0:TPC], -1.0, None, ALU.mult), "sve")
    wait("scalar", "sve", e24v)
    op("scalar", lambda e: e.activation(lanA[0:8, 0:TPC], lanA[0:8, 0:TPC], AF.Exp))
    drain("scalar")
    op("scalar", lambda e: e.activation(lanB[0:8, 0:TPC], lanB[0:8, 0:TPC], AF.Exp))
    drain("scalar")
    e24e = op("scalar", lambda e: e.activation(lanC[0:8, 0:TPC], lanC[0:8, 0:TPC], AF.Exp), "sse")
    wait("tensor", "sse", e24e)
    etv = 0
    esrc24 = [(lanA, 0), (lanB, 0), (lanC, 0)]
    for cc_ in range(NCH):
        for qi, (seg, base) in enumerate(esrc24):
            if cc_ + qi >= 1:
                wait("tensor", "sve", etv)
            tv = op("tensor", lambda e, cc_=cc_, seg=seg, base=base: e.transpose(
                pG[0:128, 0:8], seg[base:base + 8, cc_ * 128:cc_ * 128 + 128],
                idf[base:base + 8, base:base + 8]), "ste")
            wait("vector", "ste", tv)
            op("vector", lambda e, cc_=cc_, qi=qi: e.tensor_copy(
                e24t[:, cc_ * 24 + qi * 8:cc_ * 24 + qi * 8 + 8], pG[:, 0:8]))
            drain("vector")
            etv = op("vector", lambda e, cc_=cc_, qi=qi: e.tensor_copy(
                e24b[:, cc_ * 24 + qi * 8:cc_ * 24 + qi * 8 + 8],
                e24t[:, cc_ * 24 + qi * 8:cc_ * 24 + qi * 8 + 8]), "sve")

    PH(9)
    wait("tensor", "sve", C["sve"])
    gH2 = {0: 0, 1: 0}
    gTo = {0: 0, 1: 0}
    for cc_ in range(NCH):
        tslab = op("vector", lambda e, cc_=cc_: e.tensor_tensor(
            t1s[:, 0:2056].rearrange("p (h f) -> p h f", f=257),
            h16[:, cc_ * 2056: cc_ * 2056 + 2056].rearrange("p (h f) -> p h f", f=257),
            e24b[:, cc_ * 24: cc_ * 24 + 8].unsqueeze(2).to_broadcast([128, 8, 257]),
            ALU.mult), "sve")
        drain("vector")
        for h in range(NH):
            par = h % 2
            p = [pA, pB][par]
            qsl = bqk[:, h * TPC + cc_ * 128: h * TPC + cc_ * 128 + 128]
            wait("tensor", "sve", gH2[par])
            if h == 0:
                wait("tensor", "sve", tslab)
            h2mm = op("tensor", lambda e, qsl=qsl, h=h, p=p: e.matmul(
                p[:, 0:257], qsl, c16[:, 257 * h:257 * h + 257], start=True, stop=True), "ste")
            wait("vector", "ste", h2mm)
            gH2[par] = op("vector", lambda e, h=h, p=p, cc_=cc_: e.scalar_tensor_tensor(
                cst[:, 257 * h: 257 * h + 257], p[:, 0:257],
                e24t[:, cc_ * 24 + 8 + h: cc_ * 24 + 8 + h + 1],
                t1s[:, 257 * h: 257 * h + 257], ALU.mult, ALU.add), "sve")
        drain("vector")
        op("vector", lambda e: e.tensor_scalar(
            sq8[:, 40:48], cst[:, 256:2056:257], -1.0, None, ALU.mult))
        drain("vector")
        op("vector", lambda e: e.tensor_max(
            sq8[:, 0:8], sq8[:, 40:48], cst[:, 256:2056:257]))
        drain("vector")
        op("vector", lambda e, cc_=cc_: e.tensor_max(
            sq8[:, 0:8], sq8[:, 0:8], e24t[:, cc_ * 24 + 16: cc_ * 24 + 24]))
        drain("vector")
        op("vector", lambda e: e.tensor_scalar(sq8[:, 0:8], sq8[:, 0:8], 1e-6, None, ALU.add))
        drain("vector")
        op("vector", lambda e: e.reciprocal(sq8[:, 8:16], sq8[:, 0:8]))
        drain("vector")
        hnv = op("vector", lambda e: e.tensor_tensor(
            t1s[:, 0:2048].rearrange("p (h f) -> p h f", f=256),
            cst[:, 0:2056].rearrange("p (h f) -> p h f", f=257)[:, :, 0:256],
            sq8[:, 8:16].unsqueeze(2).to_broadcast([128, 8, 256]), ALU.mult), "sve")
        drain("vector")
        op("vector", lambda e: e.reduce_sum(
            sq8[:, 16:24], t1s[:, 0:2048].rearrange("p (h f) -> p h f", f=256), axis=AX.X))
        wait("scalar", "sve", hnv)
        sqv = op("scalar", lambda e: e.activation(cst[:, 0:2048], t1s[:, 0:2048], AF.Square), "sse")
        wait("vector", "sse", sqv)
        op("vector", lambda e: e.reduce_sum(
            sq8[:, 24:32], cst[:, 0:2048].rearrange("p (h f) -> p h f", f=256), axis=AX.X))
        drain("vector")
        op("vector", lambda e: e.tensor_scalar(sq8[:, 16:24], sq8[:, 16:24], -1.0 / 256, None, ALU.mult))
        op("vector", lambda e: e.tensor_scalar(sq8[:, 24:32], sq8[:, 24:32], 1.0 / 256, None, ALU.mult))
        drain("vector")
        op("vector", lambda e: e.tensor_mul(sq8[:, 32:40], sq8[:, 16:24], sq8[:, 16:24]))
        drain("vector")
        op("vector", lambda e: e.tensor_sub(sq8[:, 24:32], sq8[:, 24:32], sq8[:, 32:40]))
        drain("vector")
        op("vector", lambda e: e.tensor_scalar(sq8[:, 24:32], sq8[:, 24:32], 1e-6, None, ALU.add))
        drain("vector")
        rvv = opd("vector", lambda e: e.reciprocal(sq8[:, 32:40], sq8[:, 24:32]), "sve")
        wait("scalar", "sve", rvv)
        rsv = op("scalar", lambda e: e.activation(sq8[:, 40:48], sq8[:, 32:40], AF.Sqrt), "sse")
        wait("vector", "sse", rsv)
        op("vector", lambda e: e.tensor_tensor(
            cst[:, 0:2048].rearrange("p (h f) -> p h f", f=256),
            t1s[:, 0:2048].rearrange("p (h f) -> p h f", f=256),
            sq8[:, 16:24].unsqueeze(2).to_broadcast([128, 8, 256]), ALU.add))
        drain("vector")
        op("vector", lambda e: e.tensor_tensor(
            cst[:, 0:2048].rearrange("p (h f) -> p h f", f=256),
            cst[:, 0:2048].rearrange("p (h f) -> p h f", f=256),
            sq8[:, 40:48].unsqueeze(2).to_broadcast([128, 8, 256]), ALU.mult))
        drain("vector")
        hgv = op("vector", lambda e: e.tensor_mul(hob[:, 0:2048], cst[:, 0:2048], gab[:, :]), "sve")
        wait("tensor", "sve", hgv)
        for vck in range(NKC):
            par = vck % 2
            wait("tensor", "sve", gTo[par])
            tv = op("tensor", lambda e, vck=vck, par=par: e.transpose(
                pT[:, par * 512:par * 512 + 128], hob[:, vck * 128:vck * 128 + 128],
                idb[:, :]), "ste")
            wait("vector", "ste", tv)
            gTo[par] = op("vector", lambda e, vck=vck, cc_=cc_, par=par: e.tensor_tensor(
                v1[:, vck * TPC + cc_ * 128: vck * TPC + cc_ * 128 + 128],
                pT[:, par * 512:par * 512 + 128],
                ogb[:, vck * TPC + cc_ * 128: vck * TPC + cc_ * 128 + 128], ALU.mult), "sve")
    passb_v = C["sve"]

    PH(10)
    ring = [None] * 4
    xgrd = {0: 0, 1: 0}
    stgrd = {0: 0, 1: 0}
    for db in range(NKC):
        so = (db % 4) * 2048
        if ring[db % 4] is not None:
            wait("gpsimd", "ste", ring[db % 4])
        lv = dma("gpsimd", wbuf[:, so:so + 2048], woutT[:, db, :], "qw%d" % (db % 4))
        wait("tensor", "qw%d" % (db % 4), lv)
        if db == 0:
            wait("tensor", "sve", passb_v)
        for tb in range(NTB):
            p = [pC, pD][tb]
            sl2 = ((db * NTB + tb) % 2) * 512
            wait("sync", "sve", xgrd[(db * NTB + tb) % 2])
            xl = dma("sync", xre[:, sl2:sl2 + 512],
                     xT[db * 128: db * 128 + 128, tb * 512: tb * 512 + 512],
                     "qxr%d" % ((db * NTB + tb) % 2))
            wait("tensor", "sve", stgrd[tb])
            mmv = 0
            for kk in range(NKC):
                mmv = op("tensor", lambda e, so=so, kk=kk, tb=tb, p=p: e.matmul(
                    p[:, :], wbuf[:, so + kk * 128: so + kk * 128 + 128],
                    v1[:, kk * TPC + tb * 512: kk * TPC + tb * 512 + 512],
                    start=(kk == 0), stop=(kk == NKC - 1)), "ste")
            if tb == NTB - 1:
                ring[db % 4] = mmv
            wait("vector", "ste", mmv)
            wait("vector", "qxr%d" % ((db * NTB + tb) % 2), xl)
            wait("vector", "qst%d" % ((db * NTB + tb) % 2),
                 C["qst%d" % ((db * NTB + tb) % 2)])
            ev = op("vector", lambda e, db=db, p=p, sl2=sl2: e.scalar_tensor_tensor(
                x2f[:, sl2:sl2 + 512], p[:, :], modv[:, 32 + db:32 + db + 1],
                xre[:, sl2:sl2 + 512], ALU.mult, ALU.add), "sve")
            stgrd[tb] = ev
            wait("scalar", "sve", ev)
            dma("scalar", yT[db * 128:db * 128 + 128, tb * 512:tb * 512 + 512],
                x2f[:, sl2:sl2 + 512], "qst%d" % ((db * NTB + tb) % 2))
            drain("vector")
            x2v = op("vector", lambda e, db=db, tb=tb, sl2=sl2: e.tensor_copy(
                bqk[:, db * TPC + tb * 512: db * TPC + tb * 512 + 512],
                x2f[:, sl2:sl2 + 512]), "sve")
            xgrd[(db * NTB + tb) % 2] = x2v
    x2_qst = (C["qst0"], C["qst1"])

    PH(11)
    h2v = layer_norm(bqk, bqk, modv[:, 64:80], modv[:, 48:64], "sve", x2v)

    def h2sl(kk, tb, width=512):
        return bqk[:, kk * TPC + tb * 512: kk * TPC + tb * 512 + width]

    PH(12)
    gld = dma("gpsimd", wsm[:, 256:512], gateT[:, 0, :], "qw3")
    wait("tensor", "qw3", gld)
    wait("tensor", "sve", h2v)
    gmm = 0
    for tb in range(NTB):
        p = [pE, pF][tb]
        for kk in range(NKC):
            gmm = op("tensor", lambda e, kk=kk, tb=tb, p=p: e.matmul(
                p[0:16, :], wsm[:, 256 + kk * 16: 256 + kk * 16 + 16],
                h2sl(kk, tb), start=(kk == 0), stop=(kk == NKC - 1)), "ste")
    wait("vector", "ste", gmm)
    glv = 0
    for tb in range(NTB):
        glv = op("vector", lambda e, tb=tb: e.tensor_copy(
            lanC[0:16, tb * 512:tb * 512 + 512], [pE, pF][tb][0:16, :]), "sve")
    cwv = glv
    for tt in range(NCH):
        wait("tensor", "sve", cwv)
        ttv = op("tensor", lambda e, tt=tt: e.transpose(
            pG[0:128, 0:16], lanC[0:16, tt * 128:tt * 128 + 128],
            idf[0:16, 0:16]), "ste")
        wait("vector", "ste", ttv)
        op("vector", lambda e: e.tensor_copy(g16a[:], pG[:, 0:16]))
        drain("vector")
        nmx = op("vector", lambda e: e.reduce_max(gc1[:], g16a[:], axis=AX.X, negate=True), "sve")
        wait("scalar", "sve", nmx)
        exv = op("scalar", lambda e: e.activation(g16b[:], g16a[:], AF.Exp,
                                                  bias=gc1[:, 0:1], scale=1.0), "sse")
        wait("vector", "sse", exv)
        op("vector", lambda e: e.reduce_sum(gc2[:], g16b[:], axis=AX.X))
        drain("vector")
        op("vector", lambda e: e.reciprocal(gc1[:], gc2[:]))
        drain("vector")
        op("vector", lambda e: e.tensor_scalar(g16a[:], g16b[:], gc1[:, 0:1], None, ALU.mult))
        drain("vector")
        op("vector", lambda e: e.reduce_max(g4a[:], g16a[:].rearrange("p (g i) -> p g i", i=4), axis=AX.X))
        drain("vector")
        op("vector", lambda e: e.tensor_tensor(
            g16b[:].rearrange("p (g i) -> p g i", i=4),
            g4a[:].unsqueeze(1).to_broadcast([128, 4, 4]),
            g4a[:].unsqueeze(2).to_broadcast([128, 4, 4]), ALU.is_gt))
        op("vector", lambda e: e.tensor_tensor(
            g16d[:].rearrange("p (g i) -> p g i", i=4),
            g4a[:].unsqueeze(1).to_broadcast([128, 4, 4]),
            g4a[:].unsqueeze(2).to_broadcast([128, 4, 4]), ALU.is_equal))
        drain("vector")
        op("vector", lambda e: e.tensor_mul(g16d[:], g16d[:], t16c[:]))
        drain("vector")
        op("vector", lambda e: e.tensor_add(g16b[:], g16b[:], g16d[:]))
        drain("vector")
        op("vector", lambda e: e.reduce_sum(g4a[:], g16b[:].rearrange("p (g i) -> p g i", i=4), axis=AX.X))
        drain("vector")
        op("vector", lambda e: e.tensor_scalar(g4a[:], g4a[:], 2.0, None, ALU.is_lt))
        drain("vector")
        op("vector", lambda e: e.tensor_tensor(
            g16b[:].rearrange("p (g i) -> p g i", i=4),
            g16a[:].rearrange("p (g i) -> p g i", i=4),
            g4a[:].unsqueeze(2).to_broadcast([128, 4, 4]), ALU.mult))
        drain("vector")
        op("vector", lambda e: e.tensor_tensor(
            g256[:].rearrange("p (a b) -> p a b", b=16),
            g16b[:].unsqueeze(1).to_broadcast([128, 16, 16]),
            g16b[:].unsqueeze(2).to_broadcast([128, 16, 16]), ALU.is_gt))
        op("vector", lambda e: e.tensor_tensor(
            g256b[:].rearrange("p (a b) -> p a b", b=16),
            g16b[:].unsqueeze(1).to_broadcast([128, 16, 16]),
            g16b[:].unsqueeze(2).to_broadcast([128, 16, 16]), ALU.is_equal))
        drain("vector")
        op("vector", lambda e: e.tensor_mul(g256b[:], g256b[:], t256c[:]))
        drain("vector")
        op("vector", lambda e: e.tensor_add(g256[:], g256[:], g256b[:]))
        drain("vector")
        op("vector", lambda e: e.reduce_sum(g16d[:], g256[:].rearrange("p (a b) -> p a b", b=16), axis=AX.X))
        drain("vector")
        op("vector", lambda e: e.tensor_scalar(g16d[:], g16d[:], 4.0, None, ALU.is_lt))
        drain("vector")
        op("vector", lambda e: e.tensor_mul(g16b[:], g16b[:], g16d[:]))
        drain("vector")
        op("vector", lambda e: e.reduce_sum(gc2[:], g16b[:], axis=AX.X))
        drain("vector")
        op("vector", lambda e: e.tensor_scalar(gc2[:], gc2[:], 1e-20, None, ALU.add))
        drain("vector")
        op("vector", lambda e: e.reciprocal(gc1[:], gc2[:]))
        drain("vector")
        cwq = op("vector", lambda e: e.tensor_scalar(g16a[:], g16b[:], gc1[:, 0:1], None, ALU.mult), "sve")
        wait("tensor", "sve", cwq)
        tbv = op("tensor", lambda e, tt=tt: e.transpose(
            pG[0:16, 0:128], g16a[:], idf[:, :]), "ste")
        wait("vector", "ste", tbv)
        cwv = op("vector", lambda e, tt=tt: e.tensor_copy(
            cwl16[:, tt * 128:tt * 128 + 128], pG[0:16, 0:128]), "sve")

    PH(13)
    def aslice(s, tb):
        if s < 16:
            t_, base = h16, s * TPC
        elif s < 32:
            t_, base = ogb, (s - 16) * TPC
        else:
            t_, base = v1, (s - 32) * TPC
        return t_[:, base + tb * 512: base + tb * 512 + 512]

    wait("gpsimd", "ste", C["ste"])
    wait("gpsimd", "qst0", x2_qst[0])
    wait("gpsimd", "qst1", x2_qst[1])
    groups = [[0, 1, 2, 3], [4, 5, 6, 7], [8, 9, 10, 11], [12, 13, 14, 15], ["sh"]]
    upfree = [None, None]
    upld_i = 0
    cbe = cwv
    sgrd = {0: 0, 1: 0}
    ugrd = {0: 0, 1: 0}
    dn_guard = 0
    acc_guard = {0: 0, 1: 0}
    for grp in groups:
        wait("gpsimd", "qac0", C["qac0"])
        wait("gpsimd", "qac1", C["qac1"])
        shared = grp[0] == "sh"
        nexp = 1 if shared else len(grp)
        nj = 22 if shared else 11
        for ei in range(nexp):
            eidx = None if shared else grp[ei]
            if not shared:
                wait("sync", "sve", cbe)
                crv = dma("sync", cwrow[0:1, 0:TPC], cwl16[eidx:eidx + 1, :], "qio")
                for tb in range(NTB):
                    wait("tensor", "sve", cbe)
                    wait("tensor", "qio", crv)
                    cbv = op("tensor", lambda e, tb=tb: e.matmul(
                        pG[:, :], onesbf[0:1, :],
                        cwrow[0:1, tb * 512:tb * 512 + 512],
                        start=True, stop=True), "ste")
                    wait("vector", "ste", cbv)
                    cbe = op("vector", lambda e, tb=tb: e.tensor_copy(
                        cwbc[:, tb * 512:tb * 512 + 512], pG[:, :]), "sve")
            gmmv = 0
            for jj in range(nj):
                for half in range(2):
                    jb = jj + half * nj
                    slot = upld_i % 2
                    upld_i += 1
                    if upfree[slot] is not None:
                        wait("gpsimd", "ste", upfree[slot])
                    src = supT[:, jb, :] if shared else upT[eidx, :, jb, :]
                    lv = dma("gpsimd", wbuf[:, slot * 2048:slot * 2048 + 2048], src,
                             "qw%d" % slot)
                    wait("tensor", "qw%d" % slot, lv)
                    wait("tensor", "sse" if half == 0 else "sve",
                         sgrd[0] if half == 0 else ugrd[0])
                    mmv = 0
                    for kk in range(NKC):
                        for tb in range(NTB):
                            p = ([pA, pB] if half == 0 else [pC, pD])[tb]
                            mmv = op("tensor", lambda e, slot=slot, kk=kk, tb=tb, p=p: e.matmul(
                                p[:, :], wbuf[:, slot * 2048 + kk * 128: slot * 2048 + kk * 128 + 128],
                                h2sl(kk, tb), start=(kk == 0), stop=(kk == NKC - 1)), "ste")
                    upfree[slot] = mmv
                    if half == 0:
                        gmmv = mmv
                wait("scalar", "ste", gmmv)
                sgv = 0
                for tb in range(NTB):
                    sgv = op("scalar", lambda e, tb=tb: e.activation(
                        sgil[:, tb * 512:tb * 512 + 512], [pA, pB][tb][:, :], AF.Silu), "sse")
                sgrd[0] = sgv
                wait("vector", "ste", mmv)
                wait("vector", "sse", sgv)
                s_global = jj if shared else ei * 11 + jj
                for tb in range(NTB):
                    op("vector", lambda e, tb=tb: e.tensor_tensor(
                        tmul[:, tb * 512:tb * 512 + 512], sgil[:, tb * 512:tb * 512 + 512],
                        [pC, pD][tb][:, :], ALU.mult))
                drain("vector")
                av = 0
                for tb in range(NTB):
                    if shared:
                        av = op("vector", lambda e, s_global=s_global, tb=tb: e.tensor_copy(
                            aslice(s_global, tb), tmul[:, tb * 512:tb * 512 + 512]), "sve")
                    else:
                        av = op("vector", lambda e, s_global=s_global, tb=tb: e.tensor_mul(
                            aslice(s_global, tb), tmul[:, tb * 512:tb * 512 + 512],
                            cwbc[:, tb * 512:tb * 512 + 512]), "sve")
                ugrd[0] = av
        up_done = C["sve"]
        for db in range(NKC):
            wait("gpsimd", "ste", dn_guard)
            dls = 0
            for ei in range(nexp):
                slot = 4096 + ei * 1408
                src = sdownT[:, db, :] if shared else downT[grp[ei], :, db, :]
                dls = dma("gpsimd", wbuf[:, slot:slot + nj * 128], src, "qw2")
            wait("tensor", "qw2", dls)
            if db == 0:
                wait("tensor", "sve", up_done)
            for tb in range(NTB):
                p = [pE, pF][tb]
                wait("tensor", "sve", acc_guard[tb])
                mmv = 0
                for ei in range(nexp):
                    slot = 4096 + ei * 1408
                    for ik in range(nj):
                        s_global = ik if shared else ei * 11 + ik
                        mmv = op("tensor", lambda e, slot=slot, ik=ik, tb=tb, p=p,
                                 s_global=s_global, first=(ei == 0 and ik == 0),
                                 last=(ei == nexp - 1 and ik == nj - 1): e.matmul(
                            p[:, :], wbuf[:, slot + ik * 128: slot + ik * 128 + 128],
                            aslice(s_global, tb), start=first, stop=last), "ste")
                if tb == NTB - 1:
                    dn_guard = mmv
                wait("vector", "ste", mmv)
                sl3 = ((db * NTB + tb) % 2) * 512
                wait("vector", "qac%d" % ((db * NTB + tb) % 2),
                     C["qac%d" % ((db * NTB + tb) % 2)])
                e3 = op("vector", lambda e, db=db, p=p, sl3=sl3: e.tensor_scalar(
                    x2f[:, sl3:sl3 + 512], p[:, :], modv[:, 80 + db:80 + db + 1],
                    None, ALU.mult), "sve")
                acc_guard[tb] = e3
                wait("gpsimd", "sve", e3)
                dma("gpsimd", yT[db * 128:db * 128 + 128, tb * 512:tb * 512 + 512],
                    x2f[:, sl3:sl3 + 512], "qac%d" % ((db * NTB + tb) % 2), accum=True)

    PH(14)
    wait("gpsimd", "qac0", C["qac0"])
    wait("gpsimd", "qac1", C["qac1"])
    wait("scalar", "qst0", C["qst0"])
    wait("scalar", "qst1", C["qst1"])

    if KPH < 99:
        cuts = sorted(k for k in PHM if k > KPH)
        if cuts:
            del EV[PHM[cuts[0]]:]
            del EVG[PHM[cuts[0]]:]

    fin = EVG[-1]
    for engn in ["sync", "gpsimd", "vector", "scalar", "tensor"]:
        on(engn, lambda e, v=fin: e.wait_ge(GSEM, v))

    block = ctx.enter_context(nc.Block())

    def runner(eng_name):
        def body(eng):
            for nm, fn in EV:
                if nm == eng_name:
                    fn(eng)
        return body

    block.sync(runner("sync"))
    block.gpsimd(runner("gpsimd"))
    block.vector(runner("vector"))
    block.scalar(runner("scalar"))
    block.tensor(runner("tensor"))
    ctx.close()
    return nc


def _wtile(W, mb):
    Din, M = W.shape[1], W.shape[0]
    kb = Din // 128
    arr = np.ascontiguousarray(W.T).reshape(kb, 128, M // mb, mb)
    return np.ascontiguousarray(arr.transpose(1, 2, 0, 3)).reshape(128, M // mb, kb * mb)


_NC_CACHE = {}


def kernel(**inp):
    S = np.asarray(inp["L0_attn_x"]).shape[1]
    TPC = S // 4
    if TPC not in _NC_CACHE:
        _NC_CACHE[TPC] = build_nc(TPC)
    nc = _NC_CACHE[TPC]

    x = np.asarray(inp["L0_attn_x"], np.float32)
    cc = np.asarray(inp["L0_c"], np.float32)
    wmod = np.asarray(inp["L0_wmod"], np.float32)

    wq = _wtile(np.asarray(inp["L0_attn_Wq"], np.float32), 128)
    wk = _wtile(np.asarray(inp["L0_attn_Wk"], np.float32), 128)
    wv = _wtile(np.asarray(inp["L0_attn_Wv"], np.float32), 256)
    wog = _wtile(np.asarray(inp["L0_attn_Wog"], np.float32), 128)
    wif = _wtile(np.concatenate([np.asarray(inp["L0_attn_Wig"], np.float32),
                                 np.asarray(inp["L0_attn_Wfg"], np.float32)], 0), 16)
    wout = _wtile(np.asarray(inp["L0_attn_Wout"], np.float32), 128)
    gate = _wtile(np.asarray(inp["L0_ffn_gate_w"], np.float32), 16)
    up = np.stack([_wtile(np.asarray(inp["L0_ffn_expert_up_w"][e], np.float32), 128)
                   for e in range(E)])
    down = np.stack([_wtile(np.asarray(inp["L0_ffn_expert_down_w"][e], np.float32), 128)
                     for e in range(E)])
    sup = _wtile(np.asarray(inp["L0_ffn_shared_up_w"], np.float32), 128)
    sdn = _wtile(np.asarray(inp["L0_ffn_shared_down_w"], np.float32), 128)
    bigv = np.ascontiguousarray(np.asarray(inp["L0_attn_big"], np.float32)[:, None])
    bfgv = np.ascontiguousarray(np.asarray(inp["L0_attn_bfg"], np.float32)[:, None])
    gm = np.ascontiguousarray(np.asarray(inp["L0_attn_gamma"], np.float32).reshape(1, NH * DV))

    mask01 = np.ascontiguousarray(np.triu(np.ones((128, 128), np.float32)))
    identb = np.eye(128).astype(ml_dtypes.bfloat16)
    identf = np.eye(128, dtype=np.float32)
    tril16 = np.ascontiguousarray(
        np.tril(np.ones((4, 4), np.float32), -1).reshape(1, 16).repeat(128, 0))
    tril256 = np.ascontiguousarray(
        np.tril(np.ones((16, 16), np.float32), -1).reshape(1, 256).repeat(128, 0))

    in_maps = []
    for c in range(8):
        bidx, r = c // 4, c % 4
        xs = np.ascontiguousarray(x[bidx, r * TPC:(r + 1) * TPC, :].T)
        cT = np.ascontiguousarray(cc[bidx].reshape(NKC, 128).T)
        wms = np.ascontiguousarray(wmod[:, 3072 * r:3072 * (r + 1)])
        flag = np.zeros((8, 3), np.float32)
        flag[:, :r] = 1.0
        ccneg = (1.0 - flag) * NEG
        in_maps.append(dict(
            xT=xs, cTb=cT, wmod_sl=wms, wqT=wq, wkT=wk, wvT=wv, wogT=wog,
            wifT=wif, bifi=bigv, biff=bfgv, gamma=gm, woutT=wout, gateT=gate, upT=up,
            downT=down, supT=sup, sdownT=sdn, mask01=mask01, identb=identb,
            identf=identf, tril16=tril16, tril256=tril256,
            ccflag=flag, ccneg=ccneg))

    res = run_bass_kernel_spmd(nc, in_maps, core_ids=list(range(8)))
    out = np.empty((2, S, D), np.float32)
    for c in range(8):
        bidx, r = c // 4, c % 4
        out[bidx, r * TPC:(r + 1) * TPC, :] = res.results[c]["yT"].T
    return out


# version 11
# speedup vs baseline: 2.2560x; 2.2560x over previous
import os
import numpy as np
import ml_dtypes
from contextlib import ExitStack
import concourse.bass as bass
import concourse.mybir as mybir
from concourse.bass_utils import run_bass_kernel_spmd

F32 = mybir.dt.float32
BF16 = mybir.dt.bfloat16
AF = mybir.ActivationFunctionType
ALU = mybir.AluOpType
AX = mybir.AxisListType

D = 2048
NH = 8
DV = 256
NKC = 16
E = 16
NEG = -1e30
QSCALE = float(128) ** -0.5
NU = 128 * 2056


def build_nc(TPC):
    KPH = int(os.environ.get("KPH", "99"))
    NCH = TPC // 128
    NTB = TPC // 512
    nc = bass.Bass("TRN2", target_bir_lowering=False)

    def di(name, shape, dt=F32):
        return nc.dram_tensor(name, shape, dt, kind="ExternalInput").ap()

    xT = di("xT", [D, TPC])
    cTb = di("cTb", [128, NKC])
    wmod_sl = di("wmod_sl", [D, 3072])
    wqT = di("wqT", [128, NH, NKC * 128])
    wkT = di("wkT", [128, NH, NKC * 128])
    wvT = di("wvT", [128, NH, NKC * 256])
    wogT = di("wogT", [128, NKC, NKC * 128])
    wifT = di("wifT", [128, 1, NKC * 16])
    bifi = di("bifi", [8, 1])
    biff = di("biff", [8, 1])
    gamma = di("gamma", [1, NH * DV])
    woutT = di("woutT", [128, NKC, NKC * 128])
    gateT = di("gateT", [128, 1, NKC * 16])
    upT = di("upT", [E, 128, 22, NKC * 128])
    downT = di("downT", [E, 128, 16, 11 * 128])
    supT = di("supT", [128, 44, NKC * 128])
    sdownT = di("sdownT", [128, 16, 22 * 128])
    mask01 = di("mask01", [128, 128])
    identb = di("identb", [128, 128], BF16)
    identf = di("identf", [128, 128])
    tril16 = di("tril16", [128, 16])
    tril256 = di("tril256", [128, 256])
    ccflag = di("ccflag", [8, 3])
    ccneg = di("ccneg", [8, 3])
    yT = nc.dram_tensor("yT", [D, TPC], F32, kind="ExternalOutput").ap()

    ag1_in = nc.dram_tensor("ag1_in", [1, 3072], F32, kind="Internal").ap()
    ag1_out = nc.dram_tensor("ag1_out", [4, 3072], F32, kind="Internal").ap()
    ag2_in = nc.dram_tensor("ag2_in", [1, NU + 32], F32, kind="Internal").ap()
    ag2_out = nc.dram_tensor("ag2_out", [4, NU + 32], F32, kind="Internal").ap()

    ctx = ExitStack()
    ctx.__enter__()

    def sb(name, shape, dt=F32):
        return ctx.enter_context(nc.sbuf_tensor(name, shape, dt))

    h16 = sb("h16", [128, NCH * 2056], BF16)
    ogb = sb("ogb", [128, NKC * TPC], BF16)
    v1 = sb("v1", [128, NCH * 2056], BF16)
    bqk = sb("bqk", [128, NKC * TPC], BF16)
    wbuf = sb("wbuf", [128, 9728], BF16)
    wsm = sb("wsm", [128, 512], BF16)
    cst = sb("cst", [128, 2056])
    c16 = sb("c16", [128, 2056], BF16)
    lanA = sb("lanA", [128, TPC])
    lanB = sb("lanB", [128, TPC])
    e32t = sb("e32t", [128, NCH * 32])
    e24t = sb("e24t", [128, NCH * 24])
    e24b = sb("e24b", [128, NCH * 24], BF16)
    modv = sb("modv", [128, 6 * 16])
    gab = sb("gab", [128, NH * DV], BF16)
    mkb = sb("mkb", [128, 128])
    idb = sb("idb", [128, 128], BF16)
    idf = sb("idf", [128, 128])
    t16c = sb("t16c", [128, 16])
    t256c = sb("t256c", [128, 256])
    onesf = sb("onesf", [1, 128])
    onesbf = sb("onesbf", [1, 128], BF16)
    onescb = sb("onescb", [128, 1], BF16)
    scT = sb("scT", [128, NKC], BF16)
    bifib = sb("bifib", [8, 1])
    biffb = sb("biffb", [8, 1])
    bifis = sb("bifis", [8, 1])
    biffs = sb("biffs", [8, 1])
    gmr = sb("gmr", [1, NH * DV])
    ccf = sb("ccf", [8, 3])
    ccn = sb("ccn", [8, 3])
    wrowp = sb("wrowp", [1, 8])
    wrow1 = sb("wrow1", [1, 8])
    wbcs = sb("wbcs", [128, 16])
    scrA = sb("scrA", [128, 1024])
    a3s = sb("a3s", [128, 2 * 128], BF16)
    k1s = sb("k1s", [128, 2 * 128], BF16)
    t1s = sb("t1s", [128, 2056], BF16)
    xsq = t1s
    sq8 = sb("sq8", [128, 48])
    sgil = sb("sgil", [128, 2 * 512], BF16)
    tmul = sb("tmul", [128, 2 * 512], BF16)
    x2f = sb("x2f", [128, 2 * 512])
    xre = sb("xre", [128, 2 * 512])
    scrC = xre
    lanC = xre
    cwl16 = gab[0:16, 0:TPC]
    cwrow = scrA[0:1, :].bitcast(BF16)
    cwbc = gab[:, TPC:2 * TPC] if TPC <= NH * DV // 2 else gab[:, 0:TPC]
    g16a = sb("g16a", [128, 16])
    g16b = sb("g16b", [128, 16])
    g16d = sb("g16d", [128, 16])
    g4a = sb("g4a", [128, 4])
    g256 = e32t[:, 0:256]
    g256b = t1s[:, 0:512].bitcast(F32)
    gc1 = sb("gc1", [128, 1])
    gc2 = sb("gc2", [128, 1])

    hob = scrA[:].bitcast(BF16)
    uvw = v1[:].bitcast(F32)

    def psm(name):
        return ctx.enter_context(nc.psum_tensor(name, [128, 512], F32))

    pA, pB, pC, pD, pE, pF, pG = (psm(n) for n in "pA pB pC pD pE pF pG".split())
    PSF = [pA, pB, pC, pD, pE, pF, pG]
    pT = ctx.enter_context(nc.psum_tensor("pT", [128, 1024], BF16))

    SEM, C = {}, {}
    for s in ["qsc", "qxb", "qw0", "qw1", "qw2", "qw3", "qc", "qio",
              "qxr0", "qxr1", "qst0", "qst1", "qac0", "qac1",
              "ste", "sve", "sse", "scc"]:
        SEM[s] = ctx.enter_context(nc.semaphore(s))
        C[s] = 0
    GSEM = ctx.enter_context(nc.semaphore("gsem"))
    G = [0]
    R = {s: 0 for s in SEM}
    FGM = [False]
    ENGS = ["sync", "gpsimd", "vector", "scalar", "tensor"]
    EV = []
    EVG = []
    PHM = {}

    def PH(n):
        if n not in PHM:
            PHM[n] = len(EV)

    def on(eng, fn):
        EV.append((eng, fn))
        EVG.append(G[0])

    def chain(eng):
        v = G[0]
        if v:
            on(eng, lambda e, v=v: e.wait_ge(GSEM, v))

    def dma(eng, out, in_, sem, accum=False, slow=False):
        kw = {}
        if accum:
            kw["accum_op"] = ALU.add
        if slow:
            kw["allow_slow_non_contiguous"] = True
        C[sem] += 16
        if FGM[0]:
            R[sem] += 16
            on(eng, lambda e, o=out, i=in_, S=SEM[sem], kw=kw:
               e.dma_start(out=o, in_=i, **kw).then_inc(S, 16))
        else:
            chain(eng)
            G[0] += 16
            on(eng, lambda e, o=out, i=in_, kw=kw:
               e.dma_start(out=o, in_=i, **kw).then_inc(GSEM, 16))
        return C[sem]

    def wait(eng, sem, val):
        if FGM[0] and val:
            on(eng, lambda e, S=SEM[sem], v=val: e.wait_ge(S, v))

    def op(eng, fn, sem=None):
        if FGM[0]:
            if sem is None:
                on(eng, fn)
                return None
            C[sem] += 1
            R[sem] += 1
            on(eng, lambda e, f=fn, S=SEM[sem]: f(e).then_inc(S, 1))
            return C[sem]
        chain(eng)
        G[0] += 1
        if eng == "vector":
            on(eng, fn)
            on(eng, lambda e: e.drain())
            on(eng, lambda e: e.sem_inc(GSEM, 1))
        else:
            on(eng, lambda e, f=fn: f(e).then_inc(GSEM, 1))
        if sem is None:
            return None
        C[sem] += 1
        return C[sem]

    def drain(eng):
        on(eng, lambda e: e.drain())

    def opd(eng, fn, sem):
        C[sem] += 1
        if FGM[0]:
            R[sem] += 1
            on(eng, fn)
            on(eng, lambda e: e.drain())
            on(eng, lambda e, S=SEM[sem]: e.sem_inc(S, 1))
            return C[sem]
        chain(eng)
        G[0] += 1
        on(eng, fn)
        on(eng, lambda e: e.drain())
        on(eng, lambda e: e.sem_inc(GSEM, 1))
        return C[sem]

    def barrier():
        for engn in ENGS:
            if G[0]:
                on(engn, lambda e, v=G[0]: e.wait_ge(GSEM, v))
            for s, rv in R.items():
                if rv:
                    on(engn, lambda e, S=SEM[s], v=rv: e.wait_ge(S, v))

    def fg_on():
        barrier()
        for s in SEM:
            delta = C[s] - R[s]
            if delta:
                on("sync", lambda e, S=SEM[s], d=delta: e.sem_inc(S, d))
                R[s] += delta
        FGM[0] = True

    def fg_off():
        FGM[0] = False
        barrier()


    ld0 = 0
    for out, src in [(mkb[:], mask01), (idb[:], identb), (idf[:], identf),
                     (gmr[:], gamma), (bifib[:], bifi), (biffb[:], biff), (ccf[:], ccflag),
                     (ccn[:], ccneg), (t16c[:], tril16), (t256c[:], tril256)]:
        ld0 = dma("sync", out, src, "qc")
    csem = dma("gpsimd", scT[:], cTb, "qsc")
    xbsem = dma("gpsimd", ogb[:, 0:NKC * TPC].rearrange("p (g t) -> p g t", g=NKC),
                xT.rearrange("(g p) t -> g p t", p=128).transpose([1, 0, 2]), "qxb")
    op("vector", lambda e: e.memset(onesf[:], 1.0))
    op("vector", lambda e: e.memset(onesbf[:], 1.0))
    op("vector", lambda e: e.memset(onescb[:], 1.0))
    op("vector", lambda e: e.memset(lanB[96:104, :], 0.0))
    op("vector", lambda e: e.memset(cst[:], 0.0))
    op("vector", lambda e: e.memset(c16[:], 0.0))
    vini = op("vector", lambda e: e.memset(v1[:], 1.0), "sve")
    wait("vector", "qc", ld0)
    op("vector", lambda e: e.tensor_scalar(bifis[:], bifib[:], 1.0 / 15.0, None, ALU.mult))
    vbif = op("vector", lambda e: e.tensor_scalar(biffs[:], biffb[:], 1.0 / 15.0, None, ALU.mult), "sve")
    gv = 0
    wait("tensor", "qc", ld0)
    wait("tensor", "sve", vini)
    for h in range(NH):
        p = PSF[h % 2]
        if h >= 2:
            wait("tensor", "sve", gv - 1)
        tv = op("tensor", lambda e, h=h, p=p: e.matmul(
            p[:, 0:256], onesf[0:1, :], gmr[0:1, h * 256:h * 256 + 256],
            start=True, stop=True), "ste")
        wait("vector", "ste", tv)
        gv = op("vector", lambda e, h=h, p=p: e.tensor_copy(
            gab[:, h * 256:h * 256 + 256], p[:, 0:256]), "sve")

    PH(1)
    wait("scalar", "qsc", csem)
    scs = op("scalar", lambda e: e.activation(scT[:], scT[:], AF.Silu), "sse")
    mod_last = [0] * NKC
    for kk in range(NKC):
        so = (kk % 2) * 3072
        if kk >= 2:
            wait("gpsimd", "ste", mod_last[kk - 2])
        wl = dma("gpsimd", wbuf[:, so:so + 3072],
                 wmod_sl[kk * 128:kk * 128 + 128, :], "qw0" if kk % 2 == 0 else "qw1")
        wait("tensor", "qw0" if kk % 2 == 0 else "qw1", wl)
        if kk == 0:
            wait("tensor", "sse", scs)
            wait("tensor", "sve", gv)
        mv = 0
        for nb in range(6):
            mv = op("tensor", lambda e, kk=kk, nb=nb, so=so: e.matmul(
                PSF[nb][0:1, :], scT[:, kk:kk + 1],
                wbuf[:, so + nb * 512: so + nb * 512 + 512],
                start=(kk == 0), stop=(kk == NKC - 1)), "ste")
        mod_last[kk] = mv
    wait("vector", "ste", mod_last[NKC - 1])
    mev = 0
    for nb in range(6):
        r = (nb // 2) * 32
        co = (nb % 2) * 512
        mev = op("vector", lambda e, nb=nb, r=r, co=co: e.tensor_copy(
            scrA[r:r + 1, co:co + 512], PSF[nb][0:1, :]), "sve")
    wait("sync", "sve", mev)
    agi = dma("sync", ag1_in[0, :].rearrange("(a b) -> a b", a=3),
              scrA[0:65:32, 0:1024], "qio")
    wait("gpsimd", "qio", agi)
    cc1 = op("gpsimd", lambda e: e.collective_compute(
        "AllGather", ALU.bypass, ins=[ag1_in[:]], outs=[ag1_out[:]],
        replica_groups=[[0, 1, 2, 3], [4, 5, 6, 7]]), "scc")
    wait("sync", "scc", cc1)
    ag1f = ag1_out.rearrange("a b -> (a b)")
    mvl = 0
    for v6 in range(6):
        mvl = dma("sync", modv[:, v6 * 16:v6 * 16 + 16],
                  ag1f[2048 * v6:2048 * v6 + 2048].rearrange("(c p) -> p c", p=128), "qio",
                  slow=True)
    wait("vector", "qio", mvl)
    op("vector", lambda e: e.tensor_scalar(modv[:, 16:32], modv[:, 16:32], 1.0, None, ALU.add))
    op("vector", lambda e: e.tensor_scalar(modv[:, 64:80], modv[:, 64:80], 1.0, None, ALU.add))
    mgv = 0
    for blk in (2, 5):
        sl = modv[:, blk * 16:blk * 16 + 16]
        op("vector", lambda e, sl=sl: e.tensor_scalar(g16a[:], sl, -1.0, None, ALU.mult))
        op("vector", lambda e, sl=sl: e.tensor_scalar(g16b[:], sl, 0.0, None, ALU.is_ge))
        drain("vector")
        op("vector", lambda e, sl=sl: e.tensor_max(g16d[:], sl, g16a[:]))
        drain("vector")
        op("vector", lambda e: e.tensor_scalar(g16a[:], g16d[:], 0.25, None, ALU.max))
        drain("vector")
        op("vector", lambda e: e.tensor_scalar(g16b[:], g16b[:], 2.0, -1.0, ALU.mult, ALU.add))
        drain("vector")
        mgv = op("vector", lambda e, sl=sl: e.tensor_mul(sl, g16a[:], g16b[:]), "sve")

    def layer_norm(src16, dst16, cacol, sacol, rsem, rval):
        wait("scalar", rsem, rval)
        wait("tensor", rsem, rval)
        wait("tensor", "sve", C["sve"])
        mm_last = [0] * NKC
        for kk in range(NKC):
            so = (kk % 2) * TPC
            if kk >= 2:
                wait("scalar", "ste", mm_last[kk - 2])
            sqd = op("scalar", lambda e, kk=kk, so=so: e.activation(
                xsq[:, so:so + TPC], src16[:, kk * TPC:kk * TPC + TPC], AF.Square), "sse")
            for tb in range(NTB):
                op("tensor", lambda e, kk=kk, tb=tb: e.matmul(
                    [pA, pB][tb][0:1, :], onescb[:, 0:1],
                    src16[:, kk * TPC + tb * 512: kk * TPC + tb * 512 + 512],
                    start=(kk == 0), stop=(kk == NKC - 1)), "ste")
            wait("tensor", "sse", sqd)
            mm2 = 0
            for tb in range(NTB):
                mm2 = op("tensor", lambda e, kk=kk, tb=tb, so=so: e.matmul(
                    [pC, pD][tb][0:1, :], onescb[:, 0:1],
                    xsq[:, so + tb * 512: so + tb * 512 + 512],
                    start=(kk == 0), stop=(kk == NKC - 1)), "ste")
            mm_last[kk] = mm2
        PH(21)
        wait("vector", "ste", mm_last[NKC - 1])
        for tb in range(NTB):
            op("vector", lambda e, tb=tb: e.tensor_scalar(
                lanB[0:1, tb * 512:tb * 512 + 512], [pA, pB][tb][0:1, :],
                1.0 / D, None, ALU.mult))
            op("vector", lambda e, tb=tb: e.tensor_scalar(
                lanA[0:1, tb * 512:tb * 512 + 512], [pC, pD][tb][0:1, :],
                1.0 / D, None, ALU.mult))
        drain("vector")
        op("vector", lambda e: e.tensor_mul(lanC[0:1, 0:TPC],
                                            lanB[0:1, 0:TPC], lanB[0:1, 0:TPC]))
        drain("vector")
        op("vector", lambda e: e.tensor_sub(lanB[32:33, 0:TPC],
                                            lanA[0:1, 0:TPC], lanC[0:1, 0:TPC]))
        drain("vector")
        op("vector", lambda e: e.tensor_scalar(lanB[32:33, 0:TPC],
                                               lanB[32:33, 0:TPC], 1e-5, None, ALU.add))
        drain("vector")
        rvv = opd("vector", lambda e: e.reciprocal(lanA[32:33, 0:TPC],
                                                   lanB[32:33, 0:TPC]), "sve")
        wait("scalar", "sve", rvv)
        rsv = op("scalar", lambda e: e.activation(lanA[0:1, 0:TPC],
                                                  lanA[32:33, 0:TPC], AF.Sqrt), "sse")
        op("vector", lambda e: e.tensor_scalar(lanC[0:1, 0:TPC],
                                               lanB[0:1, 0:TPC], -1.0, None, ALU.mult))
        wait("vector", "sse", rsv)
        drain("vector")
        nmv = op("vector", lambda e: e.tensor_mul(lanB[0:1, 0:TPC],
                                                  lanC[0:1, 0:TPC],
                                                  lanA[0:1, 0:TPC]), "sve")
        PH(22)
        wait("tensor", "sve", nmv)
        bc1 = 0
        for tb in range(NTB):
            bc1 = op("tensor", lambda e, tb=tb: e.matmul(
                [pA, pB][tb][:, :], onesf[0:1, :],
                lanA[0:1, tb * 512:tb * 512 + 512], start=True, stop=True), "ste")
            bc1 = op("tensor", lambda e, tb=tb: e.matmul(
                [pC, pD][tb][:, :], onesf[0:1, :],
                lanB[0:1, tb * 512:tb * 512 + 512], start=True, stop=True), "ste")
        wait("vector", "ste", bc1)
        PH(23)
        outv = 0
        for kk in range(NKC):
            for tb in range(NTB):
                so = tb * 512
                op("vector", lambda e, kk=kk, tb=tb, so=so: e.tensor_tensor(
                    scrC[:, so:so + 512],
                    src16[:, kk * TPC + so: kk * TPC + so + 512], [pA, pB][tb][:, :], ALU.mult))
            for tb in range(NTB):
                so = tb * 512
                op("vector", lambda e, kk=kk, tb=tb, so=so: e.tensor_tensor(
                    scrC[:, so:so + 512], scrC[:, so:so + 512], [pC, pD][tb][:, :], ALU.add))
            drain("vector")
            outv = op("vector", lambda e, kk=kk: e.tensor_scalar(
                dst16[:, kk * TPC:kk * TPC + TPC], scrC[:, 0:TPC],
                cacol[:, kk:kk + 1], sacol[:, kk:kk + 1], ALU.mult, ALU.add), "sve")
            drain("vector")
        return outv

    PH(2)
    wait("vector", "sve", mgv)
    h1v = layer_norm(ogb, h16, modv[:, 16:32], modv[:, 0:16], "qxb", xbsem)

    def h1sl(kk, tb, width=512):
        return h16[:, kk * TPC + tb * 512: kk * TPC + tb * 512 + width]

    PH(3)
    blocks = [("q", h, wqT[:, h, :]) for h in range(NH)]
    blocks += [("k", h, wkT[:, h, :]) for h in range(NH)]
    blocks += [("og", ob, wogT[:, ob, :]) for ob in range(NKC)]
    ring = [None] * 4
    evpar = [0, 0]
    for bi, (kind, idx, wsl) in enumerate(blocks):
        so = (bi % 4) * 2048
        if ring[bi % 4] is not None:
            wait("gpsimd", "ste", ring[bi % 4])
        lv = dma("gpsimd", wbuf[:, so:so + 2048], wsl, "qw%d" % (bi % 4))
        wait("tensor", "qw%d" % (bi % 4), lv)
        if bi == 0:
            wait("tensor", "sve", h1v)
        if bi >= 1:
            wait("tensor", "sve" if kind != "og" or blocks[bi - 1][0] != "og" else "sse",
                 evpar[0] if False else None)
        mmv = 0
        for tb in range(NTB):
            p = [pE, pF][tb]
            if bi >= 1:
                wait("tensor", evgrd[tb][0], evgrd[tb][1])
            for kk in range(NKC):
                mmv = op("tensor", lambda e, so=so, kk=kk, tb=tb, p=p: e.matmul(
                    p[:, :], wbuf[:, so + kk * 128: so + kk * 128 + 128],
                    h1sl(kk, tb), start=(kk == 0), stop=(kk == NKC - 1)), "ste")
        ring[bi % 4] = mmv
        wait("vector" if kind != "og" else "scalar", "ste", mmv)
        evgrd = {}
        for tb in range(NTB):
            p = [pE, pF][tb]
            if kind == "q":
                ev = op("vector", lambda e, idx=idx, tb=tb, p=p: e.tensor_scalar(
                    bqk[:, idx * TPC + tb * 512: idx * TPC + tb * 512 + 512],
                    p[:, :], QSCALE, None, ALU.mult), "sve")
                evgrd[tb] = ("sve", ev)
            elif kind == "k":
                ev = op("vector", lambda e, idx=idx, tb=tb, p=p: e.tensor_copy(
                    bqk[:, (NH + idx) * TPC + tb * 512: (NH + idx) * TPC + tb * 512 + 512],
                    p[:, :]), "sve")
                evgrd[tb] = ("sve", ev)
            else:
                ev = op("scalar", lambda e, idx=idx, tb=tb, p=p: e.activation(
                    ogb[:, idx * TPC + tb * 512: idx * TPC + tb * 512 + 512],
                    p[:, :], AF.Sigmoid), "sse")
                evgrd[tb] = ("sse", ev)
    og_done = evgrd[NTB - 1][1]

    PH(35)
    wifl = dma("gpsimd", wsm[:, 0:256], wifT[:, 0, :], "qw3")
    wait("tensor", "qw3", wifl)
    ifm = 0
    for tb in range(NTB):
        for half in range(2):
            p = ([pA, pB] if half == 0 else [pC, pD])[tb]
            for kk in range(NKC):
                ifm = op("tensor", lambda e, kk=kk, tb=tb, p=p, half=half: e.matmul(
                    p[0:8, :], wsm[:, kk * 16 + half * 8: kk * 16 + half * 8 + 8],
                    h1sl(kk, tb), start=(kk == 0), stop=(kk == NKC - 1)), "ste")
    wait("scalar", "ste", ifm)
    wait("scalar", "sve", vbif)
    for tb in range(NTB):
        op("scalar", lambda e, tb=tb: e.activation(
            lanC[64:72, tb * 512:tb * 512 + 512], [pA, pB][tb][0:8, :],
            AF.Tanh, bias=bifis[:, 0:1], scale=1.0 / 15.0))
        op("scalar", lambda e, tb=tb: e.activation(
            lanC[32:40, tb * 512:tb * 512 + 512], [pC, pD][tb][0:8, :],
            AF.Tanh, bias=biffs[:, 0:1], scale=1.0 / 15.0))
    tg = op("scalar", lambda e: e.activation(gc1[0:8, 0:1], bifis[:, 0:1], AF.Square), "sse")
    wait("vector", "sse", tg)
    op("vector", lambda e: e.tensor_scalar(
        lanC[64:72, 0:TPC], lanC[64:72, 0:TPC], 15.0, None, ALU.mult))
    ifv = op("vector", lambda e: e.tensor_scalar(
        lanC[32:40, 0:TPC], lanC[32:40, 0:TPC], 15.0, None, ALU.mult), "sve")
    drain("vector")

    PH(36)
    wait("vector", "sve", vini)
    vld = [0] * NH
    vld[0] = dma("gpsimd", wbuf[:, 0:4096], wvT[:, 0, :], "qw0")
    vld[1] = dma("gpsimd", wbuf[:, 4096:8192], wvT[:, 1, :], "qw1")
    vev = {0: 0, 1: 0}
    for h in range(NH):
        so = (h % 2) * 4096
        wait("tensor", "qw%d" % (h % 2), vld[h])
        lastmm = 0
        for tt in range(NCH):
            p = [pC, pD][tt % 2]
            if tt >= 2 or h > 0:
                wait("tensor", "sve", vev[tt % 2])
            mmv = 0
            for kk in range(NKC):
                mmv = op("tensor", lambda e, h=h, tt=tt, kk=kk, so=so, p=p: e.matmul(
                    p[:, 0:256], h16[:, kk * TPC + tt * 128: kk * TPC + tt * 128 + 128],
                    wbuf[:, so + kk * 256: so + kk * 256 + 256],
                    start=(kk == 0), stop=(kk == NKC - 1)), "ste")
            wait("vector", "ste", mmv)
            vev[tt % 2] = op("vector", lambda e, h=h, tt=tt, p=p: e.tensor_copy(
                v1[:, tt * 2056 + 257 * h: tt * 2056 + 257 * h + 256], p[:, 0:256]), "sve")
            lastmm = mmv
        if h + 2 < NH:
            wait("gpsimd", "ste", lastmm)
            vld[h + 2] = dma("gpsimd", wbuf[:, so:so + 4096], wvT[:, h + 2, :], "qw%d" % (h % 2))
    v_done = vev[(NCH - 1) % 2]

    PH(4)
    wait("scalar", "sve", ifv)
    op("scalar", lambda e: e.activation(
        lanC[96:104, 0:TPC], lanC[32:40, 0:TPC], AF.Sigmoid))
    drain("scalar")
    flv = op("scalar", lambda e: e.activation(
        lanA[64:72, 0:TPC], lanC[96:104, 0:TPC], AF.Ln), "sse")
    wait("vector", "sse", flv)
    op("vector", lambda e: e.memset(lanB[64:72, 0:TPC], 0.0))
    drain("vector")
    op("vector", lambda e: e.tensor_tensor_scan(
        lanB[96:104, 0:TPC], lanA[64:72, 0:TPC],
        lanB[64:72, 0:TPC], 0.0, ALU.add, ALU.add))
    drain("vector")
    op("vector", lambda e: e.tensor_tensor_scan(
        lanB[64:72, 0:TPC], lanA[64:72, 0:TPC],
        lanC[64:72, 0:TPC], NEG, ALU.add, ALU.max))
    drain("vector")
    op("vector", lambda e: e.tensor_copy(lanA[64:72, 0:TPC], lanB[96:104, 0:TPC]))
    drain("vector")
    op("vector", lambda e: e.tensor_sub(lanB[32:40, 0:TPC],
                                        lanC[64:72, 0:TPC], lanA[64:72, 0:TPC]))
    op("vector", lambda e: e.tensor_sub(lanA[32:40, 0:TPC],
                                        lanA[64:72, 0:TPC], lanB[64:72, 0:TPC]))
    drain("vector")
    e32p = 0
    for cc_ in range(NCH):
        sl = slice(cc_ * 128, cc_ * 128 + 128)
        bc0 = lanA[64:72, cc_ * 128 - 1:cc_ * 128] if cc_ > 0 else None
        bce = lanA[64:72, cc_ * 128 + 127:cc_ * 128 + 128]
        if bc0 is not None:
            op("vector", lambda e, sl=sl, bc0=bc0: e.tensor_scalar(
                lanA[0:8, sl], lanB[32:40, sl], bc0, None, ALU.add))
            op("vector", lambda e, sl=sl, bc0=bc0: e.tensor_scalar(
                lanC[0:8, sl], lanA[32:40, sl], bc0, None, ALU.subtract))
            op("vector", lambda e, cc_=cc_, bce=bce, bc0=bc0: e.tensor_scalar(
                gc2[0:8, 0:1], bce, bc0, None, ALU.subtract))
        else:
            op("vector", lambda e, sl=sl: e.tensor_copy(lanA[0:8, sl], lanB[32:40, sl]))
            op("vector", lambda e, sl=sl: e.tensor_copy(lanC[0:8, sl], lanA[32:40, sl]))
            op("vector", lambda e, bce=bce: e.tensor_copy(gc2[0:8, 0:1], bce))
        op("vector", lambda e, sl=sl, bce=bce: e.tensor_scalar(
            lanB[0:8, sl], lanB[32:40, sl], bce, None, ALU.add))
        drain("vector")
        e32p = op("vector", lambda e, sl=sl: e.tensor_scalar(
            lanC[32:40, sl], lanB[32:40, sl], 0.0, gc2[0:8, 0:1], ALU.mult, ALU.add), "sve")
        drain("vector")
    wait("scalar", "sve", e32p)
    for seg, base in [(lanA, 0), (lanB, 0), (lanC, 0)]:
        op("scalar", lambda e, seg=seg, base=base: e.activation(
            seg[base:base + 8, 0:TPC], seg[base:base + 8, 0:TPC], AF.Exp))
        drain("scalar")
    e32e = op("scalar", lambda e: e.activation(
        lanC[32:40, 0:TPC], lanC[32:40, 0:TPC], AF.Exp), "sse")
    wait("tensor", "sse", e32e)
    etv = 0
    esrc = [(lanA, 0), (lanB, 0), (lanC, 0), (lanC, 32)]
    for cc_ in range(NCH):
        for qi, (seg, base) in enumerate(esrc):
            if cc_ + qi >= 1:
                wait("tensor", "sve", etv)
            tv = op("tensor", lambda e, cc_=cc_, seg=seg, base=base: e.transpose(
                pG[0:128, 0:8], seg[base:base + 8, cc_ * 128:cc_ * 128 + 128],
                idf[base:base + 8, base:base + 8]), "ste")
            wait("vector", "ste", tv)
            etv = op("vector", lambda e, cc_=cc_, qi=qi: e.tensor_copy(
                e32t[:, cc_ * 32 + qi * 8:cc_ * 32 + qi * 8 + 8], pG[:, 0:8]), "sve")

    PH(5)
    wait("tensor", "sve", C["sve"])
    lastC16 = [C["sve"]] * NH
    gA3 = {0: 0, 1: 0}
    gK1 = {0: 0, 1: 0}
    gHq = {0: 0, 1: 0}
    gCb = {0: 0, 1: 0}
    for cc_ in range(NCH):
        for h in range(NH):
            par = (cc_ * NH + h) % 2
            pS = [pA, pB][par]
            pH = [pC, pD][par]
            pCa = [pE, pF][par]
            qsl = bqk[:, h * TPC + cc_ * 128: h * TPC + cc_ * 128 + 128]
            ksl = bqk[:, (NH + h) * TPC + cc_ * 128: (NH + h) * TPC + cc_ * 128 + 128]
            vsl = v1[:, cc_ * 2056 + 257 * h: cc_ * 2056 + 257 * h + 257]
            colfac = e32t[:, cc_ * 32 + h: cc_ * 32 + h + 1]
            kfac = e32t[:, cc_ * 32 + 8 + h: cc_ * 32 + 8 + h + 1]
            rowsc = e32t[:, cc_ * 32 + 16 + h: cc_ * 32 + 16 + h + 1]
            dcy = e32t[:, cc_ * 32 + 24 + h: cc_ * 32 + 24 + h + 1]
            wait("tensor", "sve", gA3[par])
            smm = op("tensor", lambda e, ksl=ksl, qsl=qsl, pS=pS: e.matmul(
                pS[:, 0:128], ksl, qsl, start=True, stop=True), "ste")
            wait("tensor", "sve", gK1[par])
            tmm = op("tensor", lambda e, ksl=ksl, par=par: e.transpose(
                pT[:, par * 512:par * 512 + 128], ksl, idb[:, :]), "ste")
            wait("vector", "ste", tmm)
            gA3[par] = op("vector", lambda e, pS=pS, colfac=colfac, par=par: e.scalar_tensor_tensor(
                a3s[:, par * 128:par * 128 + 128], pS[:, 0:128], colfac, mkb[:, :],
                ALU.mult, ALU.mult), "sve")
            gK1[par] = op("vector", lambda e, par=par, kfac=kfac: e.tensor_scalar(
                k1s[:, par * 128:par * 128 + 128], pT[:, par * 512:par * 512 + 128],
                kfac, None, ALU.mult), "sve")
            wait("tensor", "sve", gA3[par])
            wait("tensor", "sve", gHq[par])
            op("tensor", lambda e, par=par, vsl=vsl, pH=pH: e.matmul(
                pH[:, 0:257], a3s[:, par * 128:par * 128 + 128], vsl,
                start=True, stop=False), "ste")
            wait("tensor", "sve", lastC16[h])
            hmm = op("tensor", lambda e, qsl=qsl, h=h, pH=pH: e.matmul(
                pH[:, 0:257], qsl, c16[:, 257 * h:257 * h + 257],
                start=False, stop=True), "ste")
            wait("tensor", "sve", gCb[par])
            cmm = op("tensor", lambda e, par=par, vsl=vsl, pCa=pCa: e.matmul(
                pCa[:, 0:257], k1s[:, par * 128:par * 128 + 128], vsl,
                start=True, stop=True), "ste")
            wait("vector", "ste", hmm)
            gHq[par] = op("vector", lambda e, cc_=cc_, h=h, pH=pH, rowsc=rowsc: e.tensor_scalar(
                h16[:, cc_ * 2056 + 257 * h: cc_ * 2056 + 257 * h + 257],
                pH[:, 0:257], rowsc, None, ALU.mult), "sve")
            wait("vector", "ste", cmm)
            gCb[par] = op("vector", lambda e, h=h, dcy=dcy, pCa=pCa: e.scalar_tensor_tensor(
                cst[:, 257 * h:257 * h + 257], cst[:, 257 * h:257 * h + 257],
                dcy, pCa[:, 0:257], ALU.mult, ALU.add), "sve")
            drain("vector")
            lastC16[h] = op("vector", lambda e, h=h: e.tensor_copy(
                c16[:, 257 * h:257 * h + 257], cst[:, 257 * h:257 * h + 257]), "sve")
    passa_v = C["sve"]

    PH(6)
    wait("sync", "sve", passa_v)
    dma("sync", ag2_in[0, 0:NU].rearrange("(p f) -> p f", p=128), cst[:], "qio")
    dma("sync", ag2_in[0, NU:NU + 8].rearrange("(p f) -> p f", p=8),
        lanA[64:72, TPC - 1:TPC], "qio")
    ag2v = dma("sync", ag2_in[0, NU + 8:NU + 16].rearrange("(p f) -> p f", p=8),
               lanB[64:72, TPC - 1:TPC], "qio")
    wait("gpsimd", "qio", ag2v)
    cc2 = op("gpsimd", lambda e: e.collective_compute(
        "AllGather", ALU.bypass, ins=[ag2_in[:]], outs=[ag2_out[:]],
        replica_groups=[[0, 1, 2, 3], [4, 5, 6, 7]]), "scc")
    wait("sync", "scc", cc2)
    uld = 0
    for j in range(3):
        uld = dma("sync", uvw[:, j * 2056:(j + 1) * 2056],
                  ag2_out[j, 0:NU].rearrange("(p f) -> p f", p=128), "qio")
        uld = dma("sync", lanC[0:8, 2 * j:2 * j + 1],
                  ag2_out[j, NU:NU + 8].rearrange("(p f) -> p f", p=8), "qio")
        uld = dma("sync", lanC[0:8, 2 * j + 1:2 * j + 2],
                  ag2_out[j, NU + 8:NU + 16].rearrange("(p f) -> p f", p=8), "qio")

    PH(7)
    op("vector", lambda e: e.memset(cst[:], 0.0))
    op("vector", lambda e: e.memset(lanC[0:8, 6:7], 0.0))
    wait("vector", "qio", uld)
    for j in range(3):
        op("vector", lambda e, j=j: e.tensor_mul(lanC[0:8, 9:10], lanC[0:8, 2 * j + 1:2 * j + 2], ccf[:, j:j + 1]))
        op("vector", lambda e, j=j: e.tensor_mul(lanC[0:8, 10:11], lanC[0:8, 2 * j:2 * j + 1], ccf[:, j:j + 1]))
        drain("vector")
        op("vector", lambda e, j=j: e.tensor_add(lanC[0:8, 9:10], lanC[0:8, 9:10], ccn[:, j:j + 1]))
        op("vector", lambda e: e.tensor_add(lanC[0:8, 11:12], lanC[0:8, 10:11], lanC[0:8, 6:7]))
        drain("vector")
        op("vector", lambda e: e.tensor_max(lanC[0:8, 12:13], lanC[0:8, 11:12], lanC[0:8, 9:10]))
        drain("vector")
        op("vector", lambda e: e.tensor_sub(lanC[0:8, 13:14], lanC[0:8, 11:12], lanC[0:8, 12:13]))
        op("vector", lambda e: e.tensor_scalar(lanC[0:8, 14:15], lanC[0:8, 12:13], -1.0, None, ALU.mult))
        drain("vector")
        wex = op("vector", lambda e: e.tensor_copy(lanC[0:8, 6:7], lanC[0:8, 12:13]), "sve")
        wait("scalar", "sve", wex)
        op("scalar", lambda e: e.activation(lanC[0:8, 13:14], lanC[0:8, 13:14], AF.Exp))
        drain("scalar")
        sxv = op("scalar", lambda e: e.activation(lanC[0:8, 14:15], lanC[0:8, 14:15], AF.Exp), "sse")
        wait("vector", "sse", sxv)
        wnv = op("vector", lambda e, j=j: e.tensor_mul(lanC[0:8, 15:16], lanC[0:8, 14:15], ccf[:, j:j + 1]), "sve")
        wait("tensor", "sve", wnv)
        op("tensor", lambda e: e.transpose(
            pG[0:1, 0:8], lanC[0:8, 13:14], idf[0:8, 0:8]), "ste")
        t1v = op("tensor", lambda e: e.transpose(
            pG[0:1, 8:16], lanC[0:8, 15:16], idf[0:8, 0:8]), "ste")
        wait("vector", "ste", t1v)
        op("vector", lambda e: e.tensor_copy(wrowp[:, :], pG[0:1, 0:8]))
        t2v = op("vector", lambda e: e.tensor_copy(wrow1[:, :], pG[0:1, 8:16]), "sve")
        wait("tensor", "sve", t2v)
        b2 = op("tensor", lambda e: e.matmul(pG[:, 0:8], onesf[0:1, :], wrowp[0:1, :],
                                             start=True, stop=True), "ste")
        b2 = op("tensor", lambda e: e.matmul(pG[:, 8:16], onesf[0:1, :], wrow1[0:1, :],
                                             start=True, stop=True), "ste")
        wait("vector", "ste", b2)
        op("vector", lambda e: e.tensor_copy(wbcs[:, :], pG[:, 0:16]))
        drain("vector")
        for h in range(NH):
            op("vector", lambda e, j=j, h=h: e.tensor_scalar(
                scrA[:, 0:257],
                uvw[:, j * 2056 + 257 * h: j * 2056 + 257 * h + 257],
                wbcs[:, 8 + h:8 + h + 1], None, ALU.mult))
            drain("vector")
            op("vector", lambda e, h=h: e.scalar_tensor_tensor(
                cst[:, 257 * h:257 * h + 257], cst[:, 257 * h:257 * h + 257],
                wbcs[:, h:h + 1], scrA[:, 0:257], ALU.mult, ALU.add))
            drain("vector")
    p16v = op("vector", lambda e: e.tensor_copy(c16[:], cst[:]), "sve")

    PH(8)
    op("vector", lambda e: e.tensor_scalar(
        lanC[64:72, 0:TPC], lanA[64:72, 0:TPC], lanC[0:8, 6:7], None, ALU.add))
    drain("vector")
    op("vector", lambda e: e.tensor_max(
        lanA[64:72, 0:TPC], lanB[64:72, 0:TPC], lanC[64:72, 0:TPC]))
    drain("vector")
    op("vector", lambda e: e.tensor_sub(
        lanA[0:8, 0:TPC], lanB[64:72, 0:TPC], lanA[64:72, 0:TPC]))
    op("vector", lambda e: e.tensor_sub(
        lanB[0:8, 0:TPC], lanC[64:72, 0:TPC], lanA[64:72, 0:TPC]))
    e24v = op("vector", lambda e: e.tensor_scalar(
        lanC[0:8, 0:TPC], lanA[64:72, 0:TPC], -1.0, None, ALU.mult), "sve")
    wait("scalar", "sve", e24v)
    op("scalar", lambda e: e.activation(lanA[0:8, 0:TPC], lanA[0:8, 0:TPC], AF.Exp))
    drain("scalar")
    op("scalar", lambda e: e.activation(lanB[0:8, 0:TPC], lanB[0:8, 0:TPC], AF.Exp))
    drain("scalar")
    e24e = op("scalar", lambda e: e.activation(lanC[0:8, 0:TPC], lanC[0:8, 0:TPC], AF.Exp), "sse")
    wait("tensor", "sse", e24e)
    etv = 0
    esrc24 = [(lanA, 0), (lanB, 0), (lanC, 0)]
    for cc_ in range(NCH):
        for qi, (seg, base) in enumerate(esrc24):
            if cc_ + qi >= 1:
                wait("tensor", "sve", etv)
            tv = op("tensor", lambda e, cc_=cc_, seg=seg, base=base: e.transpose(
                pG[0:128, 0:8], seg[base:base + 8, cc_ * 128:cc_ * 128 + 128],
                idf[base:base + 8, base:base + 8]), "ste")
            wait("vector", "ste", tv)
            op("vector", lambda e, cc_=cc_, qi=qi: e.tensor_copy(
                e24t[:, cc_ * 24 + qi * 8:cc_ * 24 + qi * 8 + 8], pG[:, 0:8]))
            drain("vector")
            etv = op("vector", lambda e, cc_=cc_, qi=qi: e.tensor_copy(
                e24b[:, cc_ * 24 + qi * 8:cc_ * 24 + qi * 8 + 8],
                e24t[:, cc_ * 24 + qi * 8:cc_ * 24 + qi * 8 + 8]), "sve")

    PH(9)
    wait("tensor", "sve", C["sve"])
    gH2 = {0: 0, 1: 0}
    gTo = {0: 0, 1: 0}
    for cc_ in range(NCH):
        tslab = op("vector", lambda e, cc_=cc_: e.tensor_tensor(
            t1s[:, 0:2056].rearrange("p (h f) -> p h f", f=257),
            h16[:, cc_ * 2056: cc_ * 2056 + 2056].rearrange("p (h f) -> p h f", f=257),
            e24b[:, cc_ * 24: cc_ * 24 + 8].unsqueeze(2).to_broadcast([128, 8, 257]),
            ALU.mult), "sve")
        drain("vector")
        for h in range(NH):
            par = h % 2
            p = [pA, pB][par]
            qsl = bqk[:, h * TPC + cc_ * 128: h * TPC + cc_ * 128 + 128]
            wait("tensor", "sve", gH2[par])
            if h == 0:
                wait("tensor", "sve", tslab)
            h2mm = op("tensor", lambda e, qsl=qsl, h=h, p=p: e.matmul(
                p[:, 0:257], qsl, c16[:, 257 * h:257 * h + 257], start=True, stop=True), "ste")
            wait("vector", "ste", h2mm)
            gH2[par] = op("vector", lambda e, h=h, p=p, cc_=cc_: e.scalar_tensor_tensor(
                cst[:, 257 * h: 257 * h + 257], p[:, 0:257],
                e24t[:, cc_ * 24 + 8 + h: cc_ * 24 + 8 + h + 1],
                t1s[:, 257 * h: 257 * h + 257], ALU.mult, ALU.add), "sve")
        drain("vector")
        op("vector", lambda e: e.tensor_scalar(
            sq8[:, 40:48], cst[:, 256:2056:257], -1.0, None, ALU.mult))
        drain("vector")
        op("vector", lambda e: e.tensor_max(
            sq8[:, 0:8], sq8[:, 40:48], cst[:, 256:2056:257]))
        drain("vector")
        op("vector", lambda e, cc_=cc_: e.tensor_max(
            sq8[:, 0:8], sq8[:, 0:8], e24t[:, cc_ * 24 + 16: cc_ * 24 + 24]))
        drain("vector")
        op("vector", lambda e: e.tensor_scalar(sq8[:, 0:8], sq8[:, 0:8], 1e-6, None, ALU.add))
        drain("vector")
        op("vector", lambda e: e.reciprocal(sq8[:, 8:16], sq8[:, 0:8]))
        drain("vector")
        hnv = op("vector", lambda e: e.tensor_tensor(
            t1s[:, 0:2048].rearrange("p (h f) -> p h f", f=256),
            cst[:, 0:2056].rearrange("p (h f) -> p h f", f=257)[:, :, 0:256],
            sq8[:, 8:16].unsqueeze(2).to_broadcast([128, 8, 256]), ALU.mult), "sve")
        drain("vector")
        op("vector", lambda e: e.reduce_sum(
            sq8[:, 16:24], t1s[:, 0:2048].rearrange("p (h f) -> p h f", f=256), axis=AX.X))
        wait("scalar", "sve", hnv)
        sqv = op("scalar", lambda e: e.activation(cst[:, 0:2048], t1s[:, 0:2048], AF.Square), "sse")
        wait("vector", "sse", sqv)
        op("vector", lambda e: e.reduce_sum(
            sq8[:, 24:32], cst[:, 0:2048].rearrange("p (h f) -> p h f", f=256), axis=AX.X))
        drain("vector")
        op("vector", lambda e: e.tensor_scalar(sq8[:, 16:24], sq8[:, 16:24], -1.0 / 256, None, ALU.mult))
        op("vector", lambda e: e.tensor_scalar(sq8[:, 24:32], sq8[:, 24:32], 1.0 / 256, None, ALU.mult))
        drain("vector")
        op("vector", lambda e: e.tensor_mul(sq8[:, 32:40], sq8[:, 16:24], sq8[:, 16:24]))
        drain("vector")
        op("vector", lambda e: e.tensor_sub(sq8[:, 24:32], sq8[:, 24:32], sq8[:, 32:40]))
        drain("vector")
        op("vector", lambda e: e.tensor_scalar(sq8[:, 24:32], sq8[:, 24:32], 1e-6, None, ALU.add))
        drain("vector")
        rvv = opd("vector", lambda e: e.reciprocal(sq8[:, 32:40], sq8[:, 24:32]), "sve")
        wait("scalar", "sve", rvv)
        rsv = op("scalar", lambda e: e.activation(sq8[:, 40:48], sq8[:, 32:40], AF.Sqrt), "sse")
        wait("vector", "sse", rsv)
        op("vector", lambda e: e.tensor_tensor(
            cst[:, 0:2048].rearrange("p (h f) -> p h f", f=256),
            t1s[:, 0:2048].rearrange("p (h f) -> p h f", f=256),
            sq8[:, 16:24].unsqueeze(2).to_broadcast([128, 8, 256]), ALU.add))
        drain("vector")
        op("vector", lambda e: e.tensor_tensor(
            cst[:, 0:2048].rearrange("p (h f) -> p h f", f=256),
            cst[:, 0:2048].rearrange("p (h f) -> p h f", f=256),
            sq8[:, 40:48].unsqueeze(2).to_broadcast([128, 8, 256]), ALU.mult))
        drain("vector")
        hgv = op("vector", lambda e: e.tensor_mul(hob[:, 0:2048], cst[:, 0:2048], gab[:, :]), "sve")
        wait("tensor", "sve", hgv)
        for vck in range(NKC):
            par = vck % 2
            wait("tensor", "sve", gTo[par])
            tv = op("tensor", lambda e, vck=vck, par=par: e.transpose(
                pT[:, par * 512:par * 512 + 128], hob[:, vck * 128:vck * 128 + 128],
                idb[:, :]), "ste")
            wait("vector", "ste", tv)
            gTo[par] = op("vector", lambda e, vck=vck, cc_=cc_, par=par: e.tensor_tensor(
                v1[:, vck * TPC + cc_ * 128: vck * TPC + cc_ * 128 + 128],
                pT[:, par * 512:par * 512 + 128],
                ogb[:, vck * TPC + cc_ * 128: vck * TPC + cc_ * 128 + 128], ALU.mult), "sve")
    passb_v = C["sve"]

    PH(10)
    ring = [None] * 4
    xgrd = {0: 0, 1: 0}
    stgrd = {0: 0, 1: 0}
    for db in range(NKC):
        so = (db % 4) * 2048
        if ring[db % 4] is not None:
            wait("gpsimd", "ste", ring[db % 4])
        lv = dma("gpsimd", wbuf[:, so:so + 2048], woutT[:, db, :], "qw%d" % (db % 4))
        wait("tensor", "qw%d" % (db % 4), lv)
        if db == 0:
            wait("tensor", "sve", passb_v)
        for tb in range(NTB):
            p = [pC, pD][tb]
            sl2 = ((db * NTB + tb) % 2) * 512
            wait("sync", "sve", xgrd[(db * NTB + tb) % 2])
            xl = dma("sync", xre[:, sl2:sl2 + 512],
                     xT[db * 128: db * 128 + 128, tb * 512: tb * 512 + 512],
                     "qxr%d" % ((db * NTB + tb) % 2))
            wait("tensor", "sve", stgrd[tb])
            mmv = 0
            for kk in range(NKC):
                mmv = op("tensor", lambda e, so=so, kk=kk, tb=tb, p=p: e.matmul(
                    p[:, :], wbuf[:, so + kk * 128: so + kk * 128 + 128],
                    v1[:, kk * TPC + tb * 512: kk * TPC + tb * 512 + 512],
                    start=(kk == 0), stop=(kk == NKC - 1)), "ste")
            if tb == NTB - 1:
                ring[db % 4] = mmv
            wait("vector", "ste", mmv)
            wait("vector", "qxr%d" % ((db * NTB + tb) % 2), xl)
            wait("vector", "qst%d" % ((db * NTB + tb) % 2),
                 C["qst%d" % ((db * NTB + tb) % 2)])
            ev = op("vector", lambda e, db=db, p=p, sl2=sl2: e.scalar_tensor_tensor(
                x2f[:, sl2:sl2 + 512], p[:, :], modv[:, 32 + db:32 + db + 1],
                xre[:, sl2:sl2 + 512], ALU.mult, ALU.add), "sve")
            stgrd[tb] = ev
            wait("scalar", "sve", ev)
            dma("scalar", yT[db * 128:db * 128 + 128, tb * 512:tb * 512 + 512],
                x2f[:, sl2:sl2 + 512], "qst%d" % ((db * NTB + tb) % 2))
            drain("vector")
            x2v = op("vector", lambda e, db=db, tb=tb, sl2=sl2: e.tensor_copy(
                bqk[:, db * TPC + tb * 512: db * TPC + tb * 512 + 512],
                x2f[:, sl2:sl2 + 512]), "sve")
            xgrd[(db * NTB + tb) % 2] = x2v
    x2_qst = (C["qst0"], C["qst1"])

    PH(11)
    h2v = layer_norm(bqk, bqk, modv[:, 64:80], modv[:, 48:64], "sve", x2v)

    def h2sl(kk, tb, width=512):
        return bqk[:, kk * TPC + tb * 512: kk * TPC + tb * 512 + width]

    PH(12)
    gld = dma("gpsimd", wsm[:, 256:512], gateT[:, 0, :], "qw3")
    wait("tensor", "qw3", gld)
    wait("tensor", "sve", h2v)
    gmm = 0
    for tb in range(NTB):
        p = [pE, pF][tb]
        for kk in range(NKC):
            gmm = op("tensor", lambda e, kk=kk, tb=tb, p=p: e.matmul(
                p[0:16, :], wsm[:, 256 + kk * 16: 256 + kk * 16 + 16],
                h2sl(kk, tb), start=(kk == 0), stop=(kk == NKC - 1)), "ste")
    wait("vector", "ste", gmm)
    glv = 0
    for tb in range(NTB):
        glv = op("vector", lambda e, tb=tb: e.tensor_copy(
            lanC[0:16, tb * 512:tb * 512 + 512], [pE, pF][tb][0:16, :]), "sve")
    cwv = glv
    for tt in range(NCH):
        wait("tensor", "sve", cwv)
        ttv = op("tensor", lambda e, tt=tt: e.transpose(
            pG[0:128, 0:16], lanC[0:16, tt * 128:tt * 128 + 128],
            idf[0:16, 0:16]), "ste")
        wait("vector", "ste", ttv)
        op("vector", lambda e: e.tensor_copy(g16a[:], pG[:, 0:16]))
        drain("vector")
        nmx = op("vector", lambda e: e.reduce_max(gc1[:], g16a[:], axis=AX.X, negate=True), "sve")
        wait("scalar", "sve", nmx)
        exv = op("scalar", lambda e: e.activation(g16b[:], g16a[:], AF.Exp,
                                                  bias=gc1[:, 0:1], scale=1.0), "sse")
        wait("vector", "sse", exv)
        op("vector", lambda e: e.reduce_sum(gc2[:], g16b[:], axis=AX.X))
        drain("vector")
        op("vector", lambda e: e.reciprocal(gc1[:], gc2[:]))
        drain("vector")
        op("vector", lambda e: e.tensor_scalar(g16a[:], g16b[:], gc1[:, 0:1], None, ALU.mult))
        drain("vector")
        op("vector", lambda e: e.reduce_max(g4a[:], g16a[:].rearrange("p (g i) -> p g i", i=4), axis=AX.X))
        drain("vector")
        op("vector", lambda e: e.tensor_tensor(
            g16b[:].rearrange("p (g i) -> p g i", i=4),
            g4a[:].unsqueeze(1).to_broadcast([128, 4, 4]),
            g4a[:].unsqueeze(2).to_broadcast([128, 4, 4]), ALU.is_gt))
        op("vector", lambda e: e.tensor_tensor(
            g16d[:].rearrange("p (g i) -> p g i", i=4),
            g4a[:].unsqueeze(1).to_broadcast([128, 4, 4]),
            g4a[:].unsqueeze(2).to_broadcast([128, 4, 4]), ALU.is_equal))
        drain("vector")
        op("vector", lambda e: e.tensor_mul(g16d[:], g16d[:], t16c[:]))
        drain("vector")
        op("vector", lambda e: e.tensor_add(g16b[:], g16b[:], g16d[:]))
        drain("vector")
        op("vector", lambda e: e.reduce_sum(g4a[:], g16b[:].rearrange("p (g i) -> p g i", i=4), axis=AX.X))
        drain("vector")
        op("vector", lambda e: e.tensor_scalar(g4a[:], g4a[:], 2.0, None, ALU.is_lt))
        drain("vector")
        op("vector", lambda e: e.tensor_tensor(
            g16b[:].rearrange("p (g i) -> p g i", i=4),
            g16a[:].rearrange("p (g i) -> p g i", i=4),
            g4a[:].unsqueeze(2).to_broadcast([128, 4, 4]), ALU.mult))
        drain("vector")
        op("vector", lambda e: e.tensor_tensor(
            g256[:].rearrange("p (a b) -> p a b", b=16),
            g16b[:].unsqueeze(1).to_broadcast([128, 16, 16]),
            g16b[:].unsqueeze(2).to_broadcast([128, 16, 16]), ALU.is_gt))
        op("vector", lambda e: e.tensor_tensor(
            g256b[:].rearrange("p (a b) -> p a b", b=16),
            g16b[:].unsqueeze(1).to_broadcast([128, 16, 16]),
            g16b[:].unsqueeze(2).to_broadcast([128, 16, 16]), ALU.is_equal))
        drain("vector")
        op("vector", lambda e: e.tensor_mul(g256b[:], g256b[:], t256c[:]))
        drain("vector")
        op("vector", lambda e: e.tensor_add(g256[:], g256[:], g256b[:]))
        drain("vector")
        op("vector", lambda e: e.reduce_sum(g16d[:], g256[:].rearrange("p (a b) -> p a b", b=16), axis=AX.X))
        drain("vector")
        op("vector", lambda e: e.tensor_scalar(g16d[:], g16d[:], 4.0, None, ALU.is_lt))
        drain("vector")
        op("vector", lambda e: e.tensor_mul(g16b[:], g16b[:], g16d[:]))
        drain("vector")
        op("vector", lambda e: e.reduce_sum(gc2[:], g16b[:], axis=AX.X))
        drain("vector")
        op("vector", lambda e: e.tensor_scalar(gc2[:], gc2[:], 1e-20, None, ALU.add))
        drain("vector")
        op("vector", lambda e: e.reciprocal(gc1[:], gc2[:]))
        drain("vector")
        cwq = op("vector", lambda e: e.tensor_scalar(g16a[:], g16b[:], gc1[:, 0:1], None, ALU.mult), "sve")
        wait("tensor", "sve", cwq)
        tbv = op("tensor", lambda e, tt=tt: e.transpose(
            pG[0:16, 0:128], g16a[:], idf[:, :]), "ste")
        wait("vector", "ste", tbv)
        cwv = op("vector", lambda e, tt=tt: e.tensor_copy(
            cwl16[:, tt * 128:tt * 128 + 128], pG[0:16, 0:128]), "sve")

    PH(13)
    fg_on()

    def aslice(s, tb):
        if s < 16:
            t_, base = h16, s * TPC
        elif s < 32:
            t_, base = ogb, (s - 16) * TPC
        else:
            t_, base = v1, (s - 32) * TPC
        return t_[:, base + tb * 512: base + tb * 512 + 512]

    wait("gpsimd", "ste", C["ste"])
    wait("gpsimd", "qst0", x2_qst[0])
    wait("gpsimd", "qst1", x2_qst[1])
    groups = [[0, 1, 2, 3], [4, 5, 6, 7], [8, 9, 10, 11], [12, 13, 14, 15], ["sh"]]
    upfree = [None, None]
    upld_i = 0
    cbe = cwv
    sgrd = {0: 0, 1: 0}
    ugrd = {0: 0, 1: 0}
    dn_guard = 0
    acc_guard = {0: 0, 1: 0}
    for grp in groups:
        wait("gpsimd", "qac0", C["qac0"])
        wait("gpsimd", "qac1", C["qac1"])
        shared = grp[0] == "sh"
        nexp = 1 if shared else len(grp)
        nj = 22 if shared else 11
        for ei in range(nexp):
            eidx = None if shared else grp[ei]
            if not shared:
                wait("sync", "sve", cbe)
                crv = dma("sync", cwrow[0:1, 0:TPC], cwl16[eidx:eidx + 1, :], "qio")
                for tb in range(NTB):
                    wait("tensor", "sve", cbe)
                    wait("tensor", "qio", crv)
                    cbv = op("tensor", lambda e, tb=tb: e.matmul(
                        pG[:, :], onesbf[0:1, :],
                        cwrow[0:1, tb * 512:tb * 512 + 512],
                        start=True, stop=True), "ste")
                    wait("vector", "ste", cbv)
                    cbe = op("vector", lambda e, tb=tb: e.tensor_copy(
                        cwbc[:, tb * 512:tb * 512 + 512], pG[:, :]), "sve")
            gmmv = 0
            for jj in range(nj):
                for half in range(2):
                    jb = jj + half * nj
                    slot = upld_i % 2
                    upld_i += 1
                    if upfree[slot] is not None:
                        wait("gpsimd", "ste", upfree[slot])
                    src = supT[:, jb, :] if shared else upT[eidx, :, jb, :]
                    lv = dma("gpsimd", wbuf[:, slot * 2048:slot * 2048 + 2048], src,
                             "qw%d" % slot)
                    wait("tensor", "qw%d" % slot, lv)
                    wait("tensor", "sse" if half == 0 else "sve",
                         sgrd[0] if half == 0 else ugrd[0])
                    mmv = 0
                    for kk in range(NKC):
                        for tb in range(NTB):
                            p = ([pA, pB] if half == 0 else [pC, pD])[tb]
                            mmv = op("tensor", lambda e, slot=slot, kk=kk, tb=tb, p=p: e.matmul(
                                p[:, :], wbuf[:, slot * 2048 + kk * 128: slot * 2048 + kk * 128 + 128],
                                h2sl(kk, tb), start=(kk == 0), stop=(kk == NKC - 1)), "ste")
                    upfree[slot] = mmv
                    if half == 0:
                        gmmv = mmv
                wait("scalar", "ste", gmmv)
                sgv = 0
                for tb in range(NTB):
                    sgv = op("scalar", lambda e, tb=tb: e.activation(
                        sgil[:, tb * 512:tb * 512 + 512], [pA, pB][tb][:, :], AF.Silu), "sse")
                sgrd[0] = sgv
                wait("vector", "ste", mmv)
                wait("vector", "sse", sgv)
                s_global = jj if shared else ei * 11 + jj
                for tb in range(NTB):
                    op("vector", lambda e, tb=tb: e.tensor_tensor(
                        tmul[:, tb * 512:tb * 512 + 512], sgil[:, tb * 512:tb * 512 + 512],
                        [pC, pD][tb][:, :], ALU.mult))
                drain("vector")
                av = 0
                for tb in range(NTB):
                    if shared:
                        av = op("vector", lambda e, s_global=s_global, tb=tb: e.tensor_copy(
                            aslice(s_global, tb), tmul[:, tb * 512:tb * 512 + 512]), "sve")
                    else:
                        av = op("vector", lambda e, s_global=s_global, tb=tb: e.tensor_mul(
                            aslice(s_global, tb), tmul[:, tb * 512:tb * 512 + 512],
                            cwbc[:, tb * 512:tb * 512 + 512]), "sve")
                ugrd[0] = av
        up_done = C["sve"]
        for db in range(NKC):
            wait("gpsimd", "ste", dn_guard)
            dls = 0
            for ei in range(nexp):
                slot = 4096 + ei * 1408
                src = sdownT[:, db, :] if shared else downT[grp[ei], :, db, :]
                dls = dma("gpsimd", wbuf[:, slot:slot + nj * 128], src, "qw2")
            wait("tensor", "qw2", dls)
            if db == 0:
                wait("tensor", "sve", up_done)
            for tb in range(NTB):
                p = [pE, pF][tb]
                wait("tensor", "sve", acc_guard[tb])
                mmv = 0
                for ei in range(nexp):
                    slot = 4096 + ei * 1408
                    for ik in range(nj):
                        s_global = ik if shared else ei * 11 + ik
                        mmv = op("tensor", lambda e, slot=slot, ik=ik, tb=tb, p=p,
                                 s_global=s_global, first=(ei == 0 and ik == 0),
                                 last=(ei == nexp - 1 and ik == nj - 1): e.matmul(
                            p[:, :], wbuf[:, slot + ik * 128: slot + ik * 128 + 128],
                            aslice(s_global, tb), start=first, stop=last), "ste")
                if tb == NTB - 1:
                    dn_guard = mmv
                wait("vector", "ste", mmv)
                sl3 = ((db * NTB + tb) % 2) * 512
                wait("vector", "qac%d" % ((db * NTB + tb) % 2),
                     C["qac%d" % ((db * NTB + tb) % 2)])
                e3 = op("vector", lambda e, db=db, p=p, sl3=sl3: e.tensor_scalar(
                    x2f[:, sl3:sl3 + 512], p[:, :], modv[:, 80 + db:80 + db + 1],
                    None, ALU.mult), "sve")
                acc_guard[tb] = e3
                wait("gpsimd", "sve", e3)
                dma("gpsimd", yT[db * 128:db * 128 + 128, tb * 512:tb * 512 + 512],
                    x2f[:, sl3:sl3 + 512], "qac%d" % ((db * NTB + tb) % 2), accum=True)

    PH(14)
    fg_off()
    wait("gpsimd", "qac0", C["qac0"])
    wait("gpsimd", "qac1", C["qac1"])
    wait("scalar", "qst0", C["qst0"])
    wait("scalar", "qst1", C["qst1"])

    if KPH < 99:
        cuts = sorted(k for k in PHM if k > KPH)
        if cuts:
            del EV[PHM[cuts[0]]:]
            del EVG[PHM[cuts[0]]:]

    fin = EVG[-1]
    for engn in ["sync", "gpsimd", "vector", "scalar", "tensor"]:
        on(engn, lambda e, v=fin: e.wait_ge(GSEM, v))

    block = ctx.enter_context(nc.Block())

    def runner(eng_name):
        def body(eng):
            for nm, fn in EV:
                if nm == eng_name:
                    fn(eng)
        return body

    block.sync(runner("sync"))
    block.gpsimd(runner("gpsimd"))
    block.vector(runner("vector"))
    block.scalar(runner("scalar"))
    block.tensor(runner("tensor"))
    ctx.close()
    return nc


def _wtile(W, mb):
    Din, M = W.shape[1], W.shape[0]
    kb = Din // 128
    arr = np.ascontiguousarray(W.T).reshape(kb, 128, M // mb, mb)
    return np.ascontiguousarray(arr.transpose(1, 2, 0, 3)).reshape(128, M // mb, kb * mb)


_NC_CACHE = {}


def kernel(**inp):
    S = np.asarray(inp["L0_attn_x"]).shape[1]
    TPC = S // 4
    if TPC not in _NC_CACHE:
        _NC_CACHE[TPC] = build_nc(TPC)
    nc = _NC_CACHE[TPC]

    x = np.asarray(inp["L0_attn_x"], np.float32)
    cc = np.asarray(inp["L0_c"], np.float32)
    wmod = np.asarray(inp["L0_wmod"], np.float32)

    wq = _wtile(np.asarray(inp["L0_attn_Wq"], np.float32), 128)
    wk = _wtile(np.asarray(inp["L0_attn_Wk"], np.float32), 128)
    wv = _wtile(np.asarray(inp["L0_attn_Wv"], np.float32), 256)
    wog = _wtile(np.asarray(inp["L0_attn_Wog"], np.float32), 128)
    wif = _wtile(np.concatenate([np.asarray(inp["L0_attn_Wig"], np.float32),
                                 np.asarray(inp["L0_attn_Wfg"], np.float32)], 0), 16)
    wout = _wtile(np.asarray(inp["L0_attn_Wout"], np.float32), 128)
    gate = _wtile(np.asarray(inp["L0_ffn_gate_w"], np.float32), 16)
    up = np.stack([_wtile(np.asarray(inp["L0_ffn_expert_up_w"][e], np.float32), 128)
                   for e in range(E)])
    down = np.stack([_wtile(np.asarray(inp["L0_ffn_expert_down_w"][e], np.float32), 128)
                     for e in range(E)])
    sup = _wtile(np.asarray(inp["L0_ffn_shared_up_w"], np.float32), 128)
    sdn = _wtile(np.asarray(inp["L0_ffn_shared_down_w"], np.float32), 128)
    bigv = np.ascontiguousarray(np.asarray(inp["L0_attn_big"], np.float32)[:, None])
    bfgv = np.ascontiguousarray(np.asarray(inp["L0_attn_bfg"], np.float32)[:, None])
    gm = np.ascontiguousarray(np.asarray(inp["L0_attn_gamma"], np.float32).reshape(1, NH * DV))

    mask01 = np.ascontiguousarray(np.triu(np.ones((128, 128), np.float32)))
    identb = np.eye(128).astype(ml_dtypes.bfloat16)
    identf = np.eye(128, dtype=np.float32)
    tril16 = np.ascontiguousarray(
        np.tril(np.ones((4, 4), np.float32), -1).reshape(1, 16).repeat(128, 0))
    tril256 = np.ascontiguousarray(
        np.tril(np.ones((16, 16), np.float32), -1).reshape(1, 256).repeat(128, 0))

    in_maps = []
    for c in range(8):
        bidx, r = c // 4, c % 4
        xs = np.ascontiguousarray(x[bidx, r * TPC:(r + 1) * TPC, :].T)
        cT = np.ascontiguousarray(cc[bidx].reshape(NKC, 128).T)
        wms = np.ascontiguousarray(wmod[:, 3072 * r:3072 * (r + 1)])
        flag = np.zeros((8, 3), np.float32)
        flag[:, :r] = 1.0
        ccneg = (1.0 - flag) * NEG
        in_maps.append(dict(
            xT=xs, cTb=cT, wmod_sl=wms, wqT=wq, wkT=wk, wvT=wv, wogT=wog,
            wifT=wif, bifi=bigv, biff=bfgv, gamma=gm, woutT=wout, gateT=gate, upT=up,
            downT=down, supT=sup, sdownT=sdn, mask01=mask01, identb=identb,
            identf=identf, tril16=tril16, tril256=tril256,
            ccflag=flag, ccneg=ccneg))

    res = run_bass_kernel_spmd(nc, in_maps, core_ids=list(range(8)))
    out = np.empty((2, S, D), np.float32)
    for c in range(8):
        bidx, r = c // 4, c % 4
        out[bidx, r * TPC:(r + 1) * TPC, :] = res.results[c]["yT"].T
    return out


# version 12
# speedup vs baseline: 2.7891x; 1.2363x over previous
import os
import numpy as np
import ml_dtypes
from contextlib import ExitStack
import concourse.bass as bass
import concourse.mybir as mybir
from concourse.bass_utils import run_bass_kernel_spmd

F32 = mybir.dt.float32
BF16 = mybir.dt.bfloat16
AF = mybir.ActivationFunctionType
ALU = mybir.AluOpType
AX = mybir.AxisListType

D = 2048
NH = 8
DV = 256
NKC = 16
E = 16
NEG = -1e30
QSCALE = float(128) ** -0.5
NU = 128 * 2056


def build_nc(TPC):
    KPH = int(os.environ.get("KPH", "99"))
    NCH = TPC // 128
    NTB = TPC // 512
    nc = bass.Bass("TRN2", target_bir_lowering=False)

    def di(name, shape, dt=F32):
        return nc.dram_tensor(name, shape, dt, kind="ExternalInput").ap()

    xT = di("xT", [D, TPC])
    cTb = di("cTb", [128, NKC])
    wmod_sl = di("wmod_sl", [D, 3072])
    wqT = di("wqT", [128, NH, NKC * 128])
    wkT = di("wkT", [128, NH, NKC * 128])
    wvT = di("wvT", [128, NH, NKC * 256])
    wogT = di("wogT", [128, NKC, NKC * 128])
    wifT = di("wifT", [128, 1, NKC * 16])
    bifi = di("bifi", [8, 1])
    biff = di("biff", [8, 1])
    gamma = di("gamma", [1, NH * DV])
    woutT = di("woutT", [128, NKC, NKC * 128])
    gateT = di("gateT", [128, 1, NKC * 16])
    upT = di("upT", [E, 128, 22, NKC * 128])
    downT = di("downT", [E, 128, 16, 11 * 128])
    supT = di("supT", [128, 44, NKC * 128])
    sdownT = di("sdownT", [128, 16, 22 * 128])
    mask01 = di("mask01", [128, 128])
    identb = di("identb", [128, 128], BF16)
    identf = di("identf", [128, 128])
    tril16 = di("tril16", [128, 16])
    tril256 = di("tril256", [128, 256])
    ccflag = di("ccflag", [8, 3])
    ccneg = di("ccneg", [8, 3])
    yT = nc.dram_tensor("yT", [D, TPC], F32, kind="ExternalOutput").ap()

    ag1_in = nc.dram_tensor("ag1_in", [1, 3072], F32, kind="Internal").ap()
    ag1_out = nc.dram_tensor("ag1_out", [4, 3072], F32, kind="Internal").ap()
    ag2_in = nc.dram_tensor("ag2_in", [1, NU + 32], F32, kind="Internal").ap()
    ag2_out = nc.dram_tensor("ag2_out", [4, NU + 32], F32, kind="Internal").ap()

    ctx = ExitStack()
    ctx.__enter__()

    def sb(name, shape, dt=F32):
        return ctx.enter_context(nc.sbuf_tensor(name, shape, dt))

    h16 = sb("h16", [128, NCH * 2056], BF16)
    ogb = sb("ogb", [128, NKC * TPC], BF16)
    v1 = sb("v1", [128, NCH * 2056], BF16)
    bqk = sb("bqk", [128, NKC * TPC], BF16)
    wbuf = sb("wbuf", [128, 9728], BF16)
    wsm = sb("wsm", [128, 512], BF16)
    cst = sb("cst", [128, 2056])
    c16 = sb("c16", [128, 2056], BF16)
    lanA = sb("lanA", [128, TPC])
    lanB = sb("lanB", [128, TPC])
    e32t = sb("e32t", [128, NCH * 32])
    e24t = sb("e24t", [128, NCH * 24])
    e24b = sb("e24b", [128, NCH * 24], BF16)
    modv = sb("modv", [128, 6 * 16])
    gab = sb("gab", [128, NH * DV], BF16)
    mkb = sb("mkb", [128, 128])
    idb = sb("idb", [128, 128], BF16)
    idf = sb("idf", [128, 128])
    t16c = sb("t16c", [128, 16])
    t256c = sb("t256c", [128, 256])
    onesf = sb("onesf", [1, 128])
    onesbf = sb("onesbf", [1, 128], BF16)
    onescb = sb("onescb", [128, 1], BF16)
    scT = sb("scT", [128, NKC], BF16)
    bifib = sb("bifib", [8, 1])
    biffb = sb("biffb", [8, 1])
    bifis = sb("bifis", [8, 1])
    biffs = sb("biffs", [8, 1])
    gmr = sb("gmr", [1, NH * DV])
    ccf = sb("ccf", [8, 3])
    ccn = sb("ccn", [8, 3])
    wrowp = sb("wrowp", [1, 8])
    wrow1 = sb("wrow1", [1, 8])
    wbcs = sb("wbcs", [128, 16])
    scrA = sb("scrA", [128, 1024])
    a3s = sb("a3s", [128, 2 * 128], BF16)
    k1s = sb("k1s", [128, 2 * 128], BF16)
    t1s = sb("t1s", [128, 2056], BF16)
    xsq = t1s
    sq8 = sb("sq8", [128, 48])
    sgil = sb("sgil", [128, 2 * 512], BF16)
    tmul = sb("tmul", [128, 2 * 512], BF16)
    x2f = sb("x2f", [128, 2 * 512])
    xre = sb("xre", [128, 2 * 512])
    scrC = xre
    lanC = xre
    cwl16 = gab[0:16, 0:TPC]
    cwrow = scrA[0:1, :].bitcast(BF16)
    cwbc = gab[:, TPC:2 * TPC] if TPC <= NH * DV // 2 else gab[:, 0:TPC]
    g16a = sb("g16a", [128, 16])
    g16b = sb("g16b", [128, 16])
    g16d = sb("g16d", [128, 16])
    g4a = sb("g4a", [128, 4])
    g256 = e32t[:, 0:256]
    g256b = t1s[:, 0:512].bitcast(F32)
    gc1 = sb("gc1", [128, 1])
    gc2 = sb("gc2", [128, 1])

    hob = scrA[:].bitcast(BF16)
    uvw = v1[:].bitcast(F32)

    def psm(name):
        return ctx.enter_context(nc.psum_tensor(name, [128, 512], F32))

    pA, pB, pC, pD, pE, pF, pG = (psm(n) for n in "pA pB pC pD pE pF pG".split())
    PSF = [pA, pB, pC, pD, pE, pF, pG]
    pT = ctx.enter_context(nc.psum_tensor("pT", [128, 1024], BF16))

    SEM, C = {}, {}
    for s in ["qsc", "qxb", "qw0", "qw1", "qw2", "qw3", "qc", "qio",
              "qxr0", "qxr1", "qst0", "qst1", "qac0", "qac1",
              "ste", "sve", "sse", "scc"]:
        SEM[s] = ctx.enter_context(nc.semaphore(s))
        C[s] = 0
    GSEM = ctx.enter_context(nc.semaphore("gsem"))
    G = [0]
    R = {s: 0 for s in SEM}
    FGM = [False]
    ENGS = ["sync", "gpsimd", "vector", "scalar", "tensor"]
    EV = []
    EVG = []
    PHM = {}

    def PH(n):
        if n not in PHM:
            PHM[n] = len(EV)

    def on(eng, fn):
        EV.append((eng, fn))
        EVG.append(G[0])

    def chain(eng):
        v = G[0]
        if v:
            on(eng, lambda e, v=v: e.wait_ge(GSEM, v))

    def dma(eng, out, in_, sem, accum=False, slow=False):
        kw = {}
        if accum:
            kw["accum_op"] = ALU.add
        if slow:
            kw["allow_slow_non_contiguous"] = True
        C[sem] += 16
        if FGM[0]:
            R[sem] += 16
            on(eng, lambda e, o=out, i=in_, S=SEM[sem], kw=kw:
               e.dma_start(out=o, in_=i, **kw).then_inc(S, 16))
        else:
            chain(eng)
            G[0] += 16
            on(eng, lambda e, o=out, i=in_, kw=kw:
               e.dma_start(out=o, in_=i, **kw).then_inc(GSEM, 16))
        return C[sem]

    def wait(eng, sem, val):
        if FGM[0] and val:
            on(eng, lambda e, S=SEM[sem], v=val: e.wait_ge(S, v))

    def op(eng, fn, sem=None):
        if FGM[0]:
            if sem is None:
                on(eng, fn)
                return None
            C[sem] += 1
            R[sem] += 1
            on(eng, lambda e, f=fn, S=SEM[sem]: f(e).then_inc(S, 1))
            return C[sem]
        chain(eng)
        G[0] += 1
        if eng == "vector":
            on(eng, fn)
            on(eng, lambda e: e.drain())
            on(eng, lambda e: e.sem_inc(GSEM, 1))
        else:
            on(eng, lambda e, f=fn: f(e).then_inc(GSEM, 1))
        if sem is None:
            return None
        C[sem] += 1
        return C[sem]

    def drain(eng):
        on(eng, lambda e: e.drain())

    def opd(eng, fn, sem):
        C[sem] += 1
        if FGM[0]:
            R[sem] += 1
            on(eng, fn)
            on(eng, lambda e: e.drain())
            on(eng, lambda e, S=SEM[sem]: e.sem_inc(S, 1))
            return C[sem]
        chain(eng)
        G[0] += 1
        on(eng, fn)
        on(eng, lambda e: e.drain())
        on(eng, lambda e: e.sem_inc(GSEM, 1))
        return C[sem]

    def barrier():
        for engn in ENGS:
            if G[0]:
                on(engn, lambda e, v=G[0]: e.wait_ge(GSEM, v))
            for s, rv in R.items():
                if rv:
                    on(engn, lambda e, S=SEM[s], v=rv: e.wait_ge(S, v))

    def fg_on():
        barrier()
        for s in SEM:
            delta = C[s] - R[s]
            if delta:
                on("sync", lambda e, S=SEM[s], d=delta: e.sem_inc(S, d))
                R[s] += delta
        FGM[0] = True

    def fg_off():
        FGM[0] = False
        barrier()


    ld0 = 0
    for out, src in [(mkb[:], mask01), (idb[:], identb), (idf[:], identf),
                     (gmr[:], gamma), (bifib[:], bifi), (biffb[:], biff), (ccf[:], ccflag),
                     (ccn[:], ccneg), (t16c[:], tril16), (t256c[:], tril256)]:
        ld0 = dma("sync", out, src, "qc")
    csem = dma("gpsimd", scT[:], cTb, "qsc")
    xbsem = dma("gpsimd", ogb[:, 0:NKC * TPC].rearrange("p (g t) -> p g t", g=NKC),
                xT.rearrange("(g p) t -> g p t", p=128).transpose([1, 0, 2]), "qxb")
    op("vector", lambda e: e.memset(onesf[:], 1.0))
    op("vector", lambda e: e.memset(onesbf[:], 1.0))
    op("vector", lambda e: e.memset(onescb[:], 1.0))
    op("vector", lambda e: e.memset(lanB[96:104, :], 0.0))
    op("vector", lambda e: e.memset(cst[:], 0.0))
    op("vector", lambda e: e.memset(c16[:], 0.0))
    vini = op("vector", lambda e: e.memset(v1[:], 1.0), "sve")
    wait("vector", "qc", ld0)
    op("vector", lambda e: e.tensor_scalar(bifis[:], bifib[:], 1.0 / 15.0, None, ALU.mult))
    vbif = op("vector", lambda e: e.tensor_scalar(biffs[:], biffb[:], 1.0 / 15.0, None, ALU.mult), "sve")
    gv = 0
    wait("tensor", "qc", ld0)
    wait("tensor", "sve", vini)
    for h in range(NH):
        p = PSF[h % 2]
        if h >= 2:
            wait("tensor", "sve", gv - 1)
        tv = op("tensor", lambda e, h=h, p=p: e.matmul(
            p[:, 0:256], onesf[0:1, :], gmr[0:1, h * 256:h * 256 + 256],
            start=True, stop=True), "ste")
        wait("vector", "ste", tv)
        gv = op("vector", lambda e, h=h, p=p: e.tensor_copy(
            gab[:, h * 256:h * 256 + 256], p[:, 0:256]), "sve")

    PH(1)
    wait("scalar", "qsc", csem)
    scs = op("scalar", lambda e: e.activation(scT[:], scT[:], AF.Silu), "sse")
    mod_last = [0] * NKC
    for kk in range(NKC):
        so = (kk % 2) * 3072
        if kk >= 2:
            wait("gpsimd", "ste", mod_last[kk - 2])
        wl = dma("gpsimd", wbuf[:, so:so + 3072],
                 wmod_sl[kk * 128:kk * 128 + 128, :], "qw0" if kk % 2 == 0 else "qw1")
        wait("tensor", "qw0" if kk % 2 == 0 else "qw1", wl)
        if kk == 0:
            wait("tensor", "sse", scs)
            wait("tensor", "sve", gv)
        mv = 0
        for nb in range(6):
            mv = op("tensor", lambda e, kk=kk, nb=nb, so=so: e.matmul(
                PSF[nb][0:1, :], scT[:, kk:kk + 1],
                wbuf[:, so + nb * 512: so + nb * 512 + 512],
                start=(kk == 0), stop=(kk == NKC - 1)), "ste")
        mod_last[kk] = mv
    wait("vector", "ste", mod_last[NKC - 1])
    mev = 0
    for nb in range(6):
        r = (nb // 2) * 32
        co = (nb % 2) * 512
        mev = op("vector", lambda e, nb=nb, r=r, co=co: e.tensor_copy(
            scrA[r:r + 1, co:co + 512], PSF[nb][0:1, :]), "sve")
    wait("sync", "sve", mev)
    agi = dma("sync", ag1_in[0, :].rearrange("(a b) -> a b", a=3),
              scrA[0:65:32, 0:1024], "qio")
    wait("gpsimd", "qio", agi)
    cc1 = op("gpsimd", lambda e: e.collective_compute(
        "AllGather", ALU.bypass, ins=[ag1_in[:]], outs=[ag1_out[:]],
        replica_groups=[[0, 1, 2, 3], [4, 5, 6, 7]]), "scc")
    wait("sync", "scc", cc1)
    ag1f = ag1_out.rearrange("a b -> (a b)")
    mvl = 0
    for v6 in range(6):
        mvl = dma("sync", modv[:, v6 * 16:v6 * 16 + 16],
                  ag1f[2048 * v6:2048 * v6 + 2048].rearrange("(c p) -> p c", p=128), "qio",
                  slow=True)
    wait("vector", "qio", mvl)
    op("vector", lambda e: e.tensor_scalar(modv[:, 16:32], modv[:, 16:32], 1.0, None, ALU.add))
    op("vector", lambda e: e.tensor_scalar(modv[:, 64:80], modv[:, 64:80], 1.0, None, ALU.add))
    mgv = 0
    for blk in (2, 5):
        sl = modv[:, blk * 16:blk * 16 + 16]
        op("vector", lambda e, sl=sl: e.tensor_scalar(g16a[:], sl, -1.0, None, ALU.mult))
        op("vector", lambda e, sl=sl: e.tensor_scalar(g16b[:], sl, 0.0, None, ALU.is_ge))
        drain("vector")
        op("vector", lambda e, sl=sl: e.tensor_max(g16d[:], sl, g16a[:]))
        drain("vector")
        op("vector", lambda e: e.tensor_scalar(g16a[:], g16d[:], 0.25, None, ALU.max))
        drain("vector")
        op("vector", lambda e: e.tensor_scalar(g16b[:], g16b[:], 2.0, -1.0, ALU.mult, ALU.add))
        drain("vector")
        mgv = op("vector", lambda e, sl=sl: e.tensor_mul(sl, g16a[:], g16b[:]), "sve")

    def layer_norm(src16, dst16, cacol, sacol, rsem, rval):
        wait("scalar", rsem, rval)
        wait("tensor", rsem, rval)
        wait("tensor", "sve", C["sve"])
        mm_last = [0] * NKC
        for kk in range(NKC):
            so = (kk % 2) * TPC
            if kk >= 2:
                wait("scalar", "ste", mm_last[kk - 2])
            sqd = op("scalar", lambda e, kk=kk, so=so: e.activation(
                xsq[:, so:so + TPC], src16[:, kk * TPC:kk * TPC + TPC], AF.Square), "sse")
            for tb in range(NTB):
                op("tensor", lambda e, kk=kk, tb=tb: e.matmul(
                    [pA, pB][tb][0:1, :], onescb[:, 0:1],
                    src16[:, kk * TPC + tb * 512: kk * TPC + tb * 512 + 512],
                    start=(kk == 0), stop=(kk == NKC - 1)), "ste")
            wait("tensor", "sse", sqd)
            mm2 = 0
            for tb in range(NTB):
                mm2 = op("tensor", lambda e, kk=kk, tb=tb, so=so: e.matmul(
                    [pC, pD][tb][0:1, :], onescb[:, 0:1],
                    xsq[:, so + tb * 512: so + tb * 512 + 512],
                    start=(kk == 0), stop=(kk == NKC - 1)), "ste")
            mm_last[kk] = mm2
        PH(21)
        wait("vector", "ste", mm_last[NKC - 1])
        for tb in range(NTB):
            op("vector", lambda e, tb=tb: e.tensor_scalar(
                lanB[0:1, tb * 512:tb * 512 + 512], [pA, pB][tb][0:1, :],
                1.0 / D, None, ALU.mult))
            op("vector", lambda e, tb=tb: e.tensor_scalar(
                lanA[0:1, tb * 512:tb * 512 + 512], [pC, pD][tb][0:1, :],
                1.0 / D, None, ALU.mult))
        drain("vector")
        op("vector", lambda e: e.tensor_mul(lanC[0:1, 0:TPC],
                                            lanB[0:1, 0:TPC], lanB[0:1, 0:TPC]))
        drain("vector")
        op("vector", lambda e: e.tensor_sub(lanB[32:33, 0:TPC],
                                            lanA[0:1, 0:TPC], lanC[0:1, 0:TPC]))
        drain("vector")
        op("vector", lambda e: e.tensor_scalar(lanB[32:33, 0:TPC],
                                               lanB[32:33, 0:TPC], 1e-5, None, ALU.add))
        drain("vector")
        rvv = opd("vector", lambda e: e.reciprocal(lanA[32:33, 0:TPC],
                                                   lanB[32:33, 0:TPC]), "sve")
        wait("scalar", "sve", rvv)
        rsv = op("scalar", lambda e: e.activation(lanA[0:1, 0:TPC],
                                                  lanA[32:33, 0:TPC], AF.Sqrt), "sse")
        op("vector", lambda e: e.tensor_scalar(lanC[0:1, 0:TPC],
                                               lanB[0:1, 0:TPC], -1.0, None, ALU.mult))
        wait("vector", "sse", rsv)
        drain("vector")
        nmv = op("vector", lambda e: e.tensor_mul(lanB[0:1, 0:TPC],
                                                  lanC[0:1, 0:TPC],
                                                  lanA[0:1, 0:TPC]), "sve")
        PH(22)
        wait("tensor", "sve", nmv)
        bc1 = 0
        for tb in range(NTB):
            bc1 = op("tensor", lambda e, tb=tb: e.matmul(
                [pA, pB][tb][:, :], onesf[0:1, :],
                lanA[0:1, tb * 512:tb * 512 + 512], start=True, stop=True), "ste")
            bc1 = op("tensor", lambda e, tb=tb: e.matmul(
                [pC, pD][tb][:, :], onesf[0:1, :],
                lanB[0:1, tb * 512:tb * 512 + 512], start=True, stop=True), "ste")
        wait("vector", "ste", bc1)
        PH(23)
        outv = 0
        for kk in range(NKC):
            for tb in range(NTB):
                so = tb * 512
                op("vector", lambda e, kk=kk, tb=tb, so=so: e.tensor_tensor(
                    scrC[:, so:so + 512],
                    src16[:, kk * TPC + so: kk * TPC + so + 512], [pA, pB][tb][:, :], ALU.mult))
            for tb in range(NTB):
                so = tb * 512
                op("vector", lambda e, kk=kk, tb=tb, so=so: e.tensor_tensor(
                    scrC[:, so:so + 512], scrC[:, so:so + 512], [pC, pD][tb][:, :], ALU.add))
            drain("vector")
            outv = op("vector", lambda e, kk=kk: e.tensor_scalar(
                dst16[:, kk * TPC:kk * TPC + TPC], scrC[:, 0:TPC],
                cacol[:, kk:kk + 1], sacol[:, kk:kk + 1], ALU.mult, ALU.add), "sve")
            drain("vector")
        return outv

    PH(2)
    wait("vector", "sve", mgv)
    h1v = layer_norm(ogb, h16, modv[:, 16:32], modv[:, 0:16], "qxb", xbsem)

    def h1sl(kk, tb, width=512):
        return h16[:, kk * TPC + tb * 512: kk * TPC + tb * 512 + width]

    PH(3)
    fg_on()
    blocks = [("q", h, wqT[:, h, :]) for h in range(NH)]
    blocks += [("k", h, wkT[:, h, :]) for h in range(NH)]
    blocks += [("og", ob, wogT[:, ob, :]) for ob in range(NKC)]
    ring = [None] * 4
    evpar = [0, 0]
    for bi, (kind, idx, wsl) in enumerate(blocks):
        so = (bi % 4) * 2048
        if ring[bi % 4] is not None:
            wait("gpsimd", "ste", ring[bi % 4])
        lv = dma("gpsimd", wbuf[:, so:so + 2048], wsl, "qw%d" % (bi % 4))
        wait("tensor", "qw%d" % (bi % 4), lv)
        if bi == 0:
            wait("tensor", "sve", h1v)
        if bi >= 1:
            wait("tensor", "sve" if kind != "og" or blocks[bi - 1][0] != "og" else "sse",
                 evpar[0] if False else None)
        mmv = 0
        for tb in range(NTB):
            p = [pE, pF][tb]
            if bi >= 1:
                wait("tensor", evgrd[tb][0], evgrd[tb][1])
            for kk in range(NKC):
                mmv = op("tensor", lambda e, so=so, kk=kk, tb=tb, p=p: e.matmul(
                    p[:, :], wbuf[:, so + kk * 128: so + kk * 128 + 128],
                    h1sl(kk, tb), start=(kk == 0), stop=(kk == NKC - 1)), "ste")
        ring[bi % 4] = mmv
        wait("vector" if kind != "og" else "scalar", "ste", mmv)
        evgrd = {}
        for tb in range(NTB):
            p = [pE, pF][tb]
            if kind == "q":
                ev = op("vector", lambda e, idx=idx, tb=tb, p=p: e.tensor_scalar(
                    bqk[:, idx * TPC + tb * 512: idx * TPC + tb * 512 + 512],
                    p[:, :], QSCALE, None, ALU.mult), "sve")
                evgrd[tb] = ("sve", ev)
            elif kind == "k":
                ev = op("vector", lambda e, idx=idx, tb=tb, p=p: e.tensor_copy(
                    bqk[:, (NH + idx) * TPC + tb * 512: (NH + idx) * TPC + tb * 512 + 512],
                    p[:, :]), "sve")
                evgrd[tb] = ("sve", ev)
            else:
                ev = op("scalar", lambda e, idx=idx, tb=tb, p=p: e.activation(
                    ogb[:, idx * TPC + tb * 512: idx * TPC + tb * 512 + 512],
                    p[:, :], AF.Sigmoid), "sse")
                evgrd[tb] = ("sse", ev)
    og_done = evgrd[NTB - 1][1]

    PH(35)
    wifl = dma("gpsimd", wsm[:, 0:256], wifT[:, 0, :], "qw3")
    wait("tensor", "qw3", wifl)
    ifm = 0
    for tb in range(NTB):
        for half in range(2):
            p = ([pA, pB] if half == 0 else [pC, pD])[tb]
            for kk in range(NKC):
                ifm = op("tensor", lambda e, kk=kk, tb=tb, p=p, half=half: e.matmul(
                    p[0:8, :], wsm[:, kk * 16 + half * 8: kk * 16 + half * 8 + 8],
                    h1sl(kk, tb), start=(kk == 0), stop=(kk == NKC - 1)), "ste")
    wait("scalar", "ste", ifm)
    wait("scalar", "sve", vbif)
    for tb in range(NTB):
        op("scalar", lambda e, tb=tb: e.activation(
            lanC[64:72, tb * 512:tb * 512 + 512], [pA, pB][tb][0:8, :],
            AF.Tanh, bias=bifis[:, 0:1], scale=1.0 / 15.0))
        op("scalar", lambda e, tb=tb: e.activation(
            lanC[32:40, tb * 512:tb * 512 + 512], [pC, pD][tb][0:8, :],
            AF.Tanh, bias=biffs[:, 0:1], scale=1.0 / 15.0))
    tg = op("scalar", lambda e: e.activation(gc1[0:8, 0:1], bifis[:, 0:1], AF.Square), "sse")
    wait("vector", "sse", tg)
    op("vector", lambda e: e.tensor_scalar(
        lanC[64:72, 0:TPC], lanC[64:72, 0:TPC], 15.0, None, ALU.mult))
    ifv = op("vector", lambda e: e.tensor_scalar(
        lanC[32:40, 0:TPC], lanC[32:40, 0:TPC], 15.0, None, ALU.mult), "sve")
    drain("vector")

    PH(36)
    wait("vector", "sve", vini)
    wait("gpsimd", "ste", C["ste"])
    vld = [0] * NH
    vld[0] = dma("gpsimd", wbuf[:, 0:4096], wvT[:, 0, :], "qw0")
    vld[1] = dma("gpsimd", wbuf[:, 4096:8192], wvT[:, 1, :], "qw1")
    vev = {0: 0, 1: 0}
    for h in range(NH):
        so = (h % 2) * 4096
        wait("tensor", "qw%d" % (h % 2), vld[h])
        lastmm = 0
        for tt in range(NCH):
            p = [pC, pD][tt % 2]
            if tt >= 2 or h > 0:
                wait("tensor", "sve", vev[tt % 2])
            mmv = 0
            for kk in range(NKC):
                mmv = op("tensor", lambda e, h=h, tt=tt, kk=kk, so=so, p=p: e.matmul(
                    p[:, 0:256], h16[:, kk * TPC + tt * 128: kk * TPC + tt * 128 + 128],
                    wbuf[:, so + kk * 256: so + kk * 256 + 256],
                    start=(kk == 0), stop=(kk == NKC - 1)), "ste")
            wait("vector", "ste", mmv)
            vev[tt % 2] = op("vector", lambda e, h=h, tt=tt, p=p: e.tensor_copy(
                v1[:, tt * 2056 + 257 * h: tt * 2056 + 257 * h + 256], p[:, 0:256]), "sve")
            lastmm = mmv
        if h + 2 < NH:
            wait("gpsimd", "ste", lastmm)
            vld[h + 2] = dma("gpsimd", wbuf[:, so:so + 4096], wvT[:, h + 2, :], "qw%d" % (h % 2))
    v_done = vev[(NCH - 1) % 2]

    PH(4)
    fg_off()
    wait("scalar", "sve", ifv)
    op("scalar", lambda e: e.activation(
        lanC[96:104, 0:TPC], lanC[32:40, 0:TPC], AF.Sigmoid))
    drain("scalar")
    flv = op("scalar", lambda e: e.activation(
        lanA[64:72, 0:TPC], lanC[96:104, 0:TPC], AF.Ln), "sse")
    wait("vector", "sse", flv)
    op("vector", lambda e: e.memset(lanB[64:72, 0:TPC], 0.0))
    drain("vector")
    op("vector", lambda e: e.tensor_tensor_scan(
        lanB[96:104, 0:TPC], lanA[64:72, 0:TPC],
        lanB[64:72, 0:TPC], 0.0, ALU.add, ALU.add))
    drain("vector")
    op("vector", lambda e: e.tensor_tensor_scan(
        lanB[64:72, 0:TPC], lanA[64:72, 0:TPC],
        lanC[64:72, 0:TPC], NEG, ALU.add, ALU.max))
    drain("vector")
    op("vector", lambda e: e.tensor_copy(lanA[64:72, 0:TPC], lanB[96:104, 0:TPC]))
    drain("vector")
    op("vector", lambda e: e.tensor_sub(lanB[32:40, 0:TPC],
                                        lanC[64:72, 0:TPC], lanA[64:72, 0:TPC]))
    op("vector", lambda e: e.tensor_sub(lanA[32:40, 0:TPC],
                                        lanA[64:72, 0:TPC], lanB[64:72, 0:TPC]))
    drain("vector")
    e32p = 0
    for cc_ in range(NCH):
        sl = slice(cc_ * 128, cc_ * 128 + 128)
        bc0 = lanA[64:72, cc_ * 128 - 1:cc_ * 128] if cc_ > 0 else None
        bce = lanA[64:72, cc_ * 128 + 127:cc_ * 128 + 128]
        if bc0 is not None:
            op("vector", lambda e, sl=sl, bc0=bc0: e.tensor_scalar(
                lanA[0:8, sl], lanB[32:40, sl], bc0, None, ALU.add))
            op("vector", lambda e, sl=sl, bc0=bc0: e.tensor_scalar(
                lanC[0:8, sl], lanA[32:40, sl], bc0, None, ALU.subtract))
            op("vector", lambda e, cc_=cc_, bce=bce, bc0=bc0: e.tensor_scalar(
                gc2[0:8, 0:1], bce, bc0, None, ALU.subtract))
        else:
            op("vector", lambda e, sl=sl: e.tensor_copy(lanA[0:8, sl], lanB[32:40, sl]))
            op("vector", lambda e, sl=sl: e.tensor_copy(lanC[0:8, sl], lanA[32:40, sl]))
            op("vector", lambda e, bce=bce: e.tensor_copy(gc2[0:8, 0:1], bce))
        op("vector", lambda e, sl=sl, bce=bce: e.tensor_scalar(
            lanB[0:8, sl], lanB[32:40, sl], bce, None, ALU.add))
        drain("vector")
        e32p = op("vector", lambda e, sl=sl: e.tensor_scalar(
            lanC[32:40, sl], lanB[32:40, sl], 0.0, gc2[0:8, 0:1], ALU.mult, ALU.add), "sve")
        drain("vector")
    wait("scalar", "sve", e32p)
    for seg, base in [(lanA, 0), (lanB, 0), (lanC, 0)]:
        op("scalar", lambda e, seg=seg, base=base: e.activation(
            seg[base:base + 8, 0:TPC], seg[base:base + 8, 0:TPC], AF.Exp))
        drain("scalar")
    e32e = op("scalar", lambda e: e.activation(
        lanC[32:40, 0:TPC], lanC[32:40, 0:TPC], AF.Exp), "sse")
    wait("tensor", "sse", e32e)
    etv = 0
    esrc = [(lanA, 0), (lanB, 0), (lanC, 0), (lanC, 32)]
    for cc_ in range(NCH):
        for qi, (seg, base) in enumerate(esrc):
            if cc_ + qi >= 1:
                wait("tensor", "sve", etv)
            tv = op("tensor", lambda e, cc_=cc_, seg=seg, base=base: e.transpose(
                pG[0:128, 0:8], seg[base:base + 8, cc_ * 128:cc_ * 128 + 128],
                idf[base:base + 8, base:base + 8]), "ste")
            wait("vector", "ste", tv)
            etv = op("vector", lambda e, cc_=cc_, qi=qi: e.tensor_copy(
                e32t[:, cc_ * 32 + qi * 8:cc_ * 32 + qi * 8 + 8], pG[:, 0:8]), "sve")

    PH(5)
    wait("tensor", "sve", C["sve"])
    lastC16 = [C["sve"]] * NH
    gA3 = {0: 0, 1: 0}
    gK1 = {0: 0, 1: 0}
    gHq = {0: 0, 1: 0}
    gCb = {0: 0, 1: 0}
    for cc_ in range(NCH):
        for h in range(NH):
            par = (cc_ * NH + h) % 2
            pS = [pA, pB][par]
            pH = [pC, pD][par]
            pCa = [pE, pF][par]
            qsl = bqk[:, h * TPC + cc_ * 128: h * TPC + cc_ * 128 + 128]
            ksl = bqk[:, (NH + h) * TPC + cc_ * 128: (NH + h) * TPC + cc_ * 128 + 128]
            vsl = v1[:, cc_ * 2056 + 257 * h: cc_ * 2056 + 257 * h + 257]
            colfac = e32t[:, cc_ * 32 + h: cc_ * 32 + h + 1]
            kfac = e32t[:, cc_ * 32 + 8 + h: cc_ * 32 + 8 + h + 1]
            rowsc = e32t[:, cc_ * 32 + 16 + h: cc_ * 32 + 16 + h + 1]
            dcy = e32t[:, cc_ * 32 + 24 + h: cc_ * 32 + 24 + h + 1]
            wait("tensor", "sve", gA3[par])
            smm = op("tensor", lambda e, ksl=ksl, qsl=qsl, pS=pS: e.matmul(
                pS[:, 0:128], ksl, qsl, start=True, stop=True), "ste")
            wait("tensor", "sve", gK1[par])
            tmm = op("tensor", lambda e, ksl=ksl, par=par: e.transpose(
                pT[:, par * 512:par * 512 + 128], ksl, idb[:, :]), "ste")
            wait("vector", "ste", tmm)
            gA3[par] = op("vector", lambda e, pS=pS, colfac=colfac, par=par: e.scalar_tensor_tensor(
                a3s[:, par * 128:par * 128 + 128], pS[:, 0:128], colfac, mkb[:, :],
                ALU.mult, ALU.mult), "sve")
            gK1[par] = op("vector", lambda e, par=par, kfac=kfac: e.tensor_scalar(
                k1s[:, par * 128:par * 128 + 128], pT[:, par * 512:par * 512 + 128],
                kfac, None, ALU.mult), "sve")
            wait("tensor", "sve", gA3[par])
            wait("tensor", "sve", gHq[par])
            op("tensor", lambda e, par=par, vsl=vsl, pH=pH: e.matmul(
                pH[:, 0:257], a3s[:, par * 128:par * 128 + 128], vsl,
                start=True, stop=False), "ste")
            wait("tensor", "sve", lastC16[h])
            hmm = op("tensor", lambda e, qsl=qsl, h=h, pH=pH: e.matmul(
                pH[:, 0:257], qsl, c16[:, 257 * h:257 * h + 257],
                start=False, stop=True), "ste")
            wait("tensor", "sve", gCb[par])
            cmm = op("tensor", lambda e, par=par, vsl=vsl, pCa=pCa: e.matmul(
                pCa[:, 0:257], k1s[:, par * 128:par * 128 + 128], vsl,
                start=True, stop=True), "ste")
            wait("vector", "ste", hmm)
            gHq[par] = op("vector", lambda e, cc_=cc_, h=h, pH=pH, rowsc=rowsc: e.tensor_scalar(
                h16[:, cc_ * 2056 + 257 * h: cc_ * 2056 + 257 * h + 257],
                pH[:, 0:257], rowsc, None, ALU.mult), "sve")
            wait("vector", "ste", cmm)
            gCb[par] = op("vector", lambda e, h=h, dcy=dcy, pCa=pCa: e.scalar_tensor_tensor(
                cst[:, 257 * h:257 * h + 257], cst[:, 257 * h:257 * h + 257],
                dcy, pCa[:, 0:257], ALU.mult, ALU.add), "sve")
            drain("vector")
            lastC16[h] = op("vector", lambda e, h=h: e.tensor_copy(
                c16[:, 257 * h:257 * h + 257], cst[:, 257 * h:257 * h + 257]), "sve")
    passa_v = C["sve"]

    PH(6)
    wait("sync", "sve", passa_v)
    dma("sync", ag2_in[0, 0:NU].rearrange("(p f) -> p f", p=128), cst[:], "qio")
    dma("sync", ag2_in[0, NU:NU + 8].rearrange("(p f) -> p f", p=8),
        lanA[64:72, TPC - 1:TPC], "qio")
    ag2v = dma("sync", ag2_in[0, NU + 8:NU + 16].rearrange("(p f) -> p f", p=8),
               lanB[64:72, TPC - 1:TPC], "qio")
    wait("gpsimd", "qio", ag2v)
    cc2 = op("gpsimd", lambda e: e.collective_compute(
        "AllGather", ALU.bypass, ins=[ag2_in[:]], outs=[ag2_out[:]],
        replica_groups=[[0, 1, 2, 3], [4, 5, 6, 7]]), "scc")
    wait("sync", "scc", cc2)
    uld = 0
    for j in range(3):
        uld = dma("sync", uvw[:, j * 2056:(j + 1) * 2056],
                  ag2_out[j, 0:NU].rearrange("(p f) -> p f", p=128), "qio")
        uld = dma("sync", lanC[0:8, 2 * j:2 * j + 1],
                  ag2_out[j, NU:NU + 8].rearrange("(p f) -> p f", p=8), "qio")
        uld = dma("sync", lanC[0:8, 2 * j + 1:2 * j + 2],
                  ag2_out[j, NU + 8:NU + 16].rearrange("(p f) -> p f", p=8), "qio")

    PH(7)
    op("vector", lambda e: e.memset(cst[:], 0.0))
    op("vector", lambda e: e.memset(lanC[0:8, 6:7], 0.0))
    wait("vector", "qio", uld)
    for j in range(3):
        op("vector", lambda e, j=j: e.tensor_mul(lanC[0:8, 9:10], lanC[0:8, 2 * j + 1:2 * j + 2], ccf[:, j:j + 1]))
        op("vector", lambda e, j=j: e.tensor_mul(lanC[0:8, 10:11], lanC[0:8, 2 * j:2 * j + 1], ccf[:, j:j + 1]))
        drain("vector")
        op("vector", lambda e, j=j: e.tensor_add(lanC[0:8, 9:10], lanC[0:8, 9:10], ccn[:, j:j + 1]))
        op("vector", lambda e: e.tensor_add(lanC[0:8, 11:12], lanC[0:8, 10:11], lanC[0:8, 6:7]))
        drain("vector")
        op("vector", lambda e: e.tensor_max(lanC[0:8, 12:13], lanC[0:8, 11:12], lanC[0:8, 9:10]))
        drain("vector")
        op("vector", lambda e: e.tensor_sub(lanC[0:8, 13:14], lanC[0:8, 11:12], lanC[0:8, 12:13]))
        op("vector", lambda e: e.tensor_scalar(lanC[0:8, 14:15], lanC[0:8, 12:13], -1.0, None, ALU.mult))
        drain("vector")
        wex = op("vector", lambda e: e.tensor_copy(lanC[0:8, 6:7], lanC[0:8, 12:13]), "sve")
        wait("scalar", "sve", wex)
        op("scalar", lambda e: e.activation(lanC[0:8, 13:14], lanC[0:8, 13:14], AF.Exp))
        drain("scalar")
        sxv = op("scalar", lambda e: e.activation(lanC[0:8, 14:15], lanC[0:8, 14:15], AF.Exp), "sse")
        wait("vector", "sse", sxv)
        wnv = op("vector", lambda e, j=j: e.tensor_mul(lanC[0:8, 15:16], lanC[0:8, 14:15], ccf[:, j:j + 1]), "sve")
        wait("tensor", "sve", wnv)
        op("tensor", lambda e: e.transpose(
            pG[0:1, 0:8], lanC[0:8, 13:14], idf[0:8, 0:8]), "ste")
        t1v = op("tensor", lambda e: e.transpose(
            pG[0:1, 8:16], lanC[0:8, 15:16], idf[0:8, 0:8]), "ste")
        wait("vector", "ste", t1v)
        op("vector", lambda e: e.tensor_copy(wrowp[:, :], pG[0:1, 0:8]))
        t2v = op("vector", lambda e: e.tensor_copy(wrow1[:, :], pG[0:1, 8:16]), "sve")
        wait("tensor", "sve", t2v)
        b2 = op("tensor", lambda e: e.matmul(pG[:, 0:8], onesf[0:1, :], wrowp[0:1, :],
                                             start=True, stop=True), "ste")
        b2 = op("tensor", lambda e: e.matmul(pG[:, 8:16], onesf[0:1, :], wrow1[0:1, :],
                                             start=True, stop=True), "ste")
        wait("vector", "ste", b2)
        op("vector", lambda e: e.tensor_copy(wbcs[:, :], pG[:, 0:16]))
        drain("vector")
        for h in range(NH):
            op("vector", lambda e, j=j, h=h: e.tensor_scalar(
                scrA[:, 0:257],
                uvw[:, j * 2056 + 257 * h: j * 2056 + 257 * h + 257],
                wbcs[:, 8 + h:8 + h + 1], None, ALU.mult))
            drain("vector")
            op("vector", lambda e, h=h: e.scalar_tensor_tensor(
                cst[:, 257 * h:257 * h + 257], cst[:, 257 * h:257 * h + 257],
                wbcs[:, h:h + 1], scrA[:, 0:257], ALU.mult, ALU.add))
            drain("vector")
    p16v = op("vector", lambda e: e.tensor_copy(c16[:], cst[:]), "sve")

    PH(8)
    op("vector", lambda e: e.tensor_scalar(
        lanC[64:72, 0:TPC], lanA[64:72, 0:TPC], lanC[0:8, 6:7], None, ALU.add))
    drain("vector")
    op("vector", lambda e: e.tensor_max(
        lanA[64:72, 0:TPC], lanB[64:72, 0:TPC], lanC[64:72, 0:TPC]))
    drain("vector")
    op("vector", lambda e: e.tensor_sub(
        lanA[0:8, 0:TPC], lanB[64:72, 0:TPC], lanA[64:72, 0:TPC]))
    op("vector", lambda e: e.tensor_sub(
        lanB[0:8, 0:TPC], lanC[64:72, 0:TPC], lanA[64:72, 0:TPC]))
    e24v = op("vector", lambda e: e.tensor_scalar(
        lanC[0:8, 0:TPC], lanA[64:72, 0:TPC], -1.0, None, ALU.mult), "sve")
    wait("scalar", "sve", e24v)
    op("scalar", lambda e: e.activation(lanA[0:8, 0:TPC], lanA[0:8, 0:TPC], AF.Exp))
    drain("scalar")
    op("scalar", lambda e: e.activation(lanB[0:8, 0:TPC], lanB[0:8, 0:TPC], AF.Exp))
    drain("scalar")
    e24e = op("scalar", lambda e: e.activation(lanC[0:8, 0:TPC], lanC[0:8, 0:TPC], AF.Exp), "sse")
    wait("tensor", "sse", e24e)
    etv = 0
    esrc24 = [(lanA, 0), (lanB, 0), (lanC, 0)]
    for cc_ in range(NCH):
        for qi, (seg, base) in enumerate(esrc24):
            if cc_ + qi >= 1:
                wait("tensor", "sve", etv)
            tv = op("tensor", lambda e, cc_=cc_, seg=seg, base=base: e.transpose(
                pG[0:128, 0:8], seg[base:base + 8, cc_ * 128:cc_ * 128 + 128],
                idf[base:base + 8, base:base + 8]), "ste")
            wait("vector", "ste", tv)
            op("vector", lambda e, cc_=cc_, qi=qi: e.tensor_copy(
                e24t[:, cc_ * 24 + qi * 8:cc_ * 24 + qi * 8 + 8], pG[:, 0:8]))
            drain("vector")
            etv = op("vector", lambda e, cc_=cc_, qi=qi: e.tensor_copy(
                e24b[:, cc_ * 24 + qi * 8:cc_ * 24 + qi * 8 + 8],
                e24t[:, cc_ * 24 + qi * 8:cc_ * 24 + qi * 8 + 8]), "sve")

    PH(9)
    wait("tensor", "sve", C["sve"])
    gH2 = {0: 0, 1: 0}
    gTo = {0: 0, 1: 0}
    for cc_ in range(NCH):
        tslab = op("vector", lambda e, cc_=cc_: e.tensor_tensor(
            t1s[:, 0:2056].rearrange("p (h f) -> p h f", f=257),
            h16[:, cc_ * 2056: cc_ * 2056 + 2056].rearrange("p (h f) -> p h f", f=257),
            e24b[:, cc_ * 24: cc_ * 24 + 8].unsqueeze(2).to_broadcast([128, 8, 257]),
            ALU.mult), "sve")
        drain("vector")
        for h in range(NH):
            par = h % 2
            p = [pA, pB][par]
            qsl = bqk[:, h * TPC + cc_ * 128: h * TPC + cc_ * 128 + 128]
            wait("tensor", "sve", gH2[par])
            if h == 0:
                wait("tensor", "sve", tslab)
            h2mm = op("tensor", lambda e, qsl=qsl, h=h, p=p: e.matmul(
                p[:, 0:257], qsl, c16[:, 257 * h:257 * h + 257], start=True, stop=True), "ste")
            wait("vector", "ste", h2mm)
            gH2[par] = op("vector", lambda e, h=h, p=p, cc_=cc_: e.scalar_tensor_tensor(
                cst[:, 257 * h: 257 * h + 257], p[:, 0:257],
                e24t[:, cc_ * 24 + 8 + h: cc_ * 24 + 8 + h + 1],
                t1s[:, 257 * h: 257 * h + 257], ALU.mult, ALU.add), "sve")
        drain("vector")
        op("vector", lambda e: e.tensor_scalar(
            sq8[:, 40:48], cst[:, 256:2056:257], -1.0, None, ALU.mult))
        drain("vector")
        op("vector", lambda e: e.tensor_max(
            sq8[:, 0:8], sq8[:, 40:48], cst[:, 256:2056:257]))
        drain("vector")
        op("vector", lambda e, cc_=cc_: e.tensor_max(
            sq8[:, 0:8], sq8[:, 0:8], e24t[:, cc_ * 24 + 16: cc_ * 24 + 24]))
        drain("vector")
        op("vector", lambda e: e.tensor_scalar(sq8[:, 0:8], sq8[:, 0:8], 1e-6, None, ALU.add))
        drain("vector")
        op("vector", lambda e: e.reciprocal(sq8[:, 8:16], sq8[:, 0:8]))
        drain("vector")
        hnv = op("vector", lambda e: e.tensor_tensor(
            t1s[:, 0:2048].rearrange("p (h f) -> p h f", f=256),
            cst[:, 0:2056].rearrange("p (h f) -> p h f", f=257)[:, :, 0:256],
            sq8[:, 8:16].unsqueeze(2).to_broadcast([128, 8, 256]), ALU.mult), "sve")
        drain("vector")
        op("vector", lambda e: e.reduce_sum(
            sq8[:, 16:24], t1s[:, 0:2048].rearrange("p (h f) -> p h f", f=256), axis=AX.X))
        wait("scalar", "sve", hnv)
        sqv = op("scalar", lambda e: e.activation(cst[:, 0:2048], t1s[:, 0:2048], AF.Square), "sse")
        wait("vector", "sse", sqv)
        op("vector", lambda e: e.reduce_sum(
            sq8[:, 24:32], cst[:, 0:2048].rearrange("p (h f) -> p h f", f=256), axis=AX.X))
        drain("vector")
        op("vector", lambda e: e.tensor_scalar(sq8[:, 16:24], sq8[:, 16:24], -1.0 / 256, None, ALU.mult))
        op("vector", lambda e: e.tensor_scalar(sq8[:, 24:32], sq8[:, 24:32], 1.0 / 256, None, ALU.mult))
        drain("vector")
        op("vector", lambda e: e.tensor_mul(sq8[:, 32:40], sq8[:, 16:24], sq8[:, 16:24]))
        drain("vector")
        op("vector", lambda e: e.tensor_sub(sq8[:, 24:32], sq8[:, 24:32], sq8[:, 32:40]))
        drain("vector")
        op("vector", lambda e: e.tensor_scalar(sq8[:, 24:32], sq8[:, 24:32], 1e-6, None, ALU.add))
        drain("vector")
        rvv = opd("vector", lambda e: e.reciprocal(sq8[:, 32:40], sq8[:, 24:32]), "sve")
        wait("scalar", "sve", rvv)
        rsv = op("scalar", lambda e: e.activation(sq8[:, 40:48], sq8[:, 32:40], AF.Sqrt), "sse")
        wait("vector", "sse", rsv)
        op("vector", lambda e: e.tensor_tensor(
            cst[:, 0:2048].rearrange("p (h f) -> p h f", f=256),
            t1s[:, 0:2048].rearrange("p (h f) -> p h f", f=256),
            sq8[:, 16:24].unsqueeze(2).to_broadcast([128, 8, 256]), ALU.add))
        drain("vector")
        op("vector", lambda e: e.tensor_tensor(
            cst[:, 0:2048].rearrange("p (h f) -> p h f", f=256),
            cst[:, 0:2048].rearrange("p (h f) -> p h f", f=256),
            sq8[:, 40:48].unsqueeze(2).to_broadcast([128, 8, 256]), ALU.mult))
        drain("vector")
        hgv = op("vector", lambda e: e.tensor_mul(hob[:, 0:2048], cst[:, 0:2048], gab[:, :]), "sve")
        wait("tensor", "sve", hgv)
        for vck in range(NKC):
            par = vck % 2
            wait("tensor", "sve", gTo[par])
            tv = op("tensor", lambda e, vck=vck, par=par: e.transpose(
                pT[:, par * 512:par * 512 + 128], hob[:, vck * 128:vck * 128 + 128],
                idb[:, :]), "ste")
            wait("vector", "ste", tv)
            gTo[par] = op("vector", lambda e, vck=vck, cc_=cc_, par=par: e.tensor_tensor(
                v1[:, vck * TPC + cc_ * 128: vck * TPC + cc_ * 128 + 128],
                pT[:, par * 512:par * 512 + 128],
                ogb[:, vck * TPC + cc_ * 128: vck * TPC + cc_ * 128 + 128], ALU.mult), "sve")
    passb_v = C["sve"]

    PH(10)
    fg_on()
    ring = [None] * 4
    xgrd = {0: 0, 1: 0}
    stgrd = {0: 0, 1: 0}
    for db in range(NKC):
        so = (db % 4) * 2048
        if ring[db % 4] is not None:
            wait("gpsimd", "ste", ring[db % 4])
        lv = dma("gpsimd", wbuf[:, so:so + 2048], woutT[:, db, :], "qw%d" % (db % 4))
        wait("tensor", "qw%d" % (db % 4), lv)
        if db == 0:
            wait("tensor", "sve", passb_v)
        for tb in range(NTB):
            p = [pC, pD][tb]
            sl2 = ((db * NTB + tb) % 2) * 512
            wait("sync", "sve", xgrd[(db * NTB + tb) % 2])
            xl = dma("sync", xre[:, sl2:sl2 + 512],
                     xT[db * 128: db * 128 + 128, tb * 512: tb * 512 + 512],
                     "qxr%d" % ((db * NTB + tb) % 2))
            wait("tensor", "sve", stgrd[tb])
            mmv = 0
            for kk in range(NKC):
                mmv = op("tensor", lambda e, so=so, kk=kk, tb=tb, p=p: e.matmul(
                    p[:, :], wbuf[:, so + kk * 128: so + kk * 128 + 128],
                    v1[:, kk * TPC + tb * 512: kk * TPC + tb * 512 + 512],
                    start=(kk == 0), stop=(kk == NKC - 1)), "ste")
            if tb == NTB - 1:
                ring[db % 4] = mmv
            wait("vector", "ste", mmv)
            wait("vector", "qxr%d" % ((db * NTB + tb) % 2), xl)
            wait("vector", "qst%d" % ((db * NTB + tb) % 2),
                 C["qst%d" % ((db * NTB + tb) % 2)])
            ev = op("vector", lambda e, db=db, p=p, sl2=sl2: e.scalar_tensor_tensor(
                x2f[:, sl2:sl2 + 512], p[:, :], modv[:, 32 + db:32 + db + 1],
                xre[:, sl2:sl2 + 512], ALU.mult, ALU.add), "sve")
            stgrd[tb] = ev
            wait("scalar", "sve", ev)
            dma("scalar", yT[db * 128:db * 128 + 128, tb * 512:tb * 512 + 512],
                x2f[:, sl2:sl2 + 512], "qst%d" % ((db * NTB + tb) % 2))
            drain("vector")
            x2v = op("vector", lambda e, db=db, tb=tb, sl2=sl2: e.tensor_copy(
                bqk[:, db * TPC + tb * 512: db * TPC + tb * 512 + 512],
                x2f[:, sl2:sl2 + 512]), "sve")
            xgrd[(db * NTB + tb) % 2] = x2v
    x2_qst = (C["qst0"], C["qst1"])

    PH(11)
    h2v = layer_norm(bqk, bqk, modv[:, 64:80], modv[:, 48:64], "sve", x2v)

    def h2sl(kk, tb, width=512):
        return bqk[:, kk * TPC + tb * 512: kk * TPC + tb * 512 + width]

    PH(12)
    gld = dma("gpsimd", wsm[:, 256:512], gateT[:, 0, :], "qw3")
    wait("tensor", "qw3", gld)
    wait("tensor", "sve", h2v)
    gmm = 0
    for tb in range(NTB):
        p = [pE, pF][tb]
        for kk in range(NKC):
            gmm = op("tensor", lambda e, kk=kk, tb=tb, p=p: e.matmul(
                p[0:16, :], wsm[:, 256 + kk * 16: 256 + kk * 16 + 16],
                h2sl(kk, tb), start=(kk == 0), stop=(kk == NKC - 1)), "ste")
    wait("vector", "ste", gmm)
    glv = 0
    for tb in range(NTB):
        glv = op("vector", lambda e, tb=tb: e.tensor_copy(
            lanC[0:16, tb * 512:tb * 512 + 512], [pE, pF][tb][0:16, :]), "sve")
    cwv = glv
    for tt in range(NCH):
        wait("tensor", "sve", cwv)
        ttv = op("tensor", lambda e, tt=tt: e.transpose(
            pG[0:128, 0:16], lanC[0:16, tt * 128:tt * 128 + 128],
            idf[0:16, 0:16]), "ste")
        wait("vector", "ste", ttv)
        op("vector", lambda e: e.tensor_copy(g16a[:], pG[:, 0:16]))
        drain("vector")
        nmx = op("vector", lambda e: e.reduce_max(gc1[:], g16a[:], axis=AX.X, negate=True), "sve")
        wait("scalar", "sve", nmx)
        exv = op("scalar", lambda e: e.activation(g16b[:], g16a[:], AF.Exp,
                                                  bias=gc1[:, 0:1], scale=1.0), "sse")
        wait("vector", "sse", exv)
        op("vector", lambda e: e.reduce_sum(gc2[:], g16b[:], axis=AX.X))
        drain("vector")
        op("vector", lambda e: e.reciprocal(gc1[:], gc2[:]))
        drain("vector")
        op("vector", lambda e: e.tensor_scalar(g16a[:], g16b[:], gc1[:, 0:1], None, ALU.mult))
        drain("vector")
        op("vector", lambda e: e.reduce_max(g4a[:], g16a[:].rearrange("p (g i) -> p g i", i=4), axis=AX.X))
        drain("vector")
        op("vector", lambda e: e.tensor_tensor(
            g16b[:].rearrange("p (g i) -> p g i", i=4),
            g4a[:].unsqueeze(1).to_broadcast([128, 4, 4]),
            g4a[:].unsqueeze(2).to_broadcast([128, 4, 4]), ALU.is_gt))
        op("vector", lambda e: e.tensor_tensor(
            g16d[:].rearrange("p (g i) -> p g i", i=4),
            g4a[:].unsqueeze(1).to_broadcast([128, 4, 4]),
            g4a[:].unsqueeze(2).to_broadcast([128, 4, 4]), ALU.is_equal))
        drain("vector")
        op("vector", lambda e: e.tensor_mul(g16d[:], g16d[:], t16c[:]))
        drain("vector")
        op("vector", lambda e: e.tensor_add(g16b[:], g16b[:], g16d[:]))
        drain("vector")
        op("vector", lambda e: e.reduce_sum(g4a[:], g16b[:].rearrange("p (g i) -> p g i", i=4), axis=AX.X))
        drain("vector")
        op("vector", lambda e: e.tensor_scalar(g4a[:], g4a[:], 2.0, None, ALU.is_lt))
        drain("vector")
        op("vector", lambda e: e.tensor_tensor(
            g16b[:].rearrange("p (g i) -> p g i", i=4),
            g16a[:].rearrange("p (g i) -> p g i", i=4),
            g4a[:].unsqueeze(2).to_broadcast([128, 4, 4]), ALU.mult))
        drain("vector")
        op("vector", lambda e: e.tensor_tensor(
            g256[:].rearrange("p (a b) -> p a b", b=16),
            g16b[:].unsqueeze(1).to_broadcast([128, 16, 16]),
            g16b[:].unsqueeze(2).to_broadcast([128, 16, 16]), ALU.is_gt))
        op("vector", lambda e: e.tensor_tensor(
            g256b[:].rearrange("p (a b) -> p a b", b=16),
            g16b[:].unsqueeze(1).to_broadcast([128, 16, 16]),
            g16b[:].unsqueeze(2).to_broadcast([128, 16, 16]), ALU.is_equal))
        drain("vector")
        op("vector", lambda e: e.tensor_mul(g256b[:], g256b[:], t256c[:]))
        drain("vector")
        op("vector", lambda e: e.tensor_add(g256[:], g256[:], g256b[:]))
        drain("vector")
        op("vector", lambda e: e.reduce_sum(g16d[:], g256[:].rearrange("p (a b) -> p a b", b=16), axis=AX.X))
        drain("vector")
        op("vector", lambda e: e.tensor_scalar(g16d[:], g16d[:], 4.0, None, ALU.is_lt))
        drain("vector")
        op("vector", lambda e: e.tensor_mul(g16b[:], g16b[:], g16d[:]))
        drain("vector")
        op("vector", lambda e: e.reduce_sum(gc2[:], g16b[:], axis=AX.X))
        drain("vector")
        op("vector", lambda e: e.tensor_scalar(gc2[:], gc2[:], 1e-20, None, ALU.add))
        drain("vector")
        op("vector", lambda e: e.reciprocal(gc1[:], gc2[:]))
        drain("vector")
        cwq = op("vector", lambda e: e.tensor_scalar(g16a[:], g16b[:], gc1[:, 0:1], None, ALU.mult), "sve")
        wait("tensor", "sve", cwq)
        tbv = op("tensor", lambda e, tt=tt: e.transpose(
            pG[0:16, 0:128], g16a[:], idf[:, :]), "ste")
        wait("vector", "ste", tbv)
        cwv = op("vector", lambda e, tt=tt: e.tensor_copy(
            cwl16[:, tt * 128:tt * 128 + 128], pG[0:16, 0:128]), "sve")

    PH(13)
    def aslice(s, tb):
        if s < 16:
            t_, base = h16, s * TPC
        elif s < 32:
            t_, base = ogb, (s - 16) * TPC
        else:
            t_, base = v1, (s - 32) * TPC
        return t_[:, base + tb * 512: base + tb * 512 + 512]

    wait("gpsimd", "ste", C["ste"])
    wait("gpsimd", "qst0", x2_qst[0])
    wait("gpsimd", "qst1", x2_qst[1])
    groups = [[0, 1, 2, 3], [4, 5, 6, 7], [8, 9, 10, 11], [12, 13, 14, 15], ["sh"]]
    upfree = [None, None]
    upld_i = 0
    cbe = cwv
    sgrd = {0: 0, 1: 0}
    ugrd = {0: 0, 1: 0}
    dn_guard = 0
    acc_guard = {0: 0, 1: 0}
    for grp in groups:
        wait("gpsimd", "qac0", C["qac0"])
        wait("gpsimd", "qac1", C["qac1"])
        shared = grp[0] == "sh"
        nexp = 1 if shared else len(grp)
        nj = 22 if shared else 11
        for ei in range(nexp):
            eidx = None if shared else grp[ei]
            if not shared:
                wait("sync", "sve", cbe)
                crv = dma("sync", cwrow[0:1, 0:TPC], cwl16[eidx:eidx + 1, :], "qio")
                for tb in range(NTB):
                    wait("tensor", "sve", cbe)
                    wait("tensor", "qio", crv)
                    cbv = op("tensor", lambda e, tb=tb: e.matmul(
                        pG[:, :], onesbf[0:1, :],
                        cwrow[0:1, tb * 512:tb * 512 + 512],
                        start=True, stop=True), "ste")
                    wait("vector", "ste", cbv)
                    cbe = op("vector", lambda e, tb=tb: e.tensor_copy(
                        cwbc[:, tb * 512:tb * 512 + 512], pG[:, :]), "sve")
            gmmv = 0
            for jj in range(nj):
                for half in range(2):
                    jb = jj + half * nj
                    slot = upld_i % 2
                    upld_i += 1
                    if upfree[slot] is not None:
                        wait("gpsimd", "ste", upfree[slot])
                    src = supT[:, jb, :] if shared else upT[eidx, :, jb, :]
                    lv = dma("gpsimd", wbuf[:, slot * 2048:slot * 2048 + 2048], src,
                             "qw%d" % slot)
                    wait("tensor", "qw%d" % slot, lv)
                    wait("tensor", "sse" if half == 0 else "sve",
                         sgrd[0] if half == 0 else ugrd[0])
                    mmv = 0
                    for kk in range(NKC):
                        for tb in range(NTB):
                            p = ([pA, pB] if half == 0 else [pC, pD])[tb]
                            mmv = op("tensor", lambda e, slot=slot, kk=kk, tb=tb, p=p: e.matmul(
                                p[:, :], wbuf[:, slot * 2048 + kk * 128: slot * 2048 + kk * 128 + 128],
                                h2sl(kk, tb), start=(kk == 0), stop=(kk == NKC - 1)), "ste")
                    upfree[slot] = mmv
                    if half == 0:
                        gmmv = mmv
                wait("scalar", "ste", gmmv)
                sgv = 0
                for tb in range(NTB):
                    sgv = op("scalar", lambda e, tb=tb: e.activation(
                        sgil[:, tb * 512:tb * 512 + 512], [pA, pB][tb][:, :], AF.Silu), "sse")
                sgrd[0] = sgv
                wait("vector", "ste", mmv)
                wait("vector", "sse", sgv)
                s_global = jj if shared else ei * 11 + jj
                for tb in range(NTB):
                    op("vector", lambda e, tb=tb: e.tensor_tensor(
                        tmul[:, tb * 512:tb * 512 + 512], sgil[:, tb * 512:tb * 512 + 512],
                        [pC, pD][tb][:, :], ALU.mult))
                drain("vector")
                av = 0
                for tb in range(NTB):
                    if shared:
                        av = op("vector", lambda e, s_global=s_global, tb=tb: e.tensor_copy(
                            aslice(s_global, tb), tmul[:, tb * 512:tb * 512 + 512]), "sve")
                    else:
                        av = op("vector", lambda e, s_global=s_global, tb=tb: e.tensor_mul(
                            aslice(s_global, tb), tmul[:, tb * 512:tb * 512 + 512],
                            cwbc[:, tb * 512:tb * 512 + 512]), "sve")
                ugrd[0] = av
        up_done = C["sve"]
        for db in range(NKC):
            wait("gpsimd", "ste", dn_guard)
            dls = 0
            for ei in range(nexp):
                slot = 4096 + ei * 1408
                src = sdownT[:, db, :] if shared else downT[grp[ei], :, db, :]
                dls = dma("gpsimd", wbuf[:, slot:slot + nj * 128], src, "qw2")
            wait("tensor", "qw2", dls)
            if db == 0:
                wait("tensor", "sve", up_done)
            for tb in range(NTB):
                p = [pE, pF][tb]
                wait("tensor", "sve", acc_guard[tb])
                mmv = 0
                for ei in range(nexp):
                    slot = 4096 + ei * 1408
                    for ik in range(nj):
                        s_global = ik if shared else ei * 11 + ik
                        mmv = op("tensor", lambda e, slot=slot, ik=ik, tb=tb, p=p,
                                 s_global=s_global, first=(ei == 0 and ik == 0),
                                 last=(ei == nexp - 1 and ik == nj - 1): e.matmul(
                            p[:, :], wbuf[:, slot + ik * 128: slot + ik * 128 + 128],
                            aslice(s_global, tb), start=first, stop=last), "ste")
                if tb == NTB - 1:
                    dn_guard = mmv
                wait("vector", "ste", mmv)
                sl3 = ((db * NTB + tb) % 2) * 512
                wait("vector", "qac%d" % ((db * NTB + tb) % 2),
                     C["qac%d" % ((db * NTB + tb) % 2)])
                e3 = op("vector", lambda e, db=db, p=p, sl3=sl3: e.tensor_scalar(
                    x2f[:, sl3:sl3 + 512], p[:, :], modv[:, 80 + db:80 + db + 1],
                    None, ALU.mult), "sve")
                acc_guard[tb] = e3
                wait("gpsimd", "sve", e3)
                dma("gpsimd", yT[db * 128:db * 128 + 128, tb * 512:tb * 512 + 512],
                    x2f[:, sl3:sl3 + 512], "qac%d" % ((db * NTB + tb) % 2), accum=True)

    PH(14)
    fg_off()
    wait("gpsimd", "qac0", C["qac0"])
    wait("gpsimd", "qac1", C["qac1"])
    wait("scalar", "qst0", C["qst0"])
    wait("scalar", "qst1", C["qst1"])

    if KPH < 99:
        cuts = sorted(k for k in PHM if k > KPH)
        if cuts:
            del EV[PHM[cuts[0]]:]
            del EVG[PHM[cuts[0]]:]

    fin = EVG[-1]
    for engn in ["sync", "gpsimd", "vector", "scalar", "tensor"]:
        on(engn, lambda e, v=fin: e.wait_ge(GSEM, v))

    block = ctx.enter_context(nc.Block())

    def runner(eng_name):
        def body(eng):
            for nm, fn in EV:
                if nm == eng_name:
                    fn(eng)
        return body

    block.sync(runner("sync"))
    block.gpsimd(runner("gpsimd"))
    block.vector(runner("vector"))
    block.scalar(runner("scalar"))
    block.tensor(runner("tensor"))
    ctx.close()
    return nc


def _wtile(W, mb):
    Din, M = W.shape[1], W.shape[0]
    kb = Din // 128
    arr = np.ascontiguousarray(W.T).reshape(kb, 128, M // mb, mb)
    return np.ascontiguousarray(arr.transpose(1, 2, 0, 3)).reshape(128, M // mb, kb * mb)


_NC_CACHE = {}


def kernel(**inp):
    S = np.asarray(inp["L0_attn_x"]).shape[1]
    TPC = S // 4
    if TPC not in _NC_CACHE:
        _NC_CACHE[TPC] = build_nc(TPC)
    nc = _NC_CACHE[TPC]

    x = np.asarray(inp["L0_attn_x"], np.float32)
    cc = np.asarray(inp["L0_c"], np.float32)
    wmod = np.asarray(inp["L0_wmod"], np.float32)

    wq = _wtile(np.asarray(inp["L0_attn_Wq"], np.float32), 128)
    wk = _wtile(np.asarray(inp["L0_attn_Wk"], np.float32), 128)
    wv = _wtile(np.asarray(inp["L0_attn_Wv"], np.float32), 256)
    wog = _wtile(np.asarray(inp["L0_attn_Wog"], np.float32), 128)
    wif = _wtile(np.concatenate([np.asarray(inp["L0_attn_Wig"], np.float32),
                                 np.asarray(inp["L0_attn_Wfg"], np.float32)], 0), 16)
    wout = _wtile(np.asarray(inp["L0_attn_Wout"], np.float32), 128)
    gate = _wtile(np.asarray(inp["L0_ffn_gate_w"], np.float32), 16)
    up = np.stack([_wtile(np.asarray(inp["L0_ffn_expert_up_w"][e], np.float32), 128)
                   for e in range(E)])
    down = np.stack([_wtile(np.asarray(inp["L0_ffn_expert_down_w"][e], np.float32), 128)
                     for e in range(E)])
    sup = _wtile(np.asarray(inp["L0_ffn_shared_up_w"], np.float32), 128)
    sdn = _wtile(np.asarray(inp["L0_ffn_shared_down_w"], np.float32), 128)
    bigv = np.ascontiguousarray(np.asarray(inp["L0_attn_big"], np.float32)[:, None])
    bfgv = np.ascontiguousarray(np.asarray(inp["L0_attn_bfg"], np.float32)[:, None])
    gm = np.ascontiguousarray(np.asarray(inp["L0_attn_gamma"], np.float32).reshape(1, NH * DV))

    mask01 = np.ascontiguousarray(np.triu(np.ones((128, 128), np.float32)))
    identb = np.eye(128).astype(ml_dtypes.bfloat16)
    identf = np.eye(128, dtype=np.float32)
    tril16 = np.ascontiguousarray(
        np.tril(np.ones((4, 4), np.float32), -1).reshape(1, 16).repeat(128, 0))
    tril256 = np.ascontiguousarray(
        np.tril(np.ones((16, 16), np.float32), -1).reshape(1, 256).repeat(128, 0))

    in_maps = []
    for c in range(8):
        bidx, r = c // 4, c % 4
        xs = np.ascontiguousarray(x[bidx, r * TPC:(r + 1) * TPC, :].T)
        cT = np.ascontiguousarray(cc[bidx].reshape(NKC, 128).T)
        wms = np.ascontiguousarray(wmod[:, 3072 * r:3072 * (r + 1)])
        flag = np.zeros((8, 3), np.float32)
        flag[:, :r] = 1.0
        ccneg = (1.0 - flag) * NEG
        in_maps.append(dict(
            xT=xs, cTb=cT, wmod_sl=wms, wqT=wq, wkT=wk, wvT=wv, wogT=wog,
            wifT=wif, bifi=bigv, biff=bfgv, gamma=gm, woutT=wout, gateT=gate, upT=up,
            downT=down, supT=sup, sdownT=sdn, mask01=mask01, identb=identb,
            identf=identf, tril16=tril16, tril256=tril256,
            ccflag=flag, ccneg=ccneg))

    res = run_bass_kernel_spmd(nc, in_maps, core_ids=list(range(8)))
    out = np.empty((2, S, D), np.float32)
    for c in range(8):
        bidx, r = c // 4, c % 4
        out[bidx, r * TPC:(r + 1) * TPC, :] = res.results[c]["yT"].T
    return out


# version 16
# speedup vs baseline: 2.8653x; 1.0273x over previous
import os
import numpy as np
import ml_dtypes
from contextlib import ExitStack
import concourse.bass as bass
import concourse.mybir as mybir
from concourse.bass_utils import run_bass_kernel_spmd

F32 = mybir.dt.float32
BF16 = mybir.dt.bfloat16
AF = mybir.ActivationFunctionType
ALU = mybir.AluOpType
AX = mybir.AxisListType

D = 2048
NH = 8
DV = 256
NKC = 16
E = 16
NEG = -1e30
QSCALE = float(128) ** -0.5
NU = 128 * 2056


def build_nc(TPC):
    KPH = int(os.environ.get("KPH", "99"))
    NCH = TPC // 128
    NTB = TPC // 512
    nc = bass.Bass("TRN2", target_bir_lowering=False)

    def di(name, shape, dt=F32):
        return nc.dram_tensor(name, shape, dt, kind="ExternalInput").ap()

    xT = di("xT", [D, TPC])
    cTb = di("cTb", [128, NKC])
    wmod_sl = di("wmod_sl", [D, 3072])
    wqT = di("wqT", [128, NH, NKC * 128])
    wkT = di("wkT", [128, NH, NKC * 128])
    wvT = di("wvT", [128, NH, NKC * 256])
    wogT = di("wogT", [128, NKC, NKC * 128])
    wifT = di("wifT", [128, 1, NKC * 16])
    bifi = di("bifi", [8, 1])
    biff = di("biff", [8, 1])
    gamma = di("gamma", [1, NH * DV])
    woutT = di("woutT", [128, NKC, NKC * 128])
    gateT = di("gateT", [128, 1, NKC * 16])
    upT = di("upT", [E, 128, 22, NKC * 128])
    downT = di("downT", [E, 128, 16, 11 * 128])
    supT = di("supT", [128, 44, NKC * 128])
    sdownT = di("sdownT", [128, 16, 22 * 128])
    mask01 = di("mask01", [128, 128])
    identb = di("identb", [128, 128], BF16)
    identf = di("identf", [128, 128])
    tril16 = di("tril16", [128, 16])
    tril256 = di("tril256", [128, 256])
    ccflag = di("ccflag", [8, 3])
    ccneg = di("ccneg", [8, 3])
    yT = nc.dram_tensor("yT", [D, TPC], F32, kind="ExternalOutput").ap()

    ag1_in = nc.dram_tensor("ag1_in", [1, 3072], F32, kind="Internal").ap()
    ag1_out = nc.dram_tensor("ag1_out", [4, 3072], F32, kind="Internal").ap()
    ag2_in = nc.dram_tensor("ag2_in", [1, NU + 32], F32, kind="Internal").ap()
    ag2_out = nc.dram_tensor("ag2_out", [4, NU + 32], F32, kind="Internal").ap()

    ctx = ExitStack()
    ctx.__enter__()

    def sb(name, shape, dt=F32):
        return ctx.enter_context(nc.sbuf_tensor(name, shape, dt))

    h16 = sb("h16", [128, NCH * 2056], BF16)
    ogb = sb("ogb", [128, NKC * TPC], BF16)
    v1 = sb("v1", [128, NCH * 2056], BF16)
    bqk = sb("bqk", [128, NKC * TPC], BF16)
    wbuf = sb("wbuf", [128, 9728], BF16)
    wsm = sb("wsm", [128, 512], BF16)
    cst = sb("cst", [128, 2056])
    c16 = sb("c16", [128, 2056], BF16)
    lanA = sb("lanA", [128, TPC])
    lanB = sb("lanB", [128, TPC])
    e32t = sb("e32t", [128, NCH * 32])
    e24t = sb("e24t", [128, NCH * 24])
    e24b = sb("e24b", [128, NCH * 24], BF16)
    modv = sb("modv", [128, 6 * 16])
    gab = sb("gab", [128, NH * DV], BF16)
    mkb = sb("mkb", [128, 128])
    idb = sb("idb", [128, 128], BF16)
    idf = sb("idf", [128, 128])
    t16c = sb("t16c", [128, 16])
    t256c = sb("t256c", [128, 256])
    onesf = sb("onesf", [1, 128])
    onesbf = sb("onesbf", [1, 128], BF16)
    onescb = sb("onescb", [128, 1], BF16)
    scT = sb("scT", [128, NKC], BF16)
    bifib = sb("bifib", [8, 1])
    biffb = sb("biffb", [8, 1])
    bifis = sb("bifis", [8, 1])
    biffs = sb("biffs", [8, 1])
    gmr = sb("gmr", [1, NH * DV])
    ccf = sb("ccf", [8, 3])
    ccn = sb("ccn", [8, 3])
    wrowp = sb("wrowp", [1, 8])
    wrow1 = sb("wrow1", [1, 8])
    wbcs = sb("wbcs", [128, 16])
    scrA = sb("scrA", [128, 1024])
    a3s = sb("a3s", [128, 2 * 128], BF16)
    k1s = sb("k1s", [128, 2 * 128], BF16)
    t1s = sb("t1s", [128, 2056], BF16)
    xsq = t1s
    sq8 = sb("sq8", [128, 48])
    sgil = sb("sgil", [128, 2 * 512], BF16)
    tmul = sb("tmul", [128, 2 * 512], BF16)
    x2f = sb("x2f", [128, 2 * 512])
    xre = sb("xre", [128, 2 * 512])
    scrC = xre
    lanC = xre
    cwl16 = gab[0:16, 0:TPC]
    cwrow = scrA[0:1, :].bitcast(BF16)
    cwbc = gab[:, TPC:2 * TPC] if TPC <= NH * DV // 2 else gab[:, 0:TPC]
    g16a = sb("g16a", [128, 16])
    g16b = sb("g16b", [128, 16])
    g16d = sb("g16d", [128, 16])
    g4a = sb("g4a", [128, 4])
    g256 = e32t[:, 0:256]
    g256b = t1s[:, 0:512].bitcast(F32)
    gc1 = sb("gc1", [128, 1])
    gc2 = sb("gc2", [128, 1])

    hob = scrA[:].bitcast(BF16)
    uvw = v1[:].bitcast(F32)

    def psm(name):
        return ctx.enter_context(nc.psum_tensor(name, [128, 512], F32))

    pA, pB, pC, pD, pE, pF, pG = (psm(n) for n in "pA pB pC pD pE pF pG".split())
    PSF = [pA, pB, pC, pD, pE, pF, pG]
    pT = ctx.enter_context(nc.psum_tensor("pT", [128, 1024], BF16))

    SEM, C = {}, {}
    for s in ["qsc", "qxb", "qw0", "qw1", "qw2", "qw3", "qc", "qio",
              "qxr0", "qxr1", "qst0", "qst1", "qac0", "qac1",
              "ste", "sve", "sse", "scc"]:
        SEM[s] = ctx.enter_context(nc.semaphore(s))
        C[s] = 0
    GSEM = ctx.enter_context(nc.semaphore("gsem"))
    G = [0]
    R = {s: 0 for s in SEM}
    FGM = [False]
    ENGS = ["sync", "gpsimd", "vector", "scalar", "tensor"]
    EV = []
    EVG = []
    PHM = {}

    def PH(n):
        if n not in PHM:
            PHM[n] = (len(EV), G[0], dict(R))

    def on(eng, fn):
        EV.append((eng, fn))
        EVG.append(G[0])

    def chain(eng):
        v = G[0]
        if v:
            on(eng, lambda e, v=v: e.wait_ge(GSEM, v))

    def dma(eng, out, in_, sem, accum=False, slow=False):
        kw = {}
        if accum:
            kw["accum_op"] = ALU.add
        if slow:
            kw["allow_slow_non_contiguous"] = True
        C[sem] += 16
        if FGM[0]:
            R[sem] += 16
            on(eng, lambda e, o=out, i=in_, S=SEM[sem], kw=kw:
               e.dma_start(out=o, in_=i, **kw).then_inc(S, 16))
        else:
            chain(eng)
            G[0] += 16
            on(eng, lambda e, o=out, i=in_, kw=kw:
               e.dma_start(out=o, in_=i, **kw).then_inc(GSEM, 16))
        return C[sem]

    def wait(eng, sem, val):
        if FGM[0] and val:
            on(eng, lambda e, S=SEM[sem], v=val: e.wait_ge(S, v))

    def op(eng, fn, sem=None):
        if FGM[0]:
            if sem is None:
                on(eng, fn)
                return None
            C[sem] += 1
            R[sem] += 1
            on(eng, lambda e, f=fn, S=SEM[sem]: f(e).then_inc(S, 1))
            return C[sem]
        chain(eng)
        G[0] += 1
        if eng == "vector":
            on(eng, fn)
            on(eng, lambda e: e.drain())
            on(eng, lambda e: e.sem_inc(GSEM, 1))
        else:
            on(eng, lambda e, f=fn: f(e).then_inc(GSEM, 1))
        if sem is None:
            return None
        C[sem] += 1
        return C[sem]

    def drain(eng):
        on(eng, lambda e: e.drain())

    def opd(eng, fn, sem):
        C[sem] += 1
        if FGM[0]:
            R[sem] += 1
            on(eng, fn)
            on(eng, lambda e: e.drain())
            on(eng, lambda e, S=SEM[sem]: e.sem_inc(S, 1))
            return C[sem]
        chain(eng)
        G[0] += 1
        on(eng, fn)
        on(eng, lambda e: e.drain())
        on(eng, lambda e: e.sem_inc(GSEM, 1))
        return C[sem]

    def barrier():
        for engn in ENGS:
            if G[0]:
                on(engn, lambda e, v=G[0]: e.wait_ge(GSEM, v))
            for s, rv in R.items():
                if rv:
                    on(engn, lambda e, S=SEM[s], v=rv: e.wait_ge(S, v))

    def fg_on():
        barrier()
        for s in SEM:
            delta = C[s] - R[s]
            if delta:
                on("sync", lambda e, S=SEM[s], d=delta: e.sem_inc(S, d))
                R[s] += delta
        FGM[0] = True

    def fg_off():
        FGM[0] = False
        barrier()


    ld0 = 0
    for out, src in [(mkb[:], mask01), (idb[:], identb), (idf[:], identf),
                     (gmr[:], gamma), (bifib[:], bifi), (biffb[:], biff), (ccf[:], ccflag),
                     (ccn[:], ccneg), (t16c[:], tril16), (t256c[:], tril256)]:
        ld0 = dma("sync", out, src, "qc")
    csem = dma("gpsimd", scT[:], cTb, "qsc")
    xbsem = dma("gpsimd", ogb[:, 0:NKC * TPC].rearrange("p (g t) -> p g t", g=NKC),
                xT.rearrange("(g p) t -> g p t", p=128).transpose([1, 0, 2]), "qxb")
    op("vector", lambda e: e.memset(onesf[:], 1.0))
    op("vector", lambda e: e.memset(onesbf[:], 1.0))
    op("vector", lambda e: e.memset(onescb[:], 1.0))
    op("vector", lambda e: e.memset(lanB[96:104, :], 0.0))
    op("vector", lambda e: e.memset(cst[:], 0.0))
    op("vector", lambda e: e.memset(c16[:], 0.0))
    vini = op("vector", lambda e: e.memset(v1[:], 1.0), "sve")
    wait("vector", "qc", ld0)
    op("vector", lambda e: e.tensor_scalar(bifis[:], bifib[:], 1.0 / 15.0, None, ALU.mult))
    vbif = op("vector", lambda e: e.tensor_scalar(biffs[:], biffb[:], 1.0 / 15.0, None, ALU.mult), "sve")
    gv = 0
    wait("tensor", "qc", ld0)
    wait("tensor", "sve", vini)
    for h in range(NH):
        p = PSF[h % 2]
        if h >= 2:
            wait("tensor", "sve", gv - 1)
        tv = op("tensor", lambda e, h=h, p=p: e.matmul(
            p[:, 0:256], onesf[0:1, :], gmr[0:1, h * 256:h * 256 + 256],
            start=True, stop=True), "ste")
        wait("vector", "ste", tv)
        gv = op("vector", lambda e, h=h, p=p: e.tensor_copy(
            gab[:, h * 256:h * 256 + 256], p[:, 0:256]), "sve")

    PH(1)
    wait("scalar", "qsc", csem)
    scs = op("scalar", lambda e: e.activation(scT[:], scT[:], AF.Silu), "sse")
    mod_last = [0] * NKC
    for kk in range(NKC):
        so = (kk % 2) * 3072
        if kk >= 2:
            wait("gpsimd", "ste", mod_last[kk - 2])
        wl = dma("gpsimd", wbuf[:, so:so + 3072],
                 wmod_sl[kk * 128:kk * 128 + 128, :], "qw0" if kk % 2 == 0 else "qw1")
        wait("tensor", "qw0" if kk % 2 == 0 else "qw1", wl)
        if kk == 0:
            wait("tensor", "sse", scs)
            wait("tensor", "sve", gv)
        mv = 0
        for nb in range(6):
            mv = op("tensor", lambda e, kk=kk, nb=nb, so=so: e.matmul(
                PSF[nb][0:1, :], scT[:, kk:kk + 1],
                wbuf[:, so + nb * 512: so + nb * 512 + 512],
                start=(kk == 0), stop=(kk == NKC - 1)), "ste")
        mod_last[kk] = mv
    wait("vector", "ste", mod_last[NKC - 1])
    mev = 0
    for nb in range(6):
        r = (nb // 2) * 32
        co = (nb % 2) * 512
        mev = op("vector", lambda e, nb=nb, r=r, co=co: e.tensor_copy(
            scrA[r:r + 1, co:co + 512], PSF[nb][0:1, :]), "sve")
    wait("sync", "sve", mev)
    agi = dma("sync", ag1_in[0, :].rearrange("(a b) -> a b", a=3),
              scrA[0:65:32, 0:1024], "qio")
    wait("gpsimd", "qio", agi)
    cc1 = op("gpsimd", lambda e: e.collective_compute(
        "AllGather", ALU.bypass, ins=[ag1_in[:]], outs=[ag1_out[:]],
        replica_groups=[[0, 1, 2, 3], [4, 5, 6, 7]]), "scc")
    wait("sync", "scc", cc1)
    ag1f = ag1_out.rearrange("a b -> (a b)")
    mvl = 0
    for v6 in range(6):
        mvl = dma("sync", modv[:, v6 * 16:v6 * 16 + 16],
                  ag1f[2048 * v6:2048 * v6 + 2048].rearrange("(c p) -> p c", p=128), "qio",
                  slow=True)
    wait("vector", "qio", mvl)
    op("vector", lambda e: e.tensor_scalar(modv[:, 16:32], modv[:, 16:32], 1.0, None, ALU.add))
    op("vector", lambda e: e.tensor_scalar(modv[:, 64:80], modv[:, 64:80], 1.0, None, ALU.add))
    mgv = 0
    for blk in (2, 5):
        sl = modv[:, blk * 16:blk * 16 + 16]
        op("vector", lambda e, sl=sl: e.tensor_scalar(g16a[:], sl, -1.0, None, ALU.mult))
        op("vector", lambda e, sl=sl: e.tensor_scalar(g16b[:], sl, 0.0, None, ALU.is_ge))
        drain("vector")
        op("vector", lambda e, sl=sl: e.tensor_max(g16d[:], sl, g16a[:]))
        drain("vector")
        op("vector", lambda e: e.tensor_scalar(g16a[:], g16d[:], 0.25, None, ALU.max))
        drain("vector")
        op("vector", lambda e: e.tensor_scalar(g16b[:], g16b[:], 2.0, -1.0, ALU.mult, ALU.add))
        drain("vector")
        mgv = op("vector", lambda e, sl=sl: e.tensor_mul(sl, g16a[:], g16b[:]), "sve")

    def layer_norm(src16, dst16, cacol, sacol, rsem, rval):
        wait("scalar", rsem, rval)
        wait("tensor", rsem, rval)
        wait("tensor", "sve", C["sve"])
        mm_last = [0] * NKC
        for kk in range(NKC):
            so = (kk % 2) * TPC
            if kk >= 2:
                wait("scalar", "ste", mm_last[kk - 2])
            sqd = op("scalar", lambda e, kk=kk, so=so: e.activation(
                xsq[:, so:so + TPC], src16[:, kk * TPC:kk * TPC + TPC], AF.Square), "sse")
            for tb in range(NTB):
                op("tensor", lambda e, kk=kk, tb=tb: e.matmul(
                    [pA, pB][tb][0:1, :], onescb[:, 0:1],
                    src16[:, kk * TPC + tb * 512: kk * TPC + tb * 512 + 512],
                    start=(kk == 0), stop=(kk == NKC - 1)), "ste")
            wait("tensor", "sse", sqd)
            mm2 = 0
            for tb in range(NTB):
                mm2 = op("tensor", lambda e, kk=kk, tb=tb, so=so: e.matmul(
                    [pC, pD][tb][0:1, :], onescb[:, 0:1],
                    xsq[:, so + tb * 512: so + tb * 512 + 512],
                    start=(kk == 0), stop=(kk == NKC - 1)), "ste")
            mm_last[kk] = mm2
        PH(21)
        wait("vector", "ste", mm_last[NKC - 1])
        for tb in range(NTB):
            op("vector", lambda e, tb=tb: e.tensor_scalar(
                lanB[0:1, tb * 512:tb * 512 + 512], [pA, pB][tb][0:1, :],
                1.0 / D, None, ALU.mult))
            op("vector", lambda e, tb=tb: e.tensor_scalar(
                lanA[0:1, tb * 512:tb * 512 + 512], [pC, pD][tb][0:1, :],
                1.0 / D, None, ALU.mult))
        drain("vector")
        op("vector", lambda e: e.tensor_mul(lanC[0:1, 0:TPC],
                                            lanB[0:1, 0:TPC], lanB[0:1, 0:TPC]))
        drain("vector")
        op("vector", lambda e: e.tensor_sub(lanB[32:33, 0:TPC],
                                            lanA[0:1, 0:TPC], lanC[0:1, 0:TPC]))
        drain("vector")
        op("vector", lambda e: e.tensor_scalar(lanB[32:33, 0:TPC],
                                               lanB[32:33, 0:TPC], 1e-5, None, ALU.add))
        drain("vector")
        rvv = opd("vector", lambda e: e.reciprocal(lanA[32:33, 0:TPC],
                                                   lanB[32:33, 0:TPC]), "sve")
        wait("scalar", "sve", rvv)
        rsv = op("scalar", lambda e: e.activation(lanA[0:1, 0:TPC],
                                                  lanA[32:33, 0:TPC], AF.Sqrt), "sse")
        op("vector", lambda e: e.tensor_scalar(lanC[0:1, 0:TPC],
                                               lanB[0:1, 0:TPC], -1.0, None, ALU.mult))
        wait("vector", "sse", rsv)
        drain("vector")
        nmv = op("vector", lambda e: e.tensor_mul(lanB[0:1, 0:TPC],
                                                  lanC[0:1, 0:TPC],
                                                  lanA[0:1, 0:TPC]), "sve")
        PH(22)
        wait("tensor", "sve", nmv)
        bc1 = 0
        for tb in range(NTB):
            bc1 = op("tensor", lambda e, tb=tb: e.matmul(
                [pA, pB][tb][:, :], onesf[0:1, :],
                lanA[0:1, tb * 512:tb * 512 + 512], start=True, stop=True), "ste")
            bc1 = op("tensor", lambda e, tb=tb: e.matmul(
                [pC, pD][tb][:, :], onesf[0:1, :],
                lanB[0:1, tb * 512:tb * 512 + 512], start=True, stop=True), "ste")
        wait("vector", "ste", bc1)
        PH(23)
        outv = 0
        for kk in range(NKC):
            for tb in range(NTB):
                so = tb * 512
                op("vector", lambda e, kk=kk, tb=tb, so=so: e.tensor_tensor(
                    scrC[:, so:so + 512],
                    src16[:, kk * TPC + so: kk * TPC + so + 512], [pA, pB][tb][:, :], ALU.mult))
            for tb in range(NTB):
                so = tb * 512
                op("vector", lambda e, kk=kk, tb=tb, so=so: e.tensor_tensor(
                    scrC[:, so:so + 512], scrC[:, so:so + 512], [pC, pD][tb][:, :], ALU.add))
            drain("vector")
            outv = op("vector", lambda e, kk=kk: e.tensor_scalar(
                dst16[:, kk * TPC:kk * TPC + TPC], scrC[:, 0:TPC],
                cacol[:, kk:kk + 1], sacol[:, kk:kk + 1], ALU.mult, ALU.add), "sve")
            drain("vector")
        return outv

    PH(2)
    wait("vector", "sve", mgv)
    h1v = layer_norm(ogb, h16, modv[:, 16:32], modv[:, 0:16], "qxb", xbsem)

    def h1sl(kk, tb, width=512):
        return h16[:, kk * TPC + tb * 512: kk * TPC + tb * 512 + width]

    PH(3)
    fg_on()
    blocks = [("q", h, wqT[:, h, :]) for h in range(NH)]
    blocks += [("k", h, wkT[:, h, :]) for h in range(NH)]
    blocks += [("og", ob, wogT[:, ob, :]) for ob in range(NKC)]
    ring = [None] * 4
    evpar = [0, 0]
    for bi, (kind, idx, wsl) in enumerate(blocks):
        so = (bi % 4) * 2048
        if ring[bi % 4] is not None:
            wait("gpsimd", "ste", ring[bi % 4])
        lv = dma("gpsimd", wbuf[:, so:so + 2048], wsl, "qw%d" % (bi % 4))
        wait("tensor", "qw%d" % (bi % 4), lv)
        if bi == 0:
            wait("tensor", "sve", h1v)
        if bi >= 1:
            wait("tensor", "sve" if kind != "og" or blocks[bi - 1][0] != "og" else "sse",
                 evpar[0] if False else None)
        mmv = 0
        for tb in range(NTB):
            p = [pE, pF][tb]
            if bi >= 1:
                wait("tensor", evgrd[tb][0], evgrd[tb][1])
            for kk in range(NKC):
                mmv = op("tensor", lambda e, so=so, kk=kk, tb=tb, p=p: e.matmul(
                    p[:, :], wbuf[:, so + kk * 128: so + kk * 128 + 128],
                    h1sl(kk, tb), start=(kk == 0), stop=(kk == NKC - 1)), "ste")
        ring[bi % 4] = mmv
        wait("vector" if kind != "og" else "scalar", "ste", mmv)
        evgrd = {}
        for tb in range(NTB):
            p = [pE, pF][tb]
            if kind == "q":
                ev = op("vector", lambda e, idx=idx, tb=tb, p=p: e.tensor_scalar(
                    bqk[:, idx * TPC + tb * 512: idx * TPC + tb * 512 + 512],
                    p[:, :], QSCALE, None, ALU.mult), "sve")
                evgrd[tb] = ("sve", ev)
            elif kind == "k":
                ev = op("vector", lambda e, idx=idx, tb=tb, p=p: e.tensor_copy(
                    bqk[:, (NH + idx) * TPC + tb * 512: (NH + idx) * TPC + tb * 512 + 512],
                    p[:, :]), "sve")
                evgrd[tb] = ("sve", ev)
            else:
                ev = op("scalar", lambda e, idx=idx, tb=tb, p=p: e.activation(
                    ogb[:, idx * TPC + tb * 512: idx * TPC + tb * 512 + 512],
                    p[:, :], AF.Sigmoid), "sse")
                evgrd[tb] = ("sse", ev)
    og_done = evgrd[NTB - 1][1]

    PH(35)
    wifl = dma("gpsimd", wsm[:, 0:256], wifT[:, 0, :], "qw3")
    wait("tensor", "qw3", wifl)
    ifm = 0
    for tb in range(NTB):
        for half in range(2):
            p = ([pA, pB] if half == 0 else [pC, pD])[tb]
            for kk in range(NKC):
                ifm = op("tensor", lambda e, kk=kk, tb=tb, p=p, half=half: e.matmul(
                    p[0:8, :], wsm[:, kk * 16 + half * 8: kk * 16 + half * 8 + 8],
                    h1sl(kk, tb), start=(kk == 0), stop=(kk == NKC - 1)), "ste")
    wait("scalar", "ste", ifm)
    wait("scalar", "sve", vbif)
    for tb in range(NTB):
        op("scalar", lambda e, tb=tb: e.activation(
            lanC[64:72, tb * 512:tb * 512 + 512], [pA, pB][tb][0:8, :],
            AF.Tanh, bias=bifis[:, 0:1], scale=1.0 / 15.0))
        op("scalar", lambda e, tb=tb: e.activation(
            lanC[32:40, tb * 512:tb * 512 + 512], [pC, pD][tb][0:8, :],
            AF.Tanh, bias=biffs[:, 0:1], scale=1.0 / 15.0))
    tg = op("scalar", lambda e: e.activation(gc1[0:8, 0:1], bifis[:, 0:1], AF.Square), "sse")
    wait("vector", "sse", tg)
    op("vector", lambda e: e.tensor_scalar(
        lanC[64:72, 0:TPC], lanC[64:72, 0:TPC], 15.0, None, ALU.mult))
    ifv = op("vector", lambda e: e.tensor_scalar(
        lanC[32:40, 0:TPC], lanC[32:40, 0:TPC], 15.0, None, ALU.mult), "sve")
    drain("vector")

    PH(36)
    wait("vector", "sve", vini)
    wait("gpsimd", "ste", C["ste"])
    vld = [0] * NH
    vld[0] = dma("gpsimd", wbuf[:, 0:4096], wvT[:, 0, :], "qw0")
    vld[1] = dma("gpsimd", wbuf[:, 4096:8192], wvT[:, 1, :], "qw1")
    vev = {0: 0, 1: 0}
    for h in range(NH):
        so = (h % 2) * 4096
        wait("tensor", "qw%d" % (h % 2), vld[h])
        lastmm = 0
        for tt in range(NCH):
            p = [pC, pD][tt % 2]
            if tt >= 2 or h > 0:
                wait("tensor", "sve", vev[tt % 2])
            mmv = 0
            for kk in range(NKC):
                mmv = op("tensor", lambda e, h=h, tt=tt, kk=kk, so=so, p=p: e.matmul(
                    p[:, 0:256], h16[:, kk * TPC + tt * 128: kk * TPC + tt * 128 + 128],
                    wbuf[:, so + kk * 256: so + kk * 256 + 256],
                    start=(kk == 0), stop=(kk == NKC - 1)), "ste")
            wait("vector", "ste", mmv)
            vev[tt % 2] = op("vector", lambda e, h=h, tt=tt, p=p: e.tensor_copy(
                v1[:, tt * 2056 + 257 * h: tt * 2056 + 257 * h + 256], p[:, 0:256]), "sve")
            lastmm = mmv
        if h + 2 < NH:
            wait("gpsimd", "ste", lastmm)
            vld[h + 2] = dma("gpsimd", wbuf[:, so:so + 4096], wvT[:, h + 2, :], "qw%d" % (h % 2))
    v_done = vev[(NCH - 1) % 2]

    PH(4)
    barrier()
    wait("scalar", "sve", ifv)
    op("scalar", lambda e: e.activation(
        lanC[96:104, 0:TPC], lanC[32:40, 0:TPC], AF.Sigmoid))
    drain("scalar")
    flv = op("scalar", lambda e: e.activation(
        lanA[64:72, 0:TPC], lanC[96:104, 0:TPC], AF.Ln), "sse")
    wait("vector", "sse", flv)
    op("vector", lambda e: e.memset(lanB[64:72, 0:TPC], 0.0))
    drain("vector")
    op("vector", lambda e: e.tensor_tensor_scan(
        lanB[96:104, 0:TPC], lanA[64:72, 0:TPC],
        lanB[64:72, 0:TPC], 0.0, ALU.add, ALU.add))
    drain("vector")
    op("vector", lambda e: e.tensor_tensor_scan(
        lanB[64:72, 0:TPC], lanA[64:72, 0:TPC],
        lanC[64:72, 0:TPC], NEG, ALU.add, ALU.max))
    drain("vector")
    op("vector", lambda e: e.tensor_copy(lanA[64:72, 0:TPC], lanB[96:104, 0:TPC]))
    drain("vector")
    op("vector", lambda e: e.tensor_sub(lanB[32:40, 0:TPC],
                                        lanC[64:72, 0:TPC], lanA[64:72, 0:TPC]))
    op("vector", lambda e: e.tensor_sub(lanA[32:40, 0:TPC],
                                        lanA[64:72, 0:TPC], lanB[64:72, 0:TPC]))
    drain("vector")
    e32p = 0
    for cc_ in range(NCH):
        sl = slice(cc_ * 128, cc_ * 128 + 128)
        bc0 = lanA[64:72, cc_ * 128 - 1:cc_ * 128] if cc_ > 0 else None
        bce = lanA[64:72, cc_ * 128 + 127:cc_ * 128 + 128]
        if bc0 is not None:
            op("vector", lambda e, sl=sl, bc0=bc0: e.tensor_scalar(
                lanA[0:8, sl], lanB[32:40, sl], bc0, None, ALU.add))
            op("vector", lambda e, sl=sl, bc0=bc0: e.tensor_scalar(
                lanC[0:8, sl], lanA[32:40, sl], bc0, None, ALU.subtract))
            op("vector", lambda e, cc_=cc_, bce=bce, bc0=bc0: e.tensor_scalar(
                gc2[0:8, 0:1], bce, bc0, None, ALU.subtract))
        else:
            op("vector", lambda e, sl=sl: e.tensor_copy(lanA[0:8, sl], lanB[32:40, sl]))
            op("vector", lambda e, sl=sl: e.tensor_copy(lanC[0:8, sl], lanA[32:40, sl]))
            op("vector", lambda e, bce=bce: e.tensor_copy(gc2[0:8, 0:1], bce))
        op("vector", lambda e, sl=sl, bce=bce: e.tensor_scalar(
            lanB[0:8, sl], lanB[32:40, sl], bce, None, ALU.add))
        drain("vector")
        e32p = op("vector", lambda e, sl=sl: e.tensor_scalar(
            lanC[32:40, sl], lanB[32:40, sl], 0.0, gc2[0:8, 0:1], ALU.mult, ALU.add), "sve")
        drain("vector")
    wait("scalar", "sve", e32p)
    for seg, base in [(lanA, 0), (lanB, 0), (lanC, 0)]:
        op("scalar", lambda e, seg=seg, base=base: e.activation(
            seg[base:base + 8, 0:TPC], seg[base:base + 8, 0:TPC], AF.Exp))
        drain("scalar")
    e32e = op("scalar", lambda e: e.activation(
        lanC[32:40, 0:TPC], lanC[32:40, 0:TPC], AF.Exp), "sse")
    wait("tensor", "sse", e32e)
    etv = 0
    esrc = [(lanA, 0), (lanB, 0), (lanC, 0), (lanC, 32)]
    for cc_ in range(NCH):
        for qi, (seg, base) in enumerate(esrc):
            if cc_ + qi >= 1:
                wait("tensor", "sve", etv)
            tv = op("tensor", lambda e, cc_=cc_, seg=seg, base=base: e.transpose(
                pG[0:128, 0:8], seg[base:base + 8, cc_ * 128:cc_ * 128 + 128],
                idf[base:base + 8, base:base + 8]), "ste")
            wait("vector", "ste", tv)
            etv = op("vector", lambda e, cc_=cc_, qi=qi: e.tensor_copy(
                e32t[:, cc_ * 32 + qi * 8:cc_ * 32 + qi * 8 + 8], pG[:, 0:8]), "sve")

    PH(5)
    barrier()
    wait("tensor", "sve", C["sve"])
    lastC16 = [C["sve"]] * NH
    gA3 = {0: 0, 1: 0}
    gK1 = {0: 0, 1: 0}
    gHq = {0: 0, 1: 0}
    gCb = {0: 0, 1: 0}
    for cc_ in range(NCH):
        for h in range(NH):
            par = (cc_ * NH + h) % 2
            pS = [pA, pB][par]
            pH = [pC, pD][par]
            pCa = [pE, pF][par]
            qsl = bqk[:, h * TPC + cc_ * 128: h * TPC + cc_ * 128 + 128]
            ksl = bqk[:, (NH + h) * TPC + cc_ * 128: (NH + h) * TPC + cc_ * 128 + 128]
            vsl = v1[:, cc_ * 2056 + 257 * h: cc_ * 2056 + 257 * h + 257]
            colfac = e32t[:, cc_ * 32 + h: cc_ * 32 + h + 1]
            kfac = e32t[:, cc_ * 32 + 8 + h: cc_ * 32 + 8 + h + 1]
            rowsc = e32t[:, cc_ * 32 + 16 + h: cc_ * 32 + 16 + h + 1]
            dcy = e32t[:, cc_ * 32 + 24 + h: cc_ * 32 + 24 + h + 1]
            wait("tensor", "sve", gA3[par])
            smm = op("tensor", lambda e, ksl=ksl, qsl=qsl, pS=pS: e.matmul(
                pS[:, 0:128], ksl, qsl, start=True, stop=True), "ste")
            wait("tensor", "sve", gK1[par])
            tmm = op("tensor", lambda e, ksl=ksl, par=par: e.transpose(
                pT[:, par * 512:par * 512 + 128], ksl, idb[:, :]), "ste")
            wait("vector", "ste", tmm)
            gA3[par] = op("vector", lambda e, pS=pS, colfac=colfac, par=par: e.scalar_tensor_tensor(
                a3s[:, par * 128:par * 128 + 128], pS[:, 0:128], colfac, mkb[:, :],
                ALU.mult, ALU.mult), "sve")
            gK1[par] = op("vector", lambda e, par=par, kfac=kfac: e.tensor_scalar(
                k1s[:, par * 128:par * 128 + 128], pT[:, par * 512:par * 512 + 128],
                kfac, None, ALU.mult), "sve")
            wait("tensor", "sve", gA3[par])
            wait("tensor", "sve", gHq[par])
            op("tensor", lambda e, par=par, vsl=vsl, pH=pH: e.matmul(
                pH[:, 0:257], a3s[:, par * 128:par * 128 + 128], vsl,
                start=True, stop=False), "ste")
            wait("tensor", "sve", lastC16[h])
            hmm = op("tensor", lambda e, qsl=qsl, h=h, pH=pH: e.matmul(
                pH[:, 0:257], qsl, c16[:, 257 * h:257 * h + 257],
                start=False, stop=True), "ste")
            wait("tensor", "sve", gCb[par])
            cmm = op("tensor", lambda e, par=par, vsl=vsl, pCa=pCa: e.matmul(
                pCa[:, 0:257], k1s[:, par * 128:par * 128 + 128], vsl,
                start=True, stop=True), "ste")
            wait("vector", "ste", hmm)
            gHq[par] = op("vector", lambda e, cc_=cc_, h=h, pH=pH, rowsc=rowsc: e.tensor_scalar(
                h16[:, cc_ * 2056 + 257 * h: cc_ * 2056 + 257 * h + 257],
                pH[:, 0:257], rowsc, None, ALU.mult), "sve")
            wait("vector", "ste", cmm)
            gCb[par] = op("vector", lambda e, h=h, dcy=dcy, pCa=pCa: e.scalar_tensor_tensor(
                cst[:, 257 * h:257 * h + 257], cst[:, 257 * h:257 * h + 257],
                dcy, pCa[:, 0:257], ALU.mult, ALU.add), "sve")
            drain("vector")
            lastC16[h] = op("vector", lambda e, h=h: e.tensor_copy(
                c16[:, 257 * h:257 * h + 257], cst[:, 257 * h:257 * h + 257]), "sve")
    passa_v = C["sve"]

    PH(6)
    barrier()
    wait("sync", "sve", passa_v)
    dma("sync", ag2_in[0, 0:NU].rearrange("(p f) -> p f", p=128), cst[:], "qio")
    dma("sync", ag2_in[0, NU:NU + 8].rearrange("(p f) -> p f", p=8),
        lanA[64:72, TPC - 1:TPC], "qio")
    ag2v = dma("sync", ag2_in[0, NU + 8:NU + 16].rearrange("(p f) -> p f", p=8),
               lanB[64:72, TPC - 1:TPC], "qio")
    wait("gpsimd", "qio", ag2v)
    cc2 = op("gpsimd", lambda e: e.collective_compute(
        "AllGather", ALU.bypass, ins=[ag2_in[:]], outs=[ag2_out[:]],
        replica_groups=[[0, 1, 2, 3], [4, 5, 6, 7]]), "scc")
    wait("sync", "scc", cc2)
    uld = 0
    for j in range(3):
        uld = dma("sync", uvw[:, j * 2056:(j + 1) * 2056],
                  ag2_out[j, 0:NU].rearrange("(p f) -> p f", p=128), "qio")
        uld = dma("sync", lanC[0:8, 2 * j:2 * j + 1],
                  ag2_out[j, NU:NU + 8].rearrange("(p f) -> p f", p=8), "qio")
        uld = dma("sync", lanC[0:8, 2 * j + 1:2 * j + 2],
                  ag2_out[j, NU + 8:NU + 16].rearrange("(p f) -> p f", p=8), "qio")

    PH(7)
    barrier()
    wait("vector", "qio", uld)
    op("vector", lambda e: e.memset(cst[:], 0.0))
    op("vector", lambda e: e.memset(lanC[0:8, 6:7], 0.0))
    for j in range(3):
        op("vector", lambda e, j=j: e.tensor_mul(lanC[0:8, 9:10], lanC[0:8, 2 * j + 1:2 * j + 2], ccf[:, j:j + 1]))
        op("vector", lambda e, j=j: e.tensor_mul(lanC[0:8, 10:11], lanC[0:8, 2 * j:2 * j + 1], ccf[:, j:j + 1]))
        drain("vector")
        op("vector", lambda e, j=j: e.tensor_add(lanC[0:8, 9:10], lanC[0:8, 9:10], ccn[:, j:j + 1]))
        op("vector", lambda e: e.tensor_add(lanC[0:8, 11:12], lanC[0:8, 10:11], lanC[0:8, 6:7]))
        drain("vector")
        op("vector", lambda e: e.tensor_max(lanC[0:8, 12:13], lanC[0:8, 11:12], lanC[0:8, 9:10]))
        drain("vector")
        op("vector", lambda e: e.tensor_sub(lanC[0:8, 13:14], lanC[0:8, 11:12], lanC[0:8, 12:13]))
        op("vector", lambda e: e.tensor_scalar(lanC[0:8, 14:15], lanC[0:8, 12:13], -1.0, None, ALU.mult))
        drain("vector")
        wex = op("vector", lambda e: e.tensor_copy(lanC[0:8, 6:7], lanC[0:8, 12:13]), "sve")
        wait("scalar", "sve", wex)
        op("scalar", lambda e: e.activation(lanC[0:8, 13:14], lanC[0:8, 13:14], AF.Exp))
        drain("scalar")
        sxv = op("scalar", lambda e: e.activation(lanC[0:8, 14:15], lanC[0:8, 14:15], AF.Exp), "sse")
        wait("vector", "sse", sxv)
        wnv = op("vector", lambda e, j=j: e.tensor_mul(lanC[0:8, 15:16], lanC[0:8, 14:15], ccf[:, j:j + 1]), "sve")
        wait("tensor", "sve", wnv)
        op("tensor", lambda e: e.transpose(
            pG[0:1, 0:8], lanC[0:8, 13:14], idf[0:8, 0:8]), "ste")
        t1v = op("tensor", lambda e: e.transpose(
            pG[0:1, 8:16], lanC[0:8, 15:16], idf[0:8, 0:8]), "ste")
        wait("vector", "ste", t1v)
        op("vector", lambda e: e.tensor_copy(wrowp[:, :], pG[0:1, 0:8]))
        t2v = op("vector", lambda e: e.tensor_copy(wrow1[:, :], pG[0:1, 8:16]), "sve")
        wait("tensor", "sve", t2v)
        b2 = op("tensor", lambda e: e.matmul(pG[:, 0:8], onesf[0:1, :], wrowp[0:1, :],
                                             start=True, stop=True), "ste")
        b2 = op("tensor", lambda e: e.matmul(pG[:, 8:16], onesf[0:1, :], wrow1[0:1, :],
                                             start=True, stop=True), "ste")
        wait("vector", "ste", b2)
        op("vector", lambda e: e.tensor_copy(wbcs[:, :], pG[:, 0:16]))
        drain("vector")
        for h in range(NH):
            op("vector", lambda e, j=j, h=h: e.tensor_scalar(
                scrA[:, 0:257],
                uvw[:, j * 2056 + 257 * h: j * 2056 + 257 * h + 257],
                wbcs[:, 8 + h:8 + h + 1], None, ALU.mult))
            drain("vector")
            op("vector", lambda e, h=h: e.scalar_tensor_tensor(
                cst[:, 257 * h:257 * h + 257], cst[:, 257 * h:257 * h + 257],
                wbcs[:, h:h + 1], scrA[:, 0:257], ALU.mult, ALU.add))
            drain("vector")
    p16v = op("vector", lambda e: e.tensor_copy(c16[:], cst[:]), "sve")

    PH(8)
    barrier()
    op("vector", lambda e: e.tensor_scalar(
        lanC[64:72, 0:TPC], lanA[64:72, 0:TPC], lanC[0:8, 6:7], None, ALU.add))
    drain("vector")
    op("vector", lambda e: e.tensor_max(
        lanA[64:72, 0:TPC], lanB[64:72, 0:TPC], lanC[64:72, 0:TPC]))
    drain("vector")
    op("vector", lambda e: e.tensor_sub(
        lanA[0:8, 0:TPC], lanB[64:72, 0:TPC], lanA[64:72, 0:TPC]))
    op("vector", lambda e: e.tensor_sub(
        lanB[0:8, 0:TPC], lanC[64:72, 0:TPC], lanA[64:72, 0:TPC]))
    e24v = op("vector", lambda e: e.tensor_scalar(
        lanC[0:8, 0:TPC], lanA[64:72, 0:TPC], -1.0, None, ALU.mult), "sve")
    wait("scalar", "sve", e24v)
    op("scalar", lambda e: e.activation(lanA[0:8, 0:TPC], lanA[0:8, 0:TPC], AF.Exp))
    drain("scalar")
    op("scalar", lambda e: e.activation(lanB[0:8, 0:TPC], lanB[0:8, 0:TPC], AF.Exp))
    drain("scalar")
    e24e = op("scalar", lambda e: e.activation(lanC[0:8, 0:TPC], lanC[0:8, 0:TPC], AF.Exp), "sse")
    wait("tensor", "sse", e24e)
    etv = 0
    esrc24 = [(lanA, 0), (lanB, 0), (lanC, 0)]
    for cc_ in range(NCH):
        for qi, (seg, base) in enumerate(esrc24):
            if cc_ + qi >= 1:
                wait("tensor", "sve", etv)
            tv = op("tensor", lambda e, cc_=cc_, seg=seg, base=base: e.transpose(
                pG[0:128, 0:8], seg[base:base + 8, cc_ * 128:cc_ * 128 + 128],
                idf[base:base + 8, base:base + 8]), "ste")
            wait("vector", "ste", tv)
            op("vector", lambda e, cc_=cc_, qi=qi: e.tensor_copy(
                e24t[:, cc_ * 24 + qi * 8:cc_ * 24 + qi * 8 + 8], pG[:, 0:8]))
            drain("vector")
            etv = op("vector", lambda e, cc_=cc_, qi=qi: e.tensor_copy(
                e24b[:, cc_ * 24 + qi * 8:cc_ * 24 + qi * 8 + 8],
                e24t[:, cc_ * 24 + qi * 8:cc_ * 24 + qi * 8 + 8]), "sve")

    PH(9)
    fg_off()
    wait("tensor", "sve", C["sve"])
    gH2 = {0: 0, 1: 0}
    gTo = {0: 0, 1: 0}
    for cc_ in range(NCH):
        tslab = op("vector", lambda e, cc_=cc_: e.tensor_tensor(
            t1s[:, 0:2056].rearrange("p (h f) -> p h f", f=257),
            h16[:, cc_ * 2056: cc_ * 2056 + 2056].rearrange("p (h f) -> p h f", f=257),
            e24b[:, cc_ * 24: cc_ * 24 + 8].unsqueeze(2).to_broadcast([128, 8, 257]),
            ALU.mult), "sve")
        drain("vector")
        for h in range(NH):
            par = h % 2
            p = [pA, pB][par]
            qsl = bqk[:, h * TPC + cc_ * 128: h * TPC + cc_ * 128 + 128]
            wait("tensor", "sve", gH2[par])
            if h == 0:
                wait("tensor", "sve", tslab)
            h2mm = op("tensor", lambda e, qsl=qsl, h=h, p=p: e.matmul(
                p[:, 0:257], qsl, c16[:, 257 * h:257 * h + 257], start=True, stop=True), "ste")
            wait("vector", "ste", h2mm)
            gH2[par] = op("vector", lambda e, h=h, p=p, cc_=cc_: e.scalar_tensor_tensor(
                cst[:, 257 * h: 257 * h + 257], p[:, 0:257],
                e24t[:, cc_ * 24 + 8 + h: cc_ * 24 + 8 + h + 1],
                t1s[:, 257 * h: 257 * h + 257], ALU.mult, ALU.add), "sve")
        drain("vector")
        op("vector", lambda e: e.tensor_scalar(
            sq8[:, 40:48], cst[:, 256:2056:257], -1.0, None, ALU.mult))
        drain("vector")
        op("vector", lambda e: e.tensor_max(
            sq8[:, 0:8], sq8[:, 40:48], cst[:, 256:2056:257]))
        drain("vector")
        op("vector", lambda e, cc_=cc_: e.tensor_max(
            sq8[:, 0:8], sq8[:, 0:8], e24t[:, cc_ * 24 + 16: cc_ * 24 + 24]))
        drain("vector")
        op("vector", lambda e: e.tensor_scalar(sq8[:, 0:8], sq8[:, 0:8], 1e-6, None, ALU.add))
        drain("vector")
        op("vector", lambda e: e.reciprocal(sq8[:, 8:16], sq8[:, 0:8]))
        drain("vector")
        hnv = op("vector", lambda e: e.tensor_tensor(
            t1s[:, 0:2048].rearrange("p (h f) -> p h f", f=256),
            cst[:, 0:2056].rearrange("p (h f) -> p h f", f=257)[:, :, 0:256],
            sq8[:, 8:16].unsqueeze(2).to_broadcast([128, 8, 256]), ALU.mult), "sve")
        drain("vector")
        op("vector", lambda e: e.reduce_sum(
            sq8[:, 16:24], t1s[:, 0:2048].rearrange("p (h f) -> p h f", f=256), axis=AX.X))
        wait("scalar", "sve", hnv)
        sqv = op("scalar", lambda e: e.activation(cst[:, 0:2048], t1s[:, 0:2048], AF.Square), "sse")
        wait("vector", "sse", sqv)
        op("vector", lambda e: e.reduce_sum(
            sq8[:, 24:32], cst[:, 0:2048].rearrange("p (h f) -> p h f", f=256), axis=AX.X))
        drain("vector")
        op("vector", lambda e: e.tensor_scalar(sq8[:, 16:24], sq8[:, 16:24], -1.0 / 256, None, ALU.mult))
        op("vector", lambda e: e.tensor_scalar(sq8[:, 24:32], sq8[:, 24:32], 1.0 / 256, None, ALU.mult))
        drain("vector")
        op("vector", lambda e: e.tensor_mul(sq8[:, 32:40], sq8[:, 16:24], sq8[:, 16:24]))
        drain("vector")
        op("vector", lambda e: e.tensor_sub(sq8[:, 24:32], sq8[:, 24:32], sq8[:, 32:40]))
        drain("vector")
        op("vector", lambda e: e.tensor_scalar(sq8[:, 24:32], sq8[:, 24:32], 1e-6, None, ALU.add))
        drain("vector")
        rvv = opd("vector", lambda e: e.reciprocal(sq8[:, 32:40], sq8[:, 24:32]), "sve")
        wait("scalar", "sve", rvv)
        rsv = op("scalar", lambda e: e.activation(sq8[:, 40:48], sq8[:, 32:40], AF.Sqrt), "sse")
        wait("vector", "sse", rsv)
        op("vector", lambda e: e.tensor_tensor(
            cst[:, 0:2048].rearrange("p (h f) -> p h f", f=256),
            t1s[:, 0:2048].rearrange("p (h f) -> p h f", f=256),
            sq8[:, 16:24].unsqueeze(2).to_broadcast([128, 8, 256]), ALU.add))
        drain("vector")
        op("vector", lambda e: e.tensor_tensor(
            cst[:, 0:2048].rearrange("p (h f) -> p h f", f=256),
            cst[:, 0:2048].rearrange("p (h f) -> p h f", f=256),
            sq8[:, 40:48].unsqueeze(2).to_broadcast([128, 8, 256]), ALU.mult))
        drain("vector")
        hgv = op("vector", lambda e: e.tensor_mul(hob[:, 0:2048], cst[:, 0:2048], gab[:, :]), "sve")
        wait("tensor", "sve", hgv)
        for vck in range(NKC):
            par = vck % 2
            wait("tensor", "sve", gTo[par])
            tv = op("tensor", lambda e, vck=vck, par=par: e.transpose(
                pT[:, par * 512:par * 512 + 128], hob[:, vck * 128:vck * 128 + 128],
                idb[:, :]), "ste")
            wait("vector", "ste", tv)
            gTo[par] = op("vector", lambda e, vck=vck, cc_=cc_, par=par: e.tensor_tensor(
                v1[:, vck * TPC + cc_ * 128: vck * TPC + cc_ * 128 + 128],
                pT[:, par * 512:par * 512 + 128],
                ogb[:, vck * TPC + cc_ * 128: vck * TPC + cc_ * 128 + 128], ALU.mult), "sve")
    passb_v = C["sve"]

    PH(10)
    fg_on()
    ring = [None] * 4
    xgrd = {0: 0, 1: 0}
    stgrd = {0: 0, 1: 0}
    for db in range(NKC):
        so = (db % 4) * 2048
        if ring[db % 4] is not None:
            wait("gpsimd", "ste", ring[db % 4])
        lv = dma("gpsimd", wbuf[:, so:so + 2048], woutT[:, db, :], "qw%d" % (db % 4))
        wait("tensor", "qw%d" % (db % 4), lv)
        if db == 0:
            wait("tensor", "sve", passb_v)
        for tb in range(NTB):
            p = [pC, pD][tb]
            sl2 = ((db * NTB + tb) % 2) * 512
            wait("sync", "sve", xgrd[(db * NTB + tb) % 2])
            xl = dma("sync", xre[:, sl2:sl2 + 512],
                     xT[db * 128: db * 128 + 128, tb * 512: tb * 512 + 512],
                     "qxr%d" % ((db * NTB + tb) % 2))
            wait("tensor", "sve", stgrd[tb])
            mmv = 0
            for kk in range(NKC):
                mmv = op("tensor", lambda e, so=so, kk=kk, tb=tb, p=p: e.matmul(
                    p[:, :], wbuf[:, so + kk * 128: so + kk * 128 + 128],
                    v1[:, kk * TPC + tb * 512: kk * TPC + tb * 512 + 512],
                    start=(kk == 0), stop=(kk == NKC - 1)), "ste")
            if tb == NTB - 1:
                ring[db % 4] = mmv
            wait("vector", "ste", mmv)
            wait("vector", "qxr%d" % ((db * NTB + tb) % 2), xl)
            wait("vector", "qst%d" % ((db * NTB + tb) % 2),
                 C["qst%d" % ((db * NTB + tb) % 2)])
            ev = op("vector", lambda e, db=db, p=p, sl2=sl2: e.scalar_tensor_tensor(
                x2f[:, sl2:sl2 + 512], p[:, :], modv[:, 32 + db:32 + db + 1],
                xre[:, sl2:sl2 + 512], ALU.mult, ALU.add), "sve")
            stgrd[tb] = ev
            wait("scalar", "sve", ev)
            dma("scalar", yT[db * 128:db * 128 + 128, tb * 512:tb * 512 + 512],
                x2f[:, sl2:sl2 + 512], "qst%d" % ((db * NTB + tb) % 2))
            drain("vector")
            x2v = op("vector", lambda e, db=db, tb=tb, sl2=sl2: e.tensor_copy(
                bqk[:, db * TPC + tb * 512: db * TPC + tb * 512 + 512],
                x2f[:, sl2:sl2 + 512]), "sve")
            xgrd[(db * NTB + tb) % 2] = x2v
    x2_qst = (C["qst0"], C["qst1"])

    PH(11)
    h2v = layer_norm(bqk, bqk, modv[:, 64:80], modv[:, 48:64], "sve", x2v)

    def h2sl(kk, tb, width=512):
        return bqk[:, kk * TPC + tb * 512: kk * TPC + tb * 512 + width]

    PH(12)
    gld = dma("gpsimd", wsm[:, 256:512], gateT[:, 0, :], "qw3")
    wait("tensor", "qw3", gld)
    wait("tensor", "sve", h2v)
    gmm = 0
    for tb in range(NTB):
        p = [pE, pF][tb]
        for kk in range(NKC):
            gmm = op("tensor", lambda e, kk=kk, tb=tb, p=p: e.matmul(
                p[0:16, :], wsm[:, 256 + kk * 16: 256 + kk * 16 + 16],
                h2sl(kk, tb), start=(kk == 0), stop=(kk == NKC - 1)), "ste")
    wait("vector", "ste", gmm)
    glv = 0
    for tb in range(NTB):
        glv = op("vector", lambda e, tb=tb: e.tensor_copy(
            lanC[0:16, tb * 512:tb * 512 + 512], [pE, pF][tb][0:16, :]), "sve")
    cwv = glv
    for tt in range(NCH):
        wait("tensor", "sve", cwv)
        ttv = op("tensor", lambda e, tt=tt: e.transpose(
            pG[0:128, 0:16], lanC[0:16, tt * 128:tt * 128 + 128],
            idf[0:16, 0:16]), "ste")
        wait("vector", "ste", ttv)
        op("vector", lambda e: e.tensor_copy(g16a[:], pG[:, 0:16]))
        drain("vector")
        nmx = op("vector", lambda e: e.reduce_max(gc1[:], g16a[:], axis=AX.X, negate=True), "sve")
        wait("scalar", "sve", nmx)
        exv = op("scalar", lambda e: e.activation(g16b[:], g16a[:], AF.Exp,
                                                  bias=gc1[:, 0:1], scale=1.0), "sse")
        wait("vector", "sse", exv)
        op("vector", lambda e: e.reduce_sum(gc2[:], g16b[:], axis=AX.X))
        drain("vector")
        op("vector", lambda e: e.reciprocal(gc1[:], gc2[:]))
        drain("vector")
        op("vector", lambda e: e.tensor_scalar(g16a[:], g16b[:], gc1[:, 0:1], None, ALU.mult))
        drain("vector")
        op("vector", lambda e: e.reduce_max(g4a[:], g16a[:].rearrange("p (g i) -> p g i", i=4), axis=AX.X))
        drain("vector")
        op("vector", lambda e: e.tensor_tensor(
            g16b[:].rearrange("p (g i) -> p g i", i=4),
            g4a[:].unsqueeze(1).to_broadcast([128, 4, 4]),
            g4a[:].unsqueeze(2).to_broadcast([128, 4, 4]), ALU.is_gt))
        op("vector", lambda e: e.tensor_tensor(
            g16d[:].rearrange("p (g i) -> p g i", i=4),
            g4a[:].unsqueeze(1).to_broadcast([128, 4, 4]),
            g4a[:].unsqueeze(2).to_broadcast([128, 4, 4]), ALU.is_equal))
        drain("vector")
        op("vector", lambda e: e.tensor_mul(g16d[:], g16d[:], t16c[:]))
        drain("vector")
        op("vector", lambda e: e.tensor_add(g16b[:], g16b[:], g16d[:]))
        drain("vector")
        op("vector", lambda e: e.reduce_sum(g4a[:], g16b[:].rearrange("p (g i) -> p g i", i=4), axis=AX.X))
        drain("vector")
        op("vector", lambda e: e.tensor_scalar(g4a[:], g4a[:], 2.0, None, ALU.is_lt))
        drain("vector")
        op("vector", lambda e: e.tensor_tensor(
            g16b[:].rearrange("p (g i) -> p g i", i=4),
            g16a[:].rearrange("p (g i) -> p g i", i=4),
            g4a[:].unsqueeze(2).to_broadcast([128, 4, 4]), ALU.mult))
        drain("vector")
        op("vector", lambda e: e.tensor_tensor(
            g256[:].rearrange("p (a b) -> p a b", b=16),
            g16b[:].unsqueeze(1).to_broadcast([128, 16, 16]),
            g16b[:].unsqueeze(2).to_broadcast([128, 16, 16]), ALU.is_gt))
        op("vector", lambda e: e.tensor_tensor(
            g256b[:].rearrange("p (a b) -> p a b", b=16),
            g16b[:].unsqueeze(1).to_broadcast([128, 16, 16]),
            g16b[:].unsqueeze(2).to_broadcast([128, 16, 16]), ALU.is_equal))
        drain("vector")
        op("vector", lambda e: e.tensor_mul(g256b[:], g256b[:], t256c[:]))
        drain("vector")
        op("vector", lambda e: e.tensor_add(g256[:], g256[:], g256b[:]))
        drain("vector")
        op("vector", lambda e: e.reduce_sum(g16d[:], g256[:].rearrange("p (a b) -> p a b", b=16), axis=AX.X))
        drain("vector")
        op("vector", lambda e: e.tensor_scalar(g16d[:], g16d[:], 4.0, None, ALU.is_lt))
        drain("vector")
        op("vector", lambda e: e.tensor_mul(g16b[:], g16b[:], g16d[:]))
        drain("vector")
        op("vector", lambda e: e.reduce_sum(gc2[:], g16b[:], axis=AX.X))
        drain("vector")
        op("vector", lambda e: e.tensor_scalar(gc2[:], gc2[:], 1e-20, None, ALU.add))
        drain("vector")
        op("vector", lambda e: e.reciprocal(gc1[:], gc2[:]))
        drain("vector")
        cwq = op("vector", lambda e: e.tensor_scalar(g16a[:], g16b[:], gc1[:, 0:1], None, ALU.mult), "sve")
        wait("tensor", "sve", cwq)
        tbv = op("tensor", lambda e, tt=tt: e.transpose(
            pG[0:16, 0:128], g16a[:], idf[:, :]), "ste")
        wait("vector", "ste", tbv)
        cwv = op("vector", lambda e, tt=tt: e.tensor_copy(
            cwl16[:, tt * 128:tt * 128 + 128], pG[0:16, 0:128]), "sve")

    PH(13)
    def aslice(s, tb):
        if s < 16:
            t_, base = h16, s * TPC
        elif s < 32:
            t_, base = ogb, (s - 16) * TPC
        else:
            t_, base = v1, (s - 32) * TPC
        return t_[:, base + tb * 512: base + tb * 512 + 512]

    wait("gpsimd", "ste", C["ste"])
    wait("gpsimd", "qst0", x2_qst[0])
    wait("gpsimd", "qst1", x2_qst[1])
    groups = [[0, 1, 2, 3], [4, 5, 6, 7], [8, 9, 10, 11], [12, 13, 14, 15], ["sh"]]
    upfree = [None, None]
    upld_i = 0
    cbe = cwv
    sgrd = {0: 0, 1: 0}
    ugrd = {0: 0, 1: 0}
    dn_guard = 0
    acc_guard = {0: 0, 1: 0}
    for grp in groups:
        wait("gpsimd", "qac0", C["qac0"])
        wait("gpsimd", "qac1", C["qac1"])
        shared = grp[0] == "sh"
        nexp = 1 if shared else len(grp)
        nj = 22 if shared else 11
        for ei in range(nexp):
            eidx = None if shared else grp[ei]
            if not shared:
                wait("sync", "sve", cbe)
                crv = dma("sync", cwrow[0:1, 0:TPC], cwl16[eidx:eidx + 1, :], "qio")
                for tb in range(NTB):
                    wait("tensor", "sve", cbe)
                    wait("tensor", "qio", crv)
                    cbv = op("tensor", lambda e, tb=tb: e.matmul(
                        pG[:, :], onesbf[0:1, :],
                        cwrow[0:1, tb * 512:tb * 512 + 512],
                        start=True, stop=True), "ste")
                    wait("vector", "ste", cbv)
                    cbe = op("vector", lambda e, tb=tb: e.tensor_copy(
                        cwbc[:, tb * 512:tb * 512 + 512], pG[:, :]), "sve")
            gmmv = 0
            for jj in range(nj):
                for half in range(2):
                    jb = jj + half * nj
                    slot = upld_i % 2
                    upld_i += 1
                    if upfree[slot] is not None:
                        wait("gpsimd", "ste", upfree[slot])
                    src = supT[:, jb, :] if shared else upT[eidx, :, jb, :]
                    lv = dma("gpsimd", wbuf[:, slot * 2048:slot * 2048 + 2048], src,
                             "qw%d" % slot)
                    wait("tensor", "qw%d" % slot, lv)
                    wait("tensor", "sse" if half == 0 else "sve",
                         sgrd[0] if half == 0 else ugrd[0])
                    mmv = 0
                    for kk in range(NKC):
                        for tb in range(NTB):
                            p = ([pA, pB] if half == 0 else [pC, pD])[tb]
                            mmv = op("tensor", lambda e, slot=slot, kk=kk, tb=tb, p=p: e.matmul(
                                p[:, :], wbuf[:, slot * 2048 + kk * 128: slot * 2048 + kk * 128 + 128],
                                h2sl(kk, tb), start=(kk == 0), stop=(kk == NKC - 1)), "ste")
                    upfree[slot] = mmv
                    if half == 0:
                        gmmv = mmv
                wait("scalar", "ste", gmmv)
                sgv = 0
                for tb in range(NTB):
                    sgv = op("scalar", lambda e, tb=tb: e.activation(
                        sgil[:, tb * 512:tb * 512 + 512], [pA, pB][tb][:, :], AF.Silu), "sse")
                sgrd[0] = sgv
                wait("vector", "ste", mmv)
                wait("vector", "sse", sgv)
                s_global = jj if shared else ei * 11 + jj
                for tb in range(NTB):
                    op("vector", lambda e, tb=tb: e.tensor_tensor(
                        tmul[:, tb * 512:tb * 512 + 512], sgil[:, tb * 512:tb * 512 + 512],
                        [pC, pD][tb][:, :], ALU.mult))
                drain("vector")
                av = 0
                for tb in range(NTB):
                    if shared:
                        av = op("vector", lambda e, s_global=s_global, tb=tb: e.tensor_copy(
                            aslice(s_global, tb), tmul[:, tb * 512:tb * 512 + 512]), "sve")
                    else:
                        av = op("vector", lambda e, s_global=s_global, tb=tb: e.tensor_mul(
                            aslice(s_global, tb), tmul[:, tb * 512:tb * 512 + 512],
                            cwbc[:, tb * 512:tb * 512 + 512]), "sve")
                ugrd[0] = av
        up_done = C["sve"]
        for db in range(NKC):
            wait("gpsimd", "ste", dn_guard)
            dls = 0
            for ei in range(nexp):
                slot = 4096 + ei * 1408
                src = sdownT[:, db, :] if shared else downT[grp[ei], :, db, :]
                dls = dma("gpsimd", wbuf[:, slot:slot + nj * 128], src, "qw2")
            wait("tensor", "qw2", dls)
            if db == 0:
                wait("tensor", "sve", up_done)
            for tb in range(NTB):
                p = [pE, pF][tb]
                wait("tensor", "sve", acc_guard[tb])
                mmv = 0
                for ei in range(nexp):
                    slot = 4096 + ei * 1408
                    for ik in range(nj):
                        s_global = ik if shared else ei * 11 + ik
                        mmv = op("tensor", lambda e, slot=slot, ik=ik, tb=tb, p=p,
                                 s_global=s_global, first=(ei == 0 and ik == 0),
                                 last=(ei == nexp - 1 and ik == nj - 1): e.matmul(
                            p[:, :], wbuf[:, slot + ik * 128: slot + ik * 128 + 128],
                            aslice(s_global, tb), start=first, stop=last), "ste")
                if tb == NTB - 1:
                    dn_guard = mmv
                wait("vector", "ste", mmv)
                sl3 = ((db * NTB + tb) % 2) * 512
                wait("vector", "qac%d" % ((db * NTB + tb) % 2),
                     C["qac%d" % ((db * NTB + tb) % 2)])
                e3 = op("vector", lambda e, db=db, p=p, sl3=sl3: e.tensor_scalar(
                    x2f[:, sl3:sl3 + 512], p[:, :], modv[:, 80 + db:80 + db + 1],
                    None, ALU.mult), "sve")
                acc_guard[tb] = e3
                wait("gpsimd", "sve", e3)
                dma("gpsimd", yT[db * 128:db * 128 + 128, tb * 512:tb * 512 + 512],
                    x2f[:, sl3:sl3 + 512], "qac%d" % ((db * NTB + tb) % 2), accum=True)

    PH(14)
    fg_off()
    wait("gpsimd", "qac0", C["qac0"])
    wait("gpsimd", "qac1", C["qac1"])
    wait("scalar", "qst0", C["qst0"])
    wait("scalar", "qst1", C["qst1"])

    if KPH < 99:
        cuts = sorted(k for k in PHM if k > KPH)
        if cuts:
            cev, cg, cr = PHM[cuts[0]]
            del EV[cev:]
            del EVG[cev:]
            for engn in ENGS:
                if cg:
                    EV.append((engn, lambda e, v=cg: e.wait_ge(GSEM, v)))
                for s, rv in cr.items():
                    if rv:
                        EV.append((engn, lambda e, S=SEM[s], v=rv: e.wait_ge(S, v)))
    else:
        fin = EVG[-1]
        for engn in ENGS:
            on(engn, lambda e, v=fin: e.wait_ge(GSEM, v))
            for s, rv in R.items():
                if rv:
                    on(engn, lambda e, S=SEM[s], v=rv: e.wait_ge(S, v))

    block = ctx.enter_context(nc.Block())

    def runner(eng_name):
        def body(eng):
            for nm, fn in EV:
                if nm == eng_name:
                    fn(eng)
        return body

    block.sync(runner("sync"))
    block.gpsimd(runner("gpsimd"))
    block.vector(runner("vector"))
    block.scalar(runner("scalar"))
    block.tensor(runner("tensor"))
    ctx.close()
    return nc


def _wtile(W, mb):
    Din, M = W.shape[1], W.shape[0]
    kb = Din // 128
    arr = np.ascontiguousarray(W.T).reshape(kb, 128, M // mb, mb)
    return np.ascontiguousarray(arr.transpose(1, 2, 0, 3)).reshape(128, M // mb, kb * mb)


_NC_CACHE = {}


def kernel(**inp):
    S = np.asarray(inp["L0_attn_x"]).shape[1]
    TPC = S // 4
    if TPC not in _NC_CACHE:
        _NC_CACHE[TPC] = build_nc(TPC)
    nc = _NC_CACHE[TPC]

    x = np.asarray(inp["L0_attn_x"], np.float32)
    cc = np.asarray(inp["L0_c"], np.float32)
    wmod = np.asarray(inp["L0_wmod"], np.float32)

    wq = _wtile(np.asarray(inp["L0_attn_Wq"], np.float32), 128)
    wk = _wtile(np.asarray(inp["L0_attn_Wk"], np.float32), 128)
    wv = _wtile(np.asarray(inp["L0_attn_Wv"], np.float32), 256)
    wog = _wtile(np.asarray(inp["L0_attn_Wog"], np.float32), 128)
    wif = _wtile(np.concatenate([np.asarray(inp["L0_attn_Wig"], np.float32),
                                 np.asarray(inp["L0_attn_Wfg"], np.float32)], 0), 16)
    wout = _wtile(np.asarray(inp["L0_attn_Wout"], np.float32), 128)
    gate = _wtile(np.asarray(inp["L0_ffn_gate_w"], np.float32), 16)
    up = np.stack([_wtile(np.asarray(inp["L0_ffn_expert_up_w"][e], np.float32), 128)
                   for e in range(E)])
    down = np.stack([_wtile(np.asarray(inp["L0_ffn_expert_down_w"][e], np.float32), 128)
                     for e in range(E)])
    sup = _wtile(np.asarray(inp["L0_ffn_shared_up_w"], np.float32), 128)
    sdn = _wtile(np.asarray(inp["L0_ffn_shared_down_w"], np.float32), 128)
    bigv = np.ascontiguousarray(np.asarray(inp["L0_attn_big"], np.float32)[:, None])
    bfgv = np.ascontiguousarray(np.asarray(inp["L0_attn_bfg"], np.float32)[:, None])
    gm = np.ascontiguousarray(np.asarray(inp["L0_attn_gamma"], np.float32).reshape(1, NH * DV))

    mask01 = np.ascontiguousarray(np.triu(np.ones((128, 128), np.float32)))
    identb = np.eye(128).astype(ml_dtypes.bfloat16)
    identf = np.eye(128, dtype=np.float32)
    tril16 = np.ascontiguousarray(
        np.tril(np.ones((4, 4), np.float32), -1).reshape(1, 16).repeat(128, 0))
    tril256 = np.ascontiguousarray(
        np.tril(np.ones((16, 16), np.float32), -1).reshape(1, 256).repeat(128, 0))

    in_maps = []
    for c in range(8):
        bidx, r = c // 4, c % 4
        xs = np.ascontiguousarray(x[bidx, r * TPC:(r + 1) * TPC, :].T)
        cT = np.ascontiguousarray(cc[bidx].reshape(NKC, 128).T)
        wms = np.ascontiguousarray(wmod[:, 3072 * r:3072 * (r + 1)])
        flag = np.zeros((8, 3), np.float32)
        flag[:, :r] = 1.0
        ccneg = (1.0 - flag) * NEG
        in_maps.append(dict(
            xT=xs, cTb=cT, wmod_sl=wms, wqT=wq, wkT=wk, wvT=wv, wogT=wog,
            wifT=wif, bifi=bigv, biff=bfgv, gamma=gm, woutT=wout, gateT=gate, upT=up,
            downT=down, supT=sup, sdownT=sdn, mask01=mask01, identb=identb,
            identf=identf, tril16=tril16, tril256=tril256,
            ccflag=flag, ccneg=ccneg))

    res = run_bass_kernel_spmd(nc, in_maps, core_ids=list(range(8)))
    out = np.empty((2, S, D), np.float32)
    for c in range(8):
        bidx, r = c // 4, c % 4
        out[bidx, r * TPC:(r + 1) * TPC, :] = res.results[c]["yT"].T
    return out


# version 17
# speedup vs baseline: 2.9544x; 1.0311x over previous
import os
import numpy as np
import ml_dtypes
from contextlib import ExitStack
import concourse.bass as bass
import concourse.mybir as mybir
from concourse.bass_utils import run_bass_kernel_spmd

F32 = mybir.dt.float32
BF16 = mybir.dt.bfloat16
AF = mybir.ActivationFunctionType
ALU = mybir.AluOpType
AX = mybir.AxisListType

D = 2048
NH = 8
DV = 256
NKC = 16
E = 16
NEG = -1e30
QSCALE = float(128) ** -0.5
NU = 128 * 2056


def build_nc(TPC):
    KPH = int(os.environ.get("KPH", "99"))
    NCH = TPC // 128
    NTB = TPC // 512
    nc = bass.Bass("TRN2", target_bir_lowering=False)

    def di(name, shape, dt=F32):
        return nc.dram_tensor(name, shape, dt, kind="ExternalInput").ap()

    xT = di("xT", [D, TPC])
    cTb = di("cTb", [128, NKC])
    wmod_sl = di("wmod_sl", [D, 3072])
    wqT = di("wqT", [128, NH, NKC * 128])
    wkT = di("wkT", [128, NH, NKC * 128])
    wvT = di("wvT", [128, NH, NKC * 256])
    wogT = di("wogT", [128, NKC, NKC * 128])
    wifT = di("wifT", [128, 1, NKC * 16])
    bifi = di("bifi", [8, 1])
    biff = di("biff", [8, 1])
    gamma = di("gamma", [1, NH * DV])
    woutT = di("woutT", [128, NKC, NKC * 128])
    gateT = di("gateT", [128, 1, NKC * 16])
    upT = di("upT", [E, 128, 22, NKC * 128])
    downT = di("downT", [E, 128, 16, 11 * 128])
    supT = di("supT", [128, 44, NKC * 128])
    sdownT = di("sdownT", [128, 16, 22 * 128])
    mask01 = di("mask01", [128, 128])
    identb = di("identb", [128, 128], BF16)
    identf = di("identf", [128, 128])
    tril16 = di("tril16", [128, 16])
    tril256 = di("tril256", [128, 256])
    ccflag = di("ccflag", [8, 3])
    ccneg = di("ccneg", [8, 3])
    yT = nc.dram_tensor("yT", [D, TPC], F32, kind="ExternalOutput").ap()

    ag1_in = nc.dram_tensor("ag1_in", [1, 3072], F32, kind="Internal").ap()
    ag1_out = nc.dram_tensor("ag1_out", [4, 3072], F32, kind="Internal").ap()
    ag2_in = nc.dram_tensor("ag2_in", [1, NU + 32], F32, kind="Internal").ap()
    ag2_out = nc.dram_tensor("ag2_out", [4, NU + 32], F32, kind="Internal").ap()

    ctx = ExitStack()
    ctx.__enter__()

    def sb(name, shape, dt=F32):
        return ctx.enter_context(nc.sbuf_tensor(name, shape, dt))

    h16 = sb("h16", [128, NCH * 2056], BF16)
    ogb = sb("ogb", [128, NKC * TPC], BF16)
    v1 = sb("v1", [128, NCH * 2056], BF16)
    bqk = sb("bqk", [128, NKC * TPC], BF16)
    wbuf = sb("wbuf", [128, 9728], BF16)
    wsm = sb("wsm", [128, 512], BF16)
    cst = sb("cst", [128, 2056])
    c16 = sb("c16", [128, 2056], BF16)
    lanA = sb("lanA", [128, TPC])
    lanB = sb("lanB", [128, TPC])
    e32t = sb("e32t", [128, NCH * 32])
    e24t = sb("e24t", [128, NCH * 24])
    e24b = sb("e24b", [128, NCH * 24], BF16)
    modv = sb("modv", [128, 6 * 16])
    gab = sb("gab", [128, NH * DV], BF16)
    mkb = sb("mkb", [128, 128])
    idb = sb("idb", [128, 128], BF16)
    idf = sb("idf", [128, 128])
    t16c = sb("t16c", [128, 16])
    t256c = sb("t256c", [128, 256])
    onesf = sb("onesf", [1, 128])
    onesbf = sb("onesbf", [1, 128], BF16)
    onescb = sb("onescb", [128, 1], BF16)
    scT = sb("scT", [128, NKC], BF16)
    bifib = sb("bifib", [8, 1])
    biffb = sb("biffb", [8, 1])
    bifis = sb("bifis", [8, 1])
    biffs = sb("biffs", [8, 1])
    gmr = sb("gmr", [1, NH * DV])
    ccf = sb("ccf", [8, 3])
    ccn = sb("ccn", [8, 3])
    wrowp = sb("wrowp", [1, 8])
    wrow1 = sb("wrow1", [1, 8])
    wbcs = sb("wbcs", [128, 16])
    scrA = sb("scrA", [128, 1024])
    a3s = sb("a3s", [128, 2 * 128], BF16)
    k1s = sb("k1s", [128, 2 * 128], BF16)
    t1s = sb("t1s", [128, 2056], BF16)
    xsq = t1s
    sq8 = sb("sq8", [128, 48])
    sgil = sb("sgil", [128, 2 * 512], BF16)
    tmul = sb("tmul", [128, 2 * 512], BF16)
    x2f = sb("x2f", [128, 2 * 512])
    xre = sb("xre", [128, 2 * 512])
    scrC = xre
    lanC = xre
    cwl16 = gab[0:16, 0:TPC]
    cwrow = scrA[0:1, :].bitcast(BF16)
    cwbc = gab[:, TPC:2 * TPC] if TPC <= NH * DV // 2 else gab[:, 0:TPC]
    g16a = sb("g16a", [128, 16])
    g16b = sb("g16b", [128, 16])
    g16d = sb("g16d", [128, 16])
    g4a = sb("g4a", [128, 4])
    g256 = e32t[:, 0:256]
    g256b = t1s[:, 0:512].bitcast(F32)
    gc1 = sb("gc1", [128, 1])
    gc2 = sb("gc2", [128, 1])

    hob = scrA[:].bitcast(BF16)
    uvw = v1[:].bitcast(F32)

    def psm(name):
        return ctx.enter_context(nc.psum_tensor(name, [128, 512], F32))

    pA, pB, pC, pD, pE, pF, pG = (psm(n) for n in "pA pB pC pD pE pF pG".split())
    PSF = [pA, pB, pC, pD, pE, pF, pG]
    pT = ctx.enter_context(nc.psum_tensor("pT", [128, 1024], BF16))

    SEM, C = {}, {}
    for s in ["qsc", "qxb", "qw0", "qw1", "qw2", "qw3", "qc", "qio",
              "qxr0", "qxr1", "qst0", "qst1", "qac0", "qac1",
              "ste", "sve", "sse", "scc"]:
        SEM[s] = ctx.enter_context(nc.semaphore(s))
        C[s] = 0
    GSEM = ctx.enter_context(nc.semaphore("gsem"))
    G = [0]
    R = {s: 0 for s in SEM}
    FGM = [False]
    ENGS = ["sync", "gpsimd", "vector", "scalar", "tensor"]
    EV = []
    EVG = []
    PHM = {}

    def PH(n):
        if n not in PHM:
            PHM[n] = (len(EV), G[0], dict(R))

    def on(eng, fn):
        EV.append((eng, fn))
        EVG.append(G[0])

    def chain(eng):
        v = G[0]
        if v:
            on(eng, lambda e, v=v: e.wait_ge(GSEM, v))

    def dma(eng, out, in_, sem, accum=False, slow=False):
        kw = {}
        if accum:
            kw["accum_op"] = ALU.add
        if slow:
            kw["allow_slow_non_contiguous"] = True
        C[sem] += 16
        if FGM[0]:
            R[sem] += 16
            on(eng, lambda e, o=out, i=in_, S=SEM[sem], kw=kw:
               e.dma_start(out=o, in_=i, **kw).then_inc(S, 16))
        else:
            chain(eng)
            G[0] += 16
            on(eng, lambda e, o=out, i=in_, kw=kw:
               e.dma_start(out=o, in_=i, **kw).then_inc(GSEM, 16))
        return C[sem]

    def wait(eng, sem, val):
        if FGM[0] and val:
            on(eng, lambda e, S=SEM[sem], v=val: e.wait_ge(S, v))

    def op(eng, fn, sem=None):
        if FGM[0]:
            if sem is None:
                on(eng, fn)
                return None
            C[sem] += 1
            R[sem] += 1
            on(eng, lambda e, f=fn, S=SEM[sem]: f(e).then_inc(S, 1))
            return C[sem]
        chain(eng)
        G[0] += 1
        if eng == "vector":
            on(eng, fn)
            on(eng, lambda e: e.drain())
            on(eng, lambda e: e.sem_inc(GSEM, 1))
        else:
            on(eng, lambda e, f=fn: f(e).then_inc(GSEM, 1))
        if sem is None:
            return None
        C[sem] += 1
        return C[sem]

    def drain(eng):
        on(eng, lambda e: e.drain())

    def opd(eng, fn, sem):
        C[sem] += 1
        if FGM[0]:
            R[sem] += 1
            on(eng, fn)
            on(eng, lambda e: e.drain())
            on(eng, lambda e, S=SEM[sem]: e.sem_inc(S, 1))
            return C[sem]
        chain(eng)
        G[0] += 1
        on(eng, fn)
        on(eng, lambda e: e.drain())
        on(eng, lambda e: e.sem_inc(GSEM, 1))
        return C[sem]

    def barrier():
        for engn in ENGS:
            if G[0]:
                on(engn, lambda e, v=G[0]: e.wait_ge(GSEM, v))
            for s, rv in R.items():
                if rv:
                    on(engn, lambda e, S=SEM[s], v=rv: e.wait_ge(S, v))

    def fg_on():
        barrier()
        for s in SEM:
            delta = C[s] - R[s]
            if delta:
                on("sync", lambda e, S=SEM[s], d=delta: e.sem_inc(S, d))
                R[s] += delta
        FGM[0] = True

    def fg_off():
        FGM[0] = False
        barrier()


    fg_on()
    ld0 = 0
    for out, src in [(mkb[:], mask01), (idb[:], identb), (idf[:], identf),
                     (gmr[:], gamma), (bifib[:], bifi), (biffb[:], biff), (ccf[:], ccflag),
                     (ccn[:], ccneg), (t16c[:], tril16), (t256c[:], tril256)]:
        ld0 = dma("sync", out, src, "qc")
    csem = dma("gpsimd", scT[:], cTb, "qsc")
    xbsem = dma("gpsimd", ogb[:, 0:NKC * TPC].rearrange("p (g t) -> p g t", g=NKC),
                xT.rearrange("(g p) t -> g p t", p=128).transpose([1, 0, 2]), "qxb")
    op("vector", lambda e: e.memset(onesf[:], 1.0))
    op("vector", lambda e: e.memset(onesbf[:], 1.0))
    op("vector", lambda e: e.memset(onescb[:], 1.0))
    op("vector", lambda e: e.memset(lanB[96:104, :], 0.0))
    op("vector", lambda e: e.memset(cst[:], 0.0))
    op("vector", lambda e: e.memset(c16[:], 0.0))
    vini = op("vector", lambda e: e.memset(v1[:], 1.0), "sve")
    wait("vector", "qc", ld0)
    op("vector", lambda e: e.tensor_scalar(bifis[:], bifib[:], 1.0 / 15.0, None, ALU.mult))
    vbif = op("vector", lambda e: e.tensor_scalar(biffs[:], biffb[:], 1.0 / 15.0, None, ALU.mult), "sve")
    gv = 0
    wait("tensor", "qc", ld0)
    wait("tensor", "sve", vini)
    for h in range(NH):
        p = PSF[h % 2]
        if h >= 2:
            wait("tensor", "sve", gv - 1)
        tv = op("tensor", lambda e, h=h, p=p: e.matmul(
            p[:, 0:256], onesf[0:1, :], gmr[0:1, h * 256:h * 256 + 256],
            start=True, stop=True), "ste")
        wait("vector", "ste", tv)
        gv = op("vector", lambda e, h=h, p=p: e.tensor_copy(
            gab[:, h * 256:h * 256 + 256], p[:, 0:256]), "sve")

    PH(1)
    wait("scalar", "qsc", csem)
    scs = op("scalar", lambda e: e.activation(scT[:], scT[:], AF.Silu), "sse")
    mod_last = [0] * NKC
    for kk in range(NKC):
        so = (kk % 2) * 3072
        if kk >= 2:
            wait("gpsimd", "ste", mod_last[kk - 2])
        wl = dma("gpsimd", wbuf[:, so:so + 3072],
                 wmod_sl[kk * 128:kk * 128 + 128, :], "qw0" if kk % 2 == 0 else "qw1")
        wait("tensor", "qw0" if kk % 2 == 0 else "qw1", wl)
        if kk == 0:
            wait("tensor", "sse", scs)
            wait("tensor", "sve", gv)
        mv = 0
        for nb in range(6):
            mv = op("tensor", lambda e, kk=kk, nb=nb, so=so: e.matmul(
                PSF[nb][0:1, :], scT[:, kk:kk + 1],
                wbuf[:, so + nb * 512: so + nb * 512 + 512],
                start=(kk == 0), stop=(kk == NKC - 1)), "ste")
        mod_last[kk] = mv
    wait("vector", "ste", mod_last[NKC - 1])
    mev = 0
    for nb in range(6):
        r = (nb // 2) * 32
        co = (nb % 2) * 512
        mev = op("vector", lambda e, nb=nb, r=r, co=co: e.tensor_copy(
            scrA[r:r + 1, co:co + 512], PSF[nb][0:1, :]), "sve")
    wait("sync", "sve", mev)
    agi = dma("sync", ag1_in[0, :].rearrange("(a b) -> a b", a=3),
              scrA[0:65:32, 0:1024], "qio")
    wait("gpsimd", "qio", agi)
    cc1 = op("gpsimd", lambda e: e.collective_compute(
        "AllGather", ALU.bypass, ins=[ag1_in[:]], outs=[ag1_out[:]],
        replica_groups=[[0, 1, 2, 3], [4, 5, 6, 7]]), "scc")
    wait("sync", "scc", cc1)
    ag1f = ag1_out.rearrange("a b -> (a b)")
    mvl = 0
    for v6 in range(6):
        mvl = dma("sync", modv[:, v6 * 16:v6 * 16 + 16],
                  ag1f[2048 * v6:2048 * v6 + 2048].rearrange("(c p) -> p c", p=128), "qio",
                  slow=True)
    wait("vector", "qio", mvl)
    op("vector", lambda e: e.tensor_scalar(modv[:, 16:32], modv[:, 16:32], 1.0, None, ALU.add))
    op("vector", lambda e: e.tensor_scalar(modv[:, 64:80], modv[:, 64:80], 1.0, None, ALU.add))
    mgv = 0
    for blk in (2, 5):
        sl = modv[:, blk * 16:blk * 16 + 16]
        op("vector", lambda e, sl=sl: e.tensor_scalar(g16a[:], sl, -1.0, None, ALU.mult))
        op("vector", lambda e, sl=sl: e.tensor_scalar(g16b[:], sl, 0.0, None, ALU.is_ge))
        drain("vector")
        op("vector", lambda e, sl=sl: e.tensor_max(g16d[:], sl, g16a[:]))
        drain("vector")
        op("vector", lambda e: e.tensor_scalar(g16a[:], g16d[:], 0.25, None, ALU.max))
        drain("vector")
        op("vector", lambda e: e.tensor_scalar(g16b[:], g16b[:], 2.0, -1.0, ALU.mult, ALU.add))
        drain("vector")
        mgv = op("vector", lambda e, sl=sl: e.tensor_mul(sl, g16a[:], g16b[:]), "sve")

    def layer_norm(src16, dst16, cacol, sacol, rsem, rval):
        wait("scalar", rsem, rval)
        wait("tensor", rsem, rval)
        wait("tensor", "sve", C["sve"])
        mm_last = [0] * NKC
        for kk in range(NKC):
            so = (kk % 2) * TPC
            if kk >= 2:
                wait("scalar", "ste", mm_last[kk - 2])
            sqd = op("scalar", lambda e, kk=kk, so=so: e.activation(
                xsq[:, so:so + TPC], src16[:, kk * TPC:kk * TPC + TPC], AF.Square), "sse")
            for tb in range(NTB):
                op("tensor", lambda e, kk=kk, tb=tb: e.matmul(
                    [pA, pB][tb][0:1, :], onescb[:, 0:1],
                    src16[:, kk * TPC + tb * 512: kk * TPC + tb * 512 + 512],
                    start=(kk == 0), stop=(kk == NKC - 1)), "ste")
            wait("tensor", "sse", sqd)
            mm2 = 0
            for tb in range(NTB):
                mm2 = op("tensor", lambda e, kk=kk, tb=tb, so=so: e.matmul(
                    [pC, pD][tb][0:1, :], onescb[:, 0:1],
                    xsq[:, so + tb * 512: so + tb * 512 + 512],
                    start=(kk == 0), stop=(kk == NKC - 1)), "ste")
            mm_last[kk] = mm2
        PH(21)
        wait("vector", "ste", mm_last[NKC - 1])
        for tb in range(NTB):
            op("vector", lambda e, tb=tb: e.tensor_scalar(
                lanB[0:1, tb * 512:tb * 512 + 512], [pA, pB][tb][0:1, :],
                1.0 / D, None, ALU.mult))
            op("vector", lambda e, tb=tb: e.tensor_scalar(
                lanA[0:1, tb * 512:tb * 512 + 512], [pC, pD][tb][0:1, :],
                1.0 / D, None, ALU.mult))
        drain("vector")
        op("vector", lambda e: e.tensor_mul(lanC[0:1, 0:TPC],
                                            lanB[0:1, 0:TPC], lanB[0:1, 0:TPC]))
        drain("vector")
        op("vector", lambda e: e.tensor_sub(lanB[32:33, 0:TPC],
                                            lanA[0:1, 0:TPC], lanC[0:1, 0:TPC]))
        drain("vector")
        op("vector", lambda e: e.tensor_scalar(lanB[32:33, 0:TPC],
                                               lanB[32:33, 0:TPC], 1e-5, None, ALU.add))
        drain("vector")
        rvv = opd("vector", lambda e: e.reciprocal(lanA[32:33, 0:TPC],
                                                   lanB[32:33, 0:TPC]), "sve")
        wait("scalar", "sve", rvv)
        rsv = op("scalar", lambda e: e.activation(lanA[0:1, 0:TPC],
                                                  lanA[32:33, 0:TPC], AF.Sqrt), "sse")
        op("vector", lambda e: e.tensor_scalar(lanC[0:1, 0:TPC],
                                               lanB[0:1, 0:TPC], -1.0, None, ALU.mult))
        wait("vector", "sse", rsv)
        drain("vector")
        nmv = op("vector", lambda e: e.tensor_mul(lanB[0:1, 0:TPC],
                                                  lanC[0:1, 0:TPC],
                                                  lanA[0:1, 0:TPC]), "sve")
        PH(22)
        wait("tensor", "sve", nmv)
        bc1 = 0
        for tb in range(NTB):
            bc1 = op("tensor", lambda e, tb=tb: e.matmul(
                [pA, pB][tb][:, :], onesf[0:1, :],
                lanA[0:1, tb * 512:tb * 512 + 512], start=True, stop=True), "ste")
            bc1 = op("tensor", lambda e, tb=tb: e.matmul(
                [pC, pD][tb][:, :], onesf[0:1, :],
                lanB[0:1, tb * 512:tb * 512 + 512], start=True, stop=True), "ste")
        wait("vector", "ste", bc1)
        PH(23)
        outv = 0
        for kk in range(NKC):
            for tb in range(NTB):
                so = tb * 512
                op("vector", lambda e, kk=kk, tb=tb, so=so: e.tensor_tensor(
                    scrC[:, so:so + 512],
                    src16[:, kk * TPC + so: kk * TPC + so + 512], [pA, pB][tb][:, :], ALU.mult))
            for tb in range(NTB):
                so = tb * 512
                op("vector", lambda e, kk=kk, tb=tb, so=so: e.tensor_tensor(
                    scrC[:, so:so + 512], scrC[:, so:so + 512], [pC, pD][tb][:, :], ALU.add))
            drain("vector")
            outv = op("vector", lambda e, kk=kk: e.tensor_scalar(
                dst16[:, kk * TPC:kk * TPC + TPC], scrC[:, 0:TPC],
                cacol[:, kk:kk + 1], sacol[:, kk:kk + 1], ALU.mult, ALU.add), "sve")
            drain("vector")
        return outv

    PH(2)
    wait("vector", "sve", mgv)
    h1v = layer_norm(ogb, h16, modv[:, 16:32], modv[:, 0:16], "qxb", xbsem)

    def h1sl(kk, tb, width=512):
        return h16[:, kk * TPC + tb * 512: kk * TPC + tb * 512 + width]

    PH(3)
    blocks = [("q", h, wqT[:, h, :]) for h in range(NH)]
    blocks += [("k", h, wkT[:, h, :]) for h in range(NH)]
    blocks += [("og", ob, wogT[:, ob, :]) for ob in range(NKC)]
    ring = [None] * 4
    evpar = [0, 0]
    for bi, (kind, idx, wsl) in enumerate(blocks):
        so = (bi % 4) * 2048
        if ring[bi % 4] is not None:
            wait("gpsimd", "ste", ring[bi % 4])
        lv = dma("gpsimd", wbuf[:, so:so + 2048], wsl, "qw%d" % (bi % 4))
        wait("tensor", "qw%d" % (bi % 4), lv)
        if bi == 0:
            wait("tensor", "sve", h1v)
        if bi >= 1:
            wait("tensor", "sve" if kind != "og" or blocks[bi - 1][0] != "og" else "sse",
                 evpar[0] if False else None)
        mmv = 0
        for tb in range(NTB):
            p = [pE, pF][tb]
            if bi >= 1:
                wait("tensor", evgrd[tb][0], evgrd[tb][1])
            for kk in range(NKC):
                mmv = op("tensor", lambda e, so=so, kk=kk, tb=tb, p=p: e.matmul(
                    p[:, :], wbuf[:, so + kk * 128: so + kk * 128 + 128],
                    h1sl(kk, tb), start=(kk == 0), stop=(kk == NKC - 1)), "ste")
        ring[bi % 4] = mmv
        wait("vector" if kind != "og" else "scalar", "ste", mmv)
        evgrd = {}
        for tb in range(NTB):
            p = [pE, pF][tb]
            if kind == "q":
                ev = op("vector", lambda e, idx=idx, tb=tb, p=p: e.tensor_scalar(
                    bqk[:, idx * TPC + tb * 512: idx * TPC + tb * 512 + 512],
                    p[:, :], QSCALE, None, ALU.mult), "sve")
                evgrd[tb] = ("sve", ev)
            elif kind == "k":
                ev = op("vector", lambda e, idx=idx, tb=tb, p=p: e.tensor_copy(
                    bqk[:, (NH + idx) * TPC + tb * 512: (NH + idx) * TPC + tb * 512 + 512],
                    p[:, :]), "sve")
                evgrd[tb] = ("sve", ev)
            else:
                ev = op("scalar", lambda e, idx=idx, tb=tb, p=p: e.activation(
                    ogb[:, idx * TPC + tb * 512: idx * TPC + tb * 512 + 512],
                    p[:, :], AF.Sigmoid), "sse")
                evgrd[tb] = ("sse", ev)
    og_done = evgrd[NTB - 1][1]

    PH(35)
    wifl = dma("gpsimd", wsm[:, 0:256], wifT[:, 0, :], "qw3")
    wait("tensor", "qw3", wifl)
    ifm = 0
    for tb in range(NTB):
        for half in range(2):
            p = ([pA, pB] if half == 0 else [pC, pD])[tb]
            for kk in range(NKC):
                ifm = op("tensor", lambda e, kk=kk, tb=tb, p=p, half=half: e.matmul(
                    p[0:8, :], wsm[:, kk * 16 + half * 8: kk * 16 + half * 8 + 8],
                    h1sl(kk, tb), start=(kk == 0), stop=(kk == NKC - 1)), "ste")
    wait("scalar", "ste", ifm)
    wait("scalar", "sve", vbif)
    for tb in range(NTB):
        op("scalar", lambda e, tb=tb: e.activation(
            lanC[64:72, tb * 512:tb * 512 + 512], [pA, pB][tb][0:8, :],
            AF.Tanh, bias=bifis[:, 0:1], scale=1.0 / 15.0))
        op("scalar", lambda e, tb=tb: e.activation(
            lanC[32:40, tb * 512:tb * 512 + 512], [pC, pD][tb][0:8, :],
            AF.Tanh, bias=biffs[:, 0:1], scale=1.0 / 15.0))
    tg = op("scalar", lambda e: e.activation(gc1[0:8, 0:1], bifis[:, 0:1], AF.Square), "sse")
    wait("vector", "sse", tg)
    op("vector", lambda e: e.tensor_scalar(
        lanC[64:72, 0:TPC], lanC[64:72, 0:TPC], 15.0, None, ALU.mult))
    ifv = op("vector", lambda e: e.tensor_scalar(
        lanC[32:40, 0:TPC], lanC[32:40, 0:TPC], 15.0, None, ALU.mult), "sve")
    drain("vector")

    PH(36)
    wait("vector", "sve", vini)
    wait("gpsimd", "ste", C["ste"])
    vld = [0] * NH
    vld[0] = dma("gpsimd", wbuf[:, 0:4096], wvT[:, 0, :], "qw0")
    vld[1] = dma("gpsimd", wbuf[:, 4096:8192], wvT[:, 1, :], "qw1")
    vev = {0: 0, 1: 0}
    for h in range(NH):
        so = (h % 2) * 4096
        wait("tensor", "qw%d" % (h % 2), vld[h])
        lastmm = 0
        for tt in range(NCH):
            p = [pC, pD][tt % 2]
            if tt >= 2 or h > 0:
                wait("tensor", "sve", vev[tt % 2])
            mmv = 0
            for kk in range(NKC):
                mmv = op("tensor", lambda e, h=h, tt=tt, kk=kk, so=so, p=p: e.matmul(
                    p[:, 0:256], h16[:, kk * TPC + tt * 128: kk * TPC + tt * 128 + 128],
                    wbuf[:, so + kk * 256: so + kk * 256 + 256],
                    start=(kk == 0), stop=(kk == NKC - 1)), "ste")
            wait("vector", "ste", mmv)
            vev[tt % 2] = op("vector", lambda e, h=h, tt=tt, p=p: e.tensor_copy(
                v1[:, tt * 2056 + 257 * h: tt * 2056 + 257 * h + 256], p[:, 0:256]), "sve")
            lastmm = mmv
        if h + 2 < NH:
            wait("gpsimd", "ste", lastmm)
            vld[h + 2] = dma("gpsimd", wbuf[:, so:so + 4096], wvT[:, h + 2, :], "qw%d" % (h % 2))
    v_done = vev[(NCH - 1) % 2]

    PH(4)
    barrier()
    wait("scalar", "sve", ifv)
    op("scalar", lambda e: e.activation(
        lanC[96:104, 0:TPC], lanC[32:40, 0:TPC], AF.Sigmoid))
    drain("scalar")
    flv = op("scalar", lambda e: e.activation(
        lanA[64:72, 0:TPC], lanC[96:104, 0:TPC], AF.Ln), "sse")
    wait("vector", "sse", flv)
    op("vector", lambda e: e.memset(lanB[64:72, 0:TPC], 0.0))
    drain("vector")
    op("vector", lambda e: e.tensor_tensor_scan(
        lanB[96:104, 0:TPC], lanA[64:72, 0:TPC],
        lanB[64:72, 0:TPC], 0.0, ALU.add, ALU.add))
    drain("vector")
    op("vector", lambda e: e.tensor_tensor_scan(
        lanB[64:72, 0:TPC], lanA[64:72, 0:TPC],
        lanC[64:72, 0:TPC], NEG, ALU.add, ALU.max))
    drain("vector")
    op("vector", lambda e: e.tensor_copy(lanA[64:72, 0:TPC], lanB[96:104, 0:TPC]))
    drain("vector")
    op("vector", lambda e: e.tensor_sub(lanB[32:40, 0:TPC],
                                        lanC[64:72, 0:TPC], lanA[64:72, 0:TPC]))
    op("vector", lambda e: e.tensor_sub(lanA[32:40, 0:TPC],
                                        lanA[64:72, 0:TPC], lanB[64:72, 0:TPC]))
    drain("vector")
    e32p = 0
    for cc_ in range(NCH):
        sl = slice(cc_ * 128, cc_ * 128 + 128)
        bc0 = lanA[64:72, cc_ * 128 - 1:cc_ * 128] if cc_ > 0 else None
        bce = lanA[64:72, cc_ * 128 + 127:cc_ * 128 + 128]
        if bc0 is not None:
            op("vector", lambda e, sl=sl, bc0=bc0: e.tensor_scalar(
                lanA[0:8, sl], lanB[32:40, sl], bc0, None, ALU.add))
            op("vector", lambda e, sl=sl, bc0=bc0: e.tensor_scalar(
                lanC[0:8, sl], lanA[32:40, sl], bc0, None, ALU.subtract))
            op("vector", lambda e, cc_=cc_, bce=bce, bc0=bc0: e.tensor_scalar(
                gc2[0:8, 0:1], bce, bc0, None, ALU.subtract))
        else:
            op("vector", lambda e, sl=sl: e.tensor_copy(lanA[0:8, sl], lanB[32:40, sl]))
            op("vector", lambda e, sl=sl: e.tensor_copy(lanC[0:8, sl], lanA[32:40, sl]))
            op("vector", lambda e, bce=bce: e.tensor_copy(gc2[0:8, 0:1], bce))
        op("vector", lambda e, sl=sl, bce=bce: e.tensor_scalar(
            lanB[0:8, sl], lanB[32:40, sl], bce, None, ALU.add))
        drain("vector")
        e32p = op("vector", lambda e, sl=sl: e.tensor_scalar(
            lanC[32:40, sl], lanB[32:40, sl], 0.0, gc2[0:8, 0:1], ALU.mult, ALU.add), "sve")
        drain("vector")
    wait("scalar", "sve", e32p)
    for seg, base in [(lanA, 0), (lanB, 0), (lanC, 0)]:
        op("scalar", lambda e, seg=seg, base=base: e.activation(
            seg[base:base + 8, 0:TPC], seg[base:base + 8, 0:TPC], AF.Exp))
        drain("scalar")
    e32e = op("scalar", lambda e: e.activation(
        lanC[32:40, 0:TPC], lanC[32:40, 0:TPC], AF.Exp), "sse")
    wait("tensor", "sse", e32e)
    etv = 0
    esrc = [(lanA, 0), (lanB, 0), (lanC, 0), (lanC, 32)]
    for cc_ in range(NCH):
        for qi, (seg, base) in enumerate(esrc):
            if cc_ + qi >= 1:
                wait("tensor", "sve", etv)
            tv = op("tensor", lambda e, cc_=cc_, seg=seg, base=base: e.transpose(
                pG[0:128, 0:8], seg[base:base + 8, cc_ * 128:cc_ * 128 + 128],
                idf[base:base + 8, base:base + 8]), "ste")
            wait("vector", "ste", tv)
            etv = op("vector", lambda e, cc_=cc_, qi=qi: e.tensor_copy(
                e32t[:, cc_ * 32 + qi * 8:cc_ * 32 + qi * 8 + 8], pG[:, 0:8]), "sve")

    PH(5)
    barrier()
    wait("tensor", "sve", C["sve"])
    lastC16 = [C["sve"]] * NH
    gA3 = {0: 0, 1: 0}
    gK1 = {0: 0, 1: 0}
    gHq = {0: 0, 1: 0}
    gCb = {0: 0, 1: 0}
    for cc_ in range(NCH):
        for h in range(NH):
            par = (cc_ * NH + h) % 2
            pS = [pA, pB][par]
            pH = [pC, pD][par]
            pCa = [pE, pF][par]
            qsl = bqk[:, h * TPC + cc_ * 128: h * TPC + cc_ * 128 + 128]
            ksl = bqk[:, (NH + h) * TPC + cc_ * 128: (NH + h) * TPC + cc_ * 128 + 128]
            vsl = v1[:, cc_ * 2056 + 257 * h: cc_ * 2056 + 257 * h + 257]
            colfac = e32t[:, cc_ * 32 + h: cc_ * 32 + h + 1]
            kfac = e32t[:, cc_ * 32 + 8 + h: cc_ * 32 + 8 + h + 1]
            rowsc = e32t[:, cc_ * 32 + 16 + h: cc_ * 32 + 16 + h + 1]
            dcy = e32t[:, cc_ * 32 + 24 + h: cc_ * 32 + 24 + h + 1]
            wait("tensor", "sve", gA3[par])
            smm = op("tensor", lambda e, ksl=ksl, qsl=qsl, pS=pS: e.matmul(
                pS[:, 0:128], ksl, qsl, start=True, stop=True), "ste")
            wait("tensor", "sve", gK1[par])
            tmm = op("tensor", lambda e, ksl=ksl, par=par: e.transpose(
                pT[:, par * 512:par * 512 + 128], ksl, idb[:, :]), "ste")
            wait("vector", "ste", tmm)
            gA3[par] = op("vector", lambda e, pS=pS, colfac=colfac, par=par: e.scalar_tensor_tensor(
                a3s[:, par * 128:par * 128 + 128], pS[:, 0:128], colfac, mkb[:, :],
                ALU.mult, ALU.mult), "sve")
            gK1[par] = op("vector", lambda e, par=par, kfac=kfac: e.tensor_scalar(
                k1s[:, par * 128:par * 128 + 128], pT[:, par * 512:par * 512 + 128],
                kfac, None, ALU.mult), "sve")
            wait("tensor", "sve", gA3[par])
            wait("tensor", "sve", gHq[par])
            op("tensor", lambda e, par=par, vsl=vsl, pH=pH: e.matmul(
                pH[:, 0:257], a3s[:, par * 128:par * 128 + 128], vsl,
                start=True, stop=False), "ste")
            wait("tensor", "sve", lastC16[h])
            hmm = op("tensor", lambda e, qsl=qsl, h=h, pH=pH: e.matmul(
                pH[:, 0:257], qsl, c16[:, 257 * h:257 * h + 257],
                start=False, stop=True), "ste")
            wait("tensor", "sve", gCb[par])
            cmm = op("tensor", lambda e, par=par, vsl=vsl, pCa=pCa: e.matmul(
                pCa[:, 0:257], k1s[:, par * 128:par * 128 + 128], vsl,
                start=True, stop=True), "ste")
            wait("vector", "ste", hmm)
            gHq[par] = op("vector", lambda e, cc_=cc_, h=h, pH=pH, rowsc=rowsc: e.tensor_scalar(
                h16[:, cc_ * 2056 + 257 * h: cc_ * 2056 + 257 * h + 257],
                pH[:, 0:257], rowsc, None, ALU.mult), "sve")
            wait("vector", "ste", cmm)
            gCb[par] = op("vector", lambda e, h=h, dcy=dcy, pCa=pCa: e.scalar_tensor_tensor(
                cst[:, 257 * h:257 * h + 257], cst[:, 257 * h:257 * h + 257],
                dcy, pCa[:, 0:257], ALU.mult, ALU.add), "sve")
            drain("vector")
            lastC16[h] = op("vector", lambda e, h=h: e.tensor_copy(
                c16[:, 257 * h:257 * h + 257], cst[:, 257 * h:257 * h + 257]), "sve")
    passa_v = C["sve"]

    PH(6)
    barrier()
    wait("sync", "sve", passa_v)
    dma("sync", ag2_in[0, 0:NU].rearrange("(p f) -> p f", p=128), cst[:], "qio")
    dma("sync", ag2_in[0, NU:NU + 8].rearrange("(p f) -> p f", p=8),
        lanA[64:72, TPC - 1:TPC], "qio")
    ag2v = dma("sync", ag2_in[0, NU + 8:NU + 16].rearrange("(p f) -> p f", p=8),
               lanB[64:72, TPC - 1:TPC], "qio")
    wait("gpsimd", "qio", ag2v)
    cc2 = op("gpsimd", lambda e: e.collective_compute(
        "AllGather", ALU.bypass, ins=[ag2_in[:]], outs=[ag2_out[:]],
        replica_groups=[[0, 1, 2, 3], [4, 5, 6, 7]]), "scc")
    wait("sync", "scc", cc2)
    uld = 0
    for j in range(3):
        uld = dma("sync", uvw[:, j * 2056:(j + 1) * 2056],
                  ag2_out[j, 0:NU].rearrange("(p f) -> p f", p=128), "qio")
        uld = dma("sync", lanC[0:8, 2 * j:2 * j + 1],
                  ag2_out[j, NU:NU + 8].rearrange("(p f) -> p f", p=8), "qio")
        uld = dma("sync", lanC[0:8, 2 * j + 1:2 * j + 2],
                  ag2_out[j, NU + 8:NU + 16].rearrange("(p f) -> p f", p=8), "qio")

    PH(7)
    barrier()
    wait("vector", "qio", uld)
    op("vector", lambda e: e.memset(cst[:], 0.0))
    op("vector", lambda e: e.memset(lanC[0:8, 6:7], 0.0))
    for j in range(3):
        op("vector", lambda e, j=j: e.tensor_mul(lanC[0:8, 9:10], lanC[0:8, 2 * j + 1:2 * j + 2], ccf[:, j:j + 1]))
        op("vector", lambda e, j=j: e.tensor_mul(lanC[0:8, 10:11], lanC[0:8, 2 * j:2 * j + 1], ccf[:, j:j + 1]))
        drain("vector")
        op("vector", lambda e, j=j: e.tensor_add(lanC[0:8, 9:10], lanC[0:8, 9:10], ccn[:, j:j + 1]))
        op("vector", lambda e: e.tensor_add(lanC[0:8, 11:12], lanC[0:8, 10:11], lanC[0:8, 6:7]))
        drain("vector")
        op("vector", lambda e: e.tensor_max(lanC[0:8, 12:13], lanC[0:8, 11:12], lanC[0:8, 9:10]))
        drain("vector")
        op("vector", lambda e: e.tensor_sub(lanC[0:8, 13:14], lanC[0:8, 11:12], lanC[0:8, 12:13]))
        op("vector", lambda e: e.tensor_scalar(lanC[0:8, 14:15], lanC[0:8, 12:13], -1.0, None, ALU.mult))
        drain("vector")
        wex = op("vector", lambda e: e.tensor_copy(lanC[0:8, 6:7], lanC[0:8, 12:13]), "sve")
        wait("scalar", "sve", wex)
        op("scalar", lambda e: e.activation(lanC[0:8, 13:14], lanC[0:8, 13:14], AF.Exp))
        drain("scalar")
        sxv = op("scalar", lambda e: e.activation(lanC[0:8, 14:15], lanC[0:8, 14:15], AF.Exp), "sse")
        wait("vector", "sse", sxv)
        wnv = op("vector", lambda e, j=j: e.tensor_mul(lanC[0:8, 15:16], lanC[0:8, 14:15], ccf[:, j:j + 1]), "sve")
        wait("tensor", "sve", wnv)
        op("tensor", lambda e: e.transpose(
            pG[0:1, 0:8], lanC[0:8, 13:14], idf[0:8, 0:8]), "ste")
        t1v = op("tensor", lambda e: e.transpose(
            pG[0:1, 8:16], lanC[0:8, 15:16], idf[0:8, 0:8]), "ste")
        wait("vector", "ste", t1v)
        op("vector", lambda e: e.tensor_copy(wrowp[:, :], pG[0:1, 0:8]))
        t2v = op("vector", lambda e: e.tensor_copy(wrow1[:, :], pG[0:1, 8:16]), "sve")
        wait("tensor", "sve", t2v)
        b2 = op("tensor", lambda e: e.matmul(pG[:, 0:8], onesf[0:1, :], wrowp[0:1, :],
                                             start=True, stop=True), "ste")
        b2 = op("tensor", lambda e: e.matmul(pG[:, 8:16], onesf[0:1, :], wrow1[0:1, :],
                                             start=True, stop=True), "ste")
        wait("vector", "ste", b2)
        op("vector", lambda e: e.tensor_copy(wbcs[:, :], pG[:, 0:16]))
        drain("vector")
        for h in range(NH):
            op("vector", lambda e, j=j, h=h: e.tensor_scalar(
                scrA[:, 0:257],
                uvw[:, j * 2056 + 257 * h: j * 2056 + 257 * h + 257],
                wbcs[:, 8 + h:8 + h + 1], None, ALU.mult))
            drain("vector")
            op("vector", lambda e, h=h: e.scalar_tensor_tensor(
                cst[:, 257 * h:257 * h + 257], cst[:, 257 * h:257 * h + 257],
                wbcs[:, h:h + 1], scrA[:, 0:257], ALU.mult, ALU.add))
            drain("vector")
    p16v = op("vector", lambda e: e.tensor_copy(c16[:], cst[:]), "sve")

    PH(8)
    barrier()
    op("vector", lambda e: e.tensor_scalar(
        lanC[64:72, 0:TPC], lanA[64:72, 0:TPC], lanC[0:8, 6:7], None, ALU.add))
    drain("vector")
    op("vector", lambda e: e.tensor_max(
        lanA[64:72, 0:TPC], lanB[64:72, 0:TPC], lanC[64:72, 0:TPC]))
    drain("vector")
    op("vector", lambda e: e.tensor_sub(
        lanA[0:8, 0:TPC], lanB[64:72, 0:TPC], lanA[64:72, 0:TPC]))
    op("vector", lambda e: e.tensor_sub(
        lanB[0:8, 0:TPC], lanC[64:72, 0:TPC], lanA[64:72, 0:TPC]))
    e24v = op("vector", lambda e: e.tensor_scalar(
        lanC[0:8, 0:TPC], lanA[64:72, 0:TPC], -1.0, None, ALU.mult), "sve")
    wait("scalar", "sve", e24v)
    op("scalar", lambda e: e.activation(lanA[0:8, 0:TPC], lanA[0:8, 0:TPC], AF.Exp))
    drain("scalar")
    op("scalar", lambda e: e.activation(lanB[0:8, 0:TPC], lanB[0:8, 0:TPC], AF.Exp))
    drain("scalar")
    e24e = op("scalar", lambda e: e.activation(lanC[0:8, 0:TPC], lanC[0:8, 0:TPC], AF.Exp), "sse")
    wait("tensor", "sse", e24e)
    etv = 0
    esrc24 = [(lanA, 0), (lanB, 0), (lanC, 0)]
    for cc_ in range(NCH):
        for qi, (seg, base) in enumerate(esrc24):
            if cc_ + qi >= 1:
                wait("tensor", "sve", etv)
            tv = op("tensor", lambda e, cc_=cc_, seg=seg, base=base: e.transpose(
                pG[0:128, 0:8], seg[base:base + 8, cc_ * 128:cc_ * 128 + 128],
                idf[base:base + 8, base:base + 8]), "ste")
            wait("vector", "ste", tv)
            op("vector", lambda e, cc_=cc_, qi=qi: e.tensor_copy(
                e24t[:, cc_ * 24 + qi * 8:cc_ * 24 + qi * 8 + 8], pG[:, 0:8]))
            drain("vector")
            etv = op("vector", lambda e, cc_=cc_, qi=qi: e.tensor_copy(
                e24b[:, cc_ * 24 + qi * 8:cc_ * 24 + qi * 8 + 8],
                e24t[:, cc_ * 24 + qi * 8:cc_ * 24 + qi * 8 + 8]), "sve")

    PH(9)
    fg_off()
    wait("tensor", "sve", C["sve"])
    gH2 = {0: 0, 1: 0}
    gTo = {0: 0, 1: 0}
    for cc_ in range(NCH):
        tslab = op("vector", lambda e, cc_=cc_: e.tensor_tensor(
            t1s[:, 0:2056].rearrange("p (h f) -> p h f", f=257),
            h16[:, cc_ * 2056: cc_ * 2056 + 2056].rearrange("p (h f) -> p h f", f=257),
            e24b[:, cc_ * 24: cc_ * 24 + 8].unsqueeze(2).to_broadcast([128, 8, 257]),
            ALU.mult), "sve")
        drain("vector")
        for h in range(NH):
            par = h % 2
            p = [pA, pB][par]
            qsl = bqk[:, h * TPC + cc_ * 128: h * TPC + cc_ * 128 + 128]
            wait("tensor", "sve", gH2[par])
            if h == 0:
                wait("tensor", "sve", tslab)
            h2mm = op("tensor", lambda e, qsl=qsl, h=h, p=p: e.matmul(
                p[:, 0:257], qsl, c16[:, 257 * h:257 * h + 257], start=True, stop=True), "ste")
            wait("vector", "ste", h2mm)
            gH2[par] = op("vector", lambda e, h=h, p=p, cc_=cc_: e.scalar_tensor_tensor(
                cst[:, 257 * h: 257 * h + 257], p[:, 0:257],
                e24t[:, cc_ * 24 + 8 + h: cc_ * 24 + 8 + h + 1],
                t1s[:, 257 * h: 257 * h + 257], ALU.mult, ALU.add), "sve")
        drain("vector")
        op("vector", lambda e: e.tensor_scalar(
            sq8[:, 40:48], cst[:, 256:2056:257], -1.0, None, ALU.mult))
        drain("vector")
        op("vector", lambda e: e.tensor_max(
            sq8[:, 0:8], sq8[:, 40:48], cst[:, 256:2056:257]))
        drain("vector")
        op("vector", lambda e, cc_=cc_: e.tensor_max(
            sq8[:, 0:8], sq8[:, 0:8], e24t[:, cc_ * 24 + 16: cc_ * 24 + 24]))
        drain("vector")
        op("vector", lambda e: e.tensor_scalar(sq8[:, 0:8], sq8[:, 0:8], 1e-6, None, ALU.add))
        drain("vector")
        op("vector", lambda e: e.reciprocal(sq8[:, 8:16], sq8[:, 0:8]))
        drain("vector")
        hnv = op("vector", lambda e: e.tensor_tensor(
            t1s[:, 0:2048].rearrange("p (h f) -> p h f", f=256),
            cst[:, 0:2056].rearrange("p (h f) -> p h f", f=257)[:, :, 0:256],
            sq8[:, 8:16].unsqueeze(2).to_broadcast([128, 8, 256]), ALU.mult), "sve")
        drain("vector")
        op("vector", lambda e: e.reduce_sum(
            sq8[:, 16:24], t1s[:, 0:2048].rearrange("p (h f) -> p h f", f=256), axis=AX.X))
        wait("scalar", "sve", hnv)
        sqv = op("scalar", lambda e: e.activation(cst[:, 0:2048], t1s[:, 0:2048], AF.Square), "sse")
        wait("vector", "sse", sqv)
        op("vector", lambda e: e.reduce_sum(
            sq8[:, 24:32], cst[:, 0:2048].rearrange("p (h f) -> p h f", f=256), axis=AX.X))
        drain("vector")
        op("vector", lambda e: e.tensor_scalar(sq8[:, 16:24], sq8[:, 16:24], -1.0 / 256, None, ALU.mult))
        op("vector", lambda e: e.tensor_scalar(sq8[:, 24:32], sq8[:, 24:32], 1.0 / 256, None, ALU.mult))
        drain("vector")
        op("vector", lambda e: e.tensor_mul(sq8[:, 32:40], sq8[:, 16:24], sq8[:, 16:24]))
        drain("vector")
        op("vector", lambda e: e.tensor_sub(sq8[:, 24:32], sq8[:, 24:32], sq8[:, 32:40]))
        drain("vector")
        op("vector", lambda e: e.tensor_scalar(sq8[:, 24:32], sq8[:, 24:32], 1e-6, None, ALU.add))
        drain("vector")
        rvv = opd("vector", lambda e: e.reciprocal(sq8[:, 32:40], sq8[:, 24:32]), "sve")
        wait("scalar", "sve", rvv)
        rsv = op("scalar", lambda e: e.activation(sq8[:, 40:48], sq8[:, 32:40], AF.Sqrt), "sse")
        wait("vector", "sse", rsv)
        op("vector", lambda e: e.tensor_tensor(
            cst[:, 0:2048].rearrange("p (h f) -> p h f", f=256),
            t1s[:, 0:2048].rearrange("p (h f) -> p h f", f=256),
            sq8[:, 16:24].unsqueeze(2).to_broadcast([128, 8, 256]), ALU.add))
        drain("vector")
        op("vector", lambda e: e.tensor_tensor(
            cst[:, 0:2048].rearrange("p (h f) -> p h f", f=256),
            cst[:, 0:2048].rearrange("p (h f) -> p h f", f=256),
            sq8[:, 40:48].unsqueeze(2).to_broadcast([128, 8, 256]), ALU.mult))
        drain("vector")
        hgv = op("vector", lambda e: e.tensor_mul(hob[:, 0:2048], cst[:, 0:2048], gab[:, :]), "sve")
        wait("tensor", "sve", hgv)
        for vck in range(NKC):
            par = vck % 2
            wait("tensor", "sve", gTo[par])
            tv = op("tensor", lambda e, vck=vck, par=par: e.transpose(
                pT[:, par * 512:par * 512 + 128], hob[:, vck * 128:vck * 128 + 128],
                idb[:, :]), "ste")
            wait("vector", "ste", tv)
            gTo[par] = op("vector", lambda e, vck=vck, cc_=cc_, par=par: e.tensor_tensor(
                v1[:, vck * TPC + cc_ * 128: vck * TPC + cc_ * 128 + 128],
                pT[:, par * 512:par * 512 + 128],
                ogb[:, vck * TPC + cc_ * 128: vck * TPC + cc_ * 128 + 128], ALU.mult), "sve")
    passb_v = C["sve"]

    PH(10)
    fg_on()
    ring = [None] * 4
    xgrd = {0: 0, 1: 0}
    stgrd = {0: 0, 1: 0}
    for db in range(NKC):
        so = (db % 4) * 2048
        if ring[db % 4] is not None:
            wait("gpsimd", "ste", ring[db % 4])
        lv = dma("gpsimd", wbuf[:, so:so + 2048], woutT[:, db, :], "qw%d" % (db % 4))
        wait("tensor", "qw%d" % (db % 4), lv)
        if db == 0:
            wait("tensor", "sve", passb_v)
        for tb in range(NTB):
            p = [pC, pD][tb]
            sl2 = ((db * NTB + tb) % 2) * 512
            wait("sync", "sve", xgrd[(db * NTB + tb) % 2])
            xl = dma("sync", xre[:, sl2:sl2 + 512],
                     xT[db * 128: db * 128 + 128, tb * 512: tb * 512 + 512],
                     "qxr%d" % ((db * NTB + tb) % 2))
            wait("tensor", "sve", stgrd[tb])
            mmv = 0
            for kk in range(NKC):
                mmv = op("tensor", lambda e, so=so, kk=kk, tb=tb, p=p: e.matmul(
                    p[:, :], wbuf[:, so + kk * 128: so + kk * 128 + 128],
                    v1[:, kk * TPC + tb * 512: kk * TPC + tb * 512 + 512],
                    start=(kk == 0), stop=(kk == NKC - 1)), "ste")
            if tb == NTB - 1:
                ring[db % 4] = mmv
            wait("vector", "ste", mmv)
            wait("vector", "qxr%d" % ((db * NTB + tb) % 2), xl)
            wait("vector", "qst%d" % ((db * NTB + tb) % 2),
                 C["qst%d" % ((db * NTB + tb) % 2)])
            ev = op("vector", lambda e, db=db, p=p, sl2=sl2: e.scalar_tensor_tensor(
                x2f[:, sl2:sl2 + 512], p[:, :], modv[:, 32 + db:32 + db + 1],
                xre[:, sl2:sl2 + 512], ALU.mult, ALU.add), "sve")
            stgrd[tb] = ev
            wait("scalar", "sve", ev)
            dma("scalar", yT[db * 128:db * 128 + 128, tb * 512:tb * 512 + 512],
                x2f[:, sl2:sl2 + 512], "qst%d" % ((db * NTB + tb) % 2))
            drain("vector")
            x2v = op("vector", lambda e, db=db, tb=tb, sl2=sl2: e.tensor_copy(
                bqk[:, db * TPC + tb * 512: db * TPC + tb * 512 + 512],
                x2f[:, sl2:sl2 + 512]), "sve")
            xgrd[(db * NTB + tb) % 2] = x2v
    x2_qst = (C["qst0"], C["qst1"])

    PH(11)
    h2v = layer_norm(bqk, bqk, modv[:, 64:80], modv[:, 48:64], "sve", x2v)

    def h2sl(kk, tb, width=512):
        return bqk[:, kk * TPC + tb * 512: kk * TPC + tb * 512 + width]

    PH(12)
    gld = dma("gpsimd", wsm[:, 256:512], gateT[:, 0, :], "qw3")
    wait("tensor", "qw3", gld)
    wait("tensor", "sve", h2v)
    gmm = 0
    for tb in range(NTB):
        p = [pE, pF][tb]
        for kk in range(NKC):
            gmm = op("tensor", lambda e, kk=kk, tb=tb, p=p: e.matmul(
                p[0:16, :], wsm[:, 256 + kk * 16: 256 + kk * 16 + 16],
                h2sl(kk, tb), start=(kk == 0), stop=(kk == NKC - 1)), "ste")
    wait("vector", "ste", gmm)
    glv = 0
    for tb in range(NTB):
        glv = op("vector", lambda e, tb=tb: e.tensor_copy(
            lanC[0:16, tb * 512:tb * 512 + 512], [pE, pF][tb][0:16, :]), "sve")
    cwv = glv
    for tt in range(NCH):
        wait("tensor", "sve", cwv)
        ttv = op("tensor", lambda e, tt=tt: e.transpose(
            pG[0:128, 0:16], lanC[0:16, tt * 128:tt * 128 + 128],
            idf[0:16, 0:16]), "ste")
        wait("vector", "ste", ttv)
        op("vector", lambda e: e.tensor_copy(g16a[:], pG[:, 0:16]))
        drain("vector")
        nmx = op("vector", lambda e: e.reduce_max(gc1[:], g16a[:], axis=AX.X, negate=True), "sve")
        wait("scalar", "sve", nmx)
        exv = op("scalar", lambda e: e.activation(g16b[:], g16a[:], AF.Exp,
                                                  bias=gc1[:, 0:1], scale=1.0), "sse")
        wait("vector", "sse", exv)
        op("vector", lambda e: e.reduce_sum(gc2[:], g16b[:], axis=AX.X))
        drain("vector")
        op("vector", lambda e: e.reciprocal(gc1[:], gc2[:]))
        drain("vector")
        op("vector", lambda e: e.tensor_scalar(g16a[:], g16b[:], gc1[:, 0:1], None, ALU.mult))
        drain("vector")
        op("vector", lambda e: e.reduce_max(g4a[:], g16a[:].rearrange("p (g i) -> p g i", i=4), axis=AX.X))
        drain("vector")
        op("vector", lambda e: e.tensor_tensor(
            g16b[:].rearrange("p (g i) -> p g i", i=4),
            g4a[:].unsqueeze(1).to_broadcast([128, 4, 4]),
            g4a[:].unsqueeze(2).to_broadcast([128, 4, 4]), ALU.is_gt))
        op("vector", lambda e: e.tensor_tensor(
            g16d[:].rearrange("p (g i) -> p g i", i=4),
            g4a[:].unsqueeze(1).to_broadcast([128, 4, 4]),
            g4a[:].unsqueeze(2).to_broadcast([128, 4, 4]), ALU.is_equal))
        drain("vector")
        op("vector", lambda e: e.tensor_mul(g16d[:], g16d[:], t16c[:]))
        drain("vector")
        op("vector", lambda e: e.tensor_add(g16b[:], g16b[:], g16d[:]))
        drain("vector")
        op("vector", lambda e: e.reduce_sum(g4a[:], g16b[:].rearrange("p (g i) -> p g i", i=4), axis=AX.X))
        drain("vector")
        op("vector", lambda e: e.tensor_scalar(g4a[:], g4a[:], 2.0, None, ALU.is_lt))
        drain("vector")
        op("vector", lambda e: e.tensor_tensor(
            g16b[:].rearrange("p (g i) -> p g i", i=4),
            g16a[:].rearrange("p (g i) -> p g i", i=4),
            g4a[:].unsqueeze(2).to_broadcast([128, 4, 4]), ALU.mult))
        drain("vector")
        op("vector", lambda e: e.tensor_tensor(
            g256[:].rearrange("p (a b) -> p a b", b=16),
            g16b[:].unsqueeze(1).to_broadcast([128, 16, 16]),
            g16b[:].unsqueeze(2).to_broadcast([128, 16, 16]), ALU.is_gt))
        op("vector", lambda e: e.tensor_tensor(
            g256b[:].rearrange("p (a b) -> p a b", b=16),
            g16b[:].unsqueeze(1).to_broadcast([128, 16, 16]),
            g16b[:].unsqueeze(2).to_broadcast([128, 16, 16]), ALU.is_equal))
        drain("vector")
        op("vector", lambda e: e.tensor_mul(g256b[:], g256b[:], t256c[:]))
        drain("vector")
        op("vector", lambda e: e.tensor_add(g256[:], g256[:], g256b[:]))
        drain("vector")
        op("vector", lambda e: e.reduce_sum(g16d[:], g256[:].rearrange("p (a b) -> p a b", b=16), axis=AX.X))
        drain("vector")
        op("vector", lambda e: e.tensor_scalar(g16d[:], g16d[:], 4.0, None, ALU.is_lt))
        drain("vector")
        op("vector", lambda e: e.tensor_mul(g16b[:], g16b[:], g16d[:]))
        drain("vector")
        op("vector", lambda e: e.reduce_sum(gc2[:], g16b[:], axis=AX.X))
        drain("vector")
        op("vector", lambda e: e.tensor_scalar(gc2[:], gc2[:], 1e-20, None, ALU.add))
        drain("vector")
        op("vector", lambda e: e.reciprocal(gc1[:], gc2[:]))
        drain("vector")
        cwq = op("vector", lambda e: e.tensor_scalar(g16a[:], g16b[:], gc1[:, 0:1], None, ALU.mult), "sve")
        wait("tensor", "sve", cwq)
        tbv = op("tensor", lambda e, tt=tt: e.transpose(
            pG[0:16, 0:128], g16a[:], idf[:, :]), "ste")
        wait("vector", "ste", tbv)
        cwv = op("vector", lambda e, tt=tt: e.tensor_copy(
            cwl16[:, tt * 128:tt * 128 + 128], pG[0:16, 0:128]), "sve")

    PH(13)
    def aslice(s, tb):
        if s < 16:
            t_, base = h16, s * TPC
        elif s < 32:
            t_, base = ogb, (s - 16) * TPC
        else:
            t_, base = v1, (s - 32) * TPC
        return t_[:, base + tb * 512: base + tb * 512 + 512]

    wait("gpsimd", "ste", C["ste"])
    wait("gpsimd", "qst0", x2_qst[0])
    wait("gpsimd", "qst1", x2_qst[1])
    groups = [[0, 1, 2, 3], [4, 5, 6, 7], [8, 9, 10, 11], [12, 13, 14, 15], ["sh"]]
    upfree = [None, None]
    upld_i = 0
    cbe = cwv
    sgrd = {0: 0, 1: 0}
    ugrd = {0: 0, 1: 0}
    dn_guard = 0
    acc_guard = {0: 0, 1: 0}
    for grp in groups:
        wait("gpsimd", "qac0", C["qac0"])
        wait("gpsimd", "qac1", C["qac1"])
        shared = grp[0] == "sh"
        nexp = 1 if shared else len(grp)
        nj = 22 if shared else 11
        for ei in range(nexp):
            eidx = None if shared else grp[ei]
            if not shared:
                wait("sync", "sve", cbe)
                crv = dma("sync", cwrow[0:1, 0:TPC], cwl16[eidx:eidx + 1, :], "qio")
                for tb in range(NTB):
                    wait("tensor", "sve", cbe)
                    wait("tensor", "qio", crv)
                    cbv = op("tensor", lambda e, tb=tb: e.matmul(
                        pG[:, :], onesbf[0:1, :],
                        cwrow[0:1, tb * 512:tb * 512 + 512],
                        start=True, stop=True), "ste")
                    wait("vector", "ste", cbv)
                    cbe = op("vector", lambda e, tb=tb: e.tensor_copy(
                        cwbc[:, tb * 512:tb * 512 + 512], pG[:, :]), "sve")
            gmmv = 0
            for jj in range(nj):
                for half in range(2):
                    jb = jj + half * nj
                    slot = upld_i % 2
                    upld_i += 1
                    if upfree[slot] is not None:
                        wait("gpsimd", "ste", upfree[slot])
                    src = supT[:, jb, :] if shared else upT[eidx, :, jb, :]
                    lv = dma("gpsimd", wbuf[:, slot * 2048:slot * 2048 + 2048], src,
                             "qw%d" % slot)
                    wait("tensor", "qw%d" % slot, lv)
                    wait("tensor", "sse" if half == 0 else "sve",
                         sgrd[0] if half == 0 else ugrd[0])
                    mmv = 0
                    for kk in range(NKC):
                        for tb in range(NTB):
                            p = ([pA, pB] if half == 0 else [pC, pD])[tb]
                            mmv = op("tensor", lambda e, slot=slot, kk=kk, tb=tb, p=p: e.matmul(
                                p[:, :], wbuf[:, slot * 2048 + kk * 128: slot * 2048 + kk * 128 + 128],
                                h2sl(kk, tb), start=(kk == 0), stop=(kk == NKC - 1)), "ste")
                    upfree[slot] = mmv
                    if half == 0:
                        gmmv = mmv
                wait("scalar", "ste", gmmv)
                sgv = 0
                for tb in range(NTB):
                    sgv = op("scalar", lambda e, tb=tb: e.activation(
                        sgil[:, tb * 512:tb * 512 + 512], [pA, pB][tb][:, :], AF.Silu), "sse")
                sgrd[0] = sgv
                wait("vector", "ste", mmv)
                wait("vector", "sse", sgv)
                s_global = jj if shared else ei * 11 + jj
                for tb in range(NTB):
                    op("vector", lambda e, tb=tb: e.tensor_tensor(
                        tmul[:, tb * 512:tb * 512 + 512], sgil[:, tb * 512:tb * 512 + 512],
                        [pC, pD][tb][:, :], ALU.mult))
                drain("vector")
                av = 0
                for tb in range(NTB):
                    if shared:
                        av = op("vector", lambda e, s_global=s_global, tb=tb: e.tensor_copy(
                            aslice(s_global, tb), tmul[:, tb * 512:tb * 512 + 512]), "sve")
                    else:
                        av = op("vector", lambda e, s_global=s_global, tb=tb: e.tensor_mul(
                            aslice(s_global, tb), tmul[:, tb * 512:tb * 512 + 512],
                            cwbc[:, tb * 512:tb * 512 + 512]), "sve")
                ugrd[0] = av
        up_done = C["sve"]
        for db in range(NKC):
            wait("gpsimd", "ste", dn_guard)
            dls = 0
            for ei in range(nexp):
                slot = 4096 + ei * 1408
                src = sdownT[:, db, :] if shared else downT[grp[ei], :, db, :]
                dls = dma("gpsimd", wbuf[:, slot:slot + nj * 128], src, "qw2")
            wait("tensor", "qw2", dls)
            if db == 0:
                wait("tensor", "sve", up_done)
            for tb in range(NTB):
                p = [pE, pF][tb]
                wait("tensor", "sve", acc_guard[tb])
                mmv = 0
                for ei in range(nexp):
                    slot = 4096 + ei * 1408
                    for ik in range(nj):
                        s_global = ik if shared else ei * 11 + ik
                        mmv = op("tensor", lambda e, slot=slot, ik=ik, tb=tb, p=p,
                                 s_global=s_global, first=(ei == 0 and ik == 0),
                                 last=(ei == nexp - 1 and ik == nj - 1): e.matmul(
                            p[:, :], wbuf[:, slot + ik * 128: slot + ik * 128 + 128],
                            aslice(s_global, tb), start=first, stop=last), "ste")
                if tb == NTB - 1:
                    dn_guard = mmv
                wait("vector", "ste", mmv)
                sl3 = ((db * NTB + tb) % 2) * 512
                wait("vector", "qac%d" % ((db * NTB + tb) % 2),
                     C["qac%d" % ((db * NTB + tb) % 2)])
                e3 = op("vector", lambda e, db=db, p=p, sl3=sl3: e.tensor_scalar(
                    x2f[:, sl3:sl3 + 512], p[:, :], modv[:, 80 + db:80 + db + 1],
                    None, ALU.mult), "sve")
                acc_guard[tb] = e3
                wait("gpsimd", "sve", e3)
                dma("gpsimd", yT[db * 128:db * 128 + 128, tb * 512:tb * 512 + 512],
                    x2f[:, sl3:sl3 + 512], "qac%d" % ((db * NTB + tb) % 2), accum=True)

    PH(14)
    fg_off()
    wait("gpsimd", "qac0", C["qac0"])
    wait("gpsimd", "qac1", C["qac1"])
    wait("scalar", "qst0", C["qst0"])
    wait("scalar", "qst1", C["qst1"])

    if KPH < 99:
        cuts = sorted(k for k in PHM if k > KPH)
        if cuts:
            cev, cg, cr = PHM[cuts[0]]
            del EV[cev:]
            del EVG[cev:]
            for engn in ENGS:
                if cg:
                    EV.append((engn, lambda e, v=cg: e.wait_ge(GSEM, v)))
                for s, rv in cr.items():
                    if rv:
                        EV.append((engn, lambda e, S=SEM[s], v=rv: e.wait_ge(S, v)))
    else:
        fin = EVG[-1]
        for engn in ENGS:
            on(engn, lambda e, v=fin: e.wait_ge(GSEM, v))
            for s, rv in R.items():
                if rv:
                    on(engn, lambda e, S=SEM[s], v=rv: e.wait_ge(S, v))

    block = ctx.enter_context(nc.Block())

    def runner(eng_name):
        def body(eng):
            for nm, fn in EV:
                if nm == eng_name:
                    fn(eng)
        return body

    block.sync(runner("sync"))
    block.gpsimd(runner("gpsimd"))
    block.vector(runner("vector"))
    block.scalar(runner("scalar"))
    block.tensor(runner("tensor"))
    ctx.close()
    return nc


def _wtile(W, mb):
    Din, M = W.shape[1], W.shape[0]
    kb = Din // 128
    arr = np.ascontiguousarray(W.T).reshape(kb, 128, M // mb, mb)
    return np.ascontiguousarray(arr.transpose(1, 2, 0, 3)).reshape(128, M // mb, kb * mb)


_NC_CACHE = {}


def kernel(**inp):
    S = np.asarray(inp["L0_attn_x"]).shape[1]
    TPC = S // 4
    if TPC not in _NC_CACHE:
        _NC_CACHE[TPC] = build_nc(TPC)
    nc = _NC_CACHE[TPC]

    x = np.asarray(inp["L0_attn_x"], np.float32)
    cc = np.asarray(inp["L0_c"], np.float32)
    wmod = np.asarray(inp["L0_wmod"], np.float32)

    wq = _wtile(np.asarray(inp["L0_attn_Wq"], np.float32), 128)
    wk = _wtile(np.asarray(inp["L0_attn_Wk"], np.float32), 128)
    wv = _wtile(np.asarray(inp["L0_attn_Wv"], np.float32), 256)
    wog = _wtile(np.asarray(inp["L0_attn_Wog"], np.float32), 128)
    wif = _wtile(np.concatenate([np.asarray(inp["L0_attn_Wig"], np.float32),
                                 np.asarray(inp["L0_attn_Wfg"], np.float32)], 0), 16)
    wout = _wtile(np.asarray(inp["L0_attn_Wout"], np.float32), 128)
    gate = _wtile(np.asarray(inp["L0_ffn_gate_w"], np.float32), 16)
    up = np.stack([_wtile(np.asarray(inp["L0_ffn_expert_up_w"][e], np.float32), 128)
                   for e in range(E)])
    down = np.stack([_wtile(np.asarray(inp["L0_ffn_expert_down_w"][e], np.float32), 128)
                     for e in range(E)])
    sup = _wtile(np.asarray(inp["L0_ffn_shared_up_w"], np.float32), 128)
    sdn = _wtile(np.asarray(inp["L0_ffn_shared_down_w"], np.float32), 128)
    bigv = np.ascontiguousarray(np.asarray(inp["L0_attn_big"], np.float32)[:, None])
    bfgv = np.ascontiguousarray(np.asarray(inp["L0_attn_bfg"], np.float32)[:, None])
    gm = np.ascontiguousarray(np.asarray(inp["L0_attn_gamma"], np.float32).reshape(1, NH * DV))

    mask01 = np.ascontiguousarray(np.triu(np.ones((128, 128), np.float32)))
    identb = np.eye(128).astype(ml_dtypes.bfloat16)
    identf = np.eye(128, dtype=np.float32)
    tril16 = np.ascontiguousarray(
        np.tril(np.ones((4, 4), np.float32), -1).reshape(1, 16).repeat(128, 0))
    tril256 = np.ascontiguousarray(
        np.tril(np.ones((16, 16), np.float32), -1).reshape(1, 256).repeat(128, 0))

    in_maps = []
    for c in range(8):
        bidx, r = c // 4, c % 4
        xs = np.ascontiguousarray(x[bidx, r * TPC:(r + 1) * TPC, :].T)
        cT = np.ascontiguousarray(cc[bidx].reshape(NKC, 128).T)
        wms = np.ascontiguousarray(wmod[:, 3072 * r:3072 * (r + 1)])
        flag = np.zeros((8, 3), np.float32)
        flag[:, :r] = 1.0
        ccneg = (1.0 - flag) * NEG
        in_maps.append(dict(
            xT=xs, cTb=cT, wmod_sl=wms, wqT=wq, wkT=wk, wvT=wv, wogT=wog,
            wifT=wif, bifi=bigv, biff=bfgv, gamma=gm, woutT=wout, gateT=gate, upT=up,
            downT=down, supT=sup, sdownT=sdn, mask01=mask01, identb=identb,
            identf=identf, tril16=tril16, tril256=tril256,
            ccflag=flag, ccneg=ccneg))

    res = run_bass_kernel_spmd(nc, in_maps, core_ids=list(range(8)))
    out = np.empty((2, S, D), np.float32)
    for c in range(8):
        bidx, r = c // 4, c % 4
        out[bidx, r * TPC:(r + 1) * TPC, :] = res.results[c]["yT"].T
    return out


# version 19 (final)
# speedup vs baseline: 2.9793x; 1.0084x over previous
import os
import numpy as np
import ml_dtypes
from contextlib import ExitStack
import concourse.bass as bass
import concourse.mybir as mybir
from concourse.bass_utils import run_bass_kernel_spmd

F32 = mybir.dt.float32
BF16 = mybir.dt.bfloat16
AF = mybir.ActivationFunctionType
ALU = mybir.AluOpType
AX = mybir.AxisListType

D = 2048
NH = 8
DV = 256
NKC = 16
E = 16
NEG = -1e30
QSCALE = float(128) ** -0.5
NU = 128 * 2056


def build_nc(TPC):
    KPH = int(os.environ.get("KPH", "99"))
    NCH = TPC // 128
    NTB = TPC // 512
    nc = bass.Bass("TRN2", target_bir_lowering=False)

    def di(name, shape, dt=F32):
        return nc.dram_tensor(name, shape, dt, kind="ExternalInput").ap()

    xT = di("xT", [D, TPC])
    cTb = di("cTb", [128, NKC])
    wmod_sl = di("wmod_sl", [D, 3072])
    wqT = di("wqT", [128, NH, NKC * 128])
    wkT = di("wkT", [128, NH, NKC * 128])
    wvT = di("wvT", [128, NH, NKC * 256])
    wogT = di("wogT", [128, NKC, NKC * 128])
    wifT = di("wifT", [128, 1, NKC * 16])
    bifi = di("bifi", [8, 1])
    biff = di("biff", [8, 1])
    gamma = di("gamma", [1, NH * DV])
    woutT = di("woutT", [128, NKC, NKC * 128])
    gateT = di("gateT", [128, 1, NKC * 16])
    upT = di("upT", [E, 128, 22, NKC * 128])
    downT = di("downT", [E, 128, 16, 11 * 128])
    supT = di("supT", [128, 44, NKC * 128])
    sdownT = di("sdownT", [128, 16, 22 * 128])
    mask01 = di("mask01", [128, 128])
    identb = di("identb", [128, 128], BF16)
    identf = di("identf", [128, 128])
    tril16 = di("tril16", [128, 16])
    tril256 = di("tril256", [128, 256])
    ccflag = di("ccflag", [8, 3])
    ccneg = di("ccneg", [8, 3])
    yT = nc.dram_tensor("yT", [D, TPC], F32, kind="ExternalOutput").ap()

    ag1_in = nc.dram_tensor("ag1_in", [1, 3072], F32, kind="Internal").ap()
    ag1_out = nc.dram_tensor("ag1_out", [4, 3072], F32, kind="Internal").ap()
    ag2_in = nc.dram_tensor("ag2_in", [1, NU + 32], F32, kind="Internal").ap()
    ag2_out = nc.dram_tensor("ag2_out", [4, NU + 32], F32, kind="Internal").ap()

    ctx = ExitStack()
    ctx.__enter__()

    def sb(name, shape, dt=F32):
        return ctx.enter_context(nc.sbuf_tensor(name, shape, dt))

    h16 = sb("h16", [128, NCH * 2056], BF16)
    ogb = sb("ogb", [128, NKC * TPC], BF16)
    v1 = sb("v1", [128, NCH * 2056], BF16)
    bqk = sb("bqk", [128, NKC * TPC], BF16)
    wbuf = sb("wbuf", [128, 9728], BF16)
    wsm = sb("wsm", [128, 512], BF16)
    cst = sb("cst", [128, 2056])
    c16 = sb("c16", [128, 2056], BF16)
    lanA = sb("lanA", [128, TPC])
    lanB = sb("lanB", [128, TPC])
    e32t = sb("e32t", [128, NCH * 32])
    e24t = sb("e24t", [128, NCH * 24])
    e24b = sb("e24b", [128, NCH * 24], BF16)
    modv = sb("modv", [128, 6 * 16])
    gab = sb("gab", [128, NH * DV], BF16)
    mkb = sb("mkb", [128, 128])
    idb = sb("idb", [128, 128], BF16)
    idf = sb("idf", [128, 128])
    t16c = sb("t16c", [128, 16])
    t256c = sb("t256c", [128, 256])
    onesf = sb("onesf", [1, 128])
    onesbf = sb("onesbf", [1, 128], BF16)
    onescb = sb("onescb", [128, 1], BF16)
    scT = sb("scT", [128, NKC], BF16)
    bifib = sb("bifib", [8, 1])
    biffb = sb("biffb", [8, 1])
    bifis = sb("bifis", [8, 1])
    biffs = sb("biffs", [8, 1])
    gmr = sb("gmr", [1, NH * DV])
    ccf = sb("ccf", [8, 3])
    ccn = sb("ccn", [8, 3])
    wrowp = sb("wrowp", [1, 8])
    wrow1 = sb("wrow1", [1, 8])
    wbcs = sb("wbcs", [128, 16])
    scrA = sb("scrA", [128, 1024])
    a3s = sb("a3s", [128, 2 * 128], BF16)
    k1s = sb("k1s", [128, 2 * 128], BF16)
    t1s = sb("t1s", [128, 2056], BF16)
    xsq = t1s
    sq8 = sb("sq8", [128, 48])
    sgil = sb("sgil", [128, 2 * 512], BF16)
    tmul = sb("tmul", [128, 2 * 512], BF16)
    x2f = sb("x2f", [128, 2 * 512])
    xre = sb("xre", [128, 2 * 512])
    scrC = xre
    lanC = xre
    cwl16 = gab[0:16, 0:TPC]
    cwrow = scrA[0:1, :].bitcast(BF16)
    cwbc = gab[:, TPC:2 * TPC] if TPC <= NH * DV // 2 else gab[:, 0:TPC]
    g16a = sb("g16a", [128, 16])
    g16b = sb("g16b", [128, 16])
    g16d = sb("g16d", [128, 16])
    g4a = sb("g4a", [128, 4])
    g256 = e32t[:, 0:256]
    g256b = t1s[:, 0:512].bitcast(F32)
    gc1 = sb("gc1", [128, 1])
    gc2 = sb("gc2", [128, 1])

    hob = scrA[:].bitcast(BF16)
    uvw = v1[:].bitcast(F32)

    def psm(name):
        return ctx.enter_context(nc.psum_tensor(name, [128, 512], F32))

    pA, pB, pC, pD, pE, pF, pG = (psm(n) for n in "pA pB pC pD pE pF pG".split())
    PSF = [pA, pB, pC, pD, pE, pF, pG]
    pT = ctx.enter_context(nc.psum_tensor("pT", [128, 1024], BF16))

    SEM, C = {}, {}
    for s in ["qsc", "qxb", "qw0", "qw1", "qw2", "qw3", "qc", "qio",
              "qxr0", "qxr1", "qst0", "qst1", "qac0", "qac1",
              "ste", "sve", "sse", "scc"]:
        SEM[s] = ctx.enter_context(nc.semaphore(s))
        C[s] = 0
    GSEM = ctx.enter_context(nc.semaphore("gsem"))
    G = [0]
    R = {s: 0 for s in SEM}
    FGM = [False]
    ENGS = ["sync", "gpsimd", "vector", "scalar", "tensor"]
    EV = []
    EVG = []
    PHM = {}

    def PH(n):
        if n not in PHM:
            PHM[n] = (len(EV), G[0], dict(R))

    def on(eng, fn):
        EV.append((eng, fn))
        EVG.append(G[0])

    def chain(eng):
        v = G[0]
        if v:
            on(eng, lambda e, v=v: e.wait_ge(GSEM, v))

    def dma(eng, out, in_, sem, accum=False, slow=False):
        kw = {}
        if accum:
            kw["accum_op"] = ALU.add
        if slow:
            kw["allow_slow_non_contiguous"] = True
        C[sem] += 16
        if FGM[0]:
            R[sem] += 16
            on(eng, lambda e, o=out, i=in_, S=SEM[sem], kw=kw:
               e.dma_start(out=o, in_=i, **kw).then_inc(S, 16))
        else:
            chain(eng)
            G[0] += 16
            on(eng, lambda e, o=out, i=in_, kw=kw:
               e.dma_start(out=o, in_=i, **kw).then_inc(GSEM, 16))
        return C[sem]

    def wait(eng, sem, val):
        if FGM[0] and val:
            on(eng, lambda e, S=SEM[sem], v=val: e.wait_ge(S, v))

    def op(eng, fn, sem=None):
        if FGM[0]:
            if sem is None:
                on(eng, fn)
                return None
            C[sem] += 1
            R[sem] += 1
            on(eng, lambda e, f=fn, S=SEM[sem]: f(e).then_inc(S, 1))
            return C[sem]
        chain(eng)
        G[0] += 1
        if eng == "vector":
            on(eng, fn)
            on(eng, lambda e: e.drain())
            on(eng, lambda e: e.sem_inc(GSEM, 1))
        else:
            on(eng, lambda e, f=fn: f(e).then_inc(GSEM, 1))
        if sem is None:
            return None
        C[sem] += 1
        return C[sem]

    def drain(eng):
        on(eng, lambda e: e.drain())

    def opd(eng, fn, sem):
        C[sem] += 1
        if FGM[0]:
            R[sem] += 1
            on(eng, fn)
            on(eng, lambda e: e.drain())
            on(eng, lambda e, S=SEM[sem]: e.sem_inc(S, 1))
            return C[sem]
        chain(eng)
        G[0] += 1
        on(eng, fn)
        on(eng, lambda e: e.drain())
        on(eng, lambda e: e.sem_inc(GSEM, 1))
        return C[sem]

    def barrier():
        for engn in ENGS:
            if G[0]:
                on(engn, lambda e, v=G[0]: e.wait_ge(GSEM, v))
            for s, rv in R.items():
                if rv:
                    on(engn, lambda e, S=SEM[s], v=rv: e.wait_ge(S, v))

    def fg_on():
        barrier()
        for s in SEM:
            delta = C[s] - R[s]
            if delta:
                on("sync", lambda e, S=SEM[s], d=delta: e.sem_inc(S, d))
                R[s] += delta
        FGM[0] = True

    def fg_off():
        FGM[0] = False
        barrier()


    fg_on()
    ld0 = 0
    for out, src in [(mkb[:], mask01), (idb[:], identb), (idf[:], identf),
                     (gmr[:], gamma), (bifib[:], bifi), (biffb[:], biff), (ccf[:], ccflag),
                     (ccn[:], ccneg), (t16c[:], tril16), (t256c[:], tril256)]:
        ld0 = dma("sync", out, src, "qc")
    csem = dma("gpsimd", scT[:], cTb, "qsc")
    xbsem = dma("gpsimd", ogb[:, 0:NKC * TPC].rearrange("p (g t) -> p g t", g=NKC),
                xT.rearrange("(g p) t -> g p t", p=128).transpose([1, 0, 2]), "qxb")
    op("vector", lambda e: e.memset(onesf[:], 1.0))
    op("vector", lambda e: e.memset(onesbf[:], 1.0))
    op("vector", lambda e: e.memset(onescb[:], 1.0))
    op("vector", lambda e: e.memset(lanB[96:104, :], 0.0))
    op("vector", lambda e: e.memset(cst[:], 0.0))
    op("vector", lambda e: e.memset(c16[:], 0.0))
    vini = op("vector", lambda e: e.memset(v1[:], 1.0), "sve")
    wait("vector", "qc", ld0)
    op("vector", lambda e: e.tensor_scalar(bifis[:], bifib[:], 1.0 / 15.0, None, ALU.mult))
    vbif = op("vector", lambda e: e.tensor_scalar(biffs[:], biffb[:], 1.0 / 15.0, None, ALU.mult), "sve")
    gv = 0
    wait("tensor", "qc", ld0)
    wait("tensor", "sve", vini)
    for h in range(NH):
        p = PSF[h % 2]
        if h >= 2:
            wait("tensor", "sve", gv - 1)
        tv = op("tensor", lambda e, h=h, p=p: e.matmul(
            p[:, 0:256], onesf[0:1, :], gmr[0:1, h * 256:h * 256 + 256],
            start=True, stop=True), "ste")
        wait("vector", "ste", tv)
        gv = op("vector", lambda e, h=h, p=p: e.tensor_copy(
            gab[:, h * 256:h * 256 + 256], p[:, 0:256]), "sve")

    PH(1)
    wait("scalar", "qsc", csem)
    scs = op("scalar", lambda e: e.activation(scT[:], scT[:], AF.Silu), "sse")
    mod_last = [0] * NKC
    for kk in range(NKC):
        so = (kk % 2) * 3072
        if kk >= 2:
            wait("gpsimd", "ste", mod_last[kk - 2])
        wl = dma("gpsimd", wbuf[:, so:so + 3072],
                 wmod_sl[kk * 128:kk * 128 + 128, :], "qw0" if kk % 2 == 0 else "qw1")
        wait("tensor", "qw0" if kk % 2 == 0 else "qw1", wl)
        if kk == 0:
            wait("tensor", "sse", scs)
            wait("tensor", "sve", gv)
        mv = 0
        for nb in range(6):
            mv = op("tensor", lambda e, kk=kk, nb=nb, so=so: e.matmul(
                PSF[nb][0:1, :], scT[:, kk:kk + 1],
                wbuf[:, so + nb * 512: so + nb * 512 + 512],
                start=(kk == 0), stop=(kk == NKC - 1)), "ste")
        mod_last[kk] = mv
    wait("vector", "ste", mod_last[NKC - 1])
    mev = 0
    for nb in range(6):
        r = (nb // 2) * 32
        co = (nb % 2) * 512
        mev = op("vector", lambda e, nb=nb, r=r, co=co: e.tensor_copy(
            scrA[r:r + 1, co:co + 512], PSF[nb][0:1, :]), "sve")
    wait("sync", "sve", mev)
    agi = dma("sync", ag1_in[0, :].rearrange("(a b) -> a b", a=3),
              scrA[0:65:32, 0:1024], "qio")
    wait("gpsimd", "qio", agi)
    cc1 = op("gpsimd", lambda e: e.collective_compute(
        "AllGather", ALU.bypass, ins=[ag1_in[:]], outs=[ag1_out[:]],
        replica_groups=[[0, 1, 2, 3], [4, 5, 6, 7]]), "scc")
    wait("sync", "scc", cc1)
    ag1f = ag1_out.rearrange("a b -> (a b)")
    mvl = 0
    for v6 in range(6):
        mvl = dma("sync", modv[:, v6 * 16:v6 * 16 + 16],
                  ag1f[2048 * v6:2048 * v6 + 2048].rearrange("(c p) -> p c", p=128), "qio",
                  slow=True)
    wait("vector", "qio", mvl)
    op("vector", lambda e: e.tensor_scalar(modv[:, 16:32], modv[:, 16:32], 1.0, None, ALU.add))
    op("vector", lambda e: e.tensor_scalar(modv[:, 64:80], modv[:, 64:80], 1.0, None, ALU.add))
    mgv = 0
    for blk in (2, 5):
        sl = modv[:, blk * 16:blk * 16 + 16]
        op("vector", lambda e, sl=sl: e.tensor_scalar(g16a[:], sl, -1.0, None, ALU.mult))
        op("vector", lambda e, sl=sl: e.tensor_scalar(g16b[:], sl, 0.0, None, ALU.is_ge))
        drain("vector")
        op("vector", lambda e, sl=sl: e.tensor_max(g16d[:], sl, g16a[:]))
        drain("vector")
        op("vector", lambda e: e.tensor_scalar(g16a[:], g16d[:], 0.25, None, ALU.max))
        drain("vector")
        op("vector", lambda e: e.tensor_scalar(g16b[:], g16b[:], 2.0, -1.0, ALU.mult, ALU.add))
        drain("vector")
        mgv = op("vector", lambda e, sl=sl: e.tensor_mul(sl, g16a[:], g16b[:]), "sve")

    def layer_norm(src16, dst16, cacol, sacol, rsem, rval):
        wait("scalar", rsem, rval)
        wait("tensor", rsem, rval)
        wait("tensor", "sve", C["sve"])
        mm_last = [0] * NKC
        for kk in range(NKC):
            so = (kk % 2) * TPC
            if kk >= 2:
                wait("scalar", "ste", mm_last[kk - 2])
            sqd = op("scalar", lambda e, kk=kk, so=so: e.activation(
                xsq[:, so:so + TPC], src16[:, kk * TPC:kk * TPC + TPC], AF.Square), "sse")
            for tb in range(NTB):
                op("tensor", lambda e, kk=kk, tb=tb: e.matmul(
                    [pA, pB][tb][0:1, :], onescb[:, 0:1],
                    src16[:, kk * TPC + tb * 512: kk * TPC + tb * 512 + 512],
                    start=(kk == 0), stop=(kk == NKC - 1)), "ste")
            wait("tensor", "sse", sqd)
            mm2 = 0
            for tb in range(NTB):
                mm2 = op("tensor", lambda e, kk=kk, tb=tb, so=so: e.matmul(
                    [pC, pD][tb][0:1, :], onescb[:, 0:1],
                    xsq[:, so + tb * 512: so + tb * 512 + 512],
                    start=(kk == 0), stop=(kk == NKC - 1)), "ste")
            mm_last[kk] = mm2
        PH(21)
        wait("vector", "ste", mm_last[NKC - 1])
        for tb in range(NTB):
            op("vector", lambda e, tb=tb: e.tensor_scalar(
                lanB[0:1, tb * 512:tb * 512 + 512], [pA, pB][tb][0:1, :],
                1.0 / D, None, ALU.mult))
            op("vector", lambda e, tb=tb: e.tensor_scalar(
                lanA[0:1, tb * 512:tb * 512 + 512], [pC, pD][tb][0:1, :],
                1.0 / D, None, ALU.mult))
        drain("vector")
        op("vector", lambda e: e.tensor_mul(lanC[0:1, 0:TPC],
                                            lanB[0:1, 0:TPC], lanB[0:1, 0:TPC]))
        drain("vector")
        op("vector", lambda e: e.tensor_sub(lanB[32:33, 0:TPC],
                                            lanA[0:1, 0:TPC], lanC[0:1, 0:TPC]))
        drain("vector")
        op("vector", lambda e: e.tensor_scalar(lanB[32:33, 0:TPC],
                                               lanB[32:33, 0:TPC], 1e-5, None, ALU.add))
        drain("vector")
        rvv = opd("vector", lambda e: e.reciprocal(lanA[32:33, 0:TPC],
                                                   lanB[32:33, 0:TPC]), "sve")
        wait("scalar", "sve", rvv)
        rsv = op("scalar", lambda e: e.activation(lanA[0:1, 0:TPC],
                                                  lanA[32:33, 0:TPC], AF.Sqrt), "sse")
        op("vector", lambda e: e.tensor_scalar(lanC[0:1, 0:TPC],
                                               lanB[0:1, 0:TPC], -1.0, None, ALU.mult))
        wait("vector", "sse", rsv)
        drain("vector")
        nmv = op("vector", lambda e: e.tensor_mul(lanB[0:1, 0:TPC],
                                                  lanC[0:1, 0:TPC],
                                                  lanA[0:1, 0:TPC]), "sve")
        PH(22)
        wait("tensor", "sve", nmv)
        bc1 = 0
        for tb in range(NTB):
            bc1 = op("tensor", lambda e, tb=tb: e.matmul(
                [pA, pB][tb][:, :], onesf[0:1, :],
                lanA[0:1, tb * 512:tb * 512 + 512], start=True, stop=True), "ste")
            bc1 = op("tensor", lambda e, tb=tb: e.matmul(
                [pC, pD][tb][:, :], onesf[0:1, :],
                lanB[0:1, tb * 512:tb * 512 + 512], start=True, stop=True), "ste")
        wait("vector", "ste", bc1)
        PH(23)
        outv = 0
        for kk in range(NKC):
            for tb in range(NTB):
                so = tb * 512
                op("vector", lambda e, kk=kk, tb=tb, so=so: e.tensor_tensor(
                    scrC[:, so:so + 512],
                    src16[:, kk * TPC + so: kk * TPC + so + 512], [pA, pB][tb][:, :], ALU.mult))
            for tb in range(NTB):
                so = tb * 512
                op("vector", lambda e, kk=kk, tb=tb, so=so: e.tensor_tensor(
                    scrC[:, so:so + 512], scrC[:, so:so + 512], [pC, pD][tb][:, :], ALU.add))
            drain("vector")
            outv = op("vector", lambda e, kk=kk: e.tensor_scalar(
                dst16[:, kk * TPC:kk * TPC + TPC], scrC[:, 0:TPC],
                cacol[:, kk:kk + 1], sacol[:, kk:kk + 1], ALU.mult, ALU.add), "sve")
            drain("vector")
        return outv

    PH(2)
    wait("vector", "sve", mgv)
    h1v = layer_norm(ogb, h16, modv[:, 16:32], modv[:, 0:16], "qxb", xbsem)

    def h1sl(kk, tb, width=512):
        return h16[:, kk * TPC + tb * 512: kk * TPC + tb * 512 + width]

    PH(3)
    blocks = [("q", h, wqT[:, h, :]) for h in range(NH)]
    blocks += [("k", h, wkT[:, h, :]) for h in range(NH)]
    blocks += [("og", ob, wogT[:, ob, :]) for ob in range(NKC)]
    ring = [None] * 4
    evpar = [0, 0]
    for bi, (kind, idx, wsl) in enumerate(blocks):
        so = (bi % 4) * 2048
        if ring[bi % 4] is not None:
            wait("gpsimd", "ste", ring[bi % 4])
        lv = dma("gpsimd", wbuf[:, so:so + 2048], wsl, "qw%d" % (bi % 4))
        wait("tensor", "qw%d" % (bi % 4), lv)
        if bi == 0:
            wait("tensor", "sve", h1v)
        if bi >= 1:
            wait("tensor", "sve" if kind != "og" or blocks[bi - 1][0] != "og" else "sse",
                 evpar[0] if False else None)
        mmv = 0
        for tb in range(NTB):
            p = [pE, pF][tb]
            if bi >= 1:
                wait("tensor", evgrd[tb][0], evgrd[tb][1])
            for kk in range(NKC):
                mmv = op("tensor", lambda e, so=so, kk=kk, tb=tb, p=p: e.matmul(
                    p[:, :], wbuf[:, so + kk * 128: so + kk * 128 + 128],
                    h1sl(kk, tb), start=(kk == 0), stop=(kk == NKC - 1)), "ste")
        ring[bi % 4] = mmv
        wait("vector" if kind != "og" else "scalar", "ste", mmv)
        evgrd = {}
        for tb in range(NTB):
            p = [pE, pF][tb]
            if kind == "q":
                ev = op("vector", lambda e, idx=idx, tb=tb, p=p: e.tensor_scalar(
                    bqk[:, idx * TPC + tb * 512: idx * TPC + tb * 512 + 512],
                    p[:, :], QSCALE, None, ALU.mult), "sve")
                evgrd[tb] = ("sve", ev)
            elif kind == "k":
                ev = op("vector", lambda e, idx=idx, tb=tb, p=p: e.tensor_copy(
                    bqk[:, (NH + idx) * TPC + tb * 512: (NH + idx) * TPC + tb * 512 + 512],
                    p[:, :]), "sve")
                evgrd[tb] = ("sve", ev)
            else:
                ev = op("scalar", lambda e, idx=idx, tb=tb, p=p: e.activation(
                    ogb[:, idx * TPC + tb * 512: idx * TPC + tb * 512 + 512],
                    p[:, :], AF.Sigmoid), "sse")
                evgrd[tb] = ("sse", ev)
    og_done = evgrd[NTB - 1][1]

    PH(35)
    wifl = dma("gpsimd", wsm[:, 0:256], wifT[:, 0, :], "qw3")
    wait("tensor", "qw3", wifl)
    ifm = 0
    for tb in range(NTB):
        for half in range(2):
            p = ([pA, pB] if half == 0 else [pC, pD])[tb]
            for kk in range(NKC):
                ifm = op("tensor", lambda e, kk=kk, tb=tb, p=p, half=half: e.matmul(
                    p[0:8, :], wsm[:, kk * 16 + half * 8: kk * 16 + half * 8 + 8],
                    h1sl(kk, tb), start=(kk == 0), stop=(kk == NKC - 1)), "ste")
    wait("scalar", "ste", ifm)
    wait("scalar", "sve", vbif)
    for tb in range(NTB):
        op("scalar", lambda e, tb=tb: e.activation(
            lanC[64:72, tb * 512:tb * 512 + 512], [pA, pB][tb][0:8, :],
            AF.Tanh, bias=bifis[:, 0:1], scale=1.0 / 15.0))
        op("scalar", lambda e, tb=tb: e.activation(
            lanC[32:40, tb * 512:tb * 512 + 512], [pC, pD][tb][0:8, :],
            AF.Tanh, bias=biffs[:, 0:1], scale=1.0 / 15.0))
    tg = op("scalar", lambda e: e.activation(gc1[0:8, 0:1], bifis[:, 0:1], AF.Square), "sse")
    wait("vector", "sse", tg)
    op("vector", lambda e: e.tensor_scalar(
        lanC[64:72, 0:TPC], lanC[64:72, 0:TPC], 15.0, None, ALU.mult))
    ifv = op("vector", lambda e: e.tensor_scalar(
        lanC[32:40, 0:TPC], lanC[32:40, 0:TPC], 15.0, None, ALU.mult), "sve")
    drain("vector")

    PH(36)
    wait("vector", "sve", vini)
    wait("gpsimd", "ste", C["ste"])
    vld = [0] * NH
    vld[0] = dma("gpsimd", wbuf[:, 0:4096], wvT[:, 0, :], "qw0")
    vld[1] = dma("gpsimd", wbuf[:, 4096:8192], wvT[:, 1, :], "qw1")
    vev = {0: 0, 1: 0}
    for h in range(NH):
        so = (h % 2) * 4096
        wait("tensor", "qw%d" % (h % 2), vld[h])
        lastmm = 0
        for tt in range(NCH):
            p = [pC, pD][tt % 2]
            if tt >= 2 or h > 0:
                wait("tensor", "sve", vev[tt % 2])
            mmv = 0
            for kk in range(NKC):
                mmv = op("tensor", lambda e, h=h, tt=tt, kk=kk, so=so, p=p: e.matmul(
                    p[:, 0:256], h16[:, kk * TPC + tt * 128: kk * TPC + tt * 128 + 128],
                    wbuf[:, so + kk * 256: so + kk * 256 + 256],
                    start=(kk == 0), stop=(kk == NKC - 1)), "ste")
            wait("vector", "ste", mmv)
            vev[tt % 2] = op("vector", lambda e, h=h, tt=tt, p=p: e.tensor_copy(
                v1[:, tt * 2056 + 257 * h: tt * 2056 + 257 * h + 256], p[:, 0:256]), "sve")
            lastmm = mmv
        if h + 2 < NH:
            wait("gpsimd", "ste", lastmm)
            vld[h + 2] = dma("gpsimd", wbuf[:, so:so + 4096], wvT[:, h + 2, :], "qw%d" % (h % 2))
    v_done = vev[(NCH - 1) % 2]

    PH(4)
    wait("scalar", "sve", ifv)
    op("scalar", lambda e: e.activation(
        lanC[96:104, 0:TPC], lanC[32:40, 0:TPC], AF.Sigmoid))
    drain("scalar")
    flv = op("scalar", lambda e: e.activation(
        lanA[64:72, 0:TPC], lanC[96:104, 0:TPC], AF.Ln), "sse")
    wait("vector", "sse", flv)
    op("vector", lambda e: e.memset(lanB[64:72, 0:TPC], 0.0))
    drain("vector")
    op("vector", lambda e: e.tensor_tensor_scan(
        lanB[96:104, 0:TPC], lanA[64:72, 0:TPC],
        lanB[64:72, 0:TPC], 0.0, ALU.add, ALU.add))
    drain("vector")
    op("vector", lambda e: e.tensor_tensor_scan(
        lanB[64:72, 0:TPC], lanA[64:72, 0:TPC],
        lanC[64:72, 0:TPC], NEG, ALU.add, ALU.max))
    drain("vector")
    op("vector", lambda e: e.tensor_copy(lanA[64:72, 0:TPC], lanB[96:104, 0:TPC]))
    drain("vector")
    op("vector", lambda e: e.tensor_sub(lanB[32:40, 0:TPC],
                                        lanC[64:72, 0:TPC], lanA[64:72, 0:TPC]))
    op("vector", lambda e: e.tensor_sub(lanA[32:40, 0:TPC],
                                        lanA[64:72, 0:TPC], lanB[64:72, 0:TPC]))
    drain("vector")
    e32p = 0
    for cc_ in range(NCH):
        sl = slice(cc_ * 128, cc_ * 128 + 128)
        bc0 = lanA[64:72, cc_ * 128 - 1:cc_ * 128] if cc_ > 0 else None
        bce = lanA[64:72, cc_ * 128 + 127:cc_ * 128 + 128]
        if bc0 is not None:
            op("vector", lambda e, sl=sl, bc0=bc0: e.tensor_scalar(
                lanA[0:8, sl], lanB[32:40, sl], bc0, None, ALU.add))
            op("vector", lambda e, sl=sl, bc0=bc0: e.tensor_scalar(
                lanC[0:8, sl], lanA[32:40, sl], bc0, None, ALU.subtract))
            op("vector", lambda e, cc_=cc_, bce=bce, bc0=bc0: e.tensor_scalar(
                gc2[0:8, 0:1], bce, bc0, None, ALU.subtract))
        else:
            op("vector", lambda e, sl=sl: e.tensor_copy(lanA[0:8, sl], lanB[32:40, sl]))
            op("vector", lambda e, sl=sl: e.tensor_copy(lanC[0:8, sl], lanA[32:40, sl]))
            op("vector", lambda e, bce=bce: e.tensor_copy(gc2[0:8, 0:1], bce))
        op("vector", lambda e, sl=sl, bce=bce: e.tensor_scalar(
            lanB[0:8, sl], lanB[32:40, sl], bce, None, ALU.add))
        drain("vector")
        e32p = op("vector", lambda e, sl=sl: e.tensor_scalar(
            lanC[32:40, sl], lanB[32:40, sl], 0.0, gc2[0:8, 0:1], ALU.mult, ALU.add), "sve")
        drain("vector")
    wait("scalar", "sve", e32p)
    for seg, base in [(lanA, 0), (lanB, 0), (lanC, 0)]:
        op("scalar", lambda e, seg=seg, base=base: e.activation(
            seg[base:base + 8, 0:TPC], seg[base:base + 8, 0:TPC], AF.Exp))
        drain("scalar")
    e32e = op("scalar", lambda e: e.activation(
        lanC[32:40, 0:TPC], lanC[32:40, 0:TPC], AF.Exp), "sse")
    wait("tensor", "sse", e32e)
    etv = 0
    esrc = [(lanA, 0), (lanB, 0), (lanC, 0), (lanC, 32)]
    for cc_ in range(NCH):
        for qi, (seg, base) in enumerate(esrc):
            if cc_ + qi >= 1:
                wait("tensor", "sve", etv)
            tv = op("tensor", lambda e, cc_=cc_, seg=seg, base=base: e.transpose(
                pG[0:128, 0:8], seg[base:base + 8, cc_ * 128:cc_ * 128 + 128],
                idf[base:base + 8, base:base + 8]), "ste")
            wait("vector", "ste", tv)
            etv = op("vector", lambda e, cc_=cc_, qi=qi: e.tensor_copy(
                e32t[:, cc_ * 32 + qi * 8:cc_ * 32 + qi * 8 + 8], pG[:, 0:8]), "sve")

    PH(5)
    wait("tensor", "sve", C["sve"])
    lastC16 = [C["sve"]] * NH
    gA3 = {0: 0, 1: 0}
    gK1 = {0: 0, 1: 0}
    gHq = {0: 0, 1: 0}
    gCb = {0: 0, 1: 0}
    for cc_ in range(NCH):
        for h in range(NH):
            par = (cc_ * NH + h) % 2
            pS = [pA, pB][par]
            pH = [pC, pD][par]
            pCa = [pE, pF][par]
            qsl = bqk[:, h * TPC + cc_ * 128: h * TPC + cc_ * 128 + 128]
            ksl = bqk[:, (NH + h) * TPC + cc_ * 128: (NH + h) * TPC + cc_ * 128 + 128]
            vsl = v1[:, cc_ * 2056 + 257 * h: cc_ * 2056 + 257 * h + 257]
            colfac = e32t[:, cc_ * 32 + h: cc_ * 32 + h + 1]
            kfac = e32t[:, cc_ * 32 + 8 + h: cc_ * 32 + 8 + h + 1]
            rowsc = e32t[:, cc_ * 32 + 16 + h: cc_ * 32 + 16 + h + 1]
            dcy = e32t[:, cc_ * 32 + 24 + h: cc_ * 32 + 24 + h + 1]
            wait("tensor", "sve", gA3[par])
            smm = op("tensor", lambda e, ksl=ksl, qsl=qsl, pS=pS: e.matmul(
                pS[:, 0:128], ksl, qsl, start=True, stop=True), "ste")
            wait("tensor", "sve", gK1[par])
            tmm = op("tensor", lambda e, ksl=ksl, par=par: e.transpose(
                pT[:, par * 512:par * 512 + 128], ksl, idb[:, :]), "ste")
            wait("vector", "ste", tmm)
            gA3[par] = op("vector", lambda e, pS=pS, colfac=colfac, par=par: e.scalar_tensor_tensor(
                a3s[:, par * 128:par * 128 + 128], pS[:, 0:128], colfac, mkb[:, :],
                ALU.mult, ALU.mult), "sve")
            gK1[par] = op("vector", lambda e, par=par, kfac=kfac: e.tensor_scalar(
                k1s[:, par * 128:par * 128 + 128], pT[:, par * 512:par * 512 + 128],
                kfac, None, ALU.mult), "sve")
            wait("tensor", "sve", gA3[par])
            wait("tensor", "sve", gHq[par])
            op("tensor", lambda e, par=par, vsl=vsl, pH=pH: e.matmul(
                pH[:, 0:257], a3s[:, par * 128:par * 128 + 128], vsl,
                start=True, stop=False), "ste")
            wait("tensor", "sve", lastC16[h])
            hmm = op("tensor", lambda e, qsl=qsl, h=h, pH=pH: e.matmul(
                pH[:, 0:257], qsl, c16[:, 257 * h:257 * h + 257],
                start=False, stop=True), "ste")
            wait("tensor", "sve", gCb[par])
            cmm = op("tensor", lambda e, par=par, vsl=vsl, pCa=pCa: e.matmul(
                pCa[:, 0:257], k1s[:, par * 128:par * 128 + 128], vsl,
                start=True, stop=True), "ste")
            wait("vector", "ste", hmm)
            gHq[par] = op("vector", lambda e, cc_=cc_, h=h, pH=pH, rowsc=rowsc: e.tensor_scalar(
                h16[:, cc_ * 2056 + 257 * h: cc_ * 2056 + 257 * h + 257],
                pH[:, 0:257], rowsc, None, ALU.mult), "sve")
            wait("vector", "ste", cmm)
            gCb[par] = op("vector", lambda e, h=h, dcy=dcy, pCa=pCa: e.scalar_tensor_tensor(
                cst[:, 257 * h:257 * h + 257], cst[:, 257 * h:257 * h + 257],
                dcy, pCa[:, 0:257], ALU.mult, ALU.add), "sve")
            drain("vector")
            lastC16[h] = op("vector", lambda e, h=h: e.tensor_copy(
                c16[:, 257 * h:257 * h + 257], cst[:, 257 * h:257 * h + 257]), "sve")
    passa_v = C["sve"]

    PH(6)
    wait("sync", "sve", passa_v)
    dma("sync", ag2_in[0, 0:NU].rearrange("(p f) -> p f", p=128), cst[:], "qio")
    dma("sync", ag2_in[0, NU:NU + 8].rearrange("(p f) -> p f", p=8),
        lanA[64:72, TPC - 1:TPC], "qio")
    ag2v = dma("sync", ag2_in[0, NU + 8:NU + 16].rearrange("(p f) -> p f", p=8),
               lanB[64:72, TPC - 1:TPC], "qio")
    wait("gpsimd", "qio", ag2v)
    cc2 = op("gpsimd", lambda e: e.collective_compute(
        "AllGather", ALU.bypass, ins=[ag2_in[:]], outs=[ag2_out[:]],
        replica_groups=[[0, 1, 2, 3], [4, 5, 6, 7]]), "scc")
    wait("sync", "scc", cc2)
    uld = 0
    for j in range(3):
        uld = dma("sync", uvw[:, j * 2056:(j + 1) * 2056],
                  ag2_out[j, 0:NU].rearrange("(p f) -> p f", p=128), "qio")
        uld = dma("sync", lanC[0:8, 2 * j:2 * j + 1],
                  ag2_out[j, NU:NU + 8].rearrange("(p f) -> p f", p=8), "qio")
        uld = dma("sync", lanC[0:8, 2 * j + 1:2 * j + 2],
                  ag2_out[j, NU + 8:NU + 16].rearrange("(p f) -> p f", p=8), "qio")

    PH(7)
    wait("vector", "qio", uld)
    op("vector", lambda e: e.memset(cst[:], 0.0))
    op("vector", lambda e: e.memset(lanC[0:8, 6:7], 0.0))
    for j in range(3):
        op("vector", lambda e, j=j: e.tensor_mul(lanC[0:8, 9:10], lanC[0:8, 2 * j + 1:2 * j + 2], ccf[:, j:j + 1]))
        op("vector", lambda e, j=j: e.tensor_mul(lanC[0:8, 10:11], lanC[0:8, 2 * j:2 * j + 1], ccf[:, j:j + 1]))
        drain("vector")
        op("vector", lambda e, j=j: e.tensor_add(lanC[0:8, 9:10], lanC[0:8, 9:10], ccn[:, j:j + 1]))
        op("vector", lambda e: e.tensor_add(lanC[0:8, 11:12], lanC[0:8, 10:11], lanC[0:8, 6:7]))
        drain("vector")
        op("vector", lambda e: e.tensor_max(lanC[0:8, 12:13], lanC[0:8, 11:12], lanC[0:8, 9:10]))
        drain("vector")
        op("vector", lambda e: e.tensor_sub(lanC[0:8, 13:14], lanC[0:8, 11:12], lanC[0:8, 12:13]))
        op("vector", lambda e: e.tensor_scalar(lanC[0:8, 14:15], lanC[0:8, 12:13], -1.0, None, ALU.mult))
        drain("vector")
        wex = op("vector", lambda e: e.tensor_copy(lanC[0:8, 6:7], lanC[0:8, 12:13]), "sve")
        wait("scalar", "sve", wex)
        op("scalar", lambda e: e.activation(lanC[0:8, 13:14], lanC[0:8, 13:14], AF.Exp))
        drain("scalar")
        sxv = op("scalar", lambda e: e.activation(lanC[0:8, 14:15], lanC[0:8, 14:15], AF.Exp), "sse")
        wait("vector", "sse", sxv)
        wnv = op("vector", lambda e, j=j: e.tensor_mul(lanC[0:8, 15:16], lanC[0:8, 14:15], ccf[:, j:j + 1]), "sve")
        wait("tensor", "sve", wnv)
        op("tensor", lambda e: e.transpose(
            pG[0:1, 0:8], lanC[0:8, 13:14], idf[0:8, 0:8]), "ste")
        t1v = op("tensor", lambda e: e.transpose(
            pG[0:1, 8:16], lanC[0:8, 15:16], idf[0:8, 0:8]), "ste")
        wait("vector", "ste", t1v)
        op("vector", lambda e: e.tensor_copy(wrowp[:, :], pG[0:1, 0:8]))
        t2v = op("vector", lambda e: e.tensor_copy(wrow1[:, :], pG[0:1, 8:16]), "sve")
        wait("tensor", "sve", t2v)
        b2 = op("tensor", lambda e: e.matmul(pG[:, 0:8], onesf[0:1, :], wrowp[0:1, :],
                                             start=True, stop=True), "ste")
        b2 = op("tensor", lambda e: e.matmul(pG[:, 8:16], onesf[0:1, :], wrow1[0:1, :],
                                             start=True, stop=True), "ste")
        wait("vector", "ste", b2)
        op("vector", lambda e: e.tensor_copy(wbcs[:, :], pG[:, 0:16]))
        drain("vector")
        for h in range(NH):
            op("vector", lambda e, j=j, h=h: e.tensor_scalar(
                scrA[:, 0:257],
                uvw[:, j * 2056 + 257 * h: j * 2056 + 257 * h + 257],
                wbcs[:, 8 + h:8 + h + 1], None, ALU.mult))
            drain("vector")
            op("vector", lambda e, h=h: e.scalar_tensor_tensor(
                cst[:, 257 * h:257 * h + 257], cst[:, 257 * h:257 * h + 257],
                wbcs[:, h:h + 1], scrA[:, 0:257], ALU.mult, ALU.add))
            drain("vector")
    p16v = op("vector", lambda e: e.tensor_copy(c16[:], cst[:]), "sve")

    PH(8)
    op("vector", lambda e: e.tensor_scalar(
        lanC[64:72, 0:TPC], lanA[64:72, 0:TPC], lanC[0:8, 6:7], None, ALU.add))
    drain("vector")
    op("vector", lambda e: e.tensor_max(
        lanA[64:72, 0:TPC], lanB[64:72, 0:TPC], lanC[64:72, 0:TPC]))
    drain("vector")
    op("vector", lambda e: e.tensor_sub(
        lanA[0:8, 0:TPC], lanB[64:72, 0:TPC], lanA[64:72, 0:TPC]))
    op("vector", lambda e: e.tensor_sub(
        lanB[0:8, 0:TPC], lanC[64:72, 0:TPC], lanA[64:72, 0:TPC]))
    e24v = op("vector", lambda e: e.tensor_scalar(
        lanC[0:8, 0:TPC], lanA[64:72, 0:TPC], -1.0, None, ALU.mult), "sve")
    wait("scalar", "sve", e24v)
    op("scalar", lambda e: e.activation(lanA[0:8, 0:TPC], lanA[0:8, 0:TPC], AF.Exp))
    drain("scalar")
    op("scalar", lambda e: e.activation(lanB[0:8, 0:TPC], lanB[0:8, 0:TPC], AF.Exp))
    drain("scalar")
    e24e = op("scalar", lambda e: e.activation(lanC[0:8, 0:TPC], lanC[0:8, 0:TPC], AF.Exp), "sse")
    wait("tensor", "sse", e24e)
    etv = 0
    esrc24 = [(lanA, 0), (lanB, 0), (lanC, 0)]
    for cc_ in range(NCH):
        for qi, (seg, base) in enumerate(esrc24):
            if cc_ + qi >= 1:
                wait("tensor", "sve", etv)
            tv = op("tensor", lambda e, cc_=cc_, seg=seg, base=base: e.transpose(
                pG[0:128, 0:8], seg[base:base + 8, cc_ * 128:cc_ * 128 + 128],
                idf[base:base + 8, base:base + 8]), "ste")
            wait("vector", "ste", tv)
            op("vector", lambda e, cc_=cc_, qi=qi: e.tensor_copy(
                e24t[:, cc_ * 24 + qi * 8:cc_ * 24 + qi * 8 + 8], pG[:, 0:8]))
            drain("vector")
            etv = op("vector", lambda e, cc_=cc_, qi=qi: e.tensor_copy(
                e24b[:, cc_ * 24 + qi * 8:cc_ * 24 + qi * 8 + 8],
                e24t[:, cc_ * 24 + qi * 8:cc_ * 24 + qi * 8 + 8]), "sve")

    PH(9)
    fg_off()
    wait("tensor", "sve", C["sve"])
    gH2 = {0: 0, 1: 0}
    gTo = {0: 0, 1: 0}
    for cc_ in range(NCH):
        tslab = op("vector", lambda e, cc_=cc_: e.tensor_tensor(
            t1s[:, 0:2056].rearrange("p (h f) -> p h f", f=257),
            h16[:, cc_ * 2056: cc_ * 2056 + 2056].rearrange("p (h f) -> p h f", f=257),
            e24b[:, cc_ * 24: cc_ * 24 + 8].unsqueeze(2).to_broadcast([128, 8, 257]),
            ALU.mult), "sve")
        drain("vector")
        for h in range(NH):
            par = h % 2
            p = [pA, pB][par]
            qsl = bqk[:, h * TPC + cc_ * 128: h * TPC + cc_ * 128 + 128]
            wait("tensor", "sve", gH2[par])
            if h == 0:
                wait("tensor", "sve", tslab)
            h2mm = op("tensor", lambda e, qsl=qsl, h=h, p=p: e.matmul(
                p[:, 0:257], qsl, c16[:, 257 * h:257 * h + 257], start=True, stop=True), "ste")
            wait("vector", "ste", h2mm)
            gH2[par] = op("vector", lambda e, h=h, p=p, cc_=cc_: e.scalar_tensor_tensor(
                cst[:, 257 * h: 257 * h + 257], p[:, 0:257],
                e24t[:, cc_ * 24 + 8 + h: cc_ * 24 + 8 + h + 1],
                t1s[:, 257 * h: 257 * h + 257], ALU.mult, ALU.add), "sve")
        drain("vector")
        op("vector", lambda e: e.tensor_scalar(
            sq8[:, 40:48], cst[:, 256:2056:257], -1.0, None, ALU.mult))
        drain("vector")
        op("vector", lambda e: e.tensor_max(
            sq8[:, 0:8], sq8[:, 40:48], cst[:, 256:2056:257]))
        drain("vector")
        op("vector", lambda e, cc_=cc_: e.tensor_max(
            sq8[:, 0:8], sq8[:, 0:8], e24t[:, cc_ * 24 + 16: cc_ * 24 + 24]))
        drain("vector")
        op("vector", lambda e: e.tensor_scalar(sq8[:, 0:8], sq8[:, 0:8], 1e-6, None, ALU.add))
        drain("vector")
        op("vector", lambda e: e.reciprocal(sq8[:, 8:16], sq8[:, 0:8]))
        drain("vector")
        hnv = op("vector", lambda e: e.tensor_tensor(
            t1s[:, 0:2048].rearrange("p (h f) -> p h f", f=256),
            cst[:, 0:2056].rearrange("p (h f) -> p h f", f=257)[:, :, 0:256],
            sq8[:, 8:16].unsqueeze(2).to_broadcast([128, 8, 256]), ALU.mult), "sve")
        drain("vector")
        op("vector", lambda e: e.reduce_sum(
            sq8[:, 16:24], t1s[:, 0:2048].rearrange("p (h f) -> p h f", f=256), axis=AX.X))
        wait("scalar", "sve", hnv)
        sqv = op("scalar", lambda e: e.activation(cst[:, 0:2048], t1s[:, 0:2048], AF.Square), "sse")
        wait("vector", "sse", sqv)
        op("vector", lambda e: e.reduce_sum(
            sq8[:, 24:32], cst[:, 0:2048].rearrange("p (h f) -> p h f", f=256), axis=AX.X))
        drain("vector")
        op("vector", lambda e: e.tensor_scalar(sq8[:, 16:24], sq8[:, 16:24], -1.0 / 256, None, ALU.mult))
        op("vector", lambda e: e.tensor_scalar(sq8[:, 24:32], sq8[:, 24:32], 1.0 / 256, None, ALU.mult))
        drain("vector")
        op("vector", lambda e: e.tensor_mul(sq8[:, 32:40], sq8[:, 16:24], sq8[:, 16:24]))
        drain("vector")
        op("vector", lambda e: e.tensor_sub(sq8[:, 24:32], sq8[:, 24:32], sq8[:, 32:40]))
        drain("vector")
        op("vector", lambda e: e.tensor_scalar(sq8[:, 24:32], sq8[:, 24:32], 1e-6, None, ALU.add))
        drain("vector")
        rvv = opd("vector", lambda e: e.reciprocal(sq8[:, 32:40], sq8[:, 24:32]), "sve")
        wait("scalar", "sve", rvv)
        rsv = op("scalar", lambda e: e.activation(sq8[:, 40:48], sq8[:, 32:40], AF.Sqrt), "sse")
        wait("vector", "sse", rsv)
        op("vector", lambda e: e.tensor_tensor(
            cst[:, 0:2048].rearrange("p (h f) -> p h f", f=256),
            t1s[:, 0:2048].rearrange("p (h f) -> p h f", f=256),
            sq8[:, 16:24].unsqueeze(2).to_broadcast([128, 8, 256]), ALU.add))
        drain("vector")
        op("vector", lambda e: e.tensor_tensor(
            cst[:, 0:2048].rearrange("p (h f) -> p h f", f=256),
            cst[:, 0:2048].rearrange("p (h f) -> p h f", f=256),
            sq8[:, 40:48].unsqueeze(2).to_broadcast([128, 8, 256]), ALU.mult))
        drain("vector")
        hgv = op("vector", lambda e: e.tensor_mul(hob[:, 0:2048], cst[:, 0:2048], gab[:, :]), "sve")
        wait("tensor", "sve", hgv)
        for vck in range(NKC):
            par = vck % 2
            wait("tensor", "sve", gTo[par])
            tv = op("tensor", lambda e, vck=vck, par=par: e.transpose(
                pT[:, par * 512:par * 512 + 128], hob[:, vck * 128:vck * 128 + 128],
                idb[:, :]), "ste")
            wait("vector", "ste", tv)
            gTo[par] = op("vector", lambda e, vck=vck, cc_=cc_, par=par: e.tensor_tensor(
                v1[:, vck * TPC + cc_ * 128: vck * TPC + cc_ * 128 + 128],
                pT[:, par * 512:par * 512 + 128],
                ogb[:, vck * TPC + cc_ * 128: vck * TPC + cc_ * 128 + 128], ALU.mult), "sve")
    passb_v = C["sve"]

    PH(10)
    fg_on()
    ring = [None] * 4
    xgrd = {0: 0, 1: 0}
    stgrd = {0: 0, 1: 0}
    for db in range(NKC):
        so = (db % 4) * 2048
        if ring[db % 4] is not None:
            wait("gpsimd", "ste", ring[db % 4])
        lv = dma("gpsimd", wbuf[:, so:so + 2048], woutT[:, db, :], "qw%d" % (db % 4))
        wait("tensor", "qw%d" % (db % 4), lv)
        if db == 0:
            wait("tensor", "sve", passb_v)
        for tb in range(NTB):
            p = [pC, pD][tb]
            sl2 = ((db * NTB + tb) % 2) * 512
            wait("sync", "sve", xgrd[(db * NTB + tb) % 2])
            xl = dma("sync", xre[:, sl2:sl2 + 512],
                     xT[db * 128: db * 128 + 128, tb * 512: tb * 512 + 512],
                     "qxr%d" % ((db * NTB + tb) % 2))
            wait("tensor", "sve", stgrd[tb])
            mmv = 0
            for kk in range(NKC):
                mmv = op("tensor", lambda e, so=so, kk=kk, tb=tb, p=p: e.matmul(
                    p[:, :], wbuf[:, so + kk * 128: so + kk * 128 + 128],
                    v1[:, kk * TPC + tb * 512: kk * TPC + tb * 512 + 512],
                    start=(kk == 0), stop=(kk == NKC - 1)), "ste")
            if tb == NTB - 1:
                ring[db % 4] = mmv
            wait("vector", "ste", mmv)
            wait("vector", "qxr%d" % ((db * NTB + tb) % 2), xl)
            wait("vector", "qst%d" % ((db * NTB + tb) % 2),
                 C["qst%d" % ((db * NTB + tb) % 2)])
            ev = op("vector", lambda e, db=db, p=p, sl2=sl2: e.scalar_tensor_tensor(
                x2f[:, sl2:sl2 + 512], p[:, :], modv[:, 32 + db:32 + db + 1],
                xre[:, sl2:sl2 + 512], ALU.mult, ALU.add), "sve")
            stgrd[tb] = ev
            wait("scalar", "sve", ev)
            dma("scalar", yT[db * 128:db * 128 + 128, tb * 512:tb * 512 + 512],
                x2f[:, sl2:sl2 + 512], "qst%d" % ((db * NTB + tb) % 2))
            drain("vector")
            x2v = op("vector", lambda e, db=db, tb=tb, sl2=sl2: e.tensor_copy(
                bqk[:, db * TPC + tb * 512: db * TPC + tb * 512 + 512],
                x2f[:, sl2:sl2 + 512]), "sve")
            xgrd[(db * NTB + tb) % 2] = x2v
    x2_qst = (C["qst0"], C["qst1"])

    PH(11)
    h2v = layer_norm(bqk, bqk, modv[:, 64:80], modv[:, 48:64], "sve", x2v)

    def h2sl(kk, tb, width=512):
        return bqk[:, kk * TPC + tb * 512: kk * TPC + tb * 512 + width]

    PH(12)
    gld = dma("gpsimd", wsm[:, 256:512], gateT[:, 0, :], "qw3")
    wait("tensor", "qw3", gld)
    wait("tensor", "sve", h2v)
    gmm = 0
    for tb in range(NTB):
        p = [pE, pF][tb]
        for kk in range(NKC):
            gmm = op("tensor", lambda e, kk=kk, tb=tb, p=p: e.matmul(
                p[0:16, :], wsm[:, 256 + kk * 16: 256 + kk * 16 + 16],
                h2sl(kk, tb), start=(kk == 0), stop=(kk == NKC - 1)), "ste")
    wait("vector", "ste", gmm)
    glv = 0
    for tb in range(NTB):
        glv = op("vector", lambda e, tb=tb: e.tensor_copy(
            lanC[0:16, tb * 512:tb * 512 + 512], [pE, pF][tb][0:16, :]), "sve")
    cwv = glv
    for tt in range(NCH):
        wait("tensor", "sve", cwv)
        ttv = op("tensor", lambda e, tt=tt: e.transpose(
            pG[0:128, 0:16], lanC[0:16, tt * 128:tt * 128 + 128],
            idf[0:16, 0:16]), "ste")
        wait("vector", "ste", ttv)
        op("vector", lambda e: e.tensor_copy(g16a[:], pG[:, 0:16]))
        drain("vector")
        nmx = op("vector", lambda e: e.reduce_max(gc1[:], g16a[:], axis=AX.X, negate=True), "sve")
        wait("scalar", "sve", nmx)
        exv = op("scalar", lambda e: e.activation(g16b[:], g16a[:], AF.Exp,
                                                  bias=gc1[:, 0:1], scale=1.0), "sse")
        wait("vector", "sse", exv)
        op("vector", lambda e: e.reduce_sum(gc2[:], g16b[:], axis=AX.X))
        drain("vector")
        op("vector", lambda e: e.reciprocal(gc1[:], gc2[:]))
        drain("vector")
        op("vector", lambda e: e.tensor_scalar(g16a[:], g16b[:], gc1[:, 0:1], None, ALU.mult))
        drain("vector")
        op("vector", lambda e: e.reduce_max(g4a[:], g16a[:].rearrange("p (g i) -> p g i", i=4), axis=AX.X))
        drain("vector")
        op("vector", lambda e: e.tensor_tensor(
            g16b[:].rearrange("p (g i) -> p g i", i=4),
            g4a[:].unsqueeze(1).to_broadcast([128, 4, 4]),
            g4a[:].unsqueeze(2).to_broadcast([128, 4, 4]), ALU.is_gt))
        op("vector", lambda e: e.tensor_tensor(
            g16d[:].rearrange("p (g i) -> p g i", i=4),
            g4a[:].unsqueeze(1).to_broadcast([128, 4, 4]),
            g4a[:].unsqueeze(2).to_broadcast([128, 4, 4]), ALU.is_equal))
        drain("vector")
        op("vector", lambda e: e.tensor_mul(g16d[:], g16d[:], t16c[:]))
        drain("vector")
        op("vector", lambda e: e.tensor_add(g16b[:], g16b[:], g16d[:]))
        drain("vector")
        op("vector", lambda e: e.reduce_sum(g4a[:], g16b[:].rearrange("p (g i) -> p g i", i=4), axis=AX.X))
        drain("vector")
        op("vector", lambda e: e.tensor_scalar(g4a[:], g4a[:], 2.0, None, ALU.is_lt))
        drain("vector")
        op("vector", lambda e: e.tensor_tensor(
            g16b[:].rearrange("p (g i) -> p g i", i=4),
            g16a[:].rearrange("p (g i) -> p g i", i=4),
            g4a[:].unsqueeze(2).to_broadcast([128, 4, 4]), ALU.mult))
        drain("vector")
        op("vector", lambda e: e.tensor_tensor(
            g256[:].rearrange("p (a b) -> p a b", b=16),
            g16b[:].unsqueeze(1).to_broadcast([128, 16, 16]),
            g16b[:].unsqueeze(2).to_broadcast([128, 16, 16]), ALU.is_gt))
        op("vector", lambda e: e.tensor_tensor(
            g256b[:].rearrange("p (a b) -> p a b", b=16),
            g16b[:].unsqueeze(1).to_broadcast([128, 16, 16]),
            g16b[:].unsqueeze(2).to_broadcast([128, 16, 16]), ALU.is_equal))
        drain("vector")
        op("vector", lambda e: e.tensor_mul(g256b[:], g256b[:], t256c[:]))
        drain("vector")
        op("vector", lambda e: e.tensor_add(g256[:], g256[:], g256b[:]))
        drain("vector")
        op("vector", lambda e: e.reduce_sum(g16d[:], g256[:].rearrange("p (a b) -> p a b", b=16), axis=AX.X))
        drain("vector")
        op("vector", lambda e: e.tensor_scalar(g16d[:], g16d[:], 4.0, None, ALU.is_lt))
        drain("vector")
        op("vector", lambda e: e.tensor_mul(g16b[:], g16b[:], g16d[:]))
        drain("vector")
        op("vector", lambda e: e.reduce_sum(gc2[:], g16b[:], axis=AX.X))
        drain("vector")
        op("vector", lambda e: e.tensor_scalar(gc2[:], gc2[:], 1e-20, None, ALU.add))
        drain("vector")
        op("vector", lambda e: e.reciprocal(gc1[:], gc2[:]))
        drain("vector")
        cwq = op("vector", lambda e: e.tensor_scalar(g16a[:], g16b[:], gc1[:, 0:1], None, ALU.mult), "sve")
        wait("tensor", "sve", cwq)
        tbv = op("tensor", lambda e, tt=tt: e.transpose(
            pG[0:16, 0:128], g16a[:], idf[:, :]), "ste")
        wait("vector", "ste", tbv)
        cwv = op("vector", lambda e, tt=tt: e.tensor_copy(
            cwl16[:, tt * 128:tt * 128 + 128], pG[0:16, 0:128]), "sve")

    PH(13)
    def aslice(s, tb):
        if s < 16:
            t_, base = h16, s * TPC
        elif s < 32:
            t_, base = ogb, (s - 16) * TPC
        else:
            t_, base = v1, (s - 32) * TPC
        return t_[:, base + tb * 512: base + tb * 512 + 512]

    wait("gpsimd", "ste", C["ste"])
    wait("gpsimd", "qst0", x2_qst[0])
    wait("gpsimd", "qst1", x2_qst[1])
    groups = [[0, 1, 2, 3], [4, 5, 6, 7], [8, 9, 10, 11], [12, 13, 14, 15], ["sh"]]
    upfree = [None, None]
    upld_i = 0
    cbe = cwv
    sgrd = {0: 0, 1: 0}
    ugrd = {0: 0, 1: 0}
    dn_guard = 0
    acc_guard = {0: 0, 1: 0}
    for grp in groups:
        wait("gpsimd", "qac0", C["qac0"])
        wait("gpsimd", "qac1", C["qac1"])
        shared = grp[0] == "sh"
        nexp = 1 if shared else len(grp)
        nj = 22 if shared else 11
        for ei in range(nexp):
            eidx = None if shared else grp[ei]
            if not shared:
                wait("sync", "sve", cbe)
                crv = dma("sync", cwrow[0:1, 0:TPC], cwl16[eidx:eidx + 1, :], "qio")
                for tb in range(NTB):
                    wait("tensor", "sve", cbe)
                    wait("tensor", "qio", crv)
                    cbv = op("tensor", lambda e, tb=tb: e.matmul(
                        pG[:, :], onesbf[0:1, :],
                        cwrow[0:1, tb * 512:tb * 512 + 512],
                        start=True, stop=True), "ste")
                    wait("vector", "ste", cbv)
                    cbe = op("vector", lambda e, tb=tb: e.tensor_copy(
                        cwbc[:, tb * 512:tb * 512 + 512], pG[:, :]), "sve")
            gmmv = 0
            for jj in range(nj):
                for half in range(2):
                    jb = jj + half * nj
                    slot = upld_i % 2
                    upld_i += 1
                    if upfree[slot] is not None:
                        wait("gpsimd", "ste", upfree[slot])
                    src = supT[:, jb, :] if shared else upT[eidx, :, jb, :]
                    lv = dma("gpsimd", wbuf[:, slot * 2048:slot * 2048 + 2048], src,
                             "qw%d" % slot)
                    wait("tensor", "qw%d" % slot, lv)
                    wait("tensor", "sse" if half == 0 else "sve",
                         sgrd[0] if half == 0 else ugrd[0])
                    mmv = 0
                    for kk in range(NKC):
                        for tb in range(NTB):
                            p = ([pA, pB] if half == 0 else [pC, pD])[tb]
                            mmv = op("tensor", lambda e, slot=slot, kk=kk, tb=tb, p=p: e.matmul(
                                p[:, :], wbuf[:, slot * 2048 + kk * 128: slot * 2048 + kk * 128 + 128],
                                h2sl(kk, tb), start=(kk == 0), stop=(kk == NKC - 1)), "ste")
                    upfree[slot] = mmv
                    if half == 0:
                        gmmv = mmv
                wait("scalar", "ste", gmmv)
                sgv = 0
                for tb in range(NTB):
                    sgv = op("scalar", lambda e, tb=tb: e.activation(
                        sgil[:, tb * 512:tb * 512 + 512], [pA, pB][tb][:, :], AF.Silu), "sse")
                sgrd[0] = sgv
                wait("vector", "ste", mmv)
                wait("vector", "sse", sgv)
                s_global = jj if shared else ei * 11 + jj
                for tb in range(NTB):
                    op("vector", lambda e, tb=tb: e.tensor_tensor(
                        tmul[:, tb * 512:tb * 512 + 512], sgil[:, tb * 512:tb * 512 + 512],
                        [pC, pD][tb][:, :], ALU.mult))
                drain("vector")
                av = 0
                for tb in range(NTB):
                    if shared:
                        av = op("vector", lambda e, s_global=s_global, tb=tb: e.tensor_copy(
                            aslice(s_global, tb), tmul[:, tb * 512:tb * 512 + 512]), "sve")
                    else:
                        av = op("vector", lambda e, s_global=s_global, tb=tb: e.tensor_mul(
                            aslice(s_global, tb), tmul[:, tb * 512:tb * 512 + 512],
                            cwbc[:, tb * 512:tb * 512 + 512]), "sve")
                ugrd[0] = av
        up_done = C["sve"]
        for db in range(NKC):
            wait("gpsimd", "ste", dn_guard)
            dls = 0
            for ei in range(nexp):
                slot = 4096 + ei * 1408
                src = sdownT[:, db, :] if shared else downT[grp[ei], :, db, :]
                dls = dma("gpsimd", wbuf[:, slot:slot + nj * 128], src, "qw2")
            wait("tensor", "qw2", dls)
            if db == 0:
                wait("tensor", "sve", up_done)
            for tb in range(NTB):
                p = [pE, pF][tb]
                wait("tensor", "sve", acc_guard[tb])
                mmv = 0
                for ei in range(nexp):
                    slot = 4096 + ei * 1408
                    for ik in range(nj):
                        s_global = ik if shared else ei * 11 + ik
                        mmv = op("tensor", lambda e, slot=slot, ik=ik, tb=tb, p=p,
                                 s_global=s_global, first=(ei == 0 and ik == 0),
                                 last=(ei == nexp - 1 and ik == nj - 1): e.matmul(
                            p[:, :], wbuf[:, slot + ik * 128: slot + ik * 128 + 128],
                            aslice(s_global, tb), start=first, stop=last), "ste")
                if tb == NTB - 1:
                    dn_guard = mmv
                wait("vector", "ste", mmv)
                sl3 = ((db * NTB + tb) % 2) * 512
                wait("vector", "qac%d" % ((db * NTB + tb) % 2),
                     C["qac%d" % ((db * NTB + tb) % 2)])
                e3 = op("vector", lambda e, db=db, p=p, sl3=sl3: e.tensor_scalar(
                    x2f[:, sl3:sl3 + 512], p[:, :], modv[:, 80 + db:80 + db + 1],
                    None, ALU.mult), "sve")
                acc_guard[tb] = e3
                wait("gpsimd", "sve", e3)
                dma("gpsimd", yT[db * 128:db * 128 + 128, tb * 512:tb * 512 + 512],
                    x2f[:, sl3:sl3 + 512], "qac%d" % ((db * NTB + tb) % 2), accum=True)

    PH(14)
    fg_off()
    wait("gpsimd", "qac0", C["qac0"])
    wait("gpsimd", "qac1", C["qac1"])
    wait("scalar", "qst0", C["qst0"])
    wait("scalar", "qst1", C["qst1"])

    if KPH < 99:
        cuts = sorted(k for k in PHM if k > KPH)
        if cuts:
            cev, cg, cr = PHM[cuts[0]]
            del EV[cev:]
            del EVG[cev:]
            for engn in ENGS:
                if cg:
                    EV.append((engn, lambda e, v=cg: e.wait_ge(GSEM, v)))
                for s, rv in cr.items():
                    if rv:
                        EV.append((engn, lambda e, S=SEM[s], v=rv: e.wait_ge(S, v)))
    else:
        fin = EVG[-1]
        for engn in ENGS:
            on(engn, lambda e, v=fin: e.wait_ge(GSEM, v))
            for s, rv in R.items():
                if rv:
                    on(engn, lambda e, S=SEM[s], v=rv: e.wait_ge(S, v))

    block = ctx.enter_context(nc.Block())

    def runner(eng_name):
        def body(eng):
            for nm, fn in EV:
                if nm == eng_name:
                    fn(eng)
        return body

    block.sync(runner("sync"))
    block.gpsimd(runner("gpsimd"))
    block.vector(runner("vector"))
    block.scalar(runner("scalar"))
    block.tensor(runner("tensor"))
    ctx.close()
    return nc


def _wtile(W, mb):
    Din, M = W.shape[1], W.shape[0]
    kb = Din // 128
    arr = np.ascontiguousarray(W.T).reshape(kb, 128, M // mb, mb)
    return np.ascontiguousarray(arr.transpose(1, 2, 0, 3)).reshape(128, M // mb, kb * mb)


_NC_CACHE = {}


def kernel(**inp):
    S = np.asarray(inp["L0_attn_x"]).shape[1]
    TPC = S // 4
    if TPC not in _NC_CACHE:
        _NC_CACHE[TPC] = build_nc(TPC)
    nc = _NC_CACHE[TPC]

    x = np.asarray(inp["L0_attn_x"], np.float32)
    cc = np.asarray(inp["L0_c"], np.float32)
    wmod = np.asarray(inp["L0_wmod"], np.float32)

    wq = _wtile(np.asarray(inp["L0_attn_Wq"], np.float32), 128)
    wk = _wtile(np.asarray(inp["L0_attn_Wk"], np.float32), 128)
    wv = _wtile(np.asarray(inp["L0_attn_Wv"], np.float32), 256)
    wog = _wtile(np.asarray(inp["L0_attn_Wog"], np.float32), 128)
    wif = _wtile(np.concatenate([np.asarray(inp["L0_attn_Wig"], np.float32),
                                 np.asarray(inp["L0_attn_Wfg"], np.float32)], 0), 16)
    wout = _wtile(np.asarray(inp["L0_attn_Wout"], np.float32), 128)
    gate = _wtile(np.asarray(inp["L0_ffn_gate_w"], np.float32), 16)
    up = np.stack([_wtile(np.asarray(inp["L0_ffn_expert_up_w"][e], np.float32), 128)
                   for e in range(E)])
    down = np.stack([_wtile(np.asarray(inp["L0_ffn_expert_down_w"][e], np.float32), 128)
                     for e in range(E)])
    sup = _wtile(np.asarray(inp["L0_ffn_shared_up_w"], np.float32), 128)
    sdn = _wtile(np.asarray(inp["L0_ffn_shared_down_w"], np.float32), 128)
    bigv = np.ascontiguousarray(np.asarray(inp["L0_attn_big"], np.float32)[:, None])
    bfgv = np.ascontiguousarray(np.asarray(inp["L0_attn_bfg"], np.float32)[:, None])
    gm = np.ascontiguousarray(np.asarray(inp["L0_attn_gamma"], np.float32).reshape(1, NH * DV))

    mask01 = np.ascontiguousarray(np.triu(np.ones((128, 128), np.float32)))
    identb = np.eye(128).astype(ml_dtypes.bfloat16)
    identf = np.eye(128, dtype=np.float32)
    tril16 = np.ascontiguousarray(
        np.tril(np.ones((4, 4), np.float32), -1).reshape(1, 16).repeat(128, 0))
    tril256 = np.ascontiguousarray(
        np.tril(np.ones((16, 16), np.float32), -1).reshape(1, 256).repeat(128, 0))

    in_maps = []
    for c in range(8):
        bidx, r = c // 4, c % 4
        xs = np.ascontiguousarray(x[bidx, r * TPC:(r + 1) * TPC, :].T)
        cT = np.ascontiguousarray(cc[bidx].reshape(NKC, 128).T)
        wms = np.ascontiguousarray(wmod[:, 3072 * r:3072 * (r + 1)])
        flag = np.zeros((8, 3), np.float32)
        flag[:, :r] = 1.0
        ccneg = (1.0 - flag) * NEG
        in_maps.append(dict(
            xT=xs, cTb=cT, wmod_sl=wms, wqT=wq, wkT=wk, wvT=wv, wogT=wog,
            wifT=wif, bifi=bigv, biff=bfgv, gamma=gm, woutT=wout, gateT=gate, upT=up,
            downT=down, supT=sup, sdownT=sdn, mask01=mask01, identb=identb,
            identf=identf, tril16=tril16, tril256=tril256,
            ccflag=flag, ccneg=ccneg))

    res = run_bass_kernel_spmd(nc, in_maps, core_ids=list(range(8)))
    out = np.empty((2, S, D), np.float32)
    for c in range(8):
        bidx, r = c // 4, c % 4
        out[bidx, r * TPC:(r + 1) * TPC, :] = res.results[c]["yT"].T
    return out
